# version 02
# baseline (speedup 1.0000x reference)
import numpy as np
import ml_dtypes
import concourse.bass as bass
import concourse.mybir as mybir
from concourse.bass_utils import run_bass_kernel_spmd

F32 = mybir.dt.float32
BF16 = mybir.dt.bfloat16
I32 = mybir.dt.int32
AF = mybir.ActivationFunctionType
ALU = mybir.AluOpType

H = 4096
DS = 4096
S = 128
NH = 64
HD = 64
KCONV = 4
CONV_DIM = DS + 2 * S
PROJ = 2 * DS + 2 * S + NH
NPROJ_T = 67
NK = 32
EPS = 1e-5
LNEPS = 1e-5

B = 2
L = 8192
TOWN = 2048
TRED = 2048
NCHUNK = TOWN // 128

NZT = 32
NXT = 34
FFN_RANK = 8
FFN_SCALE = 2.0

def _WS(pp, s):
    return (pp * 3 + s) * 128


def _WA(pp, s):
    return pp * 3 + s


def _WH(pp, s):
    return (pp * 2 + s) * 128


def _WF(pp, j):
    return pp * 4 + j


def _role_rows(q):
    par = q % 2
    opp = 1 - par
    ci = {
        "my_S_own": _WS(par, 0), "my_S_red": _WS(par, 1),
        "my_stage1": _WS(par, 2), "my_As_own": _WA(par, 0),
        "my_As_red": _WA(par, 1), "my_halo_own": _WH(par, 0),
        "my_halo_red": _WH(par, 1),
        "my_flag0": _WF(par, 0), "my_flag1": _WF(par, 1),
        "my_flag2": _WF(par, 2), "pa_flag0": _WF(opp, 0),
        "pa_flag1": _WF(opp, 1), "pa_flag2": _WF(opp, 2),
        "halo_src_own": _WH(par, 0), "halo_src_red": _WH(par, 1),
        "Sa1": _WS(opp, 0), "Sb1": _WS(opp, 0), "G1": _WA(par, 0),
        "Sa2": _WS(opp, 0), "Sb2": _WS(opp, 0), "G2": _WA(par, 0),
    }
    if q == 1:
        ci["halo_src_own"] = _WH(opp, 0)
    elif q == 2:
        ci["halo_src_own"] = _WH(par, 1)
        ci["halo_src_red"] = _WH(opp, 1)
        ci["Sa1"] = _WS(opp, 1)
        ci["G1"] = _WA(par, 1)
        ci["Sb1"] = _WS(par, 1)
    elif q == 3:
        ci["halo_src_own"] = _WH(opp, 0)
        ci["Sa2"] = _WS(opp, 2)
        ci["G2"] = _WA(opp, 0)
        ci["Sb2"] = _WS(opp, 0)
    return ci

CFG = {
    "phases": ("p1n_own", "p1i_own", "p1n_red", "p1i_red", "p2",
               "p3a", "p3b"),
    "debug": (),
}

_DBG_SHAPES = {
    "zsil_scr": ([DS, TOWN], BF16),
    "xbc_scr": ([CONV_DIM, TOWN + TRED], BF16),
    "dtp_dbg": ([NH, TOWN], F32),
    "dtpr_dbg": ([NH, TRED], F32),
    "y_scr": ([DS, TOWN], BF16),
    "attn_scr": ([H, TOWN], BF16),
    "x1f_scr": ([H, TOWN], BF16),
    "hS_dbg": ([6 * 128, NH * HD], F32),
    "hin_dbg": ([128, NH * HD], F32),
}


def _block_w(w, nk, nn):
    K, N = w.shape
    assert K == nk * 128
    wp = np.zeros((K, nn * 128), dtype=np.float32)
    wp[:, :N] = w
    wb = wp.reshape(nk, 128, nn, 128).transpose(2, 1, 0, 3)
    return np.ascontiguousarray(wb)


def _conv_diag(conv_w):
    out = np.zeros((NXT, KCONV, 128, 128), dtype=np.float32)
    idx = np.arange(128)
    for fb in range(NXT):
        for k in range(KCONV):
            out[fb, k, idx, idx] = conv_w[fb * 128:(fb + 1) * 128, k]
    return np.ascontiguousarray(out.transpose(2, 0, 1, 3))


def _expand_mat():
    E = np.zeros((64, 4, 128), dtype=np.float32)
    for j in range(4):
        for m in range(128):
            E[(j * 128 + m) // FFN_RANK, j, m] = 1.0
    return E


class P:
    def __init__(self, nc):
        self.nc = nc
        self.cnt = {}
        self.sems = {}
        self.eng = None
        self.E = None
        self.dyn = {}
        self.poll_ctr = 0
        self.pbody = 0
        self.noncereg_val = None
        self._pending = []

    def inc(self, name, by):
        self.cnt[name] = self.cnt.get(name, 0) + by
        return self.cnt[name]

    def val(self, name):
        return self.cnt.get(name, 0)

    def do(self, eng, fn, sem=None, by=None):
        inst = None
        if self.eng == eng:
            inst = fn(self.E)
            if self._pending:
                for r in self._pending:
                    self.E.free_register(r)
                self._pending = []
        if sem is not None:
            self.inc(sem, by)
            if inst is not None:
                inst.then_inc(self.sems[sem], by)
        return inst

    def roled(self, eng, build_fn, sem=None, by=None):
        if self.eng == eng:
            for role in self.E.Switch(self.role_rv, 4):
                inst = build_fn(self.E, _role_rows(role))
                if sem is not None and inst is not None:
                    inst.then_inc(self.sems[sem], by)
        if sem is not None:
            self.inc(sem, by)

    def wait(self, eng, sem, val=None):
        if val is None:
            val = self.val(sem)
        if val <= 0:
            return
        if self.eng == eng:
            self.E.wait_ge(self.sems[sem], val)


def build_program(cfg):
    nc = bass.Bass("TRN2", target_bir_lowering=False, debug=False)
    NTOK = TOWN + TRED

    ap = {}

    def ein(name, shape, dt=F32):
        ap[name] = nc.dram_tensor(name, shape, dt, kind="ExternalInput").ap()

    def eout(name, shape, dt=F32):
        ap[name] = nc.dram_tensor(name, shape, dt, kind="ExternalOutput").ap()

    ein("xin", [NTOK, H])
    ein("w_inproj", [NPROJ_T, 128, NK, 128])
    ein("w_outproj", [32, 128, NK, 128])
    ein("w_ffnA", [4, 128, NK, 128])
    ein("w_ffnB", [32, 128, 4, 128])
    ein("w_ffnQ", [128, NK, 16])
    ein("keysT", [16, 64])
    ein("convdiag", [128, NXT, KCONV, 128])
    ein("lnw_rep", [128, H])
    ein("nrmw_rep", [128, H])
    ein("nrmwc", [128, NK])
    ein("mupc", [128, NPROJ_T])
    ein("dt_bias", [NH, 1])
    ein("A_log", [NH, 1])
    ein("Dvec", [NH, 1])
    ein("expandE", [64, 4, 128])
    ein("ident", [128, 128])
    ein("tri", [128, 128])
    ein("Drep", [128, 32])
    ein("cfg_f", [1, 64])
    ein("cfg_i", [1, 64], I32)
    ein("nonce", [1, 16])

    eout("yout", [TOWN, H])
    for nm in cfg["debug"]:
        shp, dt = _DBG_SHAPES[nm]
        eout("dbg_" + nm, shp, dt)

    scr = {}

    def sloc(name, shape, dt=BF16):
        scr[name] = nc.dram_tensor(name, shape, dt).ap()

    sloc("zsil_scr", [DS, TOWN])
    sloc("dtp_scr", [128, 2048], F32)
    sloc("acum_scr", [128, 2048], F32)
    sloc("xbc_scr", [CONV_DIM, NTOK])
    sloc("y_scr", [DS, TOWN])
    sloc("attn_scr", [H, TOWN])
    sloc("x1f_scr", [H, TOWN])
    sloc("x1t_scr", [TOWN, H])
    sloc("out2t_scr", [TOWN, H])

    def sshr(name, shape, dt=F32):
        scr[name] = nc.dram_tensor(name, shape, dt, addr_space="Shared").ap()

    sshr("win_S", [2, 3, S, NH * HD])
    sshr("win_As", [2, 3, NH])
    scr["win_halo"] = nc.dram_tensor("win_halo", [2, 2, 128, NXT * 4], BF16, addr_space="Shared").ap()
    sshr("win_flag", [2, 4, 16])

    _emit(nc, ap, scr, cfg)
    return nc


def _emit(nc, ap, scr, cfg):
    import contextlib
    est = contextlib.ExitStack()

    p = P(nc)
    for s in ["sX0", "sX1", "sW0", "sW1", "sTr", "sSc0", "sSc1", "sScR",
              "sWin", "sV", "sA", "sP", "sG", "sPoll", "sEv0", "sEv1",
              "sOut"]:
        p.sems[s] = est.enter_context(nc.semaphore(s))

    sb = {}

    def sbuf(name, shape, dt):
        sb[name] = est.enter_context(nc.sbuf_tensor("sb_" + name, shape, dt))

    ps = {}

    def psum(name, shape, dt=F32):
        ps[name] = est.enter_context(nc.psum_tensor(name, shape, dt))

    sbuf("ar1", [128, 131072], mybir.dt.int8)
    sbuf("ar2", [128, 40960], mybir.dt.int8)
    sbuf("wbuf0", [128, NK * 128], BF16)
    sbuf("wbuf1", [128, NK * 128], BF16)
    sbuf("evst0", [128, 2048], BF16)
    sbuf("evst1", [128, 2048], BF16)
    sbuf("wrep", [128, H], BF16)
    sbuf("mupc", [128, NPROJ_T], F32)
    sbuf("consts", [128, 8], F32)
    sbuf("dtbias", [NH, 1], F32)
    sbuf("negA", [NH, 1], F32)
    sbuf("dvec", [NH, 1], F32)
    sbuf("small", [128, 256], F32)
    sbuf("identf", [128, 128], F32)
    sbuf("identb", [128, 128], BF16)
    sbuf("trib", [128, 128], BF16)
    sbuf("Drep", [128, 32], F32)
    sbuf("onesf", [1, 128], F32)
    sbuf("masks", [128, 64], F32)
    sbuf("noncebuf", [1, 16], F32)
    sbuf("cfgis", [1, 64], I32)
    sbuf("cfgfs", [1, 64], F32)
    sbuf("offsi", [128, 16], F32)
    sbuf("aclT", [128, 64], F32)
    sbuf("wT", [128, 64], F32)
    sbuf("B_T", [128, 128], BF16)
    sbuf("GT_sb", [128, 128], BF16)
    sbuf("Gexp", [128, 64], F32)
    sbuf("grow", [1, 64], F32)
    sbuf("halost", [128, 272], BF16)
    sbuf("halos4", [128, 544], BF16)
    sbuf("asums", [64, 16], F32)
    sbuf("nrmwc", [128, NK], F32)
    sbuf("onesb", [128, 1], BF16)
    sbuf("asumsT", [16, 64], F32)
    sbuf("flagrd", [1, 32], F32)

    _esz = {F32: 4, BF16: 2, I32: 4}

    def view(arena, off, parts, free, dt):
        return sb[arena][0:parts, off:off + free * _esz[dt]].bitcast(dt)

    psum("acc0", [128, 2048])
    psum("acc1", [128, 2048])

    p.sb_cfgis = sb["cfgis"]

    CONSTS = {}
    for i, v in enumerate((1.0, -1.0, float(EPS), float(LNEPS), 1.0 / H,
                           80.0, FFN_SCALE, 0.0)):
        CONSTS[v] = i
        nc.const_aps.register(F32, v, sb["consts"][:, i:i + 1])

    ALL_ENGS = ("sync", "gpsimd", "vector", "scalar", "tensor")

    def barrier():
        if cfg.get("snap") is not None:
            cfg["snap"].setdefault(p.eng, []).append(dict(p.cnt))
        for e in ALL_ENGS:
            for s in ("sV", "sA", "sP", "sEv0", "sEv1", "sSc0", "sSc1",
                      "sX0", "sX1", "sTr", "sW0", "sW1", "sG", "sWin",
                      "sOut"):
                p.wait(e, s)

    def phase_init():
        for v, i in CONSTS.items():
            def f(E, v=v, i=i):
                return E.memset(sb["consts"][:, i:i + 1], v)
            p.do("vector", f, "sV", 1)

        for nm_src, nm_dst in (("mupc", "mupc"), ("dt_bias", "dtbias"),
                               ("A_log", "negA"), ("Dvec", "dvec")):
            def f2(E, nm_src=nm_src, nm_dst=nm_dst):
                return E.dma_start(out=sb[nm_dst][:, :], in_=ap[nm_src])
            p.do("sync", f2, "sX0", 16)

        def f2b(E):
            return E.dma_start(out=sb["wrep"][:, :], in_=ap["lnw_rep"])
        p.do("gpsimd", f2b, "sW0", 16)
        for nm_src, nm_dst in (("ident", "identf"), ("Drep", "Drep"),
                               ("cfg_f", "cfgfs"), ("cfg_i", "cfgis"),
                               ("nonce", "noncebuf")):
            def f2c(E, nm_src=nm_src, nm_dst=nm_dst):
                return E.dma_start(out=sb[nm_dst][:, :], in_=ap[nm_src])
            p.do("sync", f2c, "sX0", 16)
        def f2cb(E):
            return E.dma_start(out=sb["identb"][:, :], in_=ap["ident"])
        p.do("gpsimd", f2cb, "sW1", 16)
        def f2cc(E):
            return E.dma_start(out=sb["trib"][:, :], in_=ap["tri"])
        p.do("gpsimd", f2cc, "sW1", 16)
        p.wait("vector", "sX0")
        p.wait("vector", "sW1")
        def f2d(E):
            E.memset(sb["halost"][:, :], 0.0)
            E.memset(sb["onesb"][:, :], 1.0)
            return E.memset(sb["onesf"][:, :], 1.0)
        p.do("vector", f2d, "sV", 1)
        p.wait("tensor", "sV")
        p.wait("tensor", "sX0")
        def f2e(E):
            return E.matmul(out=ps["acc0"][:, 0:64], lhsT=sb["onesf"][:, :],
                            rhs=sb["cfgfs"][0:1, 0:64], start=True, stop=True)
        p.do("tensor", f2e, "sP", 1)
        p.wait("scalar", "sP")
        def f2f(E):
            return E.activation(out=sb["masks"][:, :],
                                in_=ps["acc0"][:, 0:64], func=AF.Copy)
        p.do("scalar", f2f, "sEv0", 1)
        if p.eng == "sync":
            p.E.wait_ge(p.sems["sX0"], p.val("sX0"))
            nreg = p.E.alloc_register("nonceval")
            p.E.reg_load(nreg, sb["noncebuf"][0:1, 0:1].bitcast(I32))
            p.noncereg_val = p.E.snap(nreg)
            rreg = p.E.alloc_register("roleidx")
            pid = p.E.partition_id()
            p.E.reg_alu(rreg, pid, 4, ALU.mod)
            p.role_rv = p.E.snap(rreg)
        p.wait("scalar", "sV")
        p.wait("scalar", "sX0")
        def fneg(E):
            return E.activation(out=sb["negA"][:, :], in_=sb["negA"][:, :],
                                func=AF.Exp)
        p.do("scalar", fneg, "sA", 1)
        p.wait("vector", "sA")
        def fneg2(E):
            return E.tensor_scalar_mul(sb["negA"][:, :], sb["negA"][:, :],
                                       -1.0)
        p.do("vector", fneg2, "sV", 1)

    def phase_p1_norm(seg):
        T = TOWN if seg == "own" else TRED
        base = 0 if seg == "own" else TOWN
        nch = T // 128
        xnorm_f = view("ar1", 0, 128, NK * 2048, BF16)
        xc = [view("ar2", 0, 128, H, F32), view("ar2", 16384, 128, H, F32)]
        xnt = view("ar2", 32768, 128, H, BF16)
        marks = {"bx0": 0, "bx1": 0, "rs0": 0, "rs1": 0}
        for i in range(nch):
            pr = i % 2
            bx = xc[pr]
            rstd = sb["small"][:, pr:pr + 1]
            ssq = sb["small"][:, 2 + pr:3 + pr]
            p.wait("sync", "sV", marks[f"bx{pr}"])

            def fl(E, i=i, bx=bx):
                return E.dma_start(
                    out=bx,
                    in_=ap["xin"][base + i * 128: base + (i + 1) * 128, :])
            p.do("sync", fl, f"sX{pr}", 16)
            p.wait("scalar", f"sX{pr}")
            p.wait("scalar", "sV", marks[f"rs{pr}"])
            p.wait("scalar", "sTr")

            def fs(E, bx=bx, ssq=ssq):
                return E.activation(out=xnt, in_=bx,
                                    func=AF.Square, accum_out=ssq)
            p.do("scalar", fs, "sA", 1)
            p.wait("scalar", "sA")

            def fsq(E, ssq=ssq, rstd=rstd):
                return E.activation(out=rstd, in_=ssq,
                                    func=AF.Sqrt, scale=1.0 / H, bias=EPS)
            p.do("scalar", fsq, "sA", 1)
            p.wait("vector", "sA")

            def fr(E, rstd=rstd):
                return E.reciprocal(rstd, rstd)
            p.do("vector", fr, "sV", 1)
            p.wait("vector", "sV")
            p.wait("vector", "sTr")

            def fn(E, bx=bx, rstd=rstd):
                return E.scalar_tensor_tensor(
                    out=xnt, in0=bx, scalar=rstd,
                    in1=sb["wrep"][:, :], op0=ALU.mult, op1=ALU.mult)
            p.do("vector", fn, "sV", 1)
            marks[f"bx{pr}"] = p.val("sV")
            marks[f"rs{pr}"] = p.val("sV")
            p.wait("sync", "sV")

            def ft(E, i=i):
                outap = xnorm_f.rearrange(
                    "p (k t) -> p k t", k=NK)[:, :, i * 128:(i + 1) * 128]
                return E.dma_start(out=outap, in_=xnt, transpose=True)
            p.do("sync", ft, "sTr", 16)
        p.wait("tensor", "sTr")

    inproj_marks = {"w0": 0, "w1": 0, "p0": 0, "p1": 0, "e0": 0, "e1": 0,
                    "halo_w": 0}

    def phase_halo_export():
        p.wait("sync", "sV")
        def fh(E, rows):
            ro = rows["my_halo_own"]
            rr = rows["my_halo_red"]
            E.dma_start(
                out=scr["win_halo"].rearrange("a b p c -> (a b p) c")[
                    ro:ro + 128, :],
                in_=sb["halost"][:, 0:136]).then_inc(p.sems["sWin"], 16)
            return E.dma_start(
                out=scr["win_halo"].rearrange("a b p c -> (a b p) c")[
                    rr:rr + 128, :],
                in_=sb["halost"][:, 136:272])
        p.roled("sync", fh, "sWin", 16)
        p.inc("sWin", 16)
        inproj_marks["halo_w"] = p.val("sWin")
        p.wait("sync", "sWin")

    def phase_p1_inproj(seg):
        T = TOWN if seg == "own" else TRED
        base = 0 if seg == "own" else TOWN
        ntiles = list(range(NPROJ_T)) if seg == "own" \
            else list(range(NZT, NPROJ_T))
        ntt = T // 512
        mk = inproj_marks
        xnorm_f = view("ar1", 0, 128, NK * 2048, BF16)
        dtt1 = view("ar2", 0, 64, 2048, F32)
        dtt2 = view("ar2", 16384, 64, 2048, F32)
        dtrow = slice(0, 64) if seg == "own" else slice(64, 128)
        for j, n in enumerate(ntiles):
            pr = j % 2
            wb = sb["wbuf0"] if pr == 0 else sb["wbuf1"]
            pb = ps["acc0"] if pr == 0 else ps["acc1"]
            eb = sb["evst0"] if pr == 0 else sb["evst1"]
            p.wait("gpsimd", "sP", mk[f"w{pr}"])

            def fw(E, n=n, wb=wb):
                return E.dma_start(
                    out=wb[:, :].rearrange("p (k c) -> p k c", k=NK),
                    in_=ap["w_inproj"][n, :, :, :])
            p.do("gpsimd", fw, f"sW{pr}", 16)

            p.wait("tensor", f"sW{pr}")
            p.wait("tensor", f"sEv{pr}", mk[f"p{pr}"])

            def fm(E, wb=wb, pb=pb, ntt=ntt):
                last = None
                for k in range(NK):
                    lhsT = wb[:, :].rearrange(
                        "p (k c) -> p k c", k=NK)[:, k, :]
                    for t in range(ntt):
                        last = E.matmul(
                            out=pb[:, t * 512:(t + 1) * 512],
                            lhsT=lhsT,
                            rhs=xnorm_f.rearrange(
                                "p (k t) -> p k t",
                                k=NK)[:, k, t * 512:(t + 1) * 512],
                            start=(k == 0), stop=(k == NK - 1))
                return last
            p.do("tensor", fm, "sP", 1)
            mk[f"w{pr}"] = p.val("sP")

            if n < NZT:
                p.wait("scalar", "sP")
                p.wait("scalar", f"sSc{pr}", mk[f"e{pr}"])

                def fe(E, n=n, pb=pb, eb=eb, T=T):
                    return E.activation(
                        out=eb[:, 0:T], in_=pb[:, 0:T],
                        func=AF.Silu, scale=sb["mupc"][:, n:n + 1])
                p.do("scalar", fe, f"sEv{pr}", 1)
                mk[f"p{pr}"] = p.val(f"sEv{pr}")
                p.wait("sync", f"sEv{pr}")

                def fo(E, n=n, eb=eb, T=T, base=base):
                    return E.dma_start(
                        out=scr["zsil_scr"][n * 128:(n + 1) * 128,
                                            base:base + T],
                        in_=eb[:, 0:T])
                p.do("sync", fo, f"sSc{pr}", 16)
                mk[f"e{pr}"] = p.val(f"sSc{pr}")
            elif n < NZT + NXT:
                p.wait("vector", "sP")
                p.wait("vector", f"sSc{pr}", mk[f"e{pr}"])

                def fe(E, n=n, pb=pb, eb=eb, T=T):
                    return E.tensor_scalar_mul(
                        eb[:, 0:T], pb[:, 0:T], sb["mupc"][:, n:n + 1])
                p.do("vector", fe, f"sEv{pr}", 1)
                mk[f"p{pr}"] = p.val(f"sEv{pr}")
                p.wait("vector", f"sEv{pr}")
                p.wait("vector", "sWin", mk.get("halo_w", 0))
                hoff = 0 if seg == "own" else 136
                def fhc(E, n=n, eb=eb, T=T, hoff=hoff):
                    fb = n - NZT
                    return E.tensor_copy(
                        sb["halost"][:, hoff + fb * 4:hoff + fb * 4 + 3],
                        eb[:, T - 3:T])
                p.do("vector", fhc, "sV", 1)
                p.wait("sync", f"sEv{pr}")

                def fo(E, n=n, eb=eb, T=T, base=base):
                    fb = n - NZT
                    return E.dma_start(
                        out=scr["xbc_scr"][fb * 128:(fb + 1) * 128,
                                           base:base + T],
                        in_=eb[:, 0:T])
                p.do("sync", fo, f"sSc{pr}", 16)
                mk[f"e{pr}"] = p.val(f"sSc{pr}")
            else:
                p.wait("vector", "sP")

                def fe(E, pb=pb, T=T, dtt1=dtt1):
                    return E.tensor_scalar(
                        out=dtt1[:, 0:T], in0=pb[0:NH, 0:T],
                        scalar1=sb["mupc"][0:NH, NPROJ_T - 1:NPROJ_T],
                        scalar2=sb["dtbias"][:, :],
                        op0=ALU.mult, op1=ALU.add)
                p.do("vector", fe, f"sEv{pr}", 1)
                mk[f"p{pr}"] = p.val(f"sEv{pr}")
                p.wait("scalar", f"sEv{pr}")

                def fa1(E, T=T, dtt1=dtt1, dtt2=dtt2):
                    return E.activation(out=dtt2[:, 0:T],
                                        in_=dtt1[:, 0:T], func=AF.Abs)
                p.do("scalar", fa1, "sA", 1)
                p.wait("scalar", "sA")

                def fa2(E, T=T, dtt2=dtt2):
                    return E.activation(out=dtt2[:, 0:T], in_=dtt2[:, 0:T],
                                        func=AF.Exp, scale=-1.0)
                p.do("scalar", fa2, "sA", 1)
                p.wait("scalar", "sA")

                def fa3(E, T=T, dtt2=dtt2):
                    return E.activation(out=dtt2[:, 0:T], in_=dtt2[:, 0:T],
                                        func=AF.Ln, bias=1.0)
                p.do("scalar", fa3, "sA", 1)
                p.wait("vector", "sA")

                def fv1(E, T=T, dtt1=dtt1):
                    return E.tensor_relu(dtt1[:, 0:T], dtt1[:, 0:T])
                p.do("vector", fv1, "sV", 1)
                p.wait("vector", "sV")

                def fv2(E, T=T, dtt1=dtt1, dtt2=dtt2):
                    return E.tensor_add(dtt1[:, 0:T], dtt1[:, 0:T],
                                        dtt2[:, 0:T])
                p.do("vector", fv2, "sV", 1)
                p.wait("sync", "sV")

                def fdt(E, T=T, dtt1=dtt1, dtrow=dtrow):
                    return E.dma_start(out=scr["dtp_scr"][dtrow, 0:T],
                                       in_=dtt1[:, 0:T])
                p.do("sync", fdt, "sG", 16)


    def PV(t, a, b):
        return ps[t][:, a:b]

    def PVb(t, a, b):
        return ps[t][:, a:b].bitcast(BF16)

    def p2v():
        return {
            "xbcg": view("ar1", 0, 128, 34 * 516, BF16),
            "xsg": view("ar1", 35328, 128, 34 * 512, BF16),
            "S_sb": view("ar1", 70144, 128, 8192, BF16),
            "Ce_sb": view("ar1", 86528, 128, 8192, BF16),
            "cdg": view("ar1", 70144, 128, 34 * 4 * 128, BF16),
            "y_grp": view("ar1", 104960, 128, 32 * 128, BF16),
            "h_bf": view("ar1", 113152, 128, 4096, BF16),
            "acumP": view("ar1", 121344, 128, 2048, F32),
            "h": view("ar2", 0, 128, 4096, F32),
            "dtv": view("ar2", 16384, 64, 2048, F32),
            "xs_T": view("ar2", 16384, 128, 4096, BF16),
            "amlw": view("ar2", 24576, 128, 2048, F32),
            "acrow": view("ar2", 32768, 1, 2048, F32),
            "xsw_T": sb["wbuf0"][:, :].bitcast(BF16),
            "argb": sb["wbuf1"][:, 0:1024].bitcast(F32),
            "th0": sb["evst0"][:, 0:1024].bitcast(F32),
            "th1": sb["evst0"][:, 1024:2048].bitcast(F32),
            "hwA": view("ar1", 0, 128, 4096, F32),
            "hwB": view("ar1", 16384, 128, 4096, F32),
            "hst": view("ar1", 32768, 128, 4096, F32),
            "hin": view("ar1", 49152, 128, 4096, F32),
            "G1e": view("ar1", 65536, 128, 64, F32),
            "SaS": view("ar1", 66048, 128, 4096, F32),
            "SbS": view("ar1", 82432, 128, 4096, F32),
        }

    p2m = {"pb0": 0, "pb1": 0, "py0": 0, "py1": 0, "pc0": 0, "pc1": 0,
           "cv0": 0, "cv1": 0, "ptr": 0, "yg": 0, "acrow": 0, "sce": 0,
           "xbcg": 0, "hbf": 0, "hupd": [0] * 8, "th0": 0, "th1": 0,
           "xsTc": 0, "hread": [0] * 8, "exw0": 0, "exw1": 0,
           "p3acg": 0, "p3xb": 0, "p3cc": 0, "p3yz": 0, "p3ac": 0,
           "p3b0": 0, "p3b1": 0, "p3ce": 0, "p3y0": 0, "p3y1": 0,
           "p3rs": 0, "p3sq": 0, "p3rb": 0, "p3w0": 0, "p3w1": 0,
           "p3e0": 0, "p3e1": 0, "p3o0": 0, "p3o1": 0, "p3battn": 0, "p3yzE": 0, "p3yzV": 0,
           "p3g": 0, "p3za": 0, "p3zv": 0, "p3wb": 0, "p3bo0": 0,
           "p3bo1": 0, "p3ot0": 0, "p3ot1": 0, "p3f": 0}

    def rows_of(seg):
        return slice(0, 64)

    def phase_p2_prep(seg):
        V = p2v()
        ro = rows_of(seg)
        dtrow = slice(0, 64) if seg == "own" else slice(64, 128)
        def fl(E, dtrow=dtrow):
            return E.dma_start(out=V["dtv"][:, :],
                               in_=scr["dtp_scr"][dtrow, :])
        p.do("sync", fl, "sX0", 16)
        p.wait("vector", "sX0")
        def fadot(E):
            return E.tensor_scalar_mul(V["amlw"][0:64, :], V["dtv"][:, :],
                                       sb["negA"][:, :])
        p.do("vector", fadot, "sV", 1)
        p.wait("vector", "sV")
        for c in range(16):
            def fc(E, c=c):
                sl = slice(c * 128, (c + 1) * 128)
                return E.tensor_tensor_scan(
                    out=view("ar1", 121344, 128, 2048, F32)[ro, sl], data0=V["amlw"][0:64, sl],
                    data1=V["amlw"][0:64, sl], initial=0.0,
                    op0=ALU.add, op1=ALU.bypass)
            p.do("vector", fc, "sV", 1)
            p.wait("vector", "sV")
        def fg(E):
            return E.tensor_copy(
                sb["asums"][:, :],
                view("ar1", 121344, 128, 2048, F32)[ro, :].rearrange("p (c t) -> p c t",
                                             c=16)[:, :, 127])
        p.do("vector", fg, "sV", 1)
        p.wait("vector", "sV")
        def fz(E):
            return E.memset(sb["offsi"][ro, 0:1], 0.0)
        p.do("vector", fz, "sV", 1)
        p.wait("vector", "sV")
        def fsc(E):
            return E.tensor_tensor_scan(
                out=sb["offsi"][ro, 1:16],
                data0=sb["asums"][:, 0:15], data1=sb["asums"][:, 0:15],
                initial=0.0, op0=ALU.add, op1=ALU.bypass)
        p.do("vector", fsc, "sV", 1)
        ptr = PV("acc1", 1024, 1536)
        p.wait("tensor", "sV")
        p.wait("tensor", "sEv1", p2m["ptr"])
        def ftA(E):
            return E.transpose(out=ptr[0:16, 0:64], in_=sb["asums"][:, :],
                               identity=sb["identf"][0:64, 0:64])
        p.do("tensor", ftA, "sP", 1)
        p.wait("vector", "sP")
        def feA(E):
            return E.tensor_copy(sb["asumsT"][:, :], ptr[0:16, 0:64])
        p.do("vector", feA, "sEv1", 1)
        p2m["ptr"] = p.val("sEv1")
        p.wait("scalar", "sV")
        p.wait("scalar", "sX0")
        p.wait("scalar", "sEv1")
        def fln(E):
            return E.activation(out=V["amlw"][0:64, :], in_=V["dtv"][:, :],
                                func=AF.Ln)
        p.do("scalar", fln, "sA", 1)
        p.wait("vector", "sA")
        def faml(E):
            return E.tensor_sub(V["amlw"][0:64, :],
                                view("ar1", 121344, 128, 2048, F32)[ro, :],
                                V["amlw"][0:64, :])
        p.do("vector", faml, "sV", 1)
        p.wait("vector", "sV")
        for c in range(16):
            sl = slice(c * 128, (c + 1) * 128)
            def fw1(E, sl=sl, c=c):
                return E.tensor_scalar(
                    out=V["amlw"][64:128, sl], in0=V["amlw"][0:64, sl],
                    scalar1=sb["asums"][:, c:c + 1], scalar2=None,
                    op0=ALU.subtract)
            p.do("vector", fw1, "sV", 1)
        p.wait("scalar", "sV")
        def fw2(E):
            return E.activation(out=V["amlw"][64:128, :],
                                in_=V["amlw"][64:128, :], func=AF.Exp,
                                scale=-1.0)
        p.do("scalar", fw2, "sA", 1)
        p.wait("vector", "sA")
        p.wait("vector", "sWin")
        def fh(E):
            E.memset(p2v()["h"][:, :], 0.0)
            return E.memset(p2v()["h_bf"][:, :], 0.0)
        p.do("vector", fh, "sV", 1)

    def phase_p2_conv_group(seg, g):
        V = p2v()
        base = 0 if seg == "own" else TOWN
        xsgv3 = V["xsg"][:, :].rearrange("p (f t) -> p f t", f=NXT)
        xbcg3 = V["xbcg"][:, :].rearrange("p (f t) -> p f t", f=NXT)
        p.wait("gpsimd", "sP", p2m["sce"])
        p.wait("gpsimd", "sSc0", p2m["yg"])
        def fcd(E):
            return E.dma_start(
                out=V["cdg"][:, :].rearrange("p (f k c) -> p f k c",
                                             f=NXT, k=KCONV),
                in_=ap["convdiag"])
        p.do("gpsimd", fcd, "sW0", 16)
        p.wait("sync", "sP", p2m["xbcg"])
        if g == 0:
            def fx(E, base=base):
                return E.dma_start(
                    out=xbcg3[:, :, 3:515],
                    in_=scr["xbc_scr"].rearrange(
                        "(f p) t -> p f t", p=128)[:, :, base:base + 512])
            p.do("sync", fx, "sX1", 16)
            if seg == "red" and g == 0:
                for kslot in range(4):
                    def fhl(E, kslot=kslot):
                        return E.dma_start(
                            out=sb["halos4"][:, kslot * 136:
                                             (kslot + 1) * 136],
                            in_=scr["win_halo"].rearrange(
                                "a b p c -> (a b p) c")[
                                kslot * 128:(kslot + 1) * 128, :])
                    p.do("sync", fhl, "sX0", 16)
            p.wait("vector", "sX0")
            p.wait("vector", "sX1")
            mbase = 16 if seg == "own" else 20
            def fhz(E):
                return E.memset(xbcg3[:, :, 0:3], 0.0)
            p.do("vector", fhz, "sV", 1)
            p.wait("vector", "sV")
            for kslot in range(4):
                def fhm(E, mbase=mbase, kslot=kslot):
                    return E.scalar_tensor_tensor(
                        out=xbcg3[:, :, 0:3],
                        in0=sb["halos4"][:, kslot * 136:(kslot + 1) * 136]
                        .rearrange("p (f t) -> p f t", f=NXT)[:, :, 0:3],
                        scalar=sb["masks"][:, mbase + kslot:
                                           mbase + kslot + 1],
                        in1=xbcg3[:, :, 0:3], op0=ALU.mult, op1=ALU.add)
                p.do("vector", fhm, "sV", 1)
                p.wait("vector", "sV")
            p.wait("tensor", "sV")
        else:
            def fx(E, base=base, g=g):
                return E.dma_start(
                    out=xbcg3[:, :, 0:515],
                    in_=scr["xbc_scr"].rearrange(
                        "(f p) t -> p f t", p=128)[:, :,
                        base + g * 512 - 3:base + (g + 1) * 512])
            p.do("sync", fx, "sX1", 16)
        p.wait("tensor", "sX1")
        p.wait("tensor", "sW0")
        p.wait("scalar", "sV")
        p.wait("scalar", "sEv1")
        for fb in range(NXT):
            pcv = PV("acc1", 0, 512) if fb % 2 == 0 else PV("acc1", 512, 1024)
            prk = "cv0" if fb % 2 == 0 else "cv1"
            p.wait("tensor", "sEv0", p2m[prk])
            def fm(E, fb=fb, pcv=pcv):
                last = None
                for k in range(KCONV):
                    last = E.matmul(
                        out=pcv,
                        lhsT=V["cdg"][:, :].rearrange(
                            "p (f k c) -> p f k c", f=NXT,
                            k=KCONV)[:, fb, k, :],
                        rhs=xbcg3[:, fb, k:k + 512],
                        start=(k == 0), stop=(k == KCONV - 1))
                return last
            p.do("tensor", fm, "sP", 1)
            p.wait("scalar", "sP")
            def fs(E, fb=fb, pcv=pcv):
                return E.activation(out=xsgv3[:, fb, :], in_=pcv,
                                    func=AF.Silu)
            p.do("scalar", fs, "sEv0", 1)
            p2m[prk] = p.val("sEv0")
        p2m["xbcg"] = p.val("sP")
        p.wait("tensor", "sEv0")

    def phase_p2_chunk(seg, c, full):
        V = p2v()
        ro = rows_of(seg)
        rbase = 0
        gl = c % 4
        xsgv = V["xsg"][:, :].rearrange("p (f t) -> p f t", f=NXT)
        ptr = PV("acc1", 1024, 1536)
        ptrb = PVb("acc1", 1536, 2048)

        def pe_transpose_to(dst_sb, src_ap, bf, width, ibase=0):
            p.wait("tensor", "sEv1", p2m["ptr"])
            pdst = ptrb if bf else ptr
            ident = sb["identb"] if bf else sb["identf"]
            iw = src_ap.shape[0]
            def ft(E, pdst=pdst, src_ap=src_ap, ident=ident, iw=iw,
                   width=width, ibase=ibase):
                return E.transpose(
                    out=pdst[:, 0:width], in_=src_ap,
                    identity=ident[ibase:ibase + iw, ibase:ibase + iw])
            p.do("tensor", ft, "sP", 1)
            p.wait("vector", "sP")
            def fe(E, dst_sb=dst_sb, pdst=pdst, width=width):
                return E.tensor_copy(dst_sb, pdst[:, 0:width])
            p.do("vector", fe, "sEv1", 1)
            p2m["ptr"] = p.val("sEv1")

        pe_transpose_to(sb["wT"][:, :],
                        V["amlw"][64:128, c * 128:(c + 1) * 128], False, 64,
                        ibase=64)
        if full:
            pe_transpose_to(sb["aclT"][:, :],
                            V["amlw"][0:64, c * 128:(c + 1) * 128], False, 64)
        pe_transpose_to(sb["B_T"][:, :], xsgv[:, 32, gl * 128:(gl + 1) * 128],
                        True, 128)
        if full:
            p.wait("tensor", "sEv1", p2m["ptr"])
            def fgt(E, gl=gl):
                return E.matmul(out=ptr[:, 0:128],
                                lhsT=xsgv[:, 32, gl * 128:(gl + 1) * 128],
                                rhs=xsgv[:, 33, gl * 128:(gl + 1) * 128],
                                start=True, stop=True)
            p.do("tensor", fgt, "sP", 1)
            p.wait("vector", "sP")
            def fegt(E):
                return E.tensor_mul(sb["GT_sb"][:, :], ptr[:, 0:128],
                                    sb["trib"][:, :])
            p.do("vector", fegt, "sEv1", 1)
            p2m["ptr"] = p.val("sEv1")

        p.wait("vector", "sP", p2m["xsTc"])
        for fb in range(32):
            p.wait("tensor", "sEv1", p2m["ptr"])
            def ftx(E, fb=fb, gl=gl):
                return E.transpose(out=ptrb[:, 0:128],
                                   in_=xsgv[:, fb, gl * 128:(gl + 1) * 128],
                                   identity=sb["identb"][:, :])
            p.do("tensor", ftx, "sP", 1)
            p.wait("vector", "sP")
            if full:
                def fxr(E, fb=fb):
                    return E.tensor_copy(
                        V["xs_T"][:, fb * 128:(fb + 1) * 128], ptrb[:, 0:128])
                p.do("vector", fxr, "sV", 1)
            def fxw(E, fb=fb):
                in1 = sb["wT"][:, 2 * fb:2 * fb + 2].rearrange(
                    "p (r o) -> p r o", o=1).to_broadcast([128, 2, 64])
                return E.tensor_tensor(
                    out=V["xsw_T"][:, fb * 128:(fb + 1) * 128].rearrange(
                        "p (r q) -> p r q", r=2),
                    in0=ptrb[:, 0:128].rearrange("p (r q) -> p r q", r=2),
                    in1=in1, op=ALU.mult)
            p.do("vector", fxw, "sEv1", 1)
            p2m["ptr"] = p.val("sEv1")

        if full:
            p.wait("vector", "sEv1")
            for q in range(16):
                if q % 4 == 0:
                    q4 = q // 4
                    p.wait("sync", "sP", p2m["acrow"])
                    p.wait("sync", "sV")
                    def fgr(E, q4=q4, c=c):
                        srcg = view("ar1", 121344, 128, 2048, F32)[
                            rbase + 16 * q4:rbase + 16 * (q4 + 1),
                            c * 128:(c + 1) * 128]
                        dst = V["acrow"][0:1, :].rearrange(
                            "o (r t) -> o r t", r=16)
                        return E.dma_start(out=dst, in_=srcg)
                    p.do("sync", fgr, "sTr", 16)
                    p.wait("tensor", "sTr")
                pb = PV("acc0", 0, 512) if q % 2 == 0 else PV("acc0", 512,
                                                              1024)
                pbk = "pb0" if q % 2 == 0 else "pb1"
                p.wait("tensor", "sEv0", p2m[pbk])
                def fbc(E, q=q, pb=pb):
                    a0 = (q % 4) * 512
                    return E.matmul(out=pb, lhsT=sb["onesf"][:, :],
                                    rhs=V["acrow"][0:1, a0:a0 + 512],
                                    start=True, stop=True)
                p.do("tensor", fbc, "sP", 1)
                if q % 4 == 3:
                    p2m["acrow"] = p.val("sP")
                p.wait("vector", "sP")
                p.wait("vector", "sA")
                def fsub(E, q=q, pb=pb):
                    in1 = sb["aclT"][:, 4 * q:4 * q + 4].rearrange(
                        "p (r o) -> p r o", o=1).to_broadcast([128, 4, 128])
                    return E.tensor_tensor(
                        out=V["argb"][:, :].rearrange(
                            "p (r t) -> p r t", r=4),
                        in0=pb.rearrange("p (r t) -> p r t", r=4),
                        in1=in1, op=ALU.subtract)
                p.do("vector", fsub, "sV", 1)
                p.wait("vector", "sV")
                def fcl(E):
                    return E.tensor_scalar_min(V["argb"][:, :],
                                               V["argb"][:, :], 80.0)
                p.do("vector", fcl, "sV", 1)
                p.wait("scalar", "sV")
                def fes(E, q=q):
                    return E.activation(
                        out=V["S_sb"][:, q * 512:(q + 1) * 512],
                        in_=V["argb"][:, :], func=AF.Exp)
                p.do("scalar", fes, "sA", 1)
                def fee(E, q=q, pb=pb):
                    return E.activation(
                        out=V["Ce_sb"][:, q * 512:(q + 1) * 512],
                        in_=pb, func=AF.Exp)
                p.do("scalar", fee, "sEv0", 1)
                p2m[pbk] = p.val("sEv0")
                p.wait("vector", "sA")
                p.wait("vector", "sEv0")
                def fsm(E, q=q):
                    in1 = sb["GT_sb"][:, :].rearrange(
                        "p (o t) -> p o t", o=1).to_broadcast([128, 4, 128])
                    return E.tensor_tensor(
                        out=V["S_sb"][:, q * 512:(q + 1) * 512].rearrange(
                            "p (r t) -> p r t", r=4),
                        in0=V["S_sb"][:, q * 512:(q + 1) * 512].rearrange(
                            "p (r t) -> p r t", r=4),
                        in1=in1, op=ALU.mult)
                p.do("vector", fsm, "sV", 1)
                def fcm(E, q=q, gl=gl):
                    in1 = xsgv[:, 33, gl * 128:(gl + 1) * 128].rearrange(
                        "p (o t) -> p o t", o=1).to_broadcast([128, 4, 128])
                    return E.tensor_tensor(
                        out=V["Ce_sb"][:, q * 512:(q + 1) * 512].rearrange(
                            "p (r t) -> p r t", r=4),
                        in0=V["Ce_sb"][:, q * 512:(q + 1) * 512].rearrange(
                            "p (r t) -> p r t", r=4),
                        in1=in1, op=ALU.mult)
                p.do("vector", fcm, "sV", 1)
            p2m["acrow"] = p.val("sP")

            p.wait("vector", "sSc0", p2m["yg"])
            p.wait("tensor", "sV")
            p.wait("tensor", "sA")
            for pp in range(32):
                py = (PV("acc0", 1024, 1536) if pp % 2 == 0
                      else PV("acc0", 1536, 2048))
                pyk = "py0" if pp % 2 == 0 else "py1"
                p.wait("tensor", "sEv1", p2m[pyk])
                p.wait("tensor", "sV", p2m["hbf"])
                def fy(E, pp=pp, py=py):
                    last = None
                    for rr in range(2):
                        r = 2 * pp + rr
                        po = py[64 * rr:64 * (rr + 1), 0:128]
                        last = E.matmul(
                            out=po,
                            lhsT=V["xs_T"][:, pp * 128 + 64 * rr:
                                           pp * 128 + 64 * (rr + 1)],
                            rhs=V["S_sb"][:, r * 128:(r + 1) * 128],
                            start=True, stop=False)
                        last = E.matmul(
                            out=po,
                            lhsT=V["h_bf"][:, pp * 128 + 64 * rr:
                                           pp * 128 + 64 * (rr + 1)],
                            rhs=V["Ce_sb"][:, r * 128:(r + 1) * 128],
                            start=False, stop=True)
                    return last
                p.do("tensor", fy, "sP", 1)
                p.wait("vector", "sP")
                def fye(E, pp=pp, py=py, gl=gl, c=c):
                    return E.scalar_tensor_tensor(
                        out=V["y_grp"][:, :].rearrange(
                            "p (f t) -> p f t", f=32)[:, pp, :],
                        in0=xsgv[:, pp, gl * 128:(gl + 1) * 128],
                        scalar=sb["Drep"][:, pp:pp + 1],
                        in1=py[:, 0:128],
                        op0=ALU.mult, op1=ALU.add)
                p.do("vector", fye, "sEv1", 1)
                p2m[pyk] = p.val("sEv1")
            p2m["sce"] = p.val("sP")

        def fgam(E, c=c):
            return E.dma_start(out=sb["grow"][:, :],
                               in_=sb["asumsT"][c:c + 1, :])
        p.do("sync", fgam, "sX0", 16)
        p.wait("tensor", "sX0")
        pbg = PV("acc0", 0, 512)
        p.wait("tensor", "sEv0", p2m["pb0"])
        def fbg(E):
            return E.matmul(out=pbg[:, 0:64], lhsT=sb["onesf"][:, :],
                            rhs=sb["grow"][0:1, :], start=True, stop=True)
        p.do("tensor", fbg, "sP", 1)
        p.wait("scalar", "sP")
        def feg(E):
            return E.activation(out=sb["Gexp"][:, :], in_=pbg[:, 0:64],
                                func=AF.Exp)
        p.do("scalar", feg, "sEv0", 1)
        p2m["pb0"] = p.val("sEv0")

        p.wait("gpsimd", "sEv0")
        p.wait("tensor", "sEv1")
        for w in range(8):
            pc = PV("acc1", 0, 512) if w % 2 == 0 else PV("acc1", 512, 1024)
            pck = "pc0" if w % 2 == 0 else "pc1"
            cvk = "cv0" if w % 2 == 0 else "cv1"
            p.wait("tensor", "sEv1", p2m[pck])
            p.wait("tensor", "sEv0", p2m[cvk])
            def fcb(E, w=w, pc=pc):
                return E.matmul(out=pc, lhsT=sb["B_T"][:, :],
                                rhs=V["xsw_T"][:, w * 512:(w + 1) * 512],
                                start=True, stop=True)
            p.do("tensor", fcb, "sP", 1)
            thv = V["th0"] if w % 2 == 0 else V["th1"]
            thk = "th0" if w % 2 == 0 else "th1"
            p.wait("gpsimd", "sEv1", p2m[thk])
            p.wait("gpsimd", "sEv1", p2m["hupd"][w])
            def fth(E, w=w, thv=thv):
                in1 = sb["Gexp"][:, 8 * w:8 * w + 8].rearrange(
                    "p (r o) -> p r o", o=1).to_broadcast([128, 8, 64])
                return E.tensor_tensor(
                    out=thv.rearrange("p (r q) -> p r q", r=8),
                    in0=V["h"][:, w * 512:(w + 1) * 512].rearrange(
                        "p (r q) -> p r q", r=8),
                    in1=in1, op=ALU.mult)
            p.do("gpsimd", fth, "sG", 1)
            p.wait("vector", "sG")
            p.wait("vector", "sP")
            def fhu(E, w=w, pc=pc, thv=thv):
                return E.tensor_add(V["h"][:, w * 512:(w + 1) * 512],
                                    thv, pc)
            p.do("vector", fhu, "sEv1", 1)
            p2m[pck] = p.val("sEv1")
            p2m["hupd"][w] = p.val("sEv1")
            p2m[thk] = p.val("sEv1")
        p2m["xsTc"] = p.val("sP")

        if full:
            p.wait("vector", "sEv1")
            def fhb(E):
                return E.tensor_copy(V["h_bf"][:, :], V["h"][:, :])
            p.do("vector", fhb, "sV", 1)
            p2m["hbf"] = p.val("sV")
            p.wait("sync", "sEv1")
            def fyf(E, c=c):
                src2 = V["y_grp"][:, :].rearrange("p (f t) -> p f t", f=32)
                dst = scr["y_scr"].rearrange(
                    "(f p) t -> p f t", p=128)[:, :,
                    c * 128:(c + 1) * 128]
                return E.dma_start(out=dst, in_=src2)
            p.do("sync", fyf, "sSc0", 16)
            p2m["yg"] = p.val("sSc0")

    def phase_p2_export(seg):
        V = p2v()
        ro = rows_of(seg)
        slot_key = "my_S_own" if seg == "own" else "my_S_red"
        as_key = "my_As_own" if seg == "own" else "my_As_red"
        p.wait("vector", "sEv1")
        p.wait("vector", "sV")
        def fat(E, ro=ro):
            return E.scalar_tensor_tensor(
                out=sb["small"][0:64, 8:9], in0=sb["offsi"][ro, 15:16],
                scalar=1.0, in1=sb["asums"][:, 15:16],
                op0=ALU.mult, op1=ALU.add)
        p.do("vector", fat, "sV", 1)
        p.wait("sync", "sV")
        p.wait("sync", "sEv1")
        def fws(E, rows, slot_key=slot_key, as_key=as_key):
            r = rows[slot_key]
            ra = rows[as_key]
            E.dma_start(
                out=scr["win_S"].rearrange("a b p n -> (a b p) n")[
                    r:r + 128, :],
                in_=V["h"][:, :]).then_inc(p.sems["sWin"], 16)
            return E.dma_start(
                out=scr["win_As"].rearrange("a b r -> (a b) r")[
                    ra:ra + 1, :],
                in_=sb["small"][0:64, 8:9])
        p.roled("sync", fws, "sWin", 16)
        p.inc("sWin", 16)
        if seg == "own":
            def fac(E):
                return E.dma_start(out=scr["acum_scr"][0:64, :],
                                   in_=view("ar1", 121344, 64, 2048, F32))
            p.do("sync", fac, "sWin", 16)
        p.wait("sync", "sWin")

    def write_flag(j):
        def ff(E, rows, j=j):
            r = rows[f"my_flag{j}"]
            return E.dma_start(
                out=scr["win_flag"].rearrange("a b c -> (a b) c")[r:r + 1, :],
                in_=sb["noncebuf"][:, :])
        p.roled("sync", ff, "sWin", 16)
        p.wait("sync", "sWin")

    def poll_flag(j):
        def fp(E, j=j):
            r0 = _WF(0, j)
            r1 = _WF(1, j)
            with E.register(f"pf{j}_{p.poll_ctr}a") as fv, \
                    E.register(f"pf{j}_{p.poll_ctr}b") as fw:
                E.reg_mov(fv, 1)
                with E.While(fv):
                    E.sem_clear(p.sems["sPoll"])
                    E.dma_start(
                        out=sb["flagrd"][:, 0:16],
                        in_=scr["win_flag"].rearrange(
                            "a b c -> (a b) c")[r0:r0 + 1, :]
                    ).then_inc(p.sems["sPoll"], 16)
                    E.dma_start(
                        out=sb["flagrd"][:, 16:32],
                        in_=scr["win_flag"].rearrange(
                            "a b c -> (a b) c")[r1:r1 + 1, :]
                    ).then_inc(p.sems["sPoll"], 16)
                    E.wait_ge(p.sems["sPoll"], 32)
                    E.reg_load(fv, sb["flagrd"][0:1, 0:1].bitcast(I32))
                    E.reg_sub(fv, fv, p.noncereg_val)
                    E.reg_load(fw, sb["flagrd"][0:1, 16:17].bitcast(I32))
                    E.reg_sub(fw, fw, p.noncereg_val)
                    E.reg_alu(fv, fv, fw, ALU.bitwise_or)
            return None
        if p.eng == "sync":
            fp(p.E)
            p.E.sem_inc(p.sems["sG"], 1)
        p.inc("sG", 1)
        p.poll_ctr += 1

    def exchange_combine(stage):
        V = p2v()
        dst = V["hst"] if stage == 1 else V["hin"]
        slots = ([(0, 0), (0, 1), (1, 0), (1, 1)] if stage == 1
                 else [(0, 0), (0, 1), (0, 2), (1, 0), (1, 1), (1, 2)])
        selA0 = 24 if stage == 1 else 36
        selB0 = 28 if stage == 1 else 42
        selG0 = 32 if stage == 1 else 48
        mgcol = 2 if stage == 1 else 5
        macol = 0 if stage == 1 else 3
        p.wait("sync", "sV")
        def fz(E, dst=dst):
            E.memset(V["SaS"][:, :], 0.0)
            E.memset(V["SbS"][:, :], 0.0)
            return E.memset(sb["grow"][:, :], 0.0)
        p.do("vector", fz, "sV", 1)
        for ki, (par, s) in enumerate(slots):
            wrow = _WS(par, s)
            arow = _WA(par, min(s, 1))
            wb = V["hwA"] if ki % 2 == 0 else V["hwB"]
            sxk = "sX0" if ki % 2 == 0 else "sX1"
            mk_key = f"exw{ki % 2}"
            p.wait("sync", "sV", p2m.get(mk_key, 0))
            def frd(E, wrow=wrow, wb=wb):
                return E.dma_start(
                    out=wb[:, :],
                    in_=scr["win_S"].rearrange("a b p n -> (a b p) n")[
                        wrow:wrow + 128, :])
            p.do("sync", frd, sxk, 16)
            def fga(E, arow=arow, ki=ki):
                return E.dma_start(
                    out=view("ar2", 32768, 1, 384, F32)[:, ki * 64:(ki + 1) * 64],
                    in_=scr["win_As"].rearrange("a b r -> (a b) r")[
                        arow:arow + 1, :])
            p.do("sync", fga, sxk, 16)
            p.wait("vector", sxk)
            p.wait("vector", "sV")
            def facc(E, ki=ki, wb=wb, selA0=selA0):
                return E.scalar_tensor_tensor(
                    out=V["SaS"][:, :], in0=wb[:, :],
                    scalar=sb["masks"][:, selA0 + ki:selA0 + ki + 1],
                    in1=V["SaS"][:, :], op0=ALU.mult, op1=ALU.add)
            p.do("vector", facc, "sV", 1)
            p.wait("vector", "sV")
            def facb(E, ki=ki, wb=wb, selB0=selB0):
                return E.scalar_tensor_tensor(
                    out=V["SbS"][:, :], in0=wb[:, :],
                    scalar=sb["masks"][:, selB0 + ki:selB0 + ki + 1],
                    in1=V["SbS"][:, :], op0=ALU.mult, op1=ALU.add)
            p.do("vector", facb, "sV", 1)
            p.wait("vector", "sV")
            def facg(E, ki=ki, selG0=selG0):
                return E.scalar_tensor_tensor(
                    out=sb["grow"][:, :],
                    in0=view("ar2", 32768, 1, 384, F32)[:, ki * 64:(ki + 1) * 64],
                    scalar=sb["masks"][0:1, selG0 + ki:selG0 + ki + 1],
                    in1=sb["grow"][:, :], op0=ALU.mult, op1=ALU.add)
            p.do("vector", facg, "sV", 1)
            p2m[f"exw{ki % 2}"] = p.val("sV")
        pbg = PV("acc0", 0, 512)
        p.wait("tensor", "sV")
        p.wait("tensor", "sEv0", p2m["pb0"])
        def fbg(E):
            return E.matmul(out=pbg[:, 0:64], lhsT=sb["onesf"][:, :],
                            rhs=sb["grow"][0:1, :], start=True, stop=True)
        p.do("tensor", fbg, "sP", 1)
        p.wait("scalar", "sP")
        def feg(E, mgcol=mgcol):
            return E.activation(out=V["G1e"][:, 0:64], in_=pbg[:, 0:64],
                                func=AF.Exp,
                                scale=sb["masks"][:, mgcol:mgcol + 1])
        p.do("scalar", feg, "sEv0", 1)
        p2m["pb0"] = p.val("sEv0")
        p.wait("scalar", "sEv0")
        def fmg(E, macol=macol):
            return E.activation(out=V["G1e"][:, 0:64], in_=V["G1e"][:, 0:64],
                                func=AF.Copy,
                                scale=sb["masks"][:, macol:macol + 1])
        p.do("scalar", fmg, "sA", 1)
        p.wait("vector", "sA")
        def fc1(E, dst=dst):
            in1 = V["G1e"][:, 0:64].rearrange(
                "p (r o) -> p r o", o=1).to_broadcast([128, 64, 64])
            return E.tensor_tensor(
                out=dst[:, :].rearrange("p (r q) -> p r q", r=64),
                in0=V["SaS"][:, :].rearrange("p (r q) -> p r q", r=64),
                in1=in1, op=ALU.mult)
        p.do("vector", fc1, "sV", 1)
        p.wait("vector", "sV")
        def fc2(E, dst=dst):
            return E.tensor_add(dst[:, :], dst[:, :], V["SbS"][:, :])
        p.do("vector", fc2, "sV", 1)
        p.wait("vector", "sV")
        if stage == 2:
            def fc3(E, dst=dst):
                return E.scalar_tensor_tensor(
                    out=dst[:, :], in0=V["hst"][:, :],
                    scalar=sb["masks"][:, 6:7], in1=dst[:, :],
                    op0=ALU.mult, op1=ALU.add)
            p.do("vector", fc3, "sV", 1)
            p.wait("vector", "sV")

    def phase_exchange():
        V = p2v()
        write_flag(1)
        poll_flag(1)
        p.wait("vector", "sG")
        p.wait("tensor", "sG")
        exchange_combine(1)
        p.wait("sync", "sV")
        def fws(E, rows):
            r = rows["my_stage1"]
            return E.dma_start(
                out=scr["win_S"].rearrange("a b p n -> (a b p) n")[
                    r:r + 128, :],
                in_=V["hst"][:, :])
        p.roled("sync", fws, "sWin", 16)
        p.wait("sync", "sWin")
        write_flag(2)
        poll_flag(2)
        p.wait("vector", "sG")
        p.wait("tensor", "sG")
        exchange_combine(2)
        hinb = view("ar2", 0, 128, 4096, BF16)
        def fhb(E, hinb=hinb):
            return E.tensor_copy(hinb, V["hin"][:, :])
        p.do("vector", fhb, "sV", 1)


    def p3v():
        return {
            "g_norm": view("ar1", 0, 128, NK * 2048, BF16),
            "hinb": view("ar2", 0, 128, 4096, BF16),
            "ytile": view("ar2", 8192, 128, 32 * 256, BF16),
            "ztile": view("ar2", 24576, 128, 32 * 256, BF16),
            "acrow3": sb["wbuf0"][:, :].bitcast(F32),
            "cdiag33": sb["evst0"][:, 0:512],
            "cchunk": sb["evst0"][:, 512:640],
            "acg": sb["evst1"][:, 0:256].bitcast(F32),
            "gsq": sb["evst1"][:, 1024:1536],
            "rstrow": view("ar2", 8192, 1, 2048, F32),
            "gtmp": sb["evst1"][:, 512:768].bitcast(F32),
        }

    def phase_p3a():
        V = p3v()
        gn3 = V["g_norm"][:, :].rearrange("p (f t) -> p f t", f=NK)
        def fnw(E):
            return E.dma_start(out=sb["wrep"][:, :], in_=ap["nrmw_rep"])
        p.do("gpsimd", fnw, "sW0", 16)
        def fcd(E):
            return E.dma_start(
                out=V["cdiag33"].rearrange("p (k c) -> p k c", k=KCONV),
                in_=ap["convdiag"][:, 33, :, :])
        p.do("gpsimd", fcd, "sW0", 16)
        def fnc(E):
            return E.dma_start(out=sb["nrmwc"][:, :], in_=ap["nrmwc"])
        p.do("sync", fnc, "sX0", 16)
        for c in range(16):
            p.wait("sync", "sV", p2m.get("p3acg", 0))
            def fac(E, c=c):
                return E.dma_start(
                    out=V["acg"][0:64, :],
                    in_=scr["acum_scr"][0:64, c * 128:(c + 1) * 128])
            p.do("sync", fac, "sX0", 16)
            p.wait("vector", "sX0")
            def fofs(E, c=c):
                return E.tensor_scalar(
                    out=V["acg"][0:64, :], in0=V["acg"][0:64, :],
                    scalar1=sb["offsi"][0:64, c:c + 1], scalar2=None,
                    op0=ALU.add)
            p.do("vector", fofs, "sV", 1)
            p2m["p3acg"] = p.val("sV")
            p.wait("sync", "sP", p2m.get("p3xb", 0))
            def fxb(E, c=c):
                lo = c * 128 - 3
                if c == 0:
                    return E.dma_start(
                        out=sb["halost"][:, 3:131],
                        in_=scr["xbc_scr"][33 * 128:34 * 128, 0:128])
                return E.dma_start(
                    out=sb["halost"][:, 0:131],
                    in_=scr["xbc_scr"][33 * 128:34 * 128, lo:lo + 131])
            p.do("sync", fxb, "sX1", 16)
            if c == 0:
                p.wait("vector", "sX1")
                def fhz(E):
                    return E.memset(sb["halost"][:, 0:3], 0.0)
                p.do("vector", fhz, "sV", 1)
                p.wait("vector", "sV")
                for kslot in range(4):
                    def fhm(E, kslot=kslot):
                        return E.scalar_tensor_tensor(
                            out=sb["halost"][:, 0:3],
                            in0=sb["halos4"][:, kslot * 136 + 33 * 4:
                                             kslot * 136 + 33 * 4 + 3],
                            scalar=sb["masks"][:, 16 + kslot:17 + kslot],
                            in1=sb["halost"][:, 0:3],
                            op0=ALU.mult, op1=ALU.add)
                    p.do("vector", fhm, "sV", 1)
                    p.wait("vector", "sV")
            p.wait("tensor", "sX1")
            p.wait("tensor", "sV")
            p.wait("tensor", "sW0")
            pcv = PV("acc1", 0, 512)
            p.wait("tensor", "sEv0", p2m.get("p3cc", 0))
            def fcm(E, c=c):
                last = None
                for k in range(KCONV):
                    last = E.matmul(
                        out=pcv[:, 0:128],
                        lhsT=V["cdiag33"].rearrange(
                            "p (k cc) -> p k cc", k=KCONV)[:, k, :],
                        rhs=sb["halost"][:, k:k + 128],
                        start=(k == 0), stop=(k == KCONV - 1))
                return last
            p.do("tensor", fcm, "sP", 1)
            p2m["p3xb"] = p.val("sP")
            p.wait("scalar", "sP")
            def fsl(E):
                return E.activation(out=V["cchunk"], in_=pcv[:, 0:128],
                                    func=AF.Silu)
            p.do("scalar", fsl, "sEv0", 1)
            p2m["p3cc"] = p.val("sEv0")
            if c % 2 == 0:
                p.wait("sync", "sEv1", p2m.get("p3yzE", 0))
                p.wait("sync", "sV", p2m.get("p3yzV", 0))
                def fyl(E, c=c):
                    return E.dma_start(
                        out=V["ytile"][:, :].rearrange(
                            "p (f t) -> p f t", f=32),
                        in_=scr["y_scr"].rearrange(
                            "(f p) t -> p f t",
                            p=128)[:, :, c * 128:(c + 2) * 128])
                p.do("sync", fyl, "sSc0", 16)
                def fzl(E, c=c):
                    return E.dma_start(
                        out=V["ztile"][:, :].rearrange(
                            "p (f t) -> p f t", f=32),
                        in_=scr["zsil_scr"].rearrange(
                            "(f p) t -> p f t",
                            p=128)[:, :, c * 128:(c + 2) * 128])
                p.do("sync", fzl, "sSc1", 16)
            for q4 in range(4):
                p.wait("sync", "sP", p2m.get("p3ac", 0))
                p.wait("sync", "sV")
                def fgr(E, q4=q4):
                    srcg = V["acg"][16 * q4:16 * (q4 + 1), :]
                    dst = V["acrow3"][0:1, 0:2048].rearrange(
                        "o (r t) -> o r t", r=16)
                    return E.dma_start(out=dst, in_=srcg)
                p.do("sync", fgr, "sTr", 16)
                p.wait("tensor", "sTr")
                for qq in range(4):
                    q = q4 * 4 + qq
                    pb = PV("acc0", 0, 512) if q % 2 == 0 \
                        else PV("acc0", 512, 1024)
                    pbk = "p3b0" if q % 2 == 0 else "p3b1"
                    p.wait("tensor", "sEv0", p2m.get(pbk, 0))
                    def fbc(E, qq=qq, pb=pb):
                        return E.matmul(out=pb, lhsT=sb["onesf"][:, :],
                                        rhs=V["acrow3"][0:1,
                                                        qq * 512:
                                                        (qq + 1) * 512],
                                        start=True, stop=True)
                    p.do("tensor", fbc, "sP", 1)
                    if qq == 3:
                        p2m["p3ac"] = p.val("sP")
                    p.wait("scalar", "sP")
                    p.wait("scalar", "sV", p2m.get("p3ce", 0))
                    def fee(E, q=q, pb=pb):
                        return E.activation(
                            out=sb["wbuf1"][:, :].rearrange(
                                "p (r t) -> p r t", r=8)[:, q % 8, :]
                            if False else
                            sb["wbuf1"][:, (q % 8) * 512:(q % 8 + 1) * 512],
                            in_=pb, func=AF.Exp)
                    p.do("scalar", fee, "sEv0", 1)
                    p2m[pbk] = p.val("sEv0")
                    p.wait("vector", "sEv0")
                    def fcmu(E, q=q):
                        in1 = V["cchunk"].rearrange(
                            "p (o t) -> p o t", o=1).to_broadcast(
                            [128, 4, 128])
                        sl = sb["wbuf1"][:, (q % 8) * 512:(q % 8 + 1) * 512]
                        return E.tensor_tensor(
                            out=sl.rearrange("p (r t) -> p r t", r=4),
                            in0=sl.rearrange("p (r t) -> p r t", r=4),
                            in1=in1, op=ALU.mult)
                    p.do("vector", fcmu, "sV", 1)
                    p2m["p3ce"] = p.val("sV")
                    for half in range(2):
                        pp = q * 2 + half
                        py = PV("acc1", 512, 1024) if pp % 2 == 0 \
                            else PV("acc1", 1024, 1536)
                        pyk = "p3y0" if pp % 2 == 0 else "p3y1"
                        p.wait("tensor", "sV")
                        p.wait("tensor", "sEv1", p2m.get(pyk, 0))
                        def fy(E, pp=pp, q=q, half=half, py=py):
                            last = None
                            for rr in range(2):
                                po = py[64 * rr:64 * (rr + 1), 0:128]
                                ce = sb["wbuf1"][:,
                                                 (q % 8) * 512 + half * 256 +
                                                 rr * 128:
                                                 (q % 8) * 512 + half * 256 +
                                                 rr * 128 + 128]
                                last = E.matmul(
                                    out=po,
                                    lhsT=V["hinb"][:, pp * 128 + 64 * rr:
                                                   pp * 128 + 64 * rr + 64],
                                    rhs=ce, start=True, stop=True)
                            return last
                        p.do("tensor", fy, "sP", 1)
                        p.wait("vector", "sP")
                        p.wait("vector", "sSc0")
                        p.wait("vector", "sSc1")
                        def fg1(E, pp=pp, c=c, py=py):
                            yt = V["ytile"][:, :].rearrange(
                                "p (f t) -> p f t", f=32)[:, pp,
                                (c % 2) * 128:(c % 2) * 128 + 128]
                            return E.tensor_add(
                                V["gtmp"][:, 0:128], yt, py[:, 0:128])
                        p.do("vector", fg1, "sEv1", 1)
                        p2m[pyk] = p.val("sEv1")
                        p.wait("vector", "sEv1")
                        def fg2(E, pp=pp, c=c):
                            zt = V["ztile"][:, :].rearrange(
                                "p (f t) -> p f t", f=32)[:, pp,
                                (c % 2) * 128:(c % 2) * 128 + 128]
                            return E.tensor_mul(
                                gn3[:, pp, c * 128:(c + 1) * 128],
                                V["gtmp"][:, 0:128], zt)
                        p.do("vector", fg2, "sV", 1)
            p2m["p3yzE"] = p.val("sEv1")
            p2m["p3yzV"] = p.val("sV")

        barrier()
        for tt in range(4):
            prs = PV("acc1", 0, 512)
            p.wait("tensor", "sEv0", p2m.get("p3rs", 0))
            def frs(E, tt=tt):
                last = None
                for k in range(NK):
                    last = E.matmul(
                        out=prs[0:1, :],
                        lhsT=sb["onesb"][:, :],
                        rhs=V["g_norm"][:, :].rearrange(
                            "p (f t) -> p f t", f=NK)[:, k,
                            tt * 512:(tt + 1) * 512].rearrange(
                            "p t -> p (t)") if False else
                        V["gsq"][:, :],
                        start=(k == 0), stop=(k == NK - 1))
                return last
            def frs2(E, tt=tt, prs=prs):
                last = None
                for k in range(NK):
                    E.wait_ge(p.sems["sA"], 0)
                return last
            del frs, frs2
            for k in range(NK):
                p.wait("scalar", "sP", p2m.get("p3sq", 0))
                def fsq(E, tt=tt, k=k):
                    return E.activation(
                        out=V["gsq"][:, :],
                        in_=V["g_norm"][:, :].rearrange(
                            "p (f t) -> p f t", f=NK)[:, k,
                            tt * 512:(tt + 1) * 512],
                        func=AF.Square)
                p.do("scalar", fsq, "sA", 1)
                p.wait("tensor", "sA")
                def fmm(E, k=k, prs=prs):
                    return E.matmul(out=prs[0:1, :], lhsT=sb["onesb"][:, :],
                                    rhs=V["gsq"][:, :],
                                    start=(k == 0), stop=(k == NK - 1))
                p.do("tensor", fmm, "sP", 1)
                p2m["p3sq"] = p.val("sP")
            p.wait("scalar", "sP")
            def fsr(E, tt=tt):
                return E.activation(out=p3v()["rstrow"][0:1, tt * 512:
                                                     (tt + 1) * 512],
                                    in_=prs[0:1, :], func=AF.Sqrt,
                                    scale=1.0 / DS, bias=EPS)
            p.do("scalar", fsr, "sEv0", 1)
            p2m["p3rs"] = p.val("sEv0")
        p.wait("vector", "sEv0")
        def frc(E):
            return E.reciprocal(p3v()["rstrow"][0:1, :], p3v()["rstrow"][0:1, :])
        p.do("vector", frc, "sV", 1)
        for tt in range(4):
            prb = PV("acc1", 512, 1024)
            p.wait("tensor", "sV")
            p.wait("tensor", "sEv1", p2m.get("p3rb", 0))
            def fbb(E, tt=tt, prb=prb):
                return E.matmul(out=prb, lhsT=sb["onesf"][:, :],
                                rhs=p3v()["rstrow"][0:1, tt * 512:
                                                 (tt + 1) * 512],
                                start=True, stop=True)
            p.do("tensor", fbb, "sP", 1)
            p.wait("vector", "sP")
            for k in range(NK):
                def fap(E, tt=tt, k=k, prb=prb):
                    sl = V["g_norm"][:, :].rearrange(
                        "p (f t) -> p f t", f=NK)[:, k,
                        tt * 512:(tt + 1) * 512]
                    return E.scalar_tensor_tensor(
                        out=sl, in0=sl, scalar=sb["nrmwc"][:, k:k + 1],
                        in1=prb, op0=ALU.mult, op1=ALU.mult)
                p.do("vector", fap, "sEv1", 1)
            p2m["p3rb"] = p.val("sEv1")

        barrier()
        for n in range(32):
            pr = n % 2
            wb = sb["wbuf0"] if pr == 0 else sb["wbuf1"]
            pb = ps["acc0"] if pr == 0 else ps["acc1"]
            eb = sb["evst0"] if pr == 0 else sb["evst1"]
            p.wait("gpsimd", "sP", p2m.get(f"p3w{pr}", 0))
            def fw(E, n=n, wb=wb):
                return E.dma_start(
                    out=wb[:, :].rearrange("p (k c) -> p k c", k=NK),
                    in_=ap["w_outproj"][n, :, :, :])
            p.do("gpsimd", fw, f"sW{pr}", 16)
            p.wait("tensor", f"sW{pr}")
            p.wait("tensor", "sEv0" if pr == 0 else "sEv1",
                   p2m.get(f"p3e{pr}", 0))
            def fm(E, wb=wb, pb=pb):
                last = None
                for k in range(NK):
                    lhsT = wb[:, :].rearrange("p (k c) -> p k c",
                                              k=NK)[:, k, :]
                    for t in range(4):
                        last = E.matmul(
                            out=pb[:, t * 512:(t + 1) * 512], lhsT=lhsT,
                            rhs=V["g_norm"][:, :].rearrange(
                                "p (f t) -> p f t", f=NK)[:, k,
                                t * 512:(t + 1) * 512],
                            start=(k == 0), stop=(k == NK - 1))
                return last
            p.do("tensor", fm, "sP", 1)
            p2m[f"p3w{pr}"] = p.val("sP")
            sevn = "sEv0" if pr == 0 else "sEv1"
            seng = "scalar" if pr == 0 else "vector"
            p.wait(seng, "sP")
            p.wait(seng, "sSc0" if pr == 0 else "sSc1",
                   p2m.get(f"p3o{pr}", 0))
            def fe(E, pb=pb, eb=eb, pr=pr):
                if pr == 0:
                    return E.activation(out=eb[:, :], in_=pb[:, :],
                                        func=AF.Copy)
                return E.tensor_copy(eb[:, :], pb[:, :])
            p.do(seng, fe, sevn, 1)
            p2m[f"p3e{pr}"] = p.val(sevn)
            p.wait("sync", sevn)
            def fo(E, n=n, eb=eb):
                return E.dma_start(
                    out=scr["attn_scr"][n * 128:(n + 1) * 128, :],
                    in_=eb[:, :])
            p.do("sync", fo, "sSc0" if pr == 0 else "sSc1", 16)
            p2m[f"p3o{pr}"] = p.val("sSc0" if pr == 0 else "sSc1")

    def p3bv():
        return {
            "x1f": view("ar1", 0, 128, NK * 2048, BF16),
            "attn_t": view("ar2", 0, 128, H, BF16),
            "xrow": view("ar2", 8192, 128, H, F32),
            "srow": view("ar2", 8192, 128, H, F32),
            "x1t": view("ar2", 24576, 128, H, BF16),
            "qsb2": view("ar2", 0, 16, 2048, BF16),
            "scs": view("ar2", 4096, 64, 2048, F32),
            "gatef": view("ar2", 12288, 64, 2048, BF16),
            "z2g": view("ar2", 16384, 128, 4 * 2048, BF16),
            "o2t": view("ar2", 32768, 128, 16 * 128, BF16),
            "wqs": view("ar2", 36864, 128, NK * 16, BF16),
            "expE": view("ar2", 37888, 64, 4 * 128, BF16),
            "sct": view("ar2", 38912, 128, 256, F32),
            "sct2": view("ar2", 39936, 128, 64, BF16),
            "keyss": view("ar2", 40064, 16, 64, BF16),
        }

    def phase_p3b_ln1():
        V = p3bv()
        x1f3 = V["x1f"][:, :].rearrange("p (f t) -> p f t", f=NK)
        for c in range(16):
            p.wait("sync", "sV", p2m.get("p3battn", 0))
            p.wait("gpsimd", "sV", p2m.get("p3battn", 0))
            def fat(E, c=c):
                return E.dma_start(
                    out=sb["evst0"][:, :].rearrange(
                        "p (f t) -> p f t", f=16),
                    in_=scr["attn_scr"].rearrange(
                        "(f p) t -> p f t",
                        p=128)[:, 0:16, c * 128:(c + 1) * 128])
            p.do("sync", fat, "sSc0", 16)
            def fat2(E, c=c):
                return E.dma_start(
                    out=sb["evst1"][:, :].rearrange(
                        "p (f t) -> p f t", f=16),
                    in_=scr["attn_scr"].rearrange(
                        "(f p) t -> p f t",
                        p=128)[:, 16:32, c * 128:(c + 1) * 128])
            p.do("gpsimd", fat2, "sW0", 16)
            def fxa(E, c=c):
                return E.dma_start(out=V["xrow"][:, 0:2048],
                                   in_=ap["xin"][c * 128:(c + 1) * 128,
                                                 0:2048])
            p.do("sync", fxa, "sX0", 16)
            def fxb(E, c=c):
                return E.dma_start(out=V["xrow"][:, 2048:4096],
                                   in_=ap["xin"][c * 128:(c + 1) * 128,
                                                 2048:4096])
            p.do("gpsimd", fxb, "sW1", 16)
            p.wait("scalar", "sSc0")
            def ftr(E):
                return E.dma_start(
                    out=V["attn_t"][:, 0:2048].rearrange(
                        "p (f q) -> p f q", f=16),
                    in_=sb["evst0"][:, :], transpose=True)
            p.do("scalar", ftr, "sTr", 16)
            p.wait("scalar", "sW0")
            def ftr2(E):
                return E.dma_start(
                    out=V["attn_t"][:, 2048:4096].rearrange(
                        "p (f q) -> p f q", f=16),
                    in_=sb["evst1"][:, :], transpose=True)
            p.do("scalar", ftr2, "sTr", 16)
            p.wait("vector", "sTr")
            p.wait("vector", "sX0")
            p.wait("vector", "sW1")
            def fs1(E):
                return E.scalar_tensor_tensor(
                    out=V["srow"][:, :], in0=V["attn_t"][:, :], scalar=1.0,
                    in1=V["xrow"][:, :], op0=ALU.mult, op1=ALU.add,
                    accum_out=sb["small"][:, 9:10])
            p.do("vector", fs1, "sV", 1)
            p.wait("vector", "sV")
            def fs2(E):
                return E.tensor_scalar(
                    out=V["srow"][:, :], in0=V["srow"][:, :],
                    scalar1=sb["small"][:, 9:10], scalar2=-1.0 / H,
                    op0=ALU.mult if False else ALU.bypass, op1=ALU.bypass)
            del fs2
            def fmn(E):
                return E.tensor_scalar_mul(sb["small"][:, 10:11],
                                           sb["small"][:, 9:10], -1.0 / H)
            p.do("vector", fmn, "sV", 1)
            p.wait("vector", "sV")
            def fs3(E):
                return E.tensor_scalar(
                    out=V["srow"][:, :], in0=V["srow"][:, :],
                    scalar1=sb["small"][:, 10:11], scalar2=None,
                    op0=ALU.add)
            p.do("vector", fs3, "sV", 1)
            p.wait("scalar", "sV")
            def fsq(E):
                return E.activation(out=V["attn_t"][:, :],
                                    in_=V["srow"][:, :], func=AF.Square,
                                    accum_out=sb["small"][:, 11:12])
            p.do("scalar", fsq, "sA", 1)
            p.wait("scalar", "sA")
            def fsd(E):
                return E.activation(out=sb["small"][:, 12:13],
                                    in_=sb["small"][:, 11:12],
                                    func=AF.Sqrt, scale=1.0 / H, bias=LNEPS)
            p.do("scalar", fsd, "sA", 1)
            p.wait("vector", "sA")
            def frv(E):
                return E.reciprocal(sb["small"][:, 12:13],
                                    sb["small"][:, 12:13])
            p.do("vector", frv, "sV", 1)
            p.wait("vector", "sV")
            p.wait("vector", "sTr", p2m.get("l1t", 0))
            p.wait("vector", "sW0", p2m.get("l1w", 0))
            def fx1(E):
                return E.tensor_scalar_mul(V["x1t"][:, :], V["srow"][:, :],
                                           sb["small"][:, 12:13])
            p.do("vector", fx1, "sV", 1)
            p2m["p3battn"] = p.val("sV")
            p.wait("sync", "sV")
            p.wait("gpsimd", "sV")
            def fxw(E, c=c):
                return E.dma_start(
                    out=scr["x1t_scr"][c * 128:(c + 1) * 128, :],
                    in_=V["x1t"][:, :])
            p.do("gpsimd", fxw, "sW0", 16)
            p2m["l1w"] = p.val("sW0")
            def fxf(E, c=c):
                outap = x1f3[:, :, c * 128:(c + 1) * 128]
                return E.dma_start(out=outap, in_=V["x1t"][:, :],
                                   transpose=True)
            p.do("sync", fxf, "sTr", 16)
            p2m["l1t"] = p.val("sTr")
        p.wait("tensor", "sTr")

    def phase_p3b_ffn():
        V = p3bv()
        x1f3 = V["x1f"][:, :].rearrange("p (f t) -> p f t", f=NK)
        def fwq(E):
            return E.dma_start(
                out=V["wqs"][:, :].rearrange("p (k c) -> p k c", k=NK),
                in_=ap["w_ffnQ"])
        p.do("gpsimd", fwq, "sW0", 16)
        def fky(E):
            return E.dma_start(out=V["keyss"][:, :], in_=ap["keysT"])
        p.do("gpsimd", fky, "sW0", 16)
        def fex(E):
            return E.dma_start(
                out=V["expE"][:, :].rearrange("p (j c) -> p j c", j=4),
                in_=ap["expandE"])
        p.do("gpsimd", fex, "sW0", 16)
        p.wait("tensor", "sW0")
        pq = ps["acc0"][0:16, :]
        def fq(E, pq=pq):
            last = None
            for k in range(NK):
                for t in range(4):
                    last = E.matmul(
                        out=pq[:, t * 512:(t + 1) * 512],
                        lhsT=V["wqs"][:, :].rearrange(
                            "p (k c) -> p k c", k=NK)[:, k, :],
                        rhs=x1f3[:, k, t * 512:(t + 1) * 512],
                        start=(k == 0), stop=(k == NK - 1))
            return last
        p.do("tensor", fq, "sP", 1)
        p.wait("scalar", "sP")
        def feq(E, pq=pq):
            return E.activation(out=V["qsb2"][0:16, :], in_=pq,
                                func=AF.Copy)
        p.do("scalar", feq, "sEv0", 1)
        p.wait("tensor", "sEv0")
        psc = ps["acc1"][0:64, :]
        def fsc(E, psc=psc):
            last = None
            for t in range(4):
                last = E.matmul(out=psc[:, t * 512:(t + 1) * 512],
                                lhsT=V["keyss"][0:16, :],
                                rhs=V["qsb2"][0:16, t * 512:(t + 1) * 512],
                                start=True, stop=True)
            return last
        p.do("tensor", fsc, "sP", 1)
        p.wait("vector", "sP")
        def fes(E, psc=psc):
            return E.tensor_copy(V["scs"][:, :], psc)
        p.do("vector", fes, "sEv1", 1)
        for c in range(16):
            ptr2 = ps["acc0"][0:128, 1024:1536]
            p.wait("tensor", "sEv1")
            p.wait("tensor", "sEv0", p2m.get("p3g", 0))
            def ft1(E, c=c, ptr2=ptr2):
                return E.transpose(out=ptr2[:, 0:64],
                                   in_=V["scs"][:, c * 128:(c + 1) * 128],
                                   identity=sb["identf"][0:64, 0:64])
            p.do("tensor", ft1, "sP", 1)
            p.wait("vector", "sP")
            def fge(E, ptr2=ptr2):
                return E.tensor_copy(V["sct"][:, 0:64], ptr2[:, 0:64])
            p.do("vector", fge, "sV", 1)
            p.wait("vector", "sV")
            def fm8(E):
                return E.max(out=V["sct"][:, 64:72], in_=V["sct"][:, 0:64])
            p.do("vector", fm8, "sV", 1)
            p.wait("vector", "sV")
            def fmr(E):
                return E.match_replace(out=V["sct"][:, 128:192],
                                       in_to_replace=V["sct"][:, 64:72],
                                       in_values=V["sct"][:, 0:64],
                                       imm_value=-1e30)
            p.do("vector", fmr, "sV", 1)
            p.wait("vector", "sV")
            def fm8b(E):
                return E.max(out=V["sct"][:, 72:80],
                             in_=V["sct"][:, 128:192])
            p.do("vector", fm8b, "sV", 1)
            p.wait("vector", "sV")
            def fnm(E):
                return E.tensor_scalar_mul(V["sct"][:, 80:81],
                                           V["sct"][:, 64:65], -1.0)
            p.do("vector", fnm, "sV", 1)
            p.wait("scalar", "sV")
            def fex2(E):
                return E.activation(out=V["sct"][:, 128:192],
                                    in_=V["sct"][:, 0:64], func=AF.Exp,
                                    bias=V["sct"][:, 80:81])
            p.do("scalar", fex2, "sA", 1)
            p.wait("vector", "sA")
            def fmk(E):
                return E.tensor_scalar(out=V["sct"][:, 192:256],
                                       in0=V["sct"][:, 0:64],
                                       scalar1=V["sct"][:, 79:80],
                                       scalar2=None, op0=ALU.is_ge)
            p.do("vector", fmk, "sV", 1)
            p.wait("vector", "sV")
            def fme(E):
                return E.tensor_mul(V["sct"][:, 128:192],
                                    V["sct"][:, 128:192],
                                    V["sct"][:, 192:256])
            p.do("vector", fme, "sV", 1)
            p.wait("vector", "sV")
            def fsm(E):
                return E.reduce_sum(V["sct"][:, 81:82],
                                    V["sct"][:, 128:192],
                                    axis=mybir.AxisListType.X)
            p.do("vector", fsm, "sV", 1)
            p.wait("vector", "sV")
            def frz(E):
                return E.reciprocal(V["sct"][:, 81:82], V["sct"][:, 81:82])
            p.do("vector", frz, "sV", 1)
            p.wait("vector", "sV")
            def fgt(E):
                return E.tensor_scalar_mul(
                    V["sct"][:, 128:192].bitcast(BF16)[:, 0:64]
                    if False else V["sct2"][:, 0:64],
                    V["sct"][:, 128:192], V["sct"][:, 81:82])
            p.do("vector", fgt, "sV", 1)
            p.wait("tensor", "sV")
            ptg = ps["acc0"][0:64, 1536:1600].bitcast(BF16)
            def ft2(E, ptg=ptg):
                return E.transpose(out=ptg,
                                   in_=V["sct2"][:, 0:64],
                                   identity=sb["identb"][:, :])
            p.do("tensor", ft2, "sP", 1)
            p.wait("vector", "sP")
            def fgf(E, c=c, ptg=ptg):
                return E.tensor_copy(V["gatef"][:, c * 128:(c + 1) * 128],
                                     ptg)
            p.do("vector", fgf, "sEv0", 1)
            p2m["p3g"] = p.val("sEv0")
        for nb in range(4):
            pz = ps["acc0"][:, 0:2048]
            pgx = ps["acc1"][:, 0:2048]
            p.wait("gpsimd", "sP", p2m.get("p3za", 0))
            def fwa(E, nb=nb):
                return E.dma_start(
                    out=sb["wbuf1"][:, :].rearrange("p (k c) -> p k c",
                                                    k=NK),
                    in_=ap["w_ffnA"][nb, :, :, :])
            p.do("gpsimd", fwa, "sW1", 16)
            p.wait("tensor", "sW1")
            p.wait("tensor", "sV", p2m.get("p3zv", 0))
            def fz(E, pz=pz):
                last = None
                for k in range(NK):
                    for t in range(4):
                        last = E.matmul(
                            out=pz[:, t * 512:(t + 1) * 512],
                            lhsT=sb["wbuf1"][:, :].rearrange(
                                "p (k c) -> p k c", k=NK)[:, k, :],
                            rhs=x1f3[:, k, t * 512:(t + 1) * 512],
                            start=(k == 0), stop=(k == NK - 1))
                return last
            p.do("tensor", fz, "sP", 1)
            p2m["p3za"] = p.val("sP")
            def fgx(E, nb=nb, pgx=pgx):
                last = None
                for t in range(4):
                    last = E.matmul(
                        out=pgx[:, t * 512:(t + 1) * 512],
                        lhsT=V["expE"][:, :].rearrange(
                            "p (j c) -> p j c", j=4)[0:64, nb, :],
                        rhs=V["gatef"][:, t * 512:(t + 1) * 512],
                        start=True, stop=True)
                return last
            p.do("tensor", fgx, "sP", 1)
            p.wait("vector", "sP")
            def feg(E, pgx=pgx):
                return E.tensor_copy(sb["evst0"][:, :], pgx)
            p.do("vector", feg, "sV", 1)
            p.wait("vector", "sV")
            def fzg(E, nb=nb, pz=pz):
                return E.tensor_mul(
                    V["z2g"][:, nb * 2048:(nb + 1) * 2048],
                    sb["evst0"][:, :].to_broadcast([128, 2048])
                    if False else sb["evst0"][:, :], pz)
            def fzg2(E, nb=nb, pz=pz):
                return E.tensor_mul(
                    V["z2g"][:, nb * 2048:(nb + 1) * 2048],
                    sb["evst0"][:, :], pz)
            del fzg
            p.do("vector", fzg2, "sV", 1)
            p2m["p3zv"] = p.val("sV")
        for nf in range(32):
            pr = nf % 2
            po = ps["acc0"][:, 0:2048] if pr == 0 else ps["acc1"][:, 0:2048]
            eb = sb["evst0"] if pr == 0 else sb["evst1"]
            p.wait("gpsimd", "sP", p2m.get("p3wb", 0))
            if nf == 0:
                p.wait("gpsimd", "sV")
            wslot = sb["wbuf0"] if pr == 0 else sb["wbuf1"]
            def fwb(E, nf=nf, wslot=wslot):
                return E.dma_start(
                    out=wslot[:, 0:512].rearrange("p (k c) -> p k c", k=4),
                    in_=ap["w_ffnB"][nf, :, :, :])
            p.do("gpsimd", fwb, f"sW{pr}", 16)
            p.wait("tensor", f"sW{pr}")
            p.wait("tensor", "sV" if nf < 2 else "sEv1",
                   p2m.get(f"p3bo{pr}", 0))
            def fb(E, po=po, wslot=wslot):
                last = None
                for k in range(4):
                    for t in range(4):
                        last = E.matmul(
                            out=po[:, t * 512:(t + 1) * 512],
                            lhsT=wslot[:, 0:512].rearrange(
                                "p (k c) -> p k c", k=4)[:, k, :],
                            rhs=V["z2g"][:, k * 2048 + t * 512:
                                          k * 2048 + (t + 1) * 512],
                            start=(k == 0), stop=(k == 3))
                return last
            p.do("tensor", fb, "sP", 1)
            p2m["p3wb"] = p.val("sP")
            seng = "scalar" if pr == 0 else "vector"
            sevn = "sEv0" if pr == 0 else "sEv1"
            p.wait(seng, "sP")
            p.wait(seng, "sTr", p2m.get(f"p3ot{pr}", 0))
            def fe(E, po=po, eb=eb, pr=pr):
                if pr == 0:
                    return E.activation(out=eb[:, :], in_=po,
                                        func=AF.Copy, scale=FFN_SCALE)
                return E.tensor_scalar_mul(eb[:, :], po, FFN_SCALE)
            p.do(seng, fe, sevn, 1)
            p2m[f"p3bo{pr}"] = p.val(sevn)
            p.wait("sync", sevn)
            p.wait("sync", "sW0", p2m.get("p3fo0", 0))
            p.wait("sync", "sSc1", p2m.get("p3fo1", 0))
            def ftr(E, nf=nf, eb=eb):
                return E.dma_start(
                    out=V["o2t"][:, :].rearrange("p (cc f) -> p cc f",
                                                  cc=16),
                    in_=eb[:, :], transpose=True)
            p.do("sync", ftr, "sTr", 16)
            p2m[f"p3ot{pr}"] = p.val("sTr")
            oeng = "gpsimd" if pr == 0 else "scalar"
            p.wait(oeng, "sTr")
            def fo(E, nf=nf):
                return E.dma_start(
                    out=scr["out2t_scr"].rearrange(
                        "(cc p) f -> p cc f", p=128)[:, :,
                        nf * 128:(nf + 1) * 128],
                    in_=V["o2t"][:, :].rearrange("p (cc f) -> p cc f",
                                                  cc=16))
            p.do(oeng, fo, "sSc1" if pr else "sW0", 16)
            p2m[f"p3fo{pr}"] = p.val("sSc1" if pr else "sW0")

    def phase_p3b_final():
        V = p3bv()
        for c in range(16):
            par = c % 2
            at = V["attn_t"] if par == 0 else view("ar1", 0, 128, H, BF16)
            sr = V["srow"] if par == 0 else view("ar1", 8192, 128, H, F32)
            xt = V["x1t"] if par == 0 else view("ar1", 24576, 128, H, BF16)
            off = 9 if par == 0 else 13
            for eng in ("sync", "gpsimd", "scalar", "vector"):
                p.wait(eng, "sV", p2m.get(f"f3c{par}", 0))
            def fx1a(E, c=c, xt=xt):
                return E.dma_start(
                    out=xt[:, :],
                    in_=scr["x1t_scr"][c * 128:(c + 1) * 128, :])
            p.do("gpsimd", fx1a, "sW0", 16)
            def fo2a(E, c=c, at=at):
                return E.dma_start(
                    out=at[:, :],
                    in_=scr["out2t_scr"][c * 128:(c + 1) * 128, :])
            p.do("scalar", fo2a, "sX1", 16)
            p.wait("vector", "sW0")
            p.wait("vector", "sX1")
            p.wait("vector", "sOut", p2m.get(f"f3w{par}", 0))
            p.wait("vector", "sW1", p2m.get(f"f3wW{par}", 0))
            def fs1(E, at=at, sr=sr, xt=xt, off=off):
                return E.scalar_tensor_tensor(
                    out=sr[:, :], in0=at[:, :], scalar=1.0,
                    in1=xt[:, :], op0=ALU.mult, op1=ALU.add,
                    accum_out=sb["small"][:, off:off + 1])
            p.do("vector", fs1, "sV", 1)
            p.wait("vector", "sV")
            def fmn(E, off=off):
                return E.tensor_scalar_mul(sb["small"][:, off + 1:off + 2],
                                           sb["small"][:, off:off + 1],
                                           -1.0 / H)
            p.do("vector", fmn, "sV", 1)
            p.wait("vector", "sV")
            def fs3(E, sr=sr, off=off):
                return E.tensor_scalar(
                    out=sr[:, :], in0=sr[:, :],
                    scalar1=sb["small"][:, off + 1:off + 2], scalar2=None,
                    op0=ALU.add)
            p.do("vector", fs3, "sV", 1)
            p.wait("scalar", "sV")
            def fsq(E, at=at, sr=sr, off=off):
                return E.activation(out=at[:, :],
                                    in_=sr[:, :], func=AF.Square,
                                    accum_out=sb["small"][:, off + 2:off + 3])
            p.do("scalar", fsq, "sA", 1)
            p.wait("scalar", "sA")
            def fsd(E, off=off):
                return E.activation(out=sb["small"][:, off + 3:off + 4],
                                    in_=sb["small"][:, off + 2:off + 3],
                                    func=AF.Sqrt, scale=1.0 / H, bias=LNEPS)
            p.do("scalar", fsd, "sA", 1)
            p.wait("vector", "sA")
            def frv(E, off=off):
                return E.reciprocal(sb["small"][:, off + 3:off + 4],
                                    sb["small"][:, off + 3:off + 4])
            p.do("vector", frv, "sV", 1)
            p.wait("vector", "sV")
            def fyo2(E, sr=sr, off=off):
                return E.tensor_scalar_mul(sr[:, :], sr[:, :],
                                           sb["small"][:, off + 3:off + 4])
            p.do("vector", fyo2, "sV", 1)
            p2m[f"f3c{par}"] = p.val("sV")
            p.wait("sync", "sV")
            p.wait("gpsimd", "sV")
            p.wait("scalar", "sV")
            def fwa(E, c=c, sr=sr):
                return E.dma_start(
                    out=ap["yout"][c * 128:(c + 1) * 128, 0:2048],
                    in_=sr[:, 0:2048])
            p.do("sync", fwa, "sOut", 16)
            def fwb(E, c=c, sr=sr):
                return E.dma_start(
                    out=ap["yout"][c * 128:(c + 1) * 128, 2048:3072],
                    in_=sr[:, 2048:3072])
            p.do("gpsimd", fwb, "sW1", 16)
            def fwc(E, c=c, sr=sr):
                return E.dma_start(
                    out=ap["yout"][c * 128:(c + 1) * 128, 3072:4096],
                    in_=sr[:, 3072:4096])
            p.do("scalar", fwc, "sOut", 16)
            p2m[f"f3w{par}"] = p.val("sOut")
            p2m[f"f3wW{par}"] = p.val("sW1")
        p.wait("sync", "sOut")
        p.wait("sync", "sW1")

    def phase_debug():
        srcmap = {
            "zsil_scr": scr["zsil_scr"],
            "xbc_scr": scr["xbc_scr"],
            "y_scr": scr["y_scr"],
            "attn_scr": scr["attn_scr"],
            "x1f_scr": scr["x1f_scr"],
        }
        p.wait("sync", "sSc0")
        p.wait("sync", "sSc1")
        p.wait("sync", "sV")
        p.wait("sync", "sA")
        for nm in cfg["debug"]:
            if nm == "hS_dbg":
                def fd(E):
                    return E.dma_start(
                        out=ap["dbg_hS_dbg"],
                        in_=scr["win_S"].rearrange("a b p n -> (a b p) n"))
                p.do("sync", fd, "sOut", 16)
                continue
            if nm == "hin_dbg":
                def fd(E):
                    return E.dma_start(out=ap["dbg_hin_dbg"],
                                       in_=view("ar1", 49152, 128, 4096, F32))
                p.do("sync", fd, "sOut", 16)
                continue
            if nm == "dtp_dbg":
                def fd(E):
                    return E.dma_start(out=ap["dbg_dtp_dbg"],
                                       in_=scr["dtp_scr"][0:64, :])
            elif nm == "dtpr_dbg":
                def fd(E):
                    return E.dma_start(out=ap["dbg_dtpr_dbg"],
                                       in_=scr["dtp_scr"][64:128, :])
            else:
                def fd(E, nm=nm):
                    return E.dma_start(out=ap["dbg_" + nm], in_=srcmap[nm])
            p.do("sync", fd, "sOut", 16)

    def walk(eng_name, E):
        p.eng = eng_name
        p.E = E
        for k in list(p.cnt):
            p.cnt[k] = 0
        for k in list(inproj_marks):
            inproj_marks[k] = 0
        for k in list(p2m):
            if isinstance(p2m[k], list):
                p2m[k] = [0] * 8
            else:
                p2m[k] = 0
        p.poll_ctr = 0
        phase_init()
        barrier()
        ph = cfg["phases"]
        if "p1n_own" in ph:
            phase_p1_norm("own")
        if "p1i_own" in ph:
            phase_p1_inproj("own")
        if "p1n_red" in ph:
            barrier()
            phase_p1_norm("red")
        if "p1i_red" in ph:
            phase_p1_inproj("red")
            phase_halo_export()
        if "p2" in ph:
            barrier()
            write_flag(0)
            poll_flag(0)
            barrier()
            for seg in ("red", "own"):
                full = seg == "own"
                phase_p2_prep(seg)
                for g in range(4):
                    phase_p2_conv_group(seg, g)
                    for c in range(4 * g, 4 * g + 4):
                        phase_p2_chunk(seg, c, full)
                phase_p2_export(seg)
            phase_exchange()
        if "p3a" in ph:
            barrier()
            phase_p3a()
        if "p3b" in ph:
            barrier()
            phase_p3b_ln1()
            barrier()
            phase_p3b_ffn()
            barrier()
            phase_p3b_final()
        barrier()
        phase_debug()
        p.wait("sync", "sOut")

    with nc.Block() as block:
        @block.sync
        def _(E):
            walk("sync", E)

        @block.gpsimd
        def _(E):
            walk("gpsimd", E)

        @block.vector
        def _(E):
            walk("vector", E)

        @block.scalar
        def _(E):
            walk("scalar", E)

        @block.tensor
        def _(E):
            walk("tensor", E)


def _prep_base(inputs):
    w_in = _block_w(np.asarray(inputs["L0_attn_in_proj_w"], np.float32),
                    NK, NPROJ_T)
    w_out = _block_w(np.asarray(inputs["L0_attn_out_proj_w"], np.float32),
                     NK, 32)
    A_w = np.asarray(inputs["L0_ffn_A_w"], np.float32)
    B_w = np.asarray(inputs["L0_ffn_B_w"], np.float32)
    Wq = np.asarray(inputs["L0_ffn_Wq_w"], np.float32)
    keys = np.asarray(inputs["L0_ffn_keys"], np.float32)
    w_A = _block_w(np.ascontiguousarray(A_w.T), NK, 4)
    w_B = _block_w(np.ascontiguousarray(B_w.T), 4, 32)
    w_Q = np.ascontiguousarray(
        Wq.T.reshape(NK, 128, 16).transpose(1, 0, 2))
    mup = np.asarray(inputs["L0_attn_mup_vector"], np.float32)
    mup_pad = np.zeros(NPROJ_T * 128, np.float32)
    mup_pad[:PROJ] = mup
    mupc = np.ascontiguousarray(mup_pad.reshape(NPROJ_T, 128).T)
    ln_w = np.asarray(inputs["L0_attn_ln_w"], np.float32)
    nw = np.asarray(inputs["L0_attn_norm_w"], np.float32)
    A_log = np.asarray(inputs["L0_attn_A_log"], np.float32)
    return {
        "w_inproj": w_in,
        "w_outproj": w_out,
        "w_ffnA": w_A,
        "w_ffnB": w_B,
        "w_ffnQ": w_Q,
        "keysT": np.ascontiguousarray(keys.T),
        "convdiag": _conv_diag(np.asarray(inputs["L0_attn_conv_w"],
                                          np.float32)),
        "lnw_rep": np.ascontiguousarray(np.repeat(ln_w[None, :], 128, 0)),
        "nrmw_rep": np.ascontiguousarray(np.repeat(nw[None, :], 128, 0)),
        "nrmwc": np.ascontiguousarray(nw.reshape(NK, 128).T),
        "mupc": mupc,
        "dt_bias": np.asarray(inputs["L0_attn_dt_bias"],
                              np.float32)[:, None],
        "A_log": np.ascontiguousarray(A_log[:, None]),
        "Dvec": np.asarray(inputs["L0_attn_D"], np.float32)[:, None],
        "expandE": _expand_mat(),
        "ident": np.eye(128, dtype=np.float32),
        "tri": np.triu(np.ones((128, 128), np.float32)),
        "Drep": np.ascontiguousarray(
            np.asarray(inputs["L0_attn_D"], np.float32)[
                (2 * np.arange(32)[None, :] +
                 (np.arange(128)[:, None] >= 64))]),
    }


def _per_core_maps(inputs, base):
    x = np.asarray(inputs["L0_attn_hidden_states"], dtype=np.float32)
    nonce = (np.random.rand(1, 16).astype(np.float32) + 1.0)
    in_maps = []
    for core in range(8):
        b = core // 4
        q = core % 4
        own = x[b, q * 2048:(q + 1) * 2048, :]
        if q == 2:
            red = x[b, 2048:4096, :]
        elif q == 3:
            red = x[b, 0:2048, :]
        else:
            red = own
        par = core % 2
        opp = 1 - par
        cfg_f = np.zeros((1, 64), np.float32)
        if q == 1:
            cfg_f[0, 16 + opp * 2 + 0] = 1
            cfg_f[0, 28 + opp * 2 + 0] = 1
            cfg_f[0, 6] = 1
        elif q == 2:
            cfg_f[0, 16 + par * 2 + 1] = 1
            cfg_f[0, 20 + opp * 2 + 1] = 1
            cfg_f[0, 24 + opp * 2 + 1] = 1
            cfg_f[0, 28 + par * 2 + 1] = 1
            cfg_f[0, 32 + par * 2 + 1] = 1
            cfg_f[0, 0] = 1
            cfg_f[0, 2] = 1
            cfg_f[0, 6] = 1
        elif q == 3:
            cfg_f[0, 16 + opp * 2 + 0] = 1
            cfg_f[0, 36 + opp * 3 + 2] = 1
            cfg_f[0, 42 + opp * 3 + 0] = 1
            cfg_f[0, 48 + opp * 3 + 0] = 1
            cfg_f[0, 3] = 1
            cfg_f[0, 5] = 1
        cfg_i = np.zeros((1, 64), np.int32)
        m = {k: v for k, v in base.items() if not k.endswith("_")}
        m["xin"] = np.ascontiguousarray(np.concatenate([own, red], axis=0))
        m["cfg_f"] = cfg_f
        m["cfg_i"] = cfg_i
        m["nonce"] = nonce
        in_maps.append(m)
    return in_maps


_PROG_CACHE = {}


def _get_prog():
    key = (tuple(CFG["phases"]), tuple(CFG["debug"]))
    if key not in _PROG_CACHE:
        _PROG_CACHE[key] = build_program(CFG)
    return _PROG_CACHE[key]


def run_cores(inputs):
    nc = _get_prog()
    base = _prep_base(inputs)
    in_maps = _per_core_maps(inputs, base)
    return run_bass_kernel_spmd(nc, in_maps, core_ids=list(range(8)))


def kernel(**inputs):
    inputs.pop("L0_attn_residual", None)
    res = run_cores(inputs)
    out = np.zeros((B, L, H), dtype=np.float32)
    for core in range(8):
        b = core // 4
        q = core % 4
        out[b, q * 2048:(q + 1) * 2048, :] = res.results[core]["yout"]
    return out


# version 03
# speedup vs baseline: 1.2794x; 1.2794x over previous
import numpy as np
import ml_dtypes
import concourse.bass as bass
import concourse.mybir as mybir
from concourse.bass_utils import run_bass_kernel_spmd

F32 = mybir.dt.float32
BF16 = mybir.dt.bfloat16
I32 = mybir.dt.int32
AF = mybir.ActivationFunctionType
ALU = mybir.AluOpType

H = 4096
DS = 4096
S = 128
NH = 64
HD = 64
KCONV = 4
CONV_DIM = DS + 2 * S
PROJ = 2 * DS + 2 * S + NH
NPROJ_T = 67
NK = 32
EPS = 1e-5
LNEPS = 1e-5

B = 2
L = 8192
TOWN = 2048
TRED = 2048
NCHUNK = TOWN // 128

NZT = 32
NXT = 34
FFN_RANK = 8
FFN_SCALE = 2.0

def _WS(pp, s):
    return (pp * 3 + s) * 128


def _WA(pp, s):
    return pp * 3 + s


def _WH(pp, s):
    return (pp * 2 + s) * 128


def _WF(pp, j):
    return pp * 4 + j


def _role_rows(q):
    par = q % 2
    opp = 1 - par
    ci = {
        "my_S_own": _WS(par, 0), "my_S_red": _WS(par, 1),
        "my_stage1": _WS(par, 2), "my_As_own": _WA(par, 0),
        "my_As_red": _WA(par, 1), "my_halo_own": _WH(par, 0),
        "my_halo_red": _WH(par, 1),
        "my_flag0": _WF(par, 0), "my_flag1": _WF(par, 1),
        "my_flag2": _WF(par, 2), "pa_flag0": _WF(opp, 0),
        "pa_flag1": _WF(opp, 1), "pa_flag2": _WF(opp, 2),
        "halo_src_own": _WH(par, 0), "halo_src_red": _WH(par, 1),
        "Sa1": _WS(opp, 0), "Sb1": _WS(opp, 0), "G1": _WA(par, 0),
        "Sa2": _WS(opp, 0), "Sb2": _WS(opp, 0), "G2": _WA(par, 0),
    }
    if q == 1:
        ci["halo_src_own"] = _WH(opp, 0)
    elif q == 2:
        ci["halo_src_own"] = _WH(par, 1)
        ci["halo_src_red"] = _WH(opp, 1)
        ci["Sa1"] = _WS(opp, 1)
        ci["G1"] = _WA(par, 1)
        ci["Sb1"] = _WS(par, 1)
    elif q == 3:
        ci["halo_src_own"] = _WH(opp, 0)
        ci["Sa2"] = _WS(opp, 2)
        ci["G2"] = _WA(opp, 0)
        ci["Sb2"] = _WS(opp, 0)
    return ci

CFG = {
    "phases": ("p1n_own", "p1i_own", "p1n_red", "p1i_red", "p2",
               "p3a", "p3b"),
    "debug": (),
}

_DBG_SHAPES = {
    "zsil_scr": ([DS, TOWN], BF16),
    "xbc_scr": ([CONV_DIM, TOWN + TRED], BF16),
    "dtp_dbg": ([NH, TOWN], F32),
    "dtpr_dbg": ([NH, TRED], F32),
    "y_scr": ([DS, TOWN], BF16),
    "attn_scr": ([H, TOWN], BF16),
    "x1f_scr": ([H, TOWN], BF16),
    "hS_dbg": ([6 * 128, NH * HD], F32),
    "hin_dbg": ([128, NH * HD], F32),
}


def _block_w(w, nk, nn):
    K, N = w.shape
    assert K == nk * 128
    wp = np.zeros((K, nn * 128), dtype=np.float32)
    wp[:, :N] = w
    wb = wp.reshape(nk, 128, nn, 128).transpose(2, 1, 0, 3)
    return np.ascontiguousarray(wb)


def _conv_diag(conv_w):
    out = np.zeros((NXT, KCONV, 128, 128), dtype=np.float32)
    idx = np.arange(128)
    for fb in range(NXT):
        for k in range(KCONV):
            out[fb, k, idx, idx] = conv_w[fb * 128:(fb + 1) * 128, k]
    return np.ascontiguousarray(out.transpose(2, 0, 1, 3))


def _expand_mat():
    E = np.zeros((64, 4, 128), dtype=np.float32)
    for j in range(4):
        for m in range(128):
            E[(j * 128 + m) // FFN_RANK, j, m] = 1.0
    return E


class P:
    def __init__(self, nc):
        self.nc = nc
        self.cnt = {}
        self.sems = {}
        self.eng = None
        self.E = None
        self.dyn = {}
        self.poll_ctr = 0
        self.pbody = 0
        self.noncereg_val = None
        self._pending = []

    def inc(self, name, by):
        self.cnt[name] = self.cnt.get(name, 0) + by
        return self.cnt[name]

    def val(self, name):
        return self.cnt.get(name, 0)

    def do(self, eng, fn, sem=None, by=None):
        inst = None
        if self.eng == eng:
            inst = fn(self.E)
            if self._pending:
                for r in self._pending:
                    self.E.free_register(r)
                self._pending = []
        if sem is not None:
            self.inc(sem, by)
            if inst is not None:
                inst.then_inc(self.sems[sem], by)
        return inst

    def roled(self, eng, build_fn, sem=None, by=None):
        if self.eng == eng:
            for role in self.E.Switch(self.role_rv, 4):
                inst = build_fn(self.E, _role_rows(role))
                if sem is not None and inst is not None:
                    inst.then_inc(self.sems[sem], by)
        if sem is not None:
            self.inc(sem, by)

    def wait(self, eng, sem, val=None):
        if val is None:
            val = self.val(sem)
        if val <= 0:
            return
        if self.eng == eng:
            self.E.wait_ge(self.sems[sem], val)


def build_program(cfg):
    nc = bass.Bass("TRN2", target_bir_lowering=False, debug=False)
    NTOK = TOWN + TRED

    ap = {}

    def ein(name, shape, dt=F32):
        ap[name] = nc.dram_tensor(name, shape, dt, kind="ExternalInput").ap()

    def eout(name, shape, dt=F32):
        ap[name] = nc.dram_tensor(name, shape, dt, kind="ExternalOutput").ap()

    ein("xin", [NTOK, H])
    ein("w_inproj", [NPROJ_T, 128, NK, 128])
    ein("w_outproj", [32, 128, NK, 128])
    ein("w_ffnA", [4, 128, NK, 128])
    ein("w_ffnB", [32, 128, 4, 128])
    ein("w_ffnQ", [128, NK, 16])
    ein("keysT", [16, 64])
    ein("convdiag", [128, NXT, KCONV, 128])
    ein("lnw_rep", [128, H])
    ein("nrmw_rep", [128, H])
    ein("nrmwc", [128, NK])
    ein("mupc", [128, NPROJ_T])
    ein("dt_bias", [NH, 1])
    ein("A_log", [NH, 1])
    ein("Dvec", [NH, 1])
    ein("expandE", [64, 4, 128])
    ein("ident", [128, 128])
    ein("tri", [128, 128])
    ein("Drep", [128, 32])
    ein("cfg_f", [1, 64])
    ein("cfg_i", [1, 64], I32)
    ein("nonce", [1, 16])
    ein("hdrep", [64, 4096])

    eout("yout", [TOWN, H])
    for nm in cfg["debug"]:
        shp, dt = _DBG_SHAPES[nm]
        eout("dbg_" + nm, shp, dt)

    scr = {}

    def sloc(name, shape, dt=BF16):
        scr[name] = nc.dram_tensor(name, shape, dt).ap()

    sloc("zsil_scr", [DS, TOWN])
    sloc("dtp_scr", [128, 2048], F32)
    sloc("acum_scr", [128, 2048], F32)
    sloc("xbc_scr", [CONV_DIM, NTOK])
    sloc("y_scr", [DS, TOWN])
    sloc("attn_scr", [H, TOWN])
    sloc("x1f_scr", [H, TOWN])
    sloc("x1t_scr", [TOWN, H])
    sloc("out2t_scr", [TOWN, H])

    def sshr(name, shape, dt=F32):
        scr[name] = nc.dram_tensor(name, shape, dt, addr_space="Shared").ap()

    sshr("win_S", [2, 3, S, NH * HD])
    sshr("win_As", [2, 3, NH])
    scr["win_halo"] = nc.dram_tensor("win_halo", [2, 2, 128, NXT * 4], BF16, addr_space="Shared").ap()
    sshr("win_flag", [2, 4, 16])

    _emit(nc, ap, scr, cfg)
    return nc


def _emit(nc, ap, scr, cfg):
    import contextlib
    est = contextlib.ExitStack()

    p = P(nc)
    for s in ["sX0", "sX1", "sW0", "sW1", "sTr", "sSc0", "sSc1", "sScR",
              "sWin", "sV", "sA", "sP", "sG", "sPoll", "sEv0", "sEv1",
              "sOut"]:
        p.sems[s] = est.enter_context(nc.semaphore(s))

    sb = {}

    def sbuf(name, shape, dt):
        sb[name] = est.enter_context(nc.sbuf_tensor("sb_" + name, shape, dt))

    ps = {}

    def psum(name, shape, dt=F32):
        ps[name] = est.enter_context(nc.psum_tensor(name, shape, dt))

    sbuf("ar1", [128, 131072], mybir.dt.int8)
    sbuf("ar2", [128, 40960], mybir.dt.int8)
    sbuf("wbuf0", [128, NK * 128], BF16)
    sbuf("wbuf1", [128, NK * 128], BF16)
    sbuf("evst0", [128, 2048], BF16)
    sbuf("evst1", [128, 2048], BF16)
    sbuf("wrep", [128, H], BF16)
    sbuf("mupc", [128, NPROJ_T], F32)
    sbuf("consts", [128, 8], F32)
    sbuf("dtbias", [NH, 1], F32)
    sbuf("negA", [NH, 1], F32)
    sbuf("dvec", [NH, 1], F32)
    sbuf("small", [128, 256], F32)
    sbuf("identf", [128, 128], F32)
    sbuf("identb", [128, 128], BF16)
    sbuf("trib", [128, 128], BF16)
    sbuf("Drep", [128, 32], F32)
    sbuf("onesf", [1, 128], F32)
    sbuf("masks", [128, 64], F32)
    sbuf("noncebuf", [1, 16], F32)
    sbuf("cfgis", [1, 64], I32)
    sbuf("cfgfs", [1, 64], F32)
    sbuf("offsi", [128, 16], F32)
    sbuf("aclT", [128, 64], F32)
    sbuf("wT", [128, 64], F32)
    sbuf("B_T", [128, 128], BF16)
    sbuf("GT_sb", [128, 128], BF16)
    sbuf("Gexp", [128, 64], F32)
    sbuf("grow", [1, 64], F32)
    sbuf("halost", [128, 272], BF16)
    sbuf("halos4", [128, 544], BF16)
    sbuf("asums", [64, 16], F32)
    sbuf("nrmwc", [128, NK], F32)
    sbuf("onesb", [128, 1], BF16)
    sbuf("asumsT", [16, 64], F32)
    sbuf("flagrd", [1, 32], F32)

    _esz = {F32: 4, BF16: 2, I32: 4}

    def view(arena, off, parts, free, dt):
        return sb[arena][0:parts, off:off + free * _esz[dt]].bitcast(dt)

    psum("acc0", [128, 2048])
    psum("acc1", [128, 2048])

    p.sb_cfgis = sb["cfgis"]

    CONSTS = {}
    for i, v in enumerate((1.0, -1.0, float(EPS), float(LNEPS), 1.0 / H,
                           80.0, FFN_SCALE, 0.0)):
        CONSTS[v] = i
        nc.const_aps.register(F32, v, sb["consts"][:, i:i + 1])

    ALL_ENGS = ("sync", "gpsimd", "vector", "scalar", "tensor")

    def barrier():
        if cfg.get("snap") is not None:
            cfg["snap"].setdefault(p.eng, []).append(dict(p.cnt))
        for e in ALL_ENGS:
            for s in ("sV", "sA", "sP", "sEv0", "sEv1", "sSc0", "sSc1",
                      "sX0", "sX1", "sTr", "sW0", "sW1", "sG", "sWin",
                      "sOut"):
                p.wait(e, s)

    def phase_init():
        for v, i in CONSTS.items():
            def f(E, v=v, i=i):
                return E.memset(sb["consts"][:, i:i + 1], v)
            p.do("vector", f, "sV", 1)

        for nm_src, nm_dst in (("mupc", "mupc"), ("dt_bias", "dtbias"),
                               ("A_log", "negA"), ("Dvec", "dvec")):
            def f2(E, nm_src=nm_src, nm_dst=nm_dst):
                return E.dma_start(out=sb[nm_dst][:, :], in_=ap[nm_src])
            p.do("sync", f2, "sX0", 16)

        def f2b(E):
            return E.dma_start(out=sb["wrep"][:, :], in_=ap["lnw_rep"])
        p.do("gpsimd", f2b, "sW0", 16)
        for nm_src, nm_dst in (("ident", "identf"), ("Drep", "Drep"),
                               ("cfg_f", "cfgfs"), ("cfg_i", "cfgis"),
                               ("nonce", "noncebuf")):
            def f2c(E, nm_src=nm_src, nm_dst=nm_dst):
                return E.dma_start(out=sb[nm_dst][:, :], in_=ap[nm_src])
            p.do("sync", f2c, "sX0", 16)
        def f2cb(E):
            return E.dma_start(out=sb["identb"][:, :], in_=ap["ident"])
        p.do("gpsimd", f2cb, "sW1", 16)
        def f2cc(E):
            return E.dma_start(out=sb["trib"][:, :], in_=ap["tri"])
        p.do("gpsimd", f2cc, "sW1", 16)
        p.wait("vector", "sX0")
        p.wait("vector", "sW1")
        def f2d(E):
            E.memset(sb["halost"][:, :], 0.0)
            E.memset(sb["onesb"][:, :], 1.0)
            return E.memset(sb["onesf"][:, :], 1.0)
        p.do("vector", f2d, "sV", 1)
        p.wait("tensor", "sV")
        p.wait("tensor", "sX0")
        def f2e(E):
            return E.matmul(out=ps["acc0"][:, 0:64], lhsT=sb["onesf"][:, :],
                            rhs=sb["cfgfs"][0:1, 0:64], start=True, stop=True)
        p.do("tensor", f2e, "sP", 1)
        p.wait("scalar", "sP")
        def f2f(E):
            return E.activation(out=sb["masks"][:, :],
                                in_=ps["acc0"][:, 0:64], func=AF.Copy)
        p.do("scalar", f2f, "sEv0", 1)
        if p.eng == "sync":
            p.E.wait_ge(p.sems["sX0"], p.val("sX0"))
            nreg = p.E.alloc_register("nonceval")
            p.E.reg_load(nreg, sb["noncebuf"][0:1, 0:1].bitcast(I32))
            p.noncereg_val = p.E.snap(nreg)
            rreg = p.E.alloc_register("roleidx")
            pid = p.E.partition_id()
            p.E.reg_alu(rreg, pid, 4, ALU.mod)
            p.role_rv = p.E.snap(rreg)
        p.wait("scalar", "sV")
        p.wait("scalar", "sX0")
        def fneg(E):
            return E.activation(out=sb["negA"][:, :], in_=sb["negA"][:, :],
                                func=AF.Exp)
        p.do("scalar", fneg, "sA", 1)
        p.wait("vector", "sA")
        def fneg2(E):
            return E.tensor_scalar_mul(sb["negA"][:, :], sb["negA"][:, :],
                                       -1.0)
        p.do("vector", fneg2, "sV", 1)

    def phase_p1_norm(seg):
        T = TOWN if seg == "own" else TRED
        base = 0 if seg == "own" else TOWN
        nch = T // 128
        xnorm_f = view("ar1", 0, 128, NK * 2048, BF16)
        xc = [view("ar2", 0, 128, H, F32), view("ar2", 16384, 128, H, F32)]
        xnt = view("ar2", 32768, 128, H, BF16)
        marks = {"bx0": 0, "bx1": 0, "rs0": 0, "rs1": 0}
        for i in range(nch):
            pr = i % 2
            bx = xc[pr]
            rstd = sb["small"][:, pr:pr + 1]
            ssq = sb["small"][:, 2 + pr:3 + pr]
            p.wait("sync", "sV", marks[f"bx{pr}"])

            def fl(E, i=i, bx=bx):
                return E.dma_start(
                    out=bx,
                    in_=ap["xin"][base + i * 128: base + (i + 1) * 128, :])
            p.do("sync", fl, f"sX{pr}", 16)
            p.wait("scalar", f"sX{pr}")
            p.wait("scalar", "sV", marks[f"rs{pr}"])
            p.wait("scalar", "sTr")

            def fs(E, bx=bx, ssq=ssq):
                return E.activation(out=xnt, in_=bx,
                                    func=AF.Square, accum_out=ssq)
            p.do("scalar", fs, "sA", 1)
            p.wait("scalar", "sA")

            def fsq(E, ssq=ssq, rstd=rstd):
                return E.activation(out=rstd, in_=ssq,
                                    func=AF.Sqrt, scale=1.0 / H, bias=EPS)
            p.do("scalar", fsq, "sA", 1)
            p.wait("vector", "sA")

            def fr(E, rstd=rstd):
                return E.reciprocal(rstd, rstd)
            p.do("vector", fr, "sV", 1)
            p.wait("vector", "sV")
            p.wait("vector", "sTr")

            def fn(E, bx=bx, rstd=rstd):
                return E.scalar_tensor_tensor(
                    out=xnt, in0=bx, scalar=rstd,
                    in1=sb["wrep"][:, :], op0=ALU.mult, op1=ALU.mult)
            p.do("vector", fn, "sV", 1)
            marks[f"bx{pr}"] = p.val("sV")
            marks[f"rs{pr}"] = p.val("sV")
            p.wait("sync", "sV")

            def ft(E, i=i):
                outap = xnorm_f.rearrange(
                    "p (k t) -> p k t", k=NK)[:, :, i * 128:(i + 1) * 128]
                return E.dma_start(out=outap, in_=xnt, transpose=True)
            p.do("sync", ft, "sTr", 16)
        p.wait("tensor", "sTr")

    inproj_marks = {"w0": 0, "w1": 0, "p0": 0, "p1": 0, "e0": 0, "e1": 0,
                    "halo_w": 0}

    def phase_halo_export():
        p.wait("sync", "sV")
        def fh(E, rows):
            ro = rows["my_halo_own"]
            rr = rows["my_halo_red"]
            E.dma_start(
                out=scr["win_halo"].rearrange("a b p c -> (a b p) c")[
                    ro:ro + 128, :],
                in_=sb["halost"][:, 0:136]).then_inc(p.sems["sWin"], 16)
            return E.dma_start(
                out=scr["win_halo"].rearrange("a b p c -> (a b p) c")[
                    rr:rr + 128, :],
                in_=sb["halost"][:, 136:272])
        p.roled("sync", fh, "sWin", 16)
        p.inc("sWin", 16)
        inproj_marks["halo_w"] = p.val("sWin")
        p.wait("sync", "sWin")

    def phase_p1_inproj(seg):
        T = TOWN if seg == "own" else TRED
        base = 0 if seg == "own" else TOWN
        ntiles = list(range(NPROJ_T)) if seg == "own" \
            else list(range(NZT, NPROJ_T))
        ntt = T // 512
        mk = inproj_marks
        xnorm_f = view("ar1", 0, 128, NK * 2048, BF16)
        dtt1 = view("ar2", 0, 64, 2048, F32)
        dtt2 = view("ar2", 16384, 64, 2048, F32)
        dtrow = slice(0, 64) if seg == "own" else slice(64, 128)
        for j, n in enumerate(ntiles):
            pr = j % 2
            wb = sb["wbuf0"] if pr == 0 else sb["wbuf1"]
            pb = ps["acc0"] if pr == 0 else ps["acc1"]
            eb = sb["evst0"] if pr == 0 else sb["evst1"]
            p.wait("gpsimd", "sP", mk[f"w{pr}"])

            def fw(E, n=n, wb=wb):
                return E.dma_start(
                    out=wb[:, :].rearrange("p (k c) -> p k c", k=NK),
                    in_=ap["w_inproj"][n, :, :, :])
            p.do("gpsimd", fw, f"sW{pr}", 16)

            p.wait("tensor", f"sW{pr}")
            p.wait("tensor", f"sEv{pr}", mk[f"p{pr}"])

            def fm(E, wb=wb, pb=pb, ntt=ntt):
                last = None
                for k in range(NK):
                    lhsT = wb[:, :].rearrange(
                        "p (k c) -> p k c", k=NK)[:, k, :]
                    for t in range(ntt):
                        last = E.matmul(
                            out=pb[:, t * 512:(t + 1) * 512],
                            lhsT=lhsT,
                            rhs=xnorm_f.rearrange(
                                "p (k t) -> p k t",
                                k=NK)[:, k, t * 512:(t + 1) * 512],
                            start=(k == 0), stop=(k == NK - 1))
                return last
            p.do("tensor", fm, "sP", 1)
            mk[f"w{pr}"] = p.val("sP")

            if n < NZT:
                p.wait("scalar", "sP")
                p.wait("scalar", f"sSc{pr}", mk[f"e{pr}"])

                def fe(E, n=n, pb=pb, eb=eb, T=T):
                    return E.activation(
                        out=eb[:, 0:T], in_=pb[:, 0:T],
                        func=AF.Silu, scale=sb["mupc"][:, n:n + 1])
                p.do("scalar", fe, f"sEv{pr}", 1)
                mk[f"p{pr}"] = p.val(f"sEv{pr}")
                p.wait("sync", f"sEv{pr}")

                def fo(E, n=n, eb=eb, T=T, base=base):
                    return E.dma_start(
                        out=scr["zsil_scr"][n * 128:(n + 1) * 128,
                                            base:base + T],
                        in_=eb[:, 0:T])
                p.do("sync", fo, f"sSc{pr}", 16)
                mk[f"e{pr}"] = p.val(f"sSc{pr}")
            elif n < NZT + NXT:
                p.wait("vector", "sP")
                p.wait("vector", f"sSc{pr}", mk[f"e{pr}"])

                def fe(E, n=n, pb=pb, eb=eb, T=T):
                    return E.tensor_scalar_mul(
                        eb[:, 0:T], pb[:, 0:T], sb["mupc"][:, n:n + 1])
                p.do("vector", fe, f"sEv{pr}", 1)
                mk[f"p{pr}"] = p.val(f"sEv{pr}")
                p.wait("vector", f"sEv{pr}")
                p.wait("vector", "sWin", mk.get("halo_w", 0))
                hoff = 0 if seg == "own" else 136
                def fhc(E, n=n, eb=eb, T=T, hoff=hoff):
                    fb = n - NZT
                    return E.tensor_copy(
                        sb["halost"][:, hoff + fb * 4:hoff + fb * 4 + 3],
                        eb[:, T - 3:T])
                p.do("vector", fhc, "sV", 1)
                p.wait("sync", f"sEv{pr}")

                def fo(E, n=n, eb=eb, T=T, base=base):
                    fb = n - NZT
                    return E.dma_start(
                        out=scr["xbc_scr"][fb * 128:(fb + 1) * 128,
                                           base:base + T],
                        in_=eb[:, 0:T])
                p.do("sync", fo, f"sSc{pr}", 16)
                mk[f"e{pr}"] = p.val(f"sSc{pr}")
            else:
                p.wait("vector", "sP")

                def fe(E, pb=pb, T=T, dtt1=dtt1):
                    return E.tensor_scalar(
                        out=dtt1[:, 0:T], in0=pb[0:NH, 0:T],
                        scalar1=sb["mupc"][0:NH, NPROJ_T - 1:NPROJ_T],
                        scalar2=sb["dtbias"][:, :],
                        op0=ALU.mult, op1=ALU.add)
                p.do("vector", fe, f"sEv{pr}", 1)
                mk[f"p{pr}"] = p.val(f"sEv{pr}")
                p.wait("scalar", f"sEv{pr}")

                def fa1(E, T=T, dtt1=dtt1, dtt2=dtt2):
                    return E.activation(out=dtt2[:, 0:T],
                                        in_=dtt1[:, 0:T], func=AF.Abs)
                p.do("scalar", fa1, "sA", 1)
                p.wait("scalar", "sA")

                def fa2(E, T=T, dtt2=dtt2):
                    return E.activation(out=dtt2[:, 0:T], in_=dtt2[:, 0:T],
                                        func=AF.Exp, scale=-1.0)
                p.do("scalar", fa2, "sA", 1)
                p.wait("scalar", "sA")

                def fa3(E, T=T, dtt2=dtt2):
                    return E.activation(out=dtt2[:, 0:T], in_=dtt2[:, 0:T],
                                        func=AF.Ln, bias=1.0)
                p.do("scalar", fa3, "sA", 1)
                p.wait("vector", "sA")

                def fv1(E, T=T, dtt1=dtt1):
                    return E.tensor_relu(dtt1[:, 0:T], dtt1[:, 0:T])
                p.do("vector", fv1, "sV", 1)
                p.wait("vector", "sV")

                def fv2(E, T=T, dtt1=dtt1, dtt2=dtt2):
                    return E.tensor_add(dtt1[:, 0:T], dtt1[:, 0:T],
                                        dtt2[:, 0:T])
                p.do("vector", fv2, "sV", 1)
                p.wait("sync", "sV")

                def fdt(E, T=T, dtt1=dtt1, dtrow=dtrow):
                    return E.dma_start(out=scr["dtp_scr"][dtrow, 0:T],
                                       in_=dtt1[:, 0:T])
                p.do("sync", fdt, "sG", 16)


    def PV(t, a, b):
        return ps[t][:, a:b]

    def PVb(t, a, b):
        return ps[t][:, a:b].bitcast(BF16)

    def p2v():
        return {
            "xbcg": view("ar1", 0, 128, 34 * 516, BF16),
            "xsg": view("ar1", 35328, 128, 34 * 512, BF16),
            "S_sb": view("ar1", 70144, 128, 8192, BF16),
            "Ce_sb": view("ar1", 86528, 128, 8192, BF16),
            "cdg": view("ar1", 70144, 128, 34 * 4 * 128, BF16),
            "y_grp": view("ar1", 104960, 128, 32 * 128, BF16),
            "h_bf": view("ar1", 113152, 128, 4096, BF16),
            "acumP": view("ar1", 121344, 128, 2048, F32),
            "h": view("ar2", 0, 128, 4096, F32),
            "dtv": view("ar2", 16384, 64, 2048, F32),
            "xs_T": view("ar2", 16384, 128, 4096, BF16),
            "amlw": view("ar2", 24576, 128, 2048, F32),
            "acrow": view("ar2", 32768, 1, 2048, F32),
            "xsw_T": sb["wbuf0"][:, :].bitcast(BF16),
            "argb": sb["wbuf1"][:, 0:1024].bitcast(F32),
            "th0": sb["evst0"][:, 0:1024].bitcast(F32),
            "th1": sb["evst0"][:, 1024:2048].bitcast(F32),
            "hwA": view("ar1", 0, 128, 4096, F32),
            "hwB": view("ar1", 16384, 128, 4096, F32),
            "hst": view("ar1", 32768, 128, 4096, F32),
            "hin": view("ar1", 49152, 128, 4096, F32),
            "G1e": view("ar1", 65536, 128, 64, F32),
            "SaS": view("ar1", 66048, 128, 4096, F32),
            "SbS": view("ar1", 82432, 128, 4096, F32),
        }

    p2m = {"pb0": 0, "pb1": 0, "py0": 0, "py1": 0, "pc0": 0, "pc1": 0,
           "cv0": 0, "cv1": 0, "ptr": 0, "yg": 0, "acrow": 0, "sce": 0,
           "xbcg": 0, "hbf": 0, "hupd": [0] * 8, "th0": 0, "th1": 0,
           "xsTc": 0, "hread": [0] * 8, "exw0": 0, "exw1": 0,
           "p3acg": 0, "p3xb": 0, "p3cc": 0, "p3yz": 0, "p3ac": 0,
           "p3b0": 0, "p3b1": 0, "p3ce": 0, "p3y0": 0, "p3y1": 0,
           "p3rs": 0, "p3sq": 0, "p3rb": 0, "p3w0": 0, "p3w1": 0,
           "p3e0": 0, "p3e1": 0, "p3o0": 0, "p3o1": 0, "p3battn": 0, "p3yzE": 0, "p3yzV": 0,
           "p3g": 0, "p3za": 0, "p3zv": 0, "p3wb": 0, "p3bo0": 0,
           "p3bo1": 0, "p3ot0": 0, "p3ot1": 0, "p3f": 0}

    def rows_of(seg):
        return slice(0, 64)

    def phase_p2_prep(seg):
        V = p2v()
        ro = rows_of(seg)
        dtrow = slice(0, 64) if seg == "own" else slice(64, 128)
        def fl(E, dtrow=dtrow):
            return E.dma_start(out=V["dtv"][:, :],
                               in_=scr["dtp_scr"][dtrow, :])
        p.do("sync", fl, "sX0", 16)
        p.wait("vector", "sX0")
        def fadot(E):
            return E.tensor_scalar_mul(V["amlw"][0:64, :], V["dtv"][:, :],
                                       sb["negA"][:, :])
        p.do("vector", fadot, "sV", 1)
        p.wait("vector", "sV")
        for c in range(16):
            def fc(E, c=c):
                sl = slice(c * 128, (c + 1) * 128)
                return E.tensor_tensor_scan(
                    out=view("ar1", 121344, 128, 2048, F32)[ro, sl], data0=V["amlw"][0:64, sl],
                    data1=V["amlw"][0:64, sl], initial=0.0,
                    op0=ALU.add, op1=ALU.bypass)
            p.do("vector", fc, "sV", 1)
            p.wait("vector", "sV")
        def fg(E):
            return E.tensor_copy(
                sb["asums"][:, :],
                view("ar1", 121344, 128, 2048, F32)[ro, :].rearrange("p (c t) -> p c t",
                                             c=16)[:, :, 127])
        p.do("vector", fg, "sV", 1)
        p.wait("vector", "sV")
        def fz(E):
            return E.memset(sb["offsi"][ro, 0:1], 0.0)
        p.do("vector", fz, "sV", 1)
        p.wait("vector", "sV")
        def fsc(E):
            return E.tensor_tensor_scan(
                out=sb["offsi"][ro, 1:16],
                data0=sb["asums"][:, 0:15], data1=sb["asums"][:, 0:15],
                initial=0.0, op0=ALU.add, op1=ALU.bypass)
        p.do("vector", fsc, "sV", 1)
        ptr = PV("acc1", 1024, 1536)
        p.wait("tensor", "sV")
        p.wait("tensor", "sEv1", p2m["ptr"])
        def ftA(E):
            return E.transpose(out=ptr[0:16, 0:64], in_=sb["asums"][:, :],
                               identity=sb["identf"][0:64, 0:64])
        p.do("tensor", ftA, "sP", 1)
        p.wait("vector", "sP")
        def feA(E):
            return E.tensor_copy(sb["asumsT"][:, :], ptr[0:16, 0:64])
        p.do("vector", feA, "sEv1", 1)
        p2m["ptr"] = p.val("sEv1")
        p.wait("scalar", "sV")
        p.wait("scalar", "sX0")
        p.wait("scalar", "sEv1")
        def fln(E):
            return E.activation(out=V["amlw"][0:64, :], in_=V["dtv"][:, :],
                                func=AF.Ln)
        p.do("scalar", fln, "sA", 1)
        p.wait("vector", "sA")
        def faml(E):
            return E.tensor_sub(V["amlw"][0:64, :],
                                view("ar1", 121344, 128, 2048, F32)[ro, :],
                                V["amlw"][0:64, :])
        p.do("vector", faml, "sV", 1)
        p.wait("vector", "sV")
        for c in range(16):
            sl = slice(c * 128, (c + 1) * 128)
            def fw1(E, sl=sl, c=c):
                return E.tensor_scalar(
                    out=V["amlw"][64:128, sl], in0=V["amlw"][0:64, sl],
                    scalar1=sb["asums"][:, c:c + 1], scalar2=None,
                    op0=ALU.subtract)
            p.do("vector", fw1, "sV", 1)
        p.wait("scalar", "sV")
        def fw2(E):
            return E.activation(out=V["amlw"][64:128, :],
                                in_=V["amlw"][64:128, :], func=AF.Exp,
                                scale=-1.0)
        p.do("scalar", fw2, "sA", 1)
        p.wait("vector", "sA")
        p.wait("vector", "sWin")
        def fh(E):
            E.memset(p2v()["h"][:, :], 0.0)
            return E.memset(p2v()["h_bf"][:, :], 0.0)
        p.do("vector", fh, "sV", 1)

    def phase_p2_conv_group(seg, g):
        V = p2v()
        base = 0 if seg == "own" else TOWN
        xsgv3 = V["xsg"][:, :].rearrange("p (f t) -> p f t", f=NXT)
        xbcg3 = V["xbcg"][:, :].rearrange("p (f t) -> p f t", f=NXT)
        p.wait("gpsimd", "sP", p2m["sce"])
        p.wait("gpsimd", "sSc0", p2m["yg"])
        def fcd(E):
            return E.dma_start(
                out=V["cdg"][:, :].rearrange("p (f k c) -> p f k c",
                                             f=NXT, k=KCONV),
                in_=ap["convdiag"])
        p.do("gpsimd", fcd, "sW0", 16)
        p.wait("sync", "sP", p2m["xbcg"])
        if g == 0:
            def fx(E, base=base):
                return E.dma_start(
                    out=xbcg3[:, :, 3:515],
                    in_=scr["xbc_scr"].rearrange(
                        "(f p) t -> p f t", p=128)[:, :, base:base + 512])
            p.do("sync", fx, "sX1", 16)
            if seg == "red" and g == 0:
                for kslot in range(4):
                    def fhl(E, kslot=kslot):
                        return E.dma_start(
                            out=sb["halos4"][:, kslot * 136:
                                             (kslot + 1) * 136],
                            in_=scr["win_halo"].rearrange(
                                "a b p c -> (a b p) c")[
                                kslot * 128:(kslot + 1) * 128, :])
                    p.do("sync", fhl, "sX0", 16)
            p.wait("vector", "sX0")
            p.wait("vector", "sX1")
            mbase = 16 if seg == "own" else 20
            def fhz(E):
                return E.memset(xbcg3[:, :, 0:3], 0.0)
            p.do("vector", fhz, "sV", 1)
            p.wait("vector", "sV")
            for kslot in range(4):
                def fhm(E, mbase=mbase, kslot=kslot):
                    return E.scalar_tensor_tensor(
                        out=xbcg3[:, :, 0:3],
                        in0=sb["halos4"][:, kslot * 136:(kslot + 1) * 136]
                        .rearrange("p (f t) -> p f t", f=NXT)[:, :, 0:3],
                        scalar=sb["masks"][:, mbase + kslot:
                                           mbase + kslot + 1],
                        in1=xbcg3[:, :, 0:3], op0=ALU.mult, op1=ALU.add)
                p.do("vector", fhm, "sV", 1)
                p.wait("vector", "sV")
            p.wait("tensor", "sV")
        else:
            def fx(E, base=base, g=g):
                return E.dma_start(
                    out=xbcg3[:, :, 0:515],
                    in_=scr["xbc_scr"].rearrange(
                        "(f p) t -> p f t", p=128)[:, :,
                        base + g * 512 - 3:base + (g + 1) * 512])
            p.do("sync", fx, "sX1", 16)
        p.wait("tensor", "sX1")
        p.wait("tensor", "sW0")
        p.wait("scalar", "sV")
        p.wait("scalar", "sEv1")
        for fb in range(NXT):
            pcv = PV("acc1", 0, 512) if fb % 2 == 0 else PV("acc1", 512, 1024)
            prk = "cv0" if fb % 2 == 0 else "cv1"
            p.wait("tensor", "sEv0", p2m[prk])
            def fm(E, fb=fb, pcv=pcv):
                last = None
                for k in range(KCONV):
                    last = E.matmul(
                        out=pcv,
                        lhsT=V["cdg"][:, :].rearrange(
                            "p (f k c) -> p f k c", f=NXT,
                            k=KCONV)[:, fb, k, :],
                        rhs=xbcg3[:, fb, k:k + 512],
                        start=(k == 0), stop=(k == KCONV - 1))
                return last
            p.do("tensor", fm, "sP", 1)
            p.wait("scalar", "sP")
            def fs(E, fb=fb, pcv=pcv):
                return E.activation(out=xsgv3[:, fb, :], in_=pcv,
                                    func=AF.Silu)
            p.do("scalar", fs, "sEv0", 1)
            p2m[prk] = p.val("sEv0")
        p2m["xbcg"] = p.val("sP")
        p.wait("tensor", "sEv0")

    def phase_p2_chunk(seg, c, full):
        V = p2v()
        ro = rows_of(seg)
        rbase = 0
        gl = c % 4
        xsgv = V["xsg"][:, :].rearrange("p (f t) -> p f t", f=NXT)
        ptr = PV("acc1", 1024, 1536)
        ptrb = PVb("acc1", 1536, 2048)

        def pe_transpose_to(dst_sb, src_ap, bf, width, ibase=0):
            p.wait("tensor", "sEv1", p2m["ptr"])
            pdst = ptrb if bf else ptr
            ident = sb["identb"] if bf else sb["identf"]
            iw = src_ap.shape[0]
            def ft(E, pdst=pdst, src_ap=src_ap, ident=ident, iw=iw,
                   width=width, ibase=ibase):
                return E.transpose(
                    out=pdst[:, 0:width], in_=src_ap,
                    identity=ident[ibase:ibase + iw, ibase:ibase + iw])
            p.do("tensor", ft, "sP", 1)
            p.wait("vector", "sP")
            def fe(E, dst_sb=dst_sb, pdst=pdst, width=width):
                return E.tensor_copy(dst_sb, pdst[:, 0:width])
            p.do("vector", fe, "sEv1", 1)
            p2m["ptr"] = p.val("sEv1")

        pe_transpose_to(sb["wT"][:, :],
                        V["amlw"][64:128, c * 128:(c + 1) * 128], False, 64,
                        ibase=64)
        if full:
            pe_transpose_to(sb["aclT"][:, :],
                            V["amlw"][0:64, c * 128:(c + 1) * 128], False, 64)
        pe_transpose_to(sb["B_T"][:, :], xsgv[:, 32, gl * 128:(gl + 1) * 128],
                        True, 128)
        if full:
            p.wait("tensor", "sEv1", p2m["ptr"])
            def fgt(E, gl=gl):
                return E.matmul(out=ptr[:, 0:128],
                                lhsT=xsgv[:, 32, gl * 128:(gl + 1) * 128],
                                rhs=xsgv[:, 33, gl * 128:(gl + 1) * 128],
                                start=True, stop=True)
            p.do("tensor", fgt, "sP", 1)
            p.wait("vector", "sP")
            def fegt(E):
                return E.tensor_mul(sb["GT_sb"][:, :], ptr[:, 0:128],
                                    sb["trib"][:, :])
            p.do("vector", fegt, "sEv1", 1)
            p2m["ptr"] = p.val("sEv1")

        p.wait("vector", "sP", p2m["xsTc"])
        for fb in range(32):
            p.wait("tensor", "sEv1", p2m["ptr"])
            def ftx(E, fb=fb, gl=gl):
                return E.transpose(out=ptrb[:, 0:128],
                                   in_=xsgv[:, fb, gl * 128:(gl + 1) * 128],
                                   identity=sb["identb"][:, :])
            p.do("tensor", ftx, "sP", 1)
            p.wait("vector", "sP")
            if full:
                def fxr(E, fb=fb):
                    return E.tensor_copy(
                        V["xs_T"][:, fb * 128:(fb + 1) * 128], ptrb[:, 0:128])
                p.do("vector", fxr, "sV", 1)
            def fxw(E, fb=fb):
                in1 = sb["wT"][:, 2 * fb:2 * fb + 2].rearrange(
                    "p (r o) -> p r o", o=1).to_broadcast([128, 2, 64])
                return E.tensor_tensor(
                    out=V["xsw_T"][:, fb * 128:(fb + 1) * 128].rearrange(
                        "p (r q) -> p r q", r=2),
                    in0=ptrb[:, 0:128].rearrange("p (r q) -> p r q", r=2),
                    in1=in1, op=ALU.mult)
            p.do("vector", fxw, "sEv1", 1)
            p2m["ptr"] = p.val("sEv1")

        if full:
            p.wait("vector", "sEv1")
            for q in range(16):
                if q % 4 == 0:
                    q4 = q // 4
                    p.wait("sync", "sP", p2m["acrow"])
                    p.wait("sync", "sV")
                    def fgr(E, q4=q4, c=c):
                        srcg = view("ar1", 121344, 128, 2048, F32)[
                            rbase + 16 * q4:rbase + 16 * (q4 + 1),
                            c * 128:(c + 1) * 128]
                        dst = V["acrow"][0:1, :].rearrange(
                            "o (r t) -> o r t", r=16)
                        return E.dma_start(out=dst, in_=srcg)
                    p.do("sync", fgr, "sTr", 16)
                    p.wait("tensor", "sTr")
                pb = PV("acc0", 0, 512) if q % 2 == 0 else PV("acc0", 512,
                                                              1024)
                pbk = "pb0" if q % 2 == 0 else "pb1"
                p.wait("tensor", "sEv0", p2m[pbk])
                def fbc(E, q=q, pb=pb):
                    a0 = (q % 4) * 512
                    return E.matmul(out=pb, lhsT=sb["onesf"][:, :],
                                    rhs=V["acrow"][0:1, a0:a0 + 512],
                                    start=True, stop=True)
                p.do("tensor", fbc, "sP", 1)
                if q % 4 == 3:
                    p2m["acrow"] = p.val("sP")
                p.wait("vector", "sP")
                p.wait("vector", "sA")
                def fsub(E, q=q, pb=pb):
                    in1 = sb["aclT"][:, 4 * q:4 * q + 4].rearrange(
                        "p (r o) -> p r o", o=1).to_broadcast([128, 4, 128])
                    return E.tensor_tensor(
                        out=V["argb"][:, :].rearrange(
                            "p (r t) -> p r t", r=4),
                        in0=pb.rearrange("p (r t) -> p r t", r=4),
                        in1=in1, op=ALU.subtract)
                p.do("vector", fsub, "sV", 1)
                p.wait("vector", "sV")
                def fcl(E):
                    return E.tensor_scalar_min(V["argb"][:, :],
                                               V["argb"][:, :], 80.0)
                p.do("vector", fcl, "sV", 1)
                p.wait("scalar", "sV")
                def fes(E, q=q):
                    return E.activation(
                        out=V["S_sb"][:, q * 512:(q + 1) * 512],
                        in_=V["argb"][:, :], func=AF.Exp)
                p.do("scalar", fes, "sA", 1)
                def fee(E, q=q, pb=pb):
                    return E.activation(
                        out=V["Ce_sb"][:, q * 512:(q + 1) * 512],
                        in_=pb, func=AF.Exp)
                p.do("scalar", fee, "sEv0", 1)
                p2m[pbk] = p.val("sEv0")
                p.wait("vector", "sA")
                p.wait("vector", "sEv0")
                def fsm(E, q=q):
                    in1 = sb["GT_sb"][:, :].rearrange(
                        "p (o t) -> p o t", o=1).to_broadcast([128, 4, 128])
                    return E.tensor_tensor(
                        out=V["S_sb"][:, q * 512:(q + 1) * 512].rearrange(
                            "p (r t) -> p r t", r=4),
                        in0=V["S_sb"][:, q * 512:(q + 1) * 512].rearrange(
                            "p (r t) -> p r t", r=4),
                        in1=in1, op=ALU.mult)
                p.do("vector", fsm, "sV", 1)
                def fcm(E, q=q, gl=gl):
                    in1 = xsgv[:, 33, gl * 128:(gl + 1) * 128].rearrange(
                        "p (o t) -> p o t", o=1).to_broadcast([128, 4, 128])
                    return E.tensor_tensor(
                        out=V["Ce_sb"][:, q * 512:(q + 1) * 512].rearrange(
                            "p (r t) -> p r t", r=4),
                        in0=V["Ce_sb"][:, q * 512:(q + 1) * 512].rearrange(
                            "p (r t) -> p r t", r=4),
                        in1=in1, op=ALU.mult)
                p.do("vector", fcm, "sV", 1)
            p2m["acrow"] = p.val("sP")

            p.wait("vector", "sSc0", p2m["yg"])
            p.wait("tensor", "sV")
            p.wait("tensor", "sA")
            for pp in range(32):
                py = (PV("acc0", 1024, 1536) if pp % 2 == 0
                      else PV("acc0", 1536, 2048))
                pyk = "py0" if pp % 2 == 0 else "py1"
                p.wait("tensor", "sEv1", p2m[pyk])
                p.wait("tensor", "sV", p2m["hbf"])
                def fy(E, pp=pp, py=py):
                    last = None
                    for rr in range(2):
                        r = 2 * pp + rr
                        po = py[64 * rr:64 * (rr + 1), 0:128]
                        last = E.matmul(
                            out=po,
                            lhsT=V["xs_T"][:, pp * 128 + 64 * rr:
                                           pp * 128 + 64 * (rr + 1)],
                            rhs=V["S_sb"][:, r * 128:(r + 1) * 128],
                            start=True, stop=False)
                        last = E.matmul(
                            out=po,
                            lhsT=V["h_bf"][:, pp * 128 + 64 * rr:
                                           pp * 128 + 64 * (rr + 1)],
                            rhs=V["Ce_sb"][:, r * 128:(r + 1) * 128],
                            start=False, stop=True)
                    return last
                p.do("tensor", fy, "sP", 1)
                p.wait("vector", "sP")
                def fye(E, pp=pp, py=py, gl=gl, c=c):
                    return E.scalar_tensor_tensor(
                        out=V["y_grp"][:, :].rearrange(
                            "p (f t) -> p f t", f=32)[:, pp, :],
                        in0=xsgv[:, pp, gl * 128:(gl + 1) * 128],
                        scalar=sb["Drep"][:, pp:pp + 1],
                        in1=py[:, 0:128],
                        op0=ALU.mult, op1=ALU.add)
                p.do("vector", fye, "sEv1", 1)
                p2m[pyk] = p.val("sEv1")
            p2m["sce"] = p.val("sP")

        def fgam(E, c=c):
            return E.dma_start(out=sb["grow"][:, :],
                               in_=sb["asumsT"][c:c + 1, :])
        p.do("sync", fgam, "sX0", 16)
        p.wait("tensor", "sX0")
        pbg = PV("acc0", 0, 512)
        p.wait("tensor", "sEv0", p2m["pb0"])
        def fbg(E):
            return E.matmul(out=pbg[:, 0:64], lhsT=sb["onesf"][:, :],
                            rhs=sb["grow"][0:1, :], start=True, stop=True)
        p.do("tensor", fbg, "sP", 1)
        p.wait("scalar", "sP")
        def feg(E):
            return E.activation(out=sb["Gexp"][:, :], in_=pbg[:, 0:64],
                                func=AF.Exp)
        p.do("scalar", feg, "sEv0", 1)
        p2m["pb0"] = p.val("sEv0")

        p.wait("gpsimd", "sEv0")
        p.wait("tensor", "sEv1")
        for w in range(8):
            pc = PV("acc1", 0, 512) if w % 2 == 0 else PV("acc1", 512, 1024)
            pck = "pc0" if w % 2 == 0 else "pc1"
            cvk = "cv0" if w % 2 == 0 else "cv1"
            p.wait("tensor", "sEv1", p2m[pck])
            p.wait("tensor", "sEv0", p2m[cvk])
            def fcb(E, w=w, pc=pc):
                return E.matmul(out=pc, lhsT=sb["B_T"][:, :],
                                rhs=V["xsw_T"][:, w * 512:(w + 1) * 512],
                                start=True, stop=True)
            p.do("tensor", fcb, "sP", 1)
            thv = V["th0"] if w % 2 == 0 else V["th1"]
            thk = "th0" if w % 2 == 0 else "th1"
            p.wait("gpsimd", "sEv1", p2m[thk])
            p.wait("gpsimd", "sEv1", p2m["hupd"][w])
            def fth(E, w=w, thv=thv):
                in1 = sb["Gexp"][:, 8 * w:8 * w + 8].rearrange(
                    "p (r o) -> p r o", o=1).to_broadcast([128, 8, 64])
                return E.tensor_tensor(
                    out=thv.rearrange("p (r q) -> p r q", r=8),
                    in0=V["h"][:, w * 512:(w + 1) * 512].rearrange(
                        "p (r q) -> p r q", r=8),
                    in1=in1, op=ALU.mult)
            p.do("gpsimd", fth, "sG", 1)
            p.wait("vector", "sG")
            p.wait("vector", "sP")
            def fhu(E, w=w, pc=pc, thv=thv):
                return E.tensor_add(V["h"][:, w * 512:(w + 1) * 512],
                                    thv, pc)
            p.do("vector", fhu, "sEv1", 1)
            p2m[pck] = p.val("sEv1")
            p2m["hupd"][w] = p.val("sEv1")
            p2m[thk] = p.val("sEv1")
        p2m["xsTc"] = p.val("sP")

        if full:
            p.wait("vector", "sEv1")
            def fhb(E):
                return E.tensor_copy(V["h_bf"][:, :], V["h"][:, :])
            p.do("vector", fhb, "sV", 1)
            p2m["hbf"] = p.val("sV")
            p.wait("sync", "sEv1")
            def fyf(E, c=c):
                src2 = V["y_grp"][:, :].rearrange("p (f t) -> p f t", f=32)
                dst = scr["y_scr"].rearrange(
                    "(f p) t -> p f t", p=128)[:, :,
                    c * 128:(c + 1) * 128]
                return E.dma_start(out=dst, in_=src2)
            p.do("sync", fyf, "sSc0", 16)
            p2m["yg"] = p.val("sSc0")

    def phase_p2_export(seg):
        V = p2v()
        ro = rows_of(seg)
        slot_key = "my_S_own" if seg == "own" else "my_S_red"
        as_key = "my_As_own" if seg == "own" else "my_As_red"
        p.wait("vector", "sEv1")
        p.wait("vector", "sV")
        def fat(E, ro=ro):
            return E.scalar_tensor_tensor(
                out=sb["small"][0:64, 8:9], in0=sb["offsi"][ro, 15:16],
                scalar=1.0, in1=sb["asums"][:, 15:16],
                op0=ALU.mult, op1=ALU.add)
        p.do("vector", fat, "sV", 1)
        p.wait("sync", "sV")
        p.wait("sync", "sEv1")
        def fws(E, rows, slot_key=slot_key, as_key=as_key):
            r = rows[slot_key]
            ra = rows[as_key]
            E.dma_start(
                out=scr["win_S"].rearrange("a b p n -> (a b p) n")[
                    r:r + 128, :],
                in_=V["h"][:, :]).then_inc(p.sems["sWin"], 16)
            return E.dma_start(
                out=scr["win_As"].rearrange("a b r -> (a b) r")[
                    ra:ra + 1, :],
                in_=sb["small"][0:64, 8:9])
        p.roled("sync", fws, "sWin", 16)
        p.inc("sWin", 16)
        if seg == "own":
            def fac(E):
                return E.dma_start(out=scr["acum_scr"][0:64, :],
                                   in_=view("ar1", 121344, 64, 2048, F32))
            p.do("sync", fac, "sWin", 16)
        p.wait("sync", "sWin")

    def write_flag(j):
        def ff(E, rows, j=j):
            r = rows[f"my_flag{j}"]
            return E.dma_start(
                out=scr["win_flag"].rearrange("a b c -> (a b) c")[r:r + 1, :],
                in_=sb["noncebuf"][:, :])
        p.roled("sync", ff, "sWin", 16)
        p.wait("sync", "sWin")

    def poll_flag(j):
        def fp(E, j=j):
            r0 = _WF(0, j)
            r1 = _WF(1, j)
            with E.register(f"pf{j}_{p.poll_ctr}a") as fv, \
                    E.register(f"pf{j}_{p.poll_ctr}b") as fw:
                E.reg_mov(fv, 1)
                with E.While(fv):
                    E.sem_clear(p.sems["sPoll"])
                    E.dma_start(
                        out=sb["flagrd"][:, 0:16],
                        in_=scr["win_flag"].rearrange(
                            "a b c -> (a b) c")[r0:r0 + 1, :]
                    ).then_inc(p.sems["sPoll"], 16)
                    E.dma_start(
                        out=sb["flagrd"][:, 16:32],
                        in_=scr["win_flag"].rearrange(
                            "a b c -> (a b) c")[r1:r1 + 1, :]
                    ).then_inc(p.sems["sPoll"], 16)
                    E.wait_ge(p.sems["sPoll"], 32)
                    E.reg_load(fv, sb["flagrd"][0:1, 0:1].bitcast(I32))
                    E.reg_sub(fv, fv, p.noncereg_val)
                    E.reg_load(fw, sb["flagrd"][0:1, 16:17].bitcast(I32))
                    E.reg_sub(fw, fw, p.noncereg_val)
                    E.reg_alu(fv, fv, fw, ALU.bitwise_or)
            return None
        if p.eng == "sync":
            fp(p.E)
            p.E.sem_inc(p.sems["sG"], 1)
        p.inc("sG", 1)
        p.poll_ctr += 1

    def exchange_combine(stage):
        V = p2v()
        dst = V["hst"] if stage == 1 else V["hin"]
        slots = ([(0, 0), (0, 1), (1, 0), (1, 1)] if stage == 1
                 else [(0, 0), (0, 1), (0, 2), (1, 0), (1, 1), (1, 2)])
        selA0 = 24 if stage == 1 else 36
        selB0 = 28 if stage == 1 else 42
        selG0 = 32 if stage == 1 else 48
        mgcol = 2 if stage == 1 else 5
        macol = 0 if stage == 1 else 3
        p.wait("sync", "sV")
        def fz(E, dst=dst):
            E.memset(V["SaS"][:, :], 0.0)
            E.memset(V["SbS"][:, :], 0.0)
            return E.memset(sb["grow"][:, :], 0.0)
        p.do("vector", fz, "sV", 1)
        for ki, (par, s) in enumerate(slots):
            wrow = _WS(par, s)
            arow = _WA(par, min(s, 1))
            wb = V["hwA"] if ki % 2 == 0 else V["hwB"]
            sxk = "sX0" if ki % 2 == 0 else "sX1"
            mk_key = f"exw{ki % 2}"
            p.wait("sync", "sV", p2m.get(mk_key, 0))
            def frd(E, wrow=wrow, wb=wb):
                return E.dma_start(
                    out=wb[:, :],
                    in_=scr["win_S"].rearrange("a b p n -> (a b p) n")[
                        wrow:wrow + 128, :])
            p.do("sync", frd, sxk, 16)
            def fga(E, arow=arow, ki=ki):
                return E.dma_start(
                    out=view("ar2", 32768, 1, 384, F32)[:, ki * 64:(ki + 1) * 64],
                    in_=scr["win_As"].rearrange("a b r -> (a b) r")[
                        arow:arow + 1, :])
            p.do("sync", fga, sxk, 16)
            p.wait("vector", sxk)
            p.wait("vector", "sV")
            def facc(E, ki=ki, wb=wb, selA0=selA0):
                return E.scalar_tensor_tensor(
                    out=V["SaS"][:, :], in0=wb[:, :],
                    scalar=sb["masks"][:, selA0 + ki:selA0 + ki + 1],
                    in1=V["SaS"][:, :], op0=ALU.mult, op1=ALU.add)
            p.do("vector", facc, "sV", 1)
            p.wait("vector", "sV")
            def facb(E, ki=ki, wb=wb, selB0=selB0):
                return E.scalar_tensor_tensor(
                    out=V["SbS"][:, :], in0=wb[:, :],
                    scalar=sb["masks"][:, selB0 + ki:selB0 + ki + 1],
                    in1=V["SbS"][:, :], op0=ALU.mult, op1=ALU.add)
            p.do("vector", facb, "sV", 1)
            p.wait("vector", "sV")
            def facg(E, ki=ki, selG0=selG0):
                return E.scalar_tensor_tensor(
                    out=sb["grow"][:, :],
                    in0=view("ar2", 32768, 1, 384, F32)[:, ki * 64:(ki + 1) * 64],
                    scalar=sb["masks"][0:1, selG0 + ki:selG0 + ki + 1],
                    in1=sb["grow"][:, :], op0=ALU.mult, op1=ALU.add)
            p.do("vector", facg, "sV", 1)
            p2m[f"exw{ki % 2}"] = p.val("sV")
        pbg = PV("acc0", 0, 512)
        p.wait("tensor", "sV")
        p.wait("tensor", "sEv0", p2m["pb0"])
        def fbg(E):
            return E.matmul(out=pbg[:, 0:64], lhsT=sb["onesf"][:, :],
                            rhs=sb["grow"][0:1, :], start=True, stop=True)
        p.do("tensor", fbg, "sP", 1)
        p.wait("scalar", "sP")
        def feg(E, mgcol=mgcol):
            return E.activation(out=V["G1e"][:, 0:64], in_=pbg[:, 0:64],
                                func=AF.Exp,
                                scale=sb["masks"][:, mgcol:mgcol + 1])
        p.do("scalar", feg, "sEv0", 1)
        p2m["pb0"] = p.val("sEv0")
        p.wait("scalar", "sEv0")
        def fmg(E, macol=macol):
            return E.activation(out=V["G1e"][:, 0:64], in_=V["G1e"][:, 0:64],
                                func=AF.Copy,
                                scale=sb["masks"][:, macol:macol + 1])
        p.do("scalar", fmg, "sA", 1)
        p.wait("vector", "sA")
        def fc1(E, dst=dst):
            in1 = V["G1e"][:, 0:64].rearrange(
                "p (r o) -> p r o", o=1).to_broadcast([128, 64, 64])
            return E.tensor_tensor(
                out=dst[:, :].rearrange("p (r q) -> p r q", r=64),
                in0=V["SaS"][:, :].rearrange("p (r q) -> p r q", r=64),
                in1=in1, op=ALU.mult)
        p.do("vector", fc1, "sV", 1)
        p.wait("vector", "sV")
        def fc2(E, dst=dst):
            return E.tensor_add(dst[:, :], dst[:, :], V["SbS"][:, :])
        p.do("vector", fc2, "sV", 1)
        p.wait("vector", "sV")
        if stage == 2:
            def fc3(E, dst=dst):
                return E.scalar_tensor_tensor(
                    out=dst[:, :], in0=V["hst"][:, :],
                    scalar=sb["masks"][:, 6:7], in1=dst[:, :],
                    op0=ALU.mult, op1=ALU.add)
            p.do("vector", fc3, "sV", 1)
            p.wait("vector", "sV")

    def phase_exchange():
        V = p2v()
        write_flag(1)
        poll_flag(1)
        p.wait("vector", "sG")
        p.wait("tensor", "sG")
        exchange_combine(1)
        p.wait("sync", "sV")
        def fws(E, rows):
            r = rows["my_stage1"]
            return E.dma_start(
                out=scr["win_S"].rearrange("a b p n -> (a b p) n")[
                    r:r + 128, :],
                in_=V["hst"][:, :])
        p.roled("sync", fws, "sWin", 16)
        p.wait("sync", "sWin")
        write_flag(2)
        poll_flag(2)
        p.wait("vector", "sG")
        p.wait("tensor", "sG")
        exchange_combine(2)
        hinb = view("ar2", 0, 128, 4096, BF16)
        def fhb(E, hinb=hinb):
            return E.tensor_copy(hinb, V["hin"][:, :])
        p.do("vector", fhb, "sV", 1)


    def p3v():
        return {
            "g_norm": view("ar1", 0, 128, NK * 2048, BF16),
            "hinb": view("ar2", 0, 128, 4096, BF16),
            "ytile": view("ar2", 8192, 128, 32 * 256, BF16),
            "ztile": view("ar2", 24576, 128, 32 * 256, BF16),
            "acrow3": sb["wbuf0"][:, :].bitcast(F32),
            "cdiag33": sb["evst0"][:, 0:512],
            "cchunk": sb["evst0"][:, 512:640],
            "acg": sb["evst1"][:, 0:256].bitcast(F32),
            "gsq": sb["evst1"][:, 1024:1536],
            "rstrow": view("ar2", 8192, 1, 2048, F32),
            "gtmp": sb["evst1"][:, 512:768].bitcast(F32),
        }

    def phase_p3a():
        V = p3v()
        gn3 = V["g_norm"][:, :].rearrange("p (f t) -> p f t", f=NK)
        def fcd(E):
            return E.dma_start(
                out=V["cdiag33"].rearrange("p (k c) -> p k c", k=KCONV),
                in_=ap["convdiag"][:, 33, :, :])
        p.do("gpsimd", fcd, "sW0", 16)
        def fnc(E):
            return E.dma_start(out=sb["nrmwc"][:, :], in_=ap["nrmwc"])
        p.do("sync", fnc, "sX0", 16)
        def fhd(E):
            return E.dma_start(out=sb["wrep"][0:64, :], in_=ap["hdrep"])
        p.do("gpsimd", fhd, "sW0", 16)
        eac = sb["wbuf1"][0:64, 0:256]
        acgv = sb["wbuf1"][:, :].bitcast(F32)[0:64, 256:512]
        slab = sb["wbuf0"][:, 0:260]
        cc = sb["wbuf0"][:, 512:768]
        for s in range(8):
            par = s % 2
            c0 = 2 * s
            ybuf = view("ar2", 8192 + par * 16384, 128, 32 * 256, BF16)
            yeng, ysem = ("sync", "sSc0") if par == 0 else ("scalar", "sSc1")
            p.wait(yeng, "sV", p2m.get(f"a3y{par}", 0))
            def fyl(E, c0=c0, ybuf=ybuf):
                return E.dma_start(
                    out=ybuf[:, :].rearrange("p (f t) -> p f t", f=32),
                    in_=scr["y_scr"].rearrange(
                        "(f p) t -> p f t",
                        p=128)[:, :, c0 * 128:(c0 + 2) * 128])
            p.do(yeng, fyl, ysem, 16)
            p2m[f"a3yl{par}"] = p.val(ysem)
            p.wait("sync", "sP", p2m.get("a3ac", 0))
            def fac(E, s=s):
                return E.dma_start(
                    out=acgv,
                    in_=scr["acum_scr"][0:64, s * 256:(s + 1) * 256])
            p.do("sync", fac, "sX0", 16)
            p.wait("vector", "sX0")
            for i in range(2):
                def fofs(E, i=i, c0=c0):
                    return E.tensor_scalar(
                        out=acgv[:, i * 128:(i + 1) * 128],
                        in0=acgv[:, i * 128:(i + 1) * 128],
                        scalar1=sb["offsi"][0:64, c0 + i:c0 + i + 1],
                        scalar2=None, op0=ALU.add)
                p.do("vector", fofs, "sV", 1)
            p.wait("scalar", "sV")
            p.wait("scalar", "sP", p2m.get("a3ac", 0))
            def fexp(E):
                return E.activation(out=eac, in_=acgv, func=AF.Exp)
            p.do("scalar", fexp, "sA", 1)
            p2m["a3ex"] = p.val("sA")
            p.wait("gpsimd", "sP", p2m.get("a3cv", 0))
            def fsl(E, s=s):
                if s == 0:
                    return E.dma_start(
                        out=slab[:, 3:259],
                        in_=scr["xbc_scr"][33 * 128:34 * 128, 0:256])
                lo = s * 256 - 3
                return E.dma_start(
                    out=slab[:, 0:259],
                    in_=scr["xbc_scr"][33 * 128:34 * 128, lo:lo + 259])
            p.do("gpsimd", fsl, "sW1", 16)
            if s == 0:
                p.wait("vector", "sW1")
                def fhz(E):
                    return E.memset(slab[:, 0:3], 0.0)
                p.do("vector", fhz, "sV", 1)
                p.wait("vector", "sV")
                for kslot in range(4):
                    def fhm(E, kslot=kslot):
                        return E.scalar_tensor_tensor(
                            out=slab[:, 0:3],
                            in0=sb["halos4"][:, kslot * 136 + 33 * 4:
                                             kslot * 136 + 33 * 4 + 3],
                            scalar=sb["masks"][:, 16 + kslot:17 + kslot],
                            in1=slab[:, 0:3],
                            op0=ALU.mult, op1=ALU.add)
                    p.do("vector", fhm, "sV", 1)
                    p.wait("vector", "sV")
            pcv = PV("acc1", 0, 256)
            p.wait("tensor", "sW1")
            p.wait("tensor", "sW0")
            p.wait("tensor", "sA", p2m.get("a3sl", 0))
            if s == 0:
                p.wait("tensor", "sV")
            def fcm(E, pcv=pcv):
                last = None
                for k in range(KCONV):
                    last = E.matmul(
                        out=pcv,
                        lhsT=V["cdiag33"].rearrange(
                            "p (k cc) -> p k cc", k=KCONV)[:, k, :],
                        rhs=slab[:, k:k + 256],
                        start=(k == 0), stop=(k == KCONV - 1))
                return last
            p.do("tensor", fcm, "sP", 1)
            p.wait("scalar", "sP")
            def fsilu(E, pcv=pcv):
                return E.activation(out=cc, in_=pcv, func=AF.Silu)
            p.do("scalar", fsilu, "sA", 1)
            p2m["a3sl"] = p.val("sA")
            p2m["a3slv"] = p.val("sA")
            for pp in range(32):
                ppar = pp % 2
                pe = PV("acc1", 1024 + 512 * ppar, 1024 + 512 * ppar + 256)
                py = PV("acc0", 512 * ppar, 512 * ppar + 256)
                ebuf = sb["evst0"][:, 1024 + 256 * ppar:1280 + 256 * ppar]
                p.wait("tensor", "sA", p2m.get("a3ex", 0))
                p.wait("tensor", "sA", p2m.get(f"a3e{ppar}", 0))
                def fmb(E, pp=pp, pe=pe):
                    return E.matmul(
                        out=pe,
                        lhsT=sb["wrep"][0:64, pp * 128:(pp + 1) * 128],
                        rhs=sb["wbuf1"][0:64, 0:256],
                        start=True, stop=True)
                p.do("tensor", fmb, "sP", 1)
                p2m["a3mb"] = p.val("sP")
                p.wait("scalar", "sP", p2m.get("a3mb", 0))
                p.wait("scalar", "sV", p2m.get(f"a3b{ppar}", 0))
                def fev(E, pe=pe, ebuf=ebuf):
                    return E.activation(out=ebuf, in_=pe, func=AF.Copy)
                p.do("scalar", fev, "sA", 1)
                p2m[f"a3e{ppar}"] = p.val("sA")
                p.wait("tensor", "sA", p2m.get("a3slv", 0))
                p.wait("tensor", "sV", p2m.get(f"a3p{ppar}", 0))
                def fmy(E, pp=pp, py=py):
                    return E.matmul(
                        out=py,
                        lhsT=V["hinb"][:, pp * 128:(pp + 1) * 128],
                        rhs=cc, start=True, stop=True)
                p.do("tensor", fmy, "sP", 1)
                p2m["a3my"] = p.val("sP")
                p.wait("vector", "sA", p2m.get(f"a3e{ppar}", 0))
                p.wait("vector", "sP", p2m.get("a3my", 0))
                def fd1(E, py=py, ebuf=ebuf):
                    return E.tensor_mul(py, py, ebuf)
                p.do("vector", fd1, "sV", 1)
                p2m[f"a3b{ppar}"] = p.val("sV")
                p.wait("vector", "sV")
                p.wait("vector", ysem, p2m.get(f"a3yl{par}", 0))
                def fd2(E, pp=pp, s=s, py=py, ybuf=ybuf):
                    yt = ybuf[:, :].rearrange(
                        "p (f t) -> p f t", f=32)[:, pp, :]
                    return E.tensor_add(
                        gn3[:, pp, s * 256:(s + 1) * 256], py, yt)
                p.do("vector", fd2, "sV", 1)
                p2m[f"a3p{ppar}"] = p.val("sV")
            p2m["a3ac"] = p.val("sP")
            p2m["a3cv"] = p.val("sP")
            p2m[f"a3y{par}"] = p.val("sV")
        barrier()
        for tt in range(4):
            prs = PV("acc1", 0, 512)
            p.wait("tensor", "sEv0", p2m.get("p3rs", 0))
            def frs(E, tt=tt):
                last = None
                for k in range(NK):
                    last = E.matmul(
                        out=prs[0:1, :],
                        lhsT=sb["onesb"][:, :],
                        rhs=V["g_norm"][:, :].rearrange(
                            "p (f t) -> p f t", f=NK)[:, k,
                            tt * 512:(tt + 1) * 512].rearrange(
                            "p t -> p (t)") if False else
                        V["gsq"][:, :],
                        start=(k == 0), stop=(k == NK - 1))
                return last
            def frs2(E, tt=tt, prs=prs):
                last = None
                for k in range(NK):
                    E.wait_ge(p.sems["sA"], 0)
                return last
            del frs, frs2
            for k in range(NK):
                zidx = tt * NK + k
                zpar = zidx % 2
                zeng, zsem = (("sync", "sSc0") if zpar == 0
                              else ("gpsimd", "sW1"))
                zk = sb["wbuf1"][:, 1024 + 512 * zpar:1536 + 512 * zpar]
                gsqp = sb["evst1"][:, 1024 + 512 * zpar:1536 + 512 * zpar]
                p.wait(zeng, "sV", p2m.get(f"z3{zpar}", 0))
                def fzl(E, tt=tt, k=k, zk=zk):
                    return E.dma_start(
                        out=zk,
                        in_=scr["zsil_scr"][k * 128:(k + 1) * 128,
                                            tt * 512:(tt + 1) * 512])
                p.do(zeng, fzl, zsem, 16)
                p.wait("vector", zsem)
                def fzm(E, tt=tt, k=k, zk=zk):
                    sl = V["g_norm"][:, :].rearrange(
                        "p (f t) -> p f t", f=NK)[:, k,
                        tt * 512:(tt + 1) * 512]
                    return E.tensor_mul(sl, sl, zk)
                p.do("vector", fzm, "sV", 1)
                p2m[f"z3{zpar}"] = p.val("sV")
                p.wait("scalar", "sV", p2m.get(f"z3{zpar}", 0))
                p.wait("scalar", "sP", p2m.get(f"p3sq{zpar}", 0))
                def fsq(E, tt=tt, k=k, gsqp=gsqp):
                    return E.activation(
                        out=gsqp,
                        in_=V["g_norm"][:, :].rearrange(
                            "p (f t) -> p f t", f=NK)[:, k,
                            tt * 512:(tt + 1) * 512],
                        func=AF.Square)
                p.do("scalar", fsq, "sA", 1)
                p.wait("tensor", "sA")
                def fmm(E, k=k, prs=prs, gsqp=gsqp):
                    return E.matmul(out=prs[0:1, :], lhsT=sb["onesb"][:, :],
                                    rhs=gsqp,
                                    start=(k == 0), stop=(k == NK - 1))
                p.do("tensor", fmm, "sP", 1)
                p2m[f"p3sq{zpar}"] = p.val("sP")
            p.wait("scalar", "sP")
            def fsr(E, tt=tt):
                return E.activation(out=p3v()["rstrow"][0:1, tt * 512:
                                                     (tt + 1) * 512],
                                    in_=prs[0:1, :], func=AF.Sqrt,
                                    scale=1.0 / DS, bias=EPS)
            p.do("scalar", fsr, "sEv0", 1)
            p2m["p3rs"] = p.val("sEv0")
        p.wait("vector", "sEv0")
        def frc(E):
            return E.reciprocal(p3v()["rstrow"][0:1, :], p3v()["rstrow"][0:1, :])
        p.do("vector", frc, "sV", 1)
        for tt in range(4):
            prb = PV("acc1", 512, 1024)
            p.wait("tensor", "sV")
            p.wait("tensor", "sEv1", p2m.get("p3rb", 0))
            def fbb(E, tt=tt, prb=prb):
                return E.matmul(out=prb, lhsT=sb["onesf"][:, :],
                                rhs=p3v()["rstrow"][0:1, tt * 512:
                                                 (tt + 1) * 512],
                                start=True, stop=True)
            p.do("tensor", fbb, "sP", 1)
            p.wait("vector", "sP")
            for k in range(NK):
                def fap(E, tt=tt, k=k, prb=prb):
                    sl = V["g_norm"][:, :].rearrange(
                        "p (f t) -> p f t", f=NK)[:, k,
                        tt * 512:(tt + 1) * 512]
                    return E.scalar_tensor_tensor(
                        out=sl, in0=sl, scalar=sb["nrmwc"][:, k:k + 1],
                        in1=prb, op0=ALU.mult, op1=ALU.mult)
                p.do("vector", fap, "sEv1", 1)
            p2m["p3rb"] = p.val("sEv1")

        barrier()
        for n in range(32):
            pr = n % 2
            wb = sb["wbuf0"] if pr == 0 else sb["wbuf1"]
            pb = ps["acc0"] if pr == 0 else ps["acc1"]
            eb = sb["evst0"] if pr == 0 else sb["evst1"]
            p.wait("gpsimd", "sP", p2m.get(f"p3w{pr}", 0))
            def fw(E, n=n, wb=wb):
                return E.dma_start(
                    out=wb[:, :].rearrange("p (k c) -> p k c", k=NK),
                    in_=ap["w_outproj"][n, :, :, :])
            p.do("gpsimd", fw, f"sW{pr}", 16)
            p.wait("tensor", f"sW{pr}")
            p.wait("tensor", "sEv0" if pr == 0 else "sEv1",
                   p2m.get(f"p3e{pr}", 0))
            def fm(E, wb=wb, pb=pb):
                last = None
                for k in range(NK):
                    lhsT = wb[:, :].rearrange("p (k c) -> p k c",
                                              k=NK)[:, k, :]
                    for t in range(4):
                        last = E.matmul(
                            out=pb[:, t * 512:(t + 1) * 512], lhsT=lhsT,
                            rhs=V["g_norm"][:, :].rearrange(
                                "p (f t) -> p f t", f=NK)[:, k,
                                t * 512:(t + 1) * 512],
                            start=(k == 0), stop=(k == NK - 1))
                return last
            p.do("tensor", fm, "sP", 1)
            p2m[f"p3w{pr}"] = p.val("sP")
            sevn = "sEv0" if pr == 0 else "sEv1"
            seng = "scalar" if pr == 0 else "vector"
            p.wait(seng, "sP")
            p.wait(seng, "sSc0" if pr == 0 else "sSc1",
                   p2m.get(f"p3o{pr}", 0))
            def fe(E, pb=pb, eb=eb, pr=pr):
                if pr == 0:
                    return E.activation(out=eb[:, :], in_=pb[:, :],
                                        func=AF.Copy)
                return E.tensor_copy(eb[:, :], pb[:, :])
            p.do(seng, fe, sevn, 1)
            p2m[f"p3e{pr}"] = p.val(sevn)
            p.wait("sync", sevn)
            def fo(E, n=n, eb=eb):
                return E.dma_start(
                    out=scr["attn_scr"][n * 128:(n + 1) * 128, :],
                    in_=eb[:, :])
            p.do("sync", fo, "sSc0" if pr == 0 else "sSc1", 16)
            p2m[f"p3o{pr}"] = p.val("sSc0" if pr == 0 else "sSc1")

    def p3bv():
        return {
            "x1f": view("ar1", 0, 128, NK * 2048, BF16),
            "attn_t": view("ar2", 0, 128, H, BF16),
            "xrow": view("ar2", 8192, 128, H, F32),
            "srow": view("ar2", 8192, 128, H, F32),
            "x1t": view("ar2", 24576, 128, H, BF16),
            "qsb2": view("ar2", 0, 16, 2048, BF16),
            "scs": view("ar2", 4096, 64, 2048, F32),
            "gatef": view("ar2", 12288, 64, 2048, BF16),
            "z2g": view("ar2", 16384, 128, 4 * 2048, BF16),
            "o2t": view("ar2", 32768, 128, 16 * 128, BF16),
            "wqs": view("ar2", 36864, 128, NK * 16, BF16),
            "expE": view("ar2", 37888, 64, 4 * 128, BF16),
            "sct": view("ar2", 38912, 128, 256, F32),
            "sct2": view("ar2", 39936, 128, 64, BF16),
            "keyss": view("ar2", 40064, 16, 64, BF16),
        }

    def phase_p3b_ln1():
        V = p3bv()
        x1f3 = V["x1f"][:, :].rearrange("p (f t) -> p f t", f=NK)
        for c in range(16):
            p.wait("sync", "sV", p2m.get("p3battn", 0))
            p.wait("gpsimd", "sV", p2m.get("p3battn", 0))
            def fat(E, c=c):
                return E.dma_start(
                    out=sb["evst0"][:, :].rearrange(
                        "p (f t) -> p f t", f=16),
                    in_=scr["attn_scr"].rearrange(
                        "(f p) t -> p f t",
                        p=128)[:, 0:16, c * 128:(c + 1) * 128])
            p.do("sync", fat, "sSc0", 16)
            def fat2(E, c=c):
                return E.dma_start(
                    out=sb["evst1"][:, :].rearrange(
                        "p (f t) -> p f t", f=16),
                    in_=scr["attn_scr"].rearrange(
                        "(f p) t -> p f t",
                        p=128)[:, 16:32, c * 128:(c + 1) * 128])
            p.do("gpsimd", fat2, "sW0", 16)
            def fxa(E, c=c):
                return E.dma_start(out=V["xrow"][:, 0:2048],
                                   in_=ap["xin"][c * 128:(c + 1) * 128,
                                                 0:2048])
            p.do("sync", fxa, "sX0", 16)
            def fxb(E, c=c):
                return E.dma_start(out=V["xrow"][:, 2048:4096],
                                   in_=ap["xin"][c * 128:(c + 1) * 128,
                                                 2048:4096])
            p.do("gpsimd", fxb, "sW1", 16)
            p.wait("scalar", "sSc0")
            def ftr(E):
                return E.dma_start(
                    out=V["attn_t"][:, 0:2048].rearrange(
                        "p (f q) -> p f q", f=16),
                    in_=sb["evst0"][:, :], transpose=True)
            p.do("scalar", ftr, "sTr", 16)
            p.wait("scalar", "sW0")
            def ftr2(E):
                return E.dma_start(
                    out=V["attn_t"][:, 2048:4096].rearrange(
                        "p (f q) -> p f q", f=16),
                    in_=sb["evst1"][:, :], transpose=True)
            p.do("scalar", ftr2, "sTr", 16)
            p.wait("vector", "sTr")
            p.wait("vector", "sX0")
            p.wait("vector", "sW1")
            def fs1(E):
                return E.scalar_tensor_tensor(
                    out=V["srow"][:, :], in0=V["attn_t"][:, :], scalar=1.0,
                    in1=V["xrow"][:, :], op0=ALU.mult, op1=ALU.add,
                    accum_out=sb["small"][:, 9:10])
            p.do("vector", fs1, "sV", 1)
            p.wait("vector", "sV")
            def fs2(E):
                return E.tensor_scalar(
                    out=V["srow"][:, :], in0=V["srow"][:, :],
                    scalar1=sb["small"][:, 9:10], scalar2=-1.0 / H,
                    op0=ALU.mult if False else ALU.bypass, op1=ALU.bypass)
            del fs2
            def fmn(E):
                return E.tensor_scalar_mul(sb["small"][:, 10:11],
                                           sb["small"][:, 9:10], -1.0 / H)
            p.do("vector", fmn, "sV", 1)
            p.wait("vector", "sV")
            def fs3(E):
                return E.tensor_scalar(
                    out=V["srow"][:, :], in0=V["srow"][:, :],
                    scalar1=sb["small"][:, 10:11], scalar2=None,
                    op0=ALU.add)
            p.do("vector", fs3, "sV", 1)
            p.wait("scalar", "sV")
            def fsq(E):
                return E.activation(out=V["attn_t"][:, :],
                                    in_=V["srow"][:, :], func=AF.Square,
                                    accum_out=sb["small"][:, 11:12])
            p.do("scalar", fsq, "sA", 1)
            p.wait("scalar", "sA")
            def fsd(E):
                return E.activation(out=sb["small"][:, 12:13],
                                    in_=sb["small"][:, 11:12],
                                    func=AF.Sqrt, scale=1.0 / H, bias=LNEPS)
            p.do("scalar", fsd, "sA", 1)
            p.wait("vector", "sA")
            def frv(E):
                return E.reciprocal(sb["small"][:, 12:13],
                                    sb["small"][:, 12:13])
            p.do("vector", frv, "sV", 1)
            p.wait("vector", "sV")
            p.wait("vector", "sTr", p2m.get("l1t", 0))
            p.wait("vector", "sW0", p2m.get("l1w", 0))
            def fx1(E):
                return E.tensor_scalar_mul(V["x1t"][:, :], V["srow"][:, :],
                                           sb["small"][:, 12:13])
            p.do("vector", fx1, "sV", 1)
            p2m["p3battn"] = p.val("sV")
            p.wait("sync", "sV")
            p.wait("gpsimd", "sV")
            def fxw(E, c=c):
                return E.dma_start(
                    out=scr["x1t_scr"][c * 128:(c + 1) * 128, :],
                    in_=V["x1t"][:, :])
            p.do("gpsimd", fxw, "sW0", 16)
            p2m["l1w"] = p.val("sW0")
            def fxf(E, c=c):
                outap = x1f3[:, :, c * 128:(c + 1) * 128]
                return E.dma_start(out=outap, in_=V["x1t"][:, :],
                                   transpose=True)
            p.do("sync", fxf, "sTr", 16)
            p2m["l1t"] = p.val("sTr")
        p.wait("tensor", "sTr")

    def phase_p3b_ffn():
        V = p3bv()
        x1f3 = V["x1f"][:, :].rearrange("p (f t) -> p f t", f=NK)
        def fwq(E):
            return E.dma_start(
                out=V["wqs"][:, :].rearrange("p (k c) -> p k c", k=NK),
                in_=ap["w_ffnQ"])
        p.do("gpsimd", fwq, "sW0", 16)
        def fky(E):
            return E.dma_start(out=V["keyss"][:, :], in_=ap["keysT"])
        p.do("gpsimd", fky, "sW0", 16)
        def fex(E):
            return E.dma_start(
                out=V["expE"][:, :].rearrange("p (j c) -> p j c", j=4),
                in_=ap["expandE"])
        p.do("gpsimd", fex, "sW0", 16)
        p.wait("tensor", "sW0")
        pq = ps["acc0"][0:16, :]
        def fq(E, pq=pq):
            last = None
            for k in range(NK):
                for t in range(4):
                    last = E.matmul(
                        out=pq[:, t * 512:(t + 1) * 512],
                        lhsT=V["wqs"][:, :].rearrange(
                            "p (k c) -> p k c", k=NK)[:, k, :],
                        rhs=x1f3[:, k, t * 512:(t + 1) * 512],
                        start=(k == 0), stop=(k == NK - 1))
            return last
        p.do("tensor", fq, "sP", 1)
        p.wait("scalar", "sP")
        def feq(E, pq=pq):
            return E.activation(out=V["qsb2"][0:16, :], in_=pq,
                                func=AF.Copy)
        p.do("scalar", feq, "sEv0", 1)
        p.wait("tensor", "sEv0")
        psc = ps["acc1"][0:64, :]
        def fsc(E, psc=psc):
            last = None
            for t in range(4):
                last = E.matmul(out=psc[:, t * 512:(t + 1) * 512],
                                lhsT=V["keyss"][0:16, :],
                                rhs=V["qsb2"][0:16, t * 512:(t + 1) * 512],
                                start=True, stop=True)
            return last
        p.do("tensor", fsc, "sP", 1)
        p.wait("vector", "sP")
        def fes(E, psc=psc):
            return E.tensor_copy(V["scs"][:, :], psc)
        p.do("vector", fes, "sEv1", 1)
        for c in range(16):
            ptr2 = ps["acc0"][0:128, 1024:1536]
            p.wait("tensor", "sEv1")
            p.wait("tensor", "sEv0", p2m.get("p3g", 0))
            def ft1(E, c=c, ptr2=ptr2):
                return E.transpose(out=ptr2[:, 0:64],
                                   in_=V["scs"][:, c * 128:(c + 1) * 128],
                                   identity=sb["identf"][0:64, 0:64])
            p.do("tensor", ft1, "sP", 1)
            p.wait("vector", "sP")
            def fge(E, ptr2=ptr2):
                return E.tensor_copy(V["sct"][:, 0:64], ptr2[:, 0:64])
            p.do("vector", fge, "sV", 1)
            p.wait("vector", "sV")
            def fm8(E):
                return E.max(out=V["sct"][:, 64:72], in_=V["sct"][:, 0:64])
            p.do("vector", fm8, "sV", 1)
            p.wait("vector", "sV")
            def fmr(E):
                return E.match_replace(out=V["sct"][:, 128:192],
                                       in_to_replace=V["sct"][:, 64:72],
                                       in_values=V["sct"][:, 0:64],
                                       imm_value=-1e30)
            p.do("vector", fmr, "sV", 1)
            p.wait("vector", "sV")
            def fm8b(E):
                return E.max(out=V["sct"][:, 72:80],
                             in_=V["sct"][:, 128:192])
            p.do("vector", fm8b, "sV", 1)
            p.wait("vector", "sV")
            def fnm(E):
                return E.tensor_scalar_mul(V["sct"][:, 80:81],
                                           V["sct"][:, 64:65], -1.0)
            p.do("vector", fnm, "sV", 1)
            p.wait("scalar", "sV")
            def fex2(E):
                return E.activation(out=V["sct"][:, 128:192],
                                    in_=V["sct"][:, 0:64], func=AF.Exp,
                                    bias=V["sct"][:, 80:81])
            p.do("scalar", fex2, "sA", 1)
            p.wait("vector", "sA")
            def fmk(E):
                return E.tensor_scalar(out=V["sct"][:, 192:256],
                                       in0=V["sct"][:, 0:64],
                                       scalar1=V["sct"][:, 79:80],
                                       scalar2=None, op0=ALU.is_ge)
            p.do("vector", fmk, "sV", 1)
            p.wait("vector", "sV")
            def fme(E):
                return E.tensor_mul(V["sct"][:, 128:192],
                                    V["sct"][:, 128:192],
                                    V["sct"][:, 192:256])
            p.do("vector", fme, "sV", 1)
            p.wait("vector", "sV")
            def fsm(E):
                return E.reduce_sum(V["sct"][:, 81:82],
                                    V["sct"][:, 128:192],
                                    axis=mybir.AxisListType.X)
            p.do("vector", fsm, "sV", 1)
            p.wait("vector", "sV")
            def frz(E):
                return E.reciprocal(V["sct"][:, 81:82], V["sct"][:, 81:82])
            p.do("vector", frz, "sV", 1)
            p.wait("vector", "sV")
            def fgt(E):
                return E.tensor_scalar_mul(
                    V["sct"][:, 128:192].bitcast(BF16)[:, 0:64]
                    if False else V["sct2"][:, 0:64],
                    V["sct"][:, 128:192], V["sct"][:, 81:82])
            p.do("vector", fgt, "sV", 1)
            p.wait("tensor", "sV")
            ptg = ps["acc0"][0:64, 1536:1600].bitcast(BF16)
            def ft2(E, ptg=ptg):
                return E.transpose(out=ptg,
                                   in_=V["sct2"][:, 0:64],
                                   identity=sb["identb"][:, :])
            p.do("tensor", ft2, "sP", 1)
            p.wait("vector", "sP")
            def fgf(E, c=c, ptg=ptg):
                return E.tensor_copy(V["gatef"][:, c * 128:(c + 1) * 128],
                                     ptg)
            p.do("vector", fgf, "sEv0", 1)
            p2m["p3g"] = p.val("sEv0")
        for nb in range(4):
            pz = ps["acc0"][:, 0:2048]
            pgx = ps["acc1"][:, 0:2048]
            p.wait("gpsimd", "sP", p2m.get("p3za", 0))
            def fwa(E, nb=nb):
                return E.dma_start(
                    out=sb["wbuf1"][:, :].rearrange("p (k c) -> p k c",
                                                    k=NK),
                    in_=ap["w_ffnA"][nb, :, :, :])
            p.do("gpsimd", fwa, "sW1", 16)
            p.wait("tensor", "sW1")
            p.wait("tensor", "sV", p2m.get("p3zv", 0))
            def fz(E, pz=pz):
                last = None
                for k in range(NK):
                    for t in range(4):
                        last = E.matmul(
                            out=pz[:, t * 512:(t + 1) * 512],
                            lhsT=sb["wbuf1"][:, :].rearrange(
                                "p (k c) -> p k c", k=NK)[:, k, :],
                            rhs=x1f3[:, k, t * 512:(t + 1) * 512],
                            start=(k == 0), stop=(k == NK - 1))
                return last
            p.do("tensor", fz, "sP", 1)
            p2m["p3za"] = p.val("sP")
            def fgx(E, nb=nb, pgx=pgx):
                last = None
                for t in range(4):
                    last = E.matmul(
                        out=pgx[:, t * 512:(t + 1) * 512],
                        lhsT=V["expE"][:, :].rearrange(
                            "p (j c) -> p j c", j=4)[0:64, nb, :],
                        rhs=V["gatef"][:, t * 512:(t + 1) * 512],
                        start=True, stop=True)
                return last
            p.do("tensor", fgx, "sP", 1)
            p.wait("vector", "sP")
            def feg(E, pgx=pgx):
                return E.tensor_copy(sb["evst0"][:, :], pgx)
            p.do("vector", feg, "sV", 1)
            p.wait("vector", "sV")
            def fzg(E, nb=nb, pz=pz):
                return E.tensor_mul(
                    V["z2g"][:, nb * 2048:(nb + 1) * 2048],
                    sb["evst0"][:, :].to_broadcast([128, 2048])
                    if False else sb["evst0"][:, :], pz)
            def fzg2(E, nb=nb, pz=pz):
                return E.tensor_mul(
                    V["z2g"][:, nb * 2048:(nb + 1) * 2048],
                    sb["evst0"][:, :], pz)
            del fzg
            p.do("vector", fzg2, "sV", 1)
            p2m["p3zv"] = p.val("sV")
        for nf in range(32):
            pr = nf % 2
            po = ps["acc0"][:, 0:2048] if pr == 0 else ps["acc1"][:, 0:2048]
            eb = sb["evst0"] if pr == 0 else sb["evst1"]
            p.wait("gpsimd", "sP", p2m.get("p3wb", 0))
            if nf == 0:
                p.wait("gpsimd", "sV")
            wslot = sb["wbuf0"] if pr == 0 else sb["wbuf1"]
            def fwb(E, nf=nf, wslot=wslot):
                return E.dma_start(
                    out=wslot[:, 0:512].rearrange("p (k c) -> p k c", k=4),
                    in_=ap["w_ffnB"][nf, :, :, :])
            p.do("gpsimd", fwb, f"sW{pr}", 16)
            p.wait("tensor", f"sW{pr}")
            p.wait("tensor", "sV" if nf < 2 else "sEv1",
                   p2m.get(f"p3bo{pr}", 0))
            def fb(E, po=po, wslot=wslot):
                last = None
                for k in range(4):
                    for t in range(4):
                        last = E.matmul(
                            out=po[:, t * 512:(t + 1) * 512],
                            lhsT=wslot[:, 0:512].rearrange(
                                "p (k c) -> p k c", k=4)[:, k, :],
                            rhs=V["z2g"][:, k * 2048 + t * 512:
                                          k * 2048 + (t + 1) * 512],
                            start=(k == 0), stop=(k == 3))
                return last
            p.do("tensor", fb, "sP", 1)
            p2m["p3wb"] = p.val("sP")
            seng = "scalar" if pr == 0 else "vector"
            sevn = "sEv0" if pr == 0 else "sEv1"
            p.wait(seng, "sP")
            p.wait(seng, "sTr", p2m.get(f"p3ot{pr}", 0))
            def fe(E, po=po, eb=eb, pr=pr):
                if pr == 0:
                    return E.activation(out=eb[:, :], in_=po,
                                        func=AF.Copy, scale=FFN_SCALE)
                return E.tensor_scalar_mul(eb[:, :], po, FFN_SCALE)
            p.do(seng, fe, sevn, 1)
            p2m[f"p3bo{pr}"] = p.val(sevn)
            p.wait("sync", sevn)
            p.wait("sync", "sW0", p2m.get("p3fo0", 0))
            p.wait("sync", "sSc1", p2m.get("p3fo1", 0))
            def ftr(E, nf=nf, eb=eb):
                return E.dma_start(
                    out=V["o2t"][:, :].rearrange("p (cc f) -> p cc f",
                                                  cc=16),
                    in_=eb[:, :], transpose=True)
            p.do("sync", ftr, "sTr", 16)
            p2m[f"p3ot{pr}"] = p.val("sTr")
            oeng = "gpsimd" if pr == 0 else "scalar"
            p.wait(oeng, "sTr")
            def fo(E, nf=nf):
                return E.dma_start(
                    out=scr["out2t_scr"].rearrange(
                        "(cc p) f -> p cc f", p=128)[:, :,
                        nf * 128:(nf + 1) * 128],
                    in_=V["o2t"][:, :].rearrange("p (cc f) -> p cc f",
                                                  cc=16))
            p.do(oeng, fo, "sSc1" if pr else "sW0", 16)
            p2m[f"p3fo{pr}"] = p.val("sSc1" if pr else "sW0")

    def phase_p3b_final():
        V = p3bv()
        for c in range(16):
            par = c % 2
            at = V["attn_t"] if par == 0 else view("ar1", 0, 128, H, BF16)
            sr = V["srow"] if par == 0 else view("ar1", 8192, 128, H, F32)
            xt = V["x1t"] if par == 0 else view("ar1", 24576, 128, H, BF16)
            off = 9 if par == 0 else 13
            for eng in ("sync", "gpsimd", "scalar", "vector"):
                p.wait(eng, "sV", p2m.get(f"f3c{par}", 0))
            def fx1a(E, c=c, xt=xt):
                return E.dma_start(
                    out=xt[:, :],
                    in_=scr["x1t_scr"][c * 128:(c + 1) * 128, :])
            p.do("gpsimd", fx1a, "sW0", 16)
            def fo2a(E, c=c, at=at):
                return E.dma_start(
                    out=at[:, :],
                    in_=scr["out2t_scr"][c * 128:(c + 1) * 128, :])
            p.do("scalar", fo2a, "sX1", 16)
            p.wait("vector", "sW0")
            p.wait("vector", "sX1")
            p.wait("vector", "sOut", p2m.get(f"f3w{par}", 0))
            p.wait("vector", "sW1", p2m.get(f"f3wW{par}", 0))
            def fs1(E, at=at, sr=sr, xt=xt, off=off):
                return E.scalar_tensor_tensor(
                    out=sr[:, :], in0=at[:, :], scalar=1.0,
                    in1=xt[:, :], op0=ALU.mult, op1=ALU.add,
                    accum_out=sb["small"][:, off:off + 1])
            p.do("vector", fs1, "sV", 1)
            p.wait("vector", "sV")
            def fmn(E, off=off):
                return E.tensor_scalar_mul(sb["small"][:, off + 1:off + 2],
                                           sb["small"][:, off:off + 1],
                                           -1.0 / H)
            p.do("vector", fmn, "sV", 1)
            p.wait("vector", "sV")
            def fs3(E, sr=sr, off=off):
                return E.tensor_scalar(
                    out=sr[:, :], in0=sr[:, :],
                    scalar1=sb["small"][:, off + 1:off + 2], scalar2=None,
                    op0=ALU.add)
            p.do("vector", fs3, "sV", 1)
            p.wait("scalar", "sV")
            def fsq(E, at=at, sr=sr, off=off):
                return E.activation(out=at[:, :],
                                    in_=sr[:, :], func=AF.Square,
                                    accum_out=sb["small"][:, off + 2:off + 3])
            p.do("scalar", fsq, "sA", 1)
            p.wait("scalar", "sA")
            def fsd(E, off=off):
                return E.activation(out=sb["small"][:, off + 3:off + 4],
                                    in_=sb["small"][:, off + 2:off + 3],
                                    func=AF.Sqrt, scale=1.0 / H, bias=LNEPS)
            p.do("scalar", fsd, "sA", 1)
            p.wait("vector", "sA")
            def frv(E, off=off):
                return E.reciprocal(sb["small"][:, off + 3:off + 4],
                                    sb["small"][:, off + 3:off + 4])
            p.do("vector", frv, "sV", 1)
            p.wait("vector", "sV")
            def fyo2(E, sr=sr, off=off):
                return E.tensor_scalar_mul(sr[:, :], sr[:, :],
                                           sb["small"][:, off + 3:off + 4])
            p.do("vector", fyo2, "sV", 1)
            p2m[f"f3c{par}"] = p.val("sV")
            p.wait("sync", "sV")
            p.wait("gpsimd", "sV")
            p.wait("scalar", "sV")
            def fwa(E, c=c, sr=sr):
                return E.dma_start(
                    out=ap["yout"][c * 128:(c + 1) * 128, 0:2048],
                    in_=sr[:, 0:2048])
            p.do("sync", fwa, "sOut", 16)
            def fwb(E, c=c, sr=sr):
                return E.dma_start(
                    out=ap["yout"][c * 128:(c + 1) * 128, 2048:3072],
                    in_=sr[:, 2048:3072])
            p.do("gpsimd", fwb, "sW1", 16)
            def fwc(E, c=c, sr=sr):
                return E.dma_start(
                    out=ap["yout"][c * 128:(c + 1) * 128, 3072:4096],
                    in_=sr[:, 3072:4096])
            p.do("scalar", fwc, "sOut", 16)
            p2m[f"f3w{par}"] = p.val("sOut")
            p2m[f"f3wW{par}"] = p.val("sW1")
        p.wait("sync", "sOut")
        p.wait("sync", "sW1")

    def phase_debug():
        srcmap = {
            "zsil_scr": scr["zsil_scr"],
            "xbc_scr": scr["xbc_scr"],
            "y_scr": scr["y_scr"],
            "attn_scr": scr["attn_scr"],
            "x1f_scr": scr["x1f_scr"],
        }
        p.wait("sync", "sSc0")
        p.wait("sync", "sSc1")
        p.wait("sync", "sV")
        p.wait("sync", "sA")
        for nm in cfg["debug"]:
            if nm == "hS_dbg":
                def fd(E):
                    return E.dma_start(
                        out=ap["dbg_hS_dbg"],
                        in_=scr["win_S"].rearrange("a b p n -> (a b p) n"))
                p.do("sync", fd, "sOut", 16)
                continue
            if nm == "hin_dbg":
                def fd(E):
                    return E.dma_start(out=ap["dbg_hin_dbg"],
                                       in_=view("ar1", 49152, 128, 4096, F32))
                p.do("sync", fd, "sOut", 16)
                continue
            if nm == "dtp_dbg":
                def fd(E):
                    return E.dma_start(out=ap["dbg_dtp_dbg"],
                                       in_=scr["dtp_scr"][0:64, :])
            elif nm == "dtpr_dbg":
                def fd(E):
                    return E.dma_start(out=ap["dbg_dtpr_dbg"],
                                       in_=scr["dtp_scr"][64:128, :])
            else:
                def fd(E, nm=nm):
                    return E.dma_start(out=ap["dbg_" + nm], in_=srcmap[nm])
            p.do("sync", fd, "sOut", 16)

    def walk(eng_name, E):
        p.eng = eng_name
        p.E = E
        for k in list(p.cnt):
            p.cnt[k] = 0
        for k in list(inproj_marks):
            inproj_marks[k] = 0
        for k in list(p2m):
            if isinstance(p2m[k], list):
                p2m[k] = [0] * 8
            else:
                p2m[k] = 0
        p.poll_ctr = 0
        phase_init()
        barrier()
        ph = cfg["phases"]
        if "p1n_own" in ph:
            phase_p1_norm("own")
        if "p1i_own" in ph:
            phase_p1_inproj("own")
        if "p1n_red" in ph:
            barrier()
            phase_p1_norm("red")
        if "p1i_red" in ph:
            phase_p1_inproj("red")
            phase_halo_export()
        if "p2" in ph:
            barrier()
            write_flag(0)
            poll_flag(0)
            barrier()
            for seg in ("red", "own"):
                full = seg == "own"
                phase_p2_prep(seg)
                for g in range(4):
                    phase_p2_conv_group(seg, g)
                    for c in range(4 * g, 4 * g + 4):
                        phase_p2_chunk(seg, c, full)
                phase_p2_export(seg)
            phase_exchange()
        if "p3a" in ph:
            barrier()
            phase_p3a()
        if "p3b" in ph:
            barrier()
            phase_p3b_ln1()
            barrier()
            phase_p3b_ffn()
            barrier()
            phase_p3b_final()
        barrier()
        phase_debug()
        p.wait("sync", "sOut")

    with nc.Block() as block:
        @block.sync
        def _(E):
            walk("sync", E)

        @block.gpsimd
        def _(E):
            walk("gpsimd", E)

        @block.vector
        def _(E):
            walk("vector", E)

        @block.scalar
        def _(E):
            walk("scalar", E)

        @block.tensor
        def _(E):
            walk("tensor", E)


def _prep_base(inputs):
    w_in = _block_w(np.asarray(inputs["L0_attn_in_proj_w"], np.float32),
                    NK, NPROJ_T)
    w_out = _block_w(np.asarray(inputs["L0_attn_out_proj_w"], np.float32),
                     NK, 32)
    A_w = np.asarray(inputs["L0_ffn_A_w"], np.float32)
    B_w = np.asarray(inputs["L0_ffn_B_w"], np.float32)
    Wq = np.asarray(inputs["L0_ffn_Wq_w"], np.float32)
    keys = np.asarray(inputs["L0_ffn_keys"], np.float32)
    w_A = _block_w(np.ascontiguousarray(A_w.T), NK, 4)
    w_B = _block_w(np.ascontiguousarray(B_w.T), 4, 32)
    w_Q = np.ascontiguousarray(
        Wq.T.reshape(NK, 128, 16).transpose(1, 0, 2))
    mup = np.asarray(inputs["L0_attn_mup_vector"], np.float32)
    mup_pad = np.zeros(NPROJ_T * 128, np.float32)
    mup_pad[:PROJ] = mup
    mupc = np.ascontiguousarray(mup_pad.reshape(NPROJ_T, 128).T)
    ln_w = np.asarray(inputs["L0_attn_ln_w"], np.float32)
    nw = np.asarray(inputs["L0_attn_norm_w"], np.float32)
    A_log = np.asarray(inputs["L0_attn_A_log"], np.float32)
    return {
        "w_inproj": w_in,
        "w_outproj": w_out,
        "w_ffnA": w_A,
        "w_ffnB": w_B,
        "w_ffnQ": w_Q,
        "keysT": np.ascontiguousarray(keys.T),
        "convdiag": _conv_diag(np.asarray(inputs["L0_attn_conv_w"],
                                          np.float32)),
        "lnw_rep": np.ascontiguousarray(np.repeat(ln_w[None, :], 128, 0)),
        "nrmw_rep": np.ascontiguousarray(np.repeat(nw[None, :], 128, 0)),
        "nrmwc": np.ascontiguousarray(nw.reshape(NK, 128).T),
        "mupc": mupc,
        "dt_bias": np.asarray(inputs["L0_attn_dt_bias"],
                              np.float32)[:, None],
        "A_log": np.ascontiguousarray(A_log[:, None]),
        "Dvec": np.asarray(inputs["L0_attn_D"], np.float32)[:, None],
        "expandE": _expand_mat(),
        "ident": np.eye(128, dtype=np.float32),
        "hdrep": (np.arange(64)[:, None] ==
                  (2 * (np.arange(4096)[None, :] // 128) +
                   (np.arange(4096)[None, :] % 128) // 64))
        .astype(np.float32),
        "tri": np.triu(np.ones((128, 128), np.float32)),
        "Drep": np.ascontiguousarray(
            np.asarray(inputs["L0_attn_D"], np.float32)[
                (2 * np.arange(32)[None, :] +
                 (np.arange(128)[:, None] >= 64))]),
    }


def _per_core_maps(inputs, base):
    x = np.asarray(inputs["L0_attn_hidden_states"], dtype=np.float32)
    nonce = (np.random.rand(1, 16).astype(np.float32) + 1.0)
    in_maps = []
    for core in range(8):
        b = core // 4
        q = core % 4
        own = x[b, q * 2048:(q + 1) * 2048, :]
        if q == 2:
            red = x[b, 2048:4096, :]
        elif q == 3:
            red = x[b, 0:2048, :]
        else:
            red = own
        par = core % 2
        opp = 1 - par
        cfg_f = np.zeros((1, 64), np.float32)
        if q == 1:
            cfg_f[0, 16 + opp * 2 + 0] = 1
            cfg_f[0, 28 + opp * 2 + 0] = 1
            cfg_f[0, 6] = 1
        elif q == 2:
            cfg_f[0, 16 + par * 2 + 1] = 1
            cfg_f[0, 20 + opp * 2 + 1] = 1
            cfg_f[0, 24 + opp * 2 + 1] = 1
            cfg_f[0, 28 + par * 2 + 1] = 1
            cfg_f[0, 32 + par * 2 + 1] = 1
            cfg_f[0, 0] = 1
            cfg_f[0, 2] = 1
            cfg_f[0, 6] = 1
        elif q == 3:
            cfg_f[0, 16 + opp * 2 + 0] = 1
            cfg_f[0, 36 + opp * 3 + 2] = 1
            cfg_f[0, 42 + opp * 3 + 0] = 1
            cfg_f[0, 48 + opp * 3 + 0] = 1
            cfg_f[0, 3] = 1
            cfg_f[0, 5] = 1
        cfg_i = np.zeros((1, 64), np.int32)
        m = {k: v for k, v in base.items() if not k.endswith("_")}
        m["xin"] = np.ascontiguousarray(np.concatenate([own, red], axis=0))
        m["cfg_f"] = cfg_f
        m["cfg_i"] = cfg_i
        m["nonce"] = nonce
        in_maps.append(m)
    return in_maps


_PROG_CACHE = {}


def _get_prog():
    key = (tuple(CFG["phases"]), tuple(CFG["debug"]))
    if key not in _PROG_CACHE:
        _PROG_CACHE[key] = build_program(CFG)
    return _PROG_CACHE[key]


def run_cores(inputs):
    nc = _get_prog()
    base = _prep_base(inputs)
    in_maps = _per_core_maps(inputs, base)
    return run_bass_kernel_spmd(nc, in_maps, core_ids=list(range(8)))


def kernel(**inputs):
    inputs.pop("L0_attn_residual", None)
    res = run_cores(inputs)
    out = np.zeros((B, L, H), dtype=np.float32)
    for core in range(8):
        b = core // 4
        q = core % 4
        out[b, q * 2048:(q + 1) * 2048, :] = res.results[core]["yout"]
    return out


# version 04 (final)
# speedup vs baseline: 1.3538x; 1.0582x over previous
import numpy as np
import ml_dtypes
import concourse.bass as bass
import concourse.mybir as mybir
from concourse.bass_utils import run_bass_kernel_spmd

F32 = mybir.dt.float32
BF16 = mybir.dt.bfloat16
I32 = mybir.dt.int32
AF = mybir.ActivationFunctionType
ALU = mybir.AluOpType

H = 4096
DS = 4096
S = 128
NH = 64
HD = 64
KCONV = 4
CONV_DIM = DS + 2 * S
PROJ = 2 * DS + 2 * S + NH
NPROJ_T = 67
NK = 32
EPS = 1e-5
LNEPS = 1e-5

B = 2
L = 8192
TOWN = 2048
TRED = 2048
NCHUNK = TOWN // 128

NZT = 32
NXT = 34
FFN_RANK = 8
FFN_SCALE = 2.0

def _WS(pp, s):
    return (pp * 3 + s) * 128


def _WA(pp, s):
    return pp * 3 + s


def _WH(pp, s):
    return (pp * 2 + s) * 128


def _WF(pp, j):
    return pp * 4 + j


def _role_rows(q):
    par = q % 2
    opp = 1 - par
    ci = {
        "my_S_own": _WS(par, 0), "my_S_red": _WS(par, 1),
        "my_stage1": _WS(par, 2), "my_As_own": _WA(par, 0),
        "my_As_red": _WA(par, 1), "my_halo_own": _WH(par, 0),
        "my_halo_red": _WH(par, 1),
        "my_flag0": _WF(par, 0), "my_flag1": _WF(par, 1),
        "my_flag2": _WF(par, 2), "pa_flag0": _WF(opp, 0),
        "pa_flag1": _WF(opp, 1), "pa_flag2": _WF(opp, 2),
        "halo_src_own": _WH(par, 0), "halo_src_red": _WH(par, 1),
        "Sa1": _WS(opp, 0), "Sb1": _WS(opp, 0), "G1": _WA(par, 0),
        "Sa2": _WS(opp, 0), "Sb2": _WS(opp, 0), "G2": _WA(par, 0),
    }
    if q == 1:
        ci["halo_src_own"] = _WH(opp, 0)
    elif q == 2:
        ci["halo_src_own"] = _WH(par, 1)
        ci["halo_src_red"] = _WH(opp, 1)
        ci["Sa1"] = _WS(opp, 1)
        ci["G1"] = _WA(par, 1)
        ci["Sb1"] = _WS(par, 1)
    elif q == 3:
        ci["halo_src_own"] = _WH(opp, 0)
        ci["Sa2"] = _WS(opp, 2)
        ci["G2"] = _WA(opp, 0)
        ci["Sb2"] = _WS(opp, 0)
    return ci

CFG = {
    "phases": ("p1n_own", "p1i_own", "p1n_red", "p1i_red", "p2",
               "p3a", "p3b"),
    "debug": (),
}

_DBG_SHAPES = {
    "zsil_scr": ([DS, TOWN], BF16),
    "xbc_scr": ([CONV_DIM, TOWN + TRED], BF16),
    "dtp_dbg": ([NH, TOWN], F32),
    "dtpr_dbg": ([NH, TRED], F32),
    "y_scr": ([DS, TOWN], BF16),
    "attn_scr": ([H, TOWN], BF16),
    "x1f_scr": ([H, TOWN], BF16),
    "hS_dbg": ([6 * 128, NH * HD], F32),
    "hin_dbg": ([128, NH * HD], F32),
}


def _block_w(w, nk, nn):
    K, N = w.shape
    assert K == nk * 128
    wp = np.zeros((K, nn * 128), dtype=np.float32)
    wp[:, :N] = w
    wb = wp.reshape(nk, 128, nn, 128).transpose(2, 1, 0, 3)
    return np.ascontiguousarray(wb)


def _conv_diag(conv_w):
    out = np.zeros((NXT, KCONV, 128, 128), dtype=np.float32)
    idx = np.arange(128)
    for fb in range(NXT):
        for k in range(KCONV):
            out[fb, k, idx, idx] = conv_w[fb * 128:(fb + 1) * 128, k]
    return np.ascontiguousarray(out.transpose(2, 0, 1, 3))


def _expand_mat():
    E = np.zeros((64, 4, 128), dtype=np.float32)
    for j in range(4):
        for m in range(128):
            E[(j * 128 + m) // FFN_RANK, j, m] = 1.0
    return E


class P:
    def __init__(self, nc):
        self.nc = nc
        self.cnt = {}
        self.sems = {}
        self.eng = None
        self.E = None
        self.dyn = {}
        self.poll_ctr = 0
        self.pbody = 0
        self.noncereg_val = None
        self._pending = []

    def inc(self, name, by):
        self.cnt[name] = self.cnt.get(name, 0) + by
        return self.cnt[name]

    def val(self, name):
        return self.cnt.get(name, 0)

    def do(self, eng, fn, sem=None, by=None):
        inst = None
        if self.eng == eng:
            inst = fn(self.E)
            if self._pending:
                for r in self._pending:
                    self.E.free_register(r)
                self._pending = []
        if sem is not None:
            self.inc(sem, by)
            if inst is not None:
                inst.then_inc(self.sems[sem], by)
        return inst

    def roled(self, eng, build_fn, sem=None, by=None):
        if self.eng == eng:
            for role in self.E.Switch(self.role_rv, 4):
                inst = build_fn(self.E, _role_rows(role))
                if sem is not None and inst is not None:
                    inst.then_inc(self.sems[sem], by)
        if sem is not None:
            self.inc(sem, by)

    def wait(self, eng, sem, val=None):
        if val is None:
            val = self.val(sem)
        if val <= 0:
            return
        if self.eng == eng:
            self.E.wait_ge(self.sems[sem], val)


def build_program(cfg):
    nc = bass.Bass("TRN2", target_bir_lowering=False, debug=False)
    NTOK = TOWN + TRED

    ap = {}

    def ein(name, shape, dt=F32):
        ap[name] = nc.dram_tensor(name, shape, dt, kind="ExternalInput").ap()

    def eout(name, shape, dt=F32):
        ap[name] = nc.dram_tensor(name, shape, dt, kind="ExternalOutput").ap()

    ein("xin", [NTOK, H])
    ein("w_inproj", [NPROJ_T, 128, NK, 128])
    ein("w_outproj", [32, 128, NK, 128])
    ein("w_ffnA", [4, 128, NK, 128])
    ein("w_ffnB", [32, 128, 4, 128])
    ein("w_ffnQ", [128, NK, 16])
    ein("keysT", [16, 64])
    ein("convdiag", [128, NXT, KCONV, 128])
    ein("lnw_rep", [128, H])
    ein("nrmw_rep", [128, H])
    ein("nrmwc", [128, NK])
    ein("mupc", [128, NPROJ_T])
    ein("dt_bias", [NH, 1])
    ein("A_log", [NH, 1])
    ein("Dvec", [NH, 1])
    ein("expandE", [64, 4, 128])
    ein("ident", [128, 128])
    ein("tri", [128, 128])
    ein("Drep", [128, 32])
    ein("cfg_f", [1, 64])
    ein("cfg_i", [1, 64], I32)
    ein("nonce", [1, 16])
    ein("hdrep", [64, 4096])

    eout("yout", [TOWN, H])
    for nm in cfg["debug"]:
        shp, dt = _DBG_SHAPES[nm]
        eout("dbg_" + nm, shp, dt)

    scr = {}

    def sloc(name, shape, dt=BF16):
        scr[name] = nc.dram_tensor(name, shape, dt).ap()

    sloc("zsil_scr", [DS, TOWN])
    sloc("dtp_scr", [128, 2048], F32)
    sloc("acum_scr", [128, 2048], F32)
    sloc("xbc_scr", [CONV_DIM, NTOK])
    sloc("y_scr", [DS, TOWN])
    sloc("attn_scr", [H, TOWN])
    sloc("x1f_scr", [H, TOWN])
    sloc("x1t_scr", [TOWN, H])
    sloc("out2t_scr", [TOWN, H])

    def sshr(name, shape, dt=F32):
        scr[name] = nc.dram_tensor(name, shape, dt, addr_space="Shared").ap()

    sshr("win_S", [2, 3, S, NH * HD])
    sshr("win_As", [2, 3, NH])
    scr["win_halo"] = nc.dram_tensor("win_halo", [2, 2, 128, NXT * 4], BF16, addr_space="Shared").ap()
    sshr("win_flag", [2, 4, 16])

    _emit(nc, ap, scr, cfg)
    return nc


def _emit(nc, ap, scr, cfg):
    import contextlib
    est = contextlib.ExitStack()

    p = P(nc)
    for s in ["sX0", "sX1", "sW0", "sW1", "sTr", "sSc0", "sSc1", "sScR",
              "sWin", "sV", "sA", "sP", "sG", "sPoll", "sEv0", "sEv1",
              "sOut"]:
        p.sems[s] = est.enter_context(nc.semaphore(s))

    sb = {}

    def sbuf(name, shape, dt):
        sb[name] = est.enter_context(nc.sbuf_tensor("sb_" + name, shape, dt))

    ps = {}

    def psum(name, shape, dt=F32):
        ps[name] = est.enter_context(nc.psum_tensor(name, shape, dt))

    sbuf("ar1", [128, 131072], mybir.dt.int8)
    sbuf("ar2", [128, 40960], mybir.dt.int8)
    sbuf("wbuf0", [128, NK * 128], BF16)
    sbuf("wbuf1", [128, NK * 128], BF16)
    sbuf("evst0", [128, 2048], BF16)
    sbuf("evst1", [128, 2048], BF16)
    sbuf("wrep", [128, H], BF16)
    sbuf("mupc", [128, NPROJ_T], F32)
    sbuf("consts", [128, 8], F32)
    sbuf("dtbias", [NH, 1], F32)
    sbuf("negA", [NH, 1], F32)
    sbuf("dvec", [NH, 1], F32)
    sbuf("small", [128, 256], F32)
    sbuf("identf", [128, 128], F32)
    sbuf("identb", [128, 128], BF16)
    sbuf("trib", [128, 128], BF16)
    sbuf("Drep", [128, 32], F32)
    sbuf("onesf", [1, 128], F32)
    sbuf("masks", [128, 64], F32)
    sbuf("noncebuf", [1, 16], F32)
    sbuf("cfgis", [1, 64], I32)
    sbuf("cfgfs", [1, 64], F32)
    sbuf("offsi", [128, 16], F32)
    sbuf("aclT", [128, 64], F32)
    sbuf("wT", [128, 64], F32)
    sbuf("B_T", [128, 128], BF16)
    sbuf("GT_sb", [128, 128], BF16)
    sbuf("Gexp", [128, 64], F32)
    sbuf("grow", [1, 64], F32)
    sbuf("halost", [128, 272], BF16)
    sbuf("halos4", [128, 544], BF16)
    sbuf("asums", [64, 16], F32)
    sbuf("nrmwc", [128, NK], F32)
    sbuf("onesb", [128, 1], BF16)
    sbuf("asumsT", [16, 64], F32)
    sbuf("flagrd", [1, 32], F32)

    _esz = {F32: 4, BF16: 2, I32: 4}

    def view(arena, off, parts, free, dt):
        return sb[arena][0:parts, off:off + free * _esz[dt]].bitcast(dt)

    psum("acc0", [128, 2048])
    psum("acc1", [128, 2048])

    p.sb_cfgis = sb["cfgis"]

    CONSTS = {}
    for i, v in enumerate((1.0, -1.0, float(EPS), float(LNEPS), 1.0 / H,
                           80.0, FFN_SCALE, 0.0)):
        CONSTS[v] = i
        nc.const_aps.register(F32, v, sb["consts"][:, i:i + 1])

    ALL_ENGS = ("sync", "gpsimd", "vector", "scalar", "tensor")

    def barrier():
        if cfg.get("snap") is not None:
            cfg["snap"].setdefault(p.eng, []).append(dict(p.cnt))
        for e in ALL_ENGS:
            for s in ("sV", "sA", "sP", "sEv0", "sEv1", "sSc0", "sSc1",
                      "sX0", "sX1", "sTr", "sW0", "sW1", "sG", "sWin",
                      "sOut"):
                p.wait(e, s)

    def phase_init():
        for v, i in CONSTS.items():
            def f(E, v=v, i=i):
                return E.memset(sb["consts"][:, i:i + 1], v)
            p.do("vector", f, "sV", 1)

        for nm_src, nm_dst in (("mupc", "mupc"), ("dt_bias", "dtbias"),
                               ("A_log", "negA"), ("Dvec", "dvec")):
            def f2(E, nm_src=nm_src, nm_dst=nm_dst):
                return E.dma_start(out=sb[nm_dst][:, :], in_=ap[nm_src])
            p.do("sync", f2, "sX0", 16)

        def f2b(E):
            return E.dma_start(out=sb["wrep"][:, :], in_=ap["lnw_rep"])
        p.do("gpsimd", f2b, "sW0", 16)
        for nm_src, nm_dst in (("ident", "identf"), ("Drep", "Drep"),
                               ("cfg_f", "cfgfs"), ("cfg_i", "cfgis"),
                               ("nonce", "noncebuf")):
            def f2c(E, nm_src=nm_src, nm_dst=nm_dst):
                return E.dma_start(out=sb[nm_dst][:, :], in_=ap[nm_src])
            p.do("sync", f2c, "sX0", 16)
        def f2cb(E):
            return E.dma_start(out=sb["identb"][:, :], in_=ap["ident"])
        p.do("gpsimd", f2cb, "sW1", 16)
        def f2cc(E):
            return E.dma_start(out=sb["trib"][:, :], in_=ap["tri"])
        p.do("gpsimd", f2cc, "sW1", 16)
        p.wait("vector", "sX0")
        p.wait("vector", "sW1")
        def f2d(E):
            E.memset(sb["halost"][:, :], 0.0)
            E.memset(sb["onesb"][:, :], 1.0)
            return E.memset(sb["onesf"][:, :], 1.0)
        p.do("vector", f2d, "sV", 1)
        p.wait("tensor", "sV")
        p.wait("tensor", "sX0")
        def f2e(E):
            return E.matmul(out=ps["acc0"][:, 0:64], lhsT=sb["onesf"][:, :],
                            rhs=sb["cfgfs"][0:1, 0:64], start=True, stop=True)
        p.do("tensor", f2e, "sP", 1)
        p.wait("scalar", "sP")
        def f2f(E):
            return E.activation(out=sb["masks"][:, :],
                                in_=ps["acc0"][:, 0:64], func=AF.Copy)
        p.do("scalar", f2f, "sEv0", 1)
        if p.eng == "sync":
            p.E.wait_ge(p.sems["sX0"], p.val("sX0"))
            nreg = p.E.alloc_register("nonceval")
            p.E.reg_load(nreg, sb["noncebuf"][0:1, 0:1].bitcast(I32))
            p.noncereg_val = p.E.snap(nreg)
            rreg = p.E.alloc_register("roleidx")
            pid = p.E.partition_id()
            p.E.reg_alu(rreg, pid, 4, ALU.mod)
            p.role_rv = p.E.snap(rreg)
        p.wait("scalar", "sV")
        p.wait("scalar", "sX0")
        def fneg(E):
            return E.activation(out=sb["negA"][:, :], in_=sb["negA"][:, :],
                                func=AF.Exp)
        p.do("scalar", fneg, "sA", 1)
        p.wait("vector", "sA")
        def fneg2(E):
            return E.tensor_scalar_mul(sb["negA"][:, :], sb["negA"][:, :],
                                       -1.0)
        p.do("vector", fneg2, "sV", 1)

    def phase_p1_norm(seg):
        T = TOWN if seg == "own" else TRED
        base = 0 if seg == "own" else TOWN
        nch = T // 128
        xnorm_f = view("ar1", 0, 128, NK * 2048, BF16)
        xc = [view("ar2", 0, 128, H, F32), view("ar2", 16384, 128, H, F32)]
        xnt = view("ar2", 32768, 128, H, BF16)
        marks = {"bx0": 0, "bx1": 0, "rs0": 0, "rs1": 0}
        for i in range(nch):
            pr = i % 2
            bx = xc[pr]
            rstd = sb["small"][:, pr:pr + 1]
            ssq = sb["small"][:, 2 + pr:3 + pr]
            p.wait("sync", "sV", marks[f"bx{pr}"])

            def fl(E, i=i, bx=bx):
                return E.dma_start(
                    out=bx,
                    in_=ap["xin"][base + i * 128: base + (i + 1) * 128, :])
            p.do("sync", fl, f"sX{pr}", 16)
            p.wait("scalar", f"sX{pr}")
            p.wait("scalar", "sV", marks[f"rs{pr}"])
            p.wait("scalar", "sTr")

            def fs(E, bx=bx, ssq=ssq):
                return E.activation(out=xnt, in_=bx,
                                    func=AF.Square, accum_out=ssq)
            p.do("scalar", fs, "sA", 1)
            p.wait("scalar", "sA")

            def fsq(E, ssq=ssq, rstd=rstd):
                return E.activation(out=rstd, in_=ssq,
                                    func=AF.Sqrt, scale=1.0 / H, bias=EPS)
            p.do("scalar", fsq, "sA", 1)
            p.wait("vector", "sA")

            def fr(E, rstd=rstd):
                return E.reciprocal(rstd, rstd)
            p.do("vector", fr, "sV", 1)
            p.wait("vector", "sV")
            p.wait("vector", "sTr")

            def fn(E, bx=bx, rstd=rstd):
                return E.scalar_tensor_tensor(
                    out=xnt, in0=bx, scalar=rstd,
                    in1=sb["wrep"][:, :], op0=ALU.mult, op1=ALU.mult)
            p.do("vector", fn, "sV", 1)
            marks[f"bx{pr}"] = p.val("sV")
            marks[f"rs{pr}"] = p.val("sV")
            p.wait("sync", "sV")

            def ft(E, i=i):
                outap = xnorm_f.rearrange(
                    "p (k t) -> p k t", k=NK)[:, :, i * 128:(i + 1) * 128]
                return E.dma_start(out=outap, in_=xnt, transpose=True)
            p.do("sync", ft, "sTr", 16)
        p.wait("tensor", "sTr")

    inproj_marks = {"w0": 0, "w1": 0, "p0": 0, "p1": 0, "e0": 0, "e1": 0,
                    "halo_w": 0}

    def phase_halo_export():
        p.wait("sync", "sV")
        def fh(E, rows):
            ro = rows["my_halo_own"]
            rr = rows["my_halo_red"]
            E.dma_start(
                out=scr["win_halo"].rearrange("a b p c -> (a b p) c")[
                    ro:ro + 128, :],
                in_=sb["halost"][:, 0:136]).then_inc(p.sems["sWin"], 16)
            return E.dma_start(
                out=scr["win_halo"].rearrange("a b p c -> (a b p) c")[
                    rr:rr + 128, :],
                in_=sb["halost"][:, 136:272])
        p.roled("sync", fh, "sWin", 16)
        p.inc("sWin", 16)
        inproj_marks["halo_w"] = p.val("sWin")
        p.wait("sync", "sWin")

    def phase_p1_inproj(seg):
        T = TOWN if seg == "own" else TRED
        base = 0 if seg == "own" else TOWN
        ntiles = list(range(NPROJ_T)) if seg == "own" \
            else list(range(NZT, NPROJ_T))
        ntt = T // 512
        mk = inproj_marks
        xnorm_f = view("ar1", 0, 128, NK * 2048, BF16)
        dtt1 = view("ar2", 0, 64, 2048, F32)
        dtt2 = view("ar2", 16384, 64, 2048, F32)
        dtrow = slice(0, 64) if seg == "own" else slice(64, 128)
        for j, n in enumerate(ntiles):
            pr = j % 2
            wb = sb["wbuf0"] if pr == 0 else sb["wbuf1"]
            pb = ps["acc0"] if pr == 0 else ps["acc1"]
            eb = sb["evst0"] if pr == 0 else sb["evst1"]
            p.wait("gpsimd", "sP", mk[f"w{pr}"])

            def fw(E, n=n, wb=wb):
                return E.dma_start(
                    out=wb[:, :].rearrange("p (k c) -> p k c", k=NK),
                    in_=ap["w_inproj"][n, :, :, :])
            p.do("gpsimd", fw, f"sW{pr}", 16)

            p.wait("tensor", f"sW{pr}")
            p.wait("tensor", f"sEv{pr}", mk[f"p{pr}"])

            def fm(E, wb=wb, pb=pb, ntt=ntt):
                last = None
                for k in range(NK):
                    lhsT = wb[:, :].rearrange(
                        "p (k c) -> p k c", k=NK)[:, k, :]
                    for t in range(ntt):
                        last = E.matmul(
                            out=pb[:, t * 512:(t + 1) * 512],
                            lhsT=lhsT,
                            rhs=xnorm_f.rearrange(
                                "p (k t) -> p k t",
                                k=NK)[:, k, t * 512:(t + 1) * 512],
                            start=(k == 0), stop=(k == NK - 1))
                return last
            p.do("tensor", fm, "sP", 1)
            mk[f"w{pr}"] = p.val("sP")

            if n < NZT:
                p.wait("scalar", "sP")
                p.wait("scalar", f"sSc{pr}", mk[f"e{pr}"])

                def fe(E, n=n, pb=pb, eb=eb, T=T):
                    return E.activation(
                        out=eb[:, 0:T], in_=pb[:, 0:T],
                        func=AF.Silu, scale=sb["mupc"][:, n:n + 1])
                p.do("scalar", fe, f"sEv{pr}", 1)
                mk[f"p{pr}"] = p.val(f"sEv{pr}")
                p.wait("sync", f"sEv{pr}")

                def fo(E, n=n, eb=eb, T=T, base=base):
                    return E.dma_start(
                        out=scr["zsil_scr"][n * 128:(n + 1) * 128,
                                            base:base + T],
                        in_=eb[:, 0:T])
                p.do("sync", fo, f"sSc{pr}", 16)
                mk[f"e{pr}"] = p.val(f"sSc{pr}")
            elif n < NZT + NXT:
                p.wait("vector", "sP")
                p.wait("vector", f"sSc{pr}", mk[f"e{pr}"])

                def fe(E, n=n, pb=pb, eb=eb, T=T):
                    return E.tensor_scalar_mul(
                        eb[:, 0:T], pb[:, 0:T], sb["mupc"][:, n:n + 1])
                p.do("vector", fe, f"sEv{pr}", 1)
                mk[f"p{pr}"] = p.val(f"sEv{pr}")
                p.wait("vector", f"sEv{pr}")
                p.wait("vector", "sWin", mk.get("halo_w", 0))
                hoff = 0 if seg == "own" else 136
                def fhc(E, n=n, eb=eb, T=T, hoff=hoff):
                    fb = n - NZT
                    return E.tensor_copy(
                        sb["halost"][:, hoff + fb * 4:hoff + fb * 4 + 3],
                        eb[:, T - 3:T])
                p.do("vector", fhc, "sV", 1)
                p.wait("sync", f"sEv{pr}")

                def fo(E, n=n, eb=eb, T=T, base=base):
                    fb = n - NZT
                    return E.dma_start(
                        out=scr["xbc_scr"][fb * 128:(fb + 1) * 128,
                                           base:base + T],
                        in_=eb[:, 0:T])
                p.do("sync", fo, f"sSc{pr}", 16)
                mk[f"e{pr}"] = p.val(f"sSc{pr}")
            else:
                p.wait("vector", "sP")

                def fe(E, pb=pb, T=T, dtt1=dtt1):
                    return E.tensor_scalar(
                        out=dtt1[:, 0:T], in0=pb[0:NH, 0:T],
                        scalar1=sb["mupc"][0:NH, NPROJ_T - 1:NPROJ_T],
                        scalar2=sb["dtbias"][:, :],
                        op0=ALU.mult, op1=ALU.add)
                p.do("vector", fe, f"sEv{pr}", 1)
                mk[f"p{pr}"] = p.val(f"sEv{pr}")
                p.wait("scalar", f"sEv{pr}")

                def fa1(E, T=T, dtt1=dtt1, dtt2=dtt2):
                    return E.activation(out=dtt2[:, 0:T],
                                        in_=dtt1[:, 0:T], func=AF.Abs)
                p.do("scalar", fa1, "sA", 1)
                p.wait("scalar", "sA")

                def fa2(E, T=T, dtt2=dtt2):
                    return E.activation(out=dtt2[:, 0:T], in_=dtt2[:, 0:T],
                                        func=AF.Exp, scale=-1.0)
                p.do("scalar", fa2, "sA", 1)
                p.wait("scalar", "sA")

                def fa3(E, T=T, dtt2=dtt2):
                    return E.activation(out=dtt2[:, 0:T], in_=dtt2[:, 0:T],
                                        func=AF.Ln, bias=1.0)
                p.do("scalar", fa3, "sA", 1)
                p.wait("vector", "sA")

                def fv1(E, T=T, dtt1=dtt1):
                    return E.tensor_relu(dtt1[:, 0:T], dtt1[:, 0:T])
                p.do("vector", fv1, "sV", 1)
                p.wait("vector", "sV")

                def fv2(E, T=T, dtt1=dtt1, dtt2=dtt2):
                    return E.tensor_add(dtt1[:, 0:T], dtt1[:, 0:T],
                                        dtt2[:, 0:T])
                p.do("vector", fv2, "sV", 1)
                p.wait("sync", "sV")

                def fdt(E, T=T, dtt1=dtt1, dtrow=dtrow):
                    return E.dma_start(out=scr["dtp_scr"][dtrow, 0:T],
                                       in_=dtt1[:, 0:T])
                p.do("sync", fdt, "sG", 16)


    def PV(t, a, b):
        return ps[t][:, a:b]

    def PVb(t, a, b):
        return ps[t][:, a:b].bitcast(BF16)

    def p2v():
        return {
            "xbcg": view("ar1", 0, 128, 34 * 516, BF16),
            "xsg": view("ar1", 35328, 128, 34 * 512, BF16),
            "S_sb": view("ar1", 70144, 128, 8192, BF16),
            "Ce_sb": view("ar1", 86528, 128, 8192, BF16),
            "cdg": view("ar1", 70144, 128, 34 * 4 * 128, BF16),
            "y_grp": view("ar1", 104960, 128, 32 * 128, BF16),
            "h_bf": view("ar1", 113152, 128, 4096, BF16),
            "acumP": view("ar1", 121344, 128, 2048, F32),
            "h": view("ar2", 0, 128, 4096, F32),
            "dtv": view("ar2", 16384, 64, 2048, F32),
            "xs_T": view("ar2", 16384, 128, 4096, BF16),
            "amlw": view("ar2", 24576, 128, 2048, F32),
            "acrow": view("ar2", 32768, 1, 2048, F32),
            "xsw_T": sb["wbuf0"][:, :].bitcast(BF16),
            "argb": sb["wbuf1"][:, 0:1024].bitcast(F32),
            "th0": sb["evst0"][:, 0:1024].bitcast(F32),
            "th1": sb["evst0"][:, 1024:2048].bitcast(F32),
            "hwA": view("ar1", 0, 128, 4096, F32),
            "hwB": view("ar1", 16384, 128, 4096, F32),
            "hst": view("ar1", 32768, 128, 4096, F32),
            "hin": view("ar1", 49152, 128, 4096, F32),
            "G1e": view("ar1", 65536, 128, 64, F32),
            "SaS": view("ar1", 66048, 128, 4096, F32),
            "SbS": view("ar1", 82432, 128, 4096, F32),
        }

    p2m = {"pb0": 0, "pb1": 0, "py0": 0, "py1": 0, "pc0": 0, "pc1": 0,
           "cv0": 0, "cv1": 0, "ptr": 0, "yg": 0, "acrow": 0, "sce": 0,
           "xbcg": 0, "hbf": 0, "hupd": [0] * 8, "th0": 0, "th1": 0,
           "xsTc": 0, "hread": [0] * 8, "exw0": 0, "exw1": 0,
           "p3acg": 0, "p3xb": 0, "p3cc": 0, "p3yz": 0, "p3ac": 0,
           "p3b0": 0, "p3b1": 0, "p3ce": 0, "p3y0": 0, "p3y1": 0,
           "p3rs": 0, "p3sq": 0, "p3rb": 0, "p3w0": 0, "p3w1": 0,
           "p3e0": 0, "p3e1": 0, "p3o0": 0, "p3o1": 0, "p3battn": 0, "p3yzE": 0, "p3yzV": 0,
           "p3g": 0, "p3za": 0, "p3zv": 0, "p3wb": 0, "p3bo0": 0,
           "p3bo1": 0, "p3ot0": 0, "p3ot1": 0, "p3f": 0}

    def rows_of(seg):
        return slice(0, 64)

    def phase_p2_prep(seg):
        V = p2v()
        ro = rows_of(seg)
        dtrow = slice(0, 64) if seg == "own" else slice(64, 128)
        def fl(E, dtrow=dtrow):
            return E.dma_start(out=V["dtv"][:, :],
                               in_=scr["dtp_scr"][dtrow, :])
        p.do("sync", fl, "sX0", 16)
        p.wait("vector", "sX0")
        def fadot(E):
            return E.tensor_scalar_mul(V["amlw"][0:64, :], V["dtv"][:, :],
                                       sb["negA"][:, :])
        p.do("vector", fadot, "sV", 1)
        p.wait("vector", "sV")
        for c in range(16):
            def fc(E, c=c):
                sl = slice(c * 128, (c + 1) * 128)
                return E.tensor_tensor_scan(
                    out=view("ar1", 121344, 128, 2048, F32)[ro, sl], data0=V["amlw"][0:64, sl],
                    data1=V["amlw"][0:64, sl], initial=0.0,
                    op0=ALU.add, op1=ALU.bypass)
            p.do("vector", fc, "sV", 1)
            p.wait("vector", "sV")
        def fg(E):
            return E.tensor_copy(
                sb["asums"][:, :],
                view("ar1", 121344, 128, 2048, F32)[ro, :].rearrange("p (c t) -> p c t",
                                             c=16)[:, :, 127])
        p.do("vector", fg, "sV", 1)
        p.wait("vector", "sV")
        def fz(E):
            return E.memset(sb["offsi"][ro, 0:1], 0.0)
        p.do("vector", fz, "sV", 1)
        p.wait("vector", "sV")
        def fsc(E):
            return E.tensor_tensor_scan(
                out=sb["offsi"][ro, 1:16],
                data0=sb["asums"][:, 0:15], data1=sb["asums"][:, 0:15],
                initial=0.0, op0=ALU.add, op1=ALU.bypass)
        p.do("vector", fsc, "sV", 1)
        ptr = PV("acc1", 1024, 1536)
        p.wait("tensor", "sV")
        p.wait("tensor", "sEv1", p2m["ptr"])
        def ftA(E):
            return E.transpose(out=ptr[0:16, 0:64], in_=sb["asums"][:, :],
                               identity=sb["identf"][0:64, 0:64])
        p.do("tensor", ftA, "sP", 1)
        p.wait("vector", "sP")
        def feA(E):
            return E.tensor_copy(sb["asumsT"][:, :], ptr[0:16, 0:64])
        p.do("vector", feA, "sEv1", 1)
        p2m["ptr"] = p.val("sEv1")
        p.wait("scalar", "sV")
        p.wait("scalar", "sX0")
        p.wait("scalar", "sEv1")
        def fln(E):
            return E.activation(out=V["amlw"][0:64, :], in_=V["dtv"][:, :],
                                func=AF.Ln)
        p.do("scalar", fln, "sA", 1)
        p.wait("vector", "sA")
        def faml(E):
            return E.tensor_sub(V["amlw"][0:64, :],
                                view("ar1", 121344, 128, 2048, F32)[ro, :],
                                V["amlw"][0:64, :])
        p.do("vector", faml, "sV", 1)
        p.wait("vector", "sV")
        for c in range(16):
            sl = slice(c * 128, (c + 1) * 128)
            def fw1(E, sl=sl, c=c):
                return E.tensor_scalar(
                    out=V["amlw"][64:128, sl], in0=V["amlw"][0:64, sl],
                    scalar1=sb["asums"][:, c:c + 1], scalar2=None,
                    op0=ALU.subtract)
            p.do("vector", fw1, "sV", 1)
        p.wait("scalar", "sV")
        def fw2(E):
            return E.activation(out=V["amlw"][64:128, :],
                                in_=V["amlw"][64:128, :], func=AF.Exp,
                                scale=-1.0)
        p.do("scalar", fw2, "sA", 1)
        p.wait("vector", "sA")
        p.wait("vector", "sWin")
        def fh(E):
            E.memset(p2v()["h"][:, :], 0.0)
            return E.memset(p2v()["h_bf"][:, :], 0.0)
        p.do("vector", fh, "sV", 1)

    def phase_p2_conv_group(seg, g):
        V = p2v()
        base = 0 if seg == "own" else TOWN
        xsgv3 = V["xsg"][:, :].rearrange("p (f t) -> p f t", f=NXT)
        xbcg3 = V["xbcg"][:, :].rearrange("p (f t) -> p f t", f=NXT)
        p.wait("gpsimd", "sP", p2m["sce"])
        p.wait("gpsimd", "sSc0", p2m["yg"])
        def fcd(E):
            return E.dma_start(
                out=V["cdg"][:, :].rearrange("p (f k c) -> p f k c",
                                             f=NXT, k=KCONV),
                in_=ap["convdiag"])
        p.do("gpsimd", fcd, "sW0", 16)
        p.wait("sync", "sP", p2m["xbcg"])
        if g == 0:
            def fx(E, base=base):
                return E.dma_start(
                    out=xbcg3[:, :, 3:515],
                    in_=scr["xbc_scr"].rearrange(
                        "(f p) t -> p f t", p=128)[:, :, base:base + 512])
            p.do("sync", fx, "sX1", 16)
            if seg == "red" and g == 0:
                for kslot in range(4):
                    def fhl(E, kslot=kslot):
                        return E.dma_start(
                            out=sb["halos4"][:, kslot * 136:
                                             (kslot + 1) * 136],
                            in_=scr["win_halo"].rearrange(
                                "a b p c -> (a b p) c")[
                                kslot * 128:(kslot + 1) * 128, :])
                    p.do("sync", fhl, "sX0", 16)
            p.wait("vector", "sX0")
            p.wait("vector", "sX1")
            mbase = 16 if seg == "own" else 20
            def fhz(E):
                return E.memset(xbcg3[:, :, 0:3], 0.0)
            p.do("vector", fhz, "sV", 1)
            p.wait("vector", "sV")
            for kslot in range(4):
                def fhm(E, mbase=mbase, kslot=kslot):
                    return E.scalar_tensor_tensor(
                        out=xbcg3[:, :, 0:3],
                        in0=sb["halos4"][:, kslot * 136:(kslot + 1) * 136]
                        .rearrange("p (f t) -> p f t", f=NXT)[:, :, 0:3],
                        scalar=sb["masks"][:, mbase + kslot:
                                           mbase + kslot + 1],
                        in1=xbcg3[:, :, 0:3], op0=ALU.mult, op1=ALU.add)
                p.do("vector", fhm, "sV", 1)
                p.wait("vector", "sV")
            p.wait("tensor", "sV")
        else:
            def fx(E, base=base, g=g):
                return E.dma_start(
                    out=xbcg3[:, :, 0:515],
                    in_=scr["xbc_scr"].rearrange(
                        "(f p) t -> p f t", p=128)[:, :,
                        base + g * 512 - 3:base + (g + 1) * 512])
            p.do("sync", fx, "sX1", 16)
        p.wait("tensor", "sX1")
        p.wait("tensor", "sW0")
        p.wait("scalar", "sV")
        p.wait("scalar", "sEv1")
        for fb in range(NXT):
            pcv = PV("acc1", 0, 512) if fb % 2 == 0 else PV("acc1", 512, 1024)
            prk = "cv0" if fb % 2 == 0 else "cv1"
            p.wait("tensor", "sEv0", p2m[prk])
            def fm(E, fb=fb, pcv=pcv):
                last = None
                for k in range(KCONV):
                    last = E.matmul(
                        out=pcv,
                        lhsT=V["cdg"][:, :].rearrange(
                            "p (f k c) -> p f k c", f=NXT,
                            k=KCONV)[:, fb, k, :],
                        rhs=xbcg3[:, fb, k:k + 512],
                        start=(k == 0), stop=(k == KCONV - 1))
                return last
            p.do("tensor", fm, "sP", 1)
            p.wait("scalar", "sP")
            def fs(E, fb=fb, pcv=pcv):
                return E.activation(out=xsgv3[:, fb, :], in_=pcv,
                                    func=AF.Silu)
            p.do("scalar", fs, "sEv0", 1)
            p2m[prk] = p.val("sEv0")
        p2m["xbcg"] = p.val("sP")
        p.wait("tensor", "sEv0")

    def phase_p2_chunk(seg, c, full):
        V = p2v()
        ro = rows_of(seg)
        rbase = 0
        gl = c % 4
        xsgv = V["xsg"][:, :].rearrange("p (f t) -> p f t", f=NXT)
        ptr = PV("acc1", 1024, 1536)
        ptrb = PVb("acc1", 1536, 2048)

        def pe_transpose_to(dst_sb, src_ap, bf, width, ibase=0):
            p.wait("tensor", "sEv1", p2m["ptr"])
            pdst = ptrb if bf else ptr
            ident = sb["identb"] if bf else sb["identf"]
            iw = src_ap.shape[0]
            def ft(E, pdst=pdst, src_ap=src_ap, ident=ident, iw=iw,
                   width=width, ibase=ibase):
                return E.transpose(
                    out=pdst[:, 0:width], in_=src_ap,
                    identity=ident[ibase:ibase + iw, ibase:ibase + iw])
            p.do("tensor", ft, "sP", 1)
            p.wait("vector", "sP")
            def fe(E, dst_sb=dst_sb, pdst=pdst, width=width):
                return E.tensor_copy(dst_sb, pdst[:, 0:width])
            p.do("vector", fe, "sEv1", 1)
            p2m["ptr"] = p.val("sEv1")

        pe_transpose_to(sb["wT"][:, :],
                        V["amlw"][64:128, c * 128:(c + 1) * 128], False, 64,
                        ibase=64)
        if full:
            pe_transpose_to(sb["aclT"][:, :],
                            V["amlw"][0:64, c * 128:(c + 1) * 128], False, 64)
        pe_transpose_to(sb["B_T"][:, :], xsgv[:, 32, gl * 128:(gl + 1) * 128],
                        True, 128)
        if full:
            p.wait("tensor", "sEv1", p2m["ptr"])
            def fgt(E, gl=gl):
                return E.matmul(out=ptr[:, 0:128],
                                lhsT=xsgv[:, 32, gl * 128:(gl + 1) * 128],
                                rhs=xsgv[:, 33, gl * 128:(gl + 1) * 128],
                                start=True, stop=True)
            p.do("tensor", fgt, "sP", 1)
            p.wait("vector", "sP")
            def fegt(E):
                return E.tensor_mul(sb["GT_sb"][:, :], ptr[:, 0:128],
                                    sb["trib"][:, :])
            p.do("vector", fegt, "sEv1", 1)
            p2m["ptr"] = p.val("sEv1")

        p.wait("vector", "sP", p2m["xsTc"])
        for fb in range(32):
            p.wait("tensor", "sEv1", p2m["ptr"])
            def ftx(E, fb=fb, gl=gl):
                return E.transpose(out=ptrb[:, 0:128],
                                   in_=xsgv[:, fb, gl * 128:(gl + 1) * 128],
                                   identity=sb["identb"][:, :])
            p.do("tensor", ftx, "sP", 1)
            p.wait("vector", "sP")
            if full:
                def fxr(E, fb=fb):
                    return E.tensor_copy(
                        V["xs_T"][:, fb * 128:(fb + 1) * 128], ptrb[:, 0:128])
                p.do("vector", fxr, "sV", 1)
            def fxw(E, fb=fb):
                in1 = sb["wT"][:, 2 * fb:2 * fb + 2].rearrange(
                    "p (r o) -> p r o", o=1).to_broadcast([128, 2, 64])
                return E.tensor_tensor(
                    out=V["xsw_T"][:, fb * 128:(fb + 1) * 128].rearrange(
                        "p (r q) -> p r q", r=2),
                    in0=ptrb[:, 0:128].rearrange("p (r q) -> p r q", r=2),
                    in1=in1, op=ALU.mult)
            p.do("vector", fxw, "sEv1", 1)
            p2m["ptr"] = p.val("sEv1")

        if full:
            p.wait("vector", "sEv1")
            p.wait("gpsimd", "sEv1")
            for q in range(16):
                if q % 4 == 0:
                    q4 = q // 4
                    p.wait("sync", "sP", p2m["acrow"])
                    p.wait("sync", "sV")
                    def fgr(E, q4=q4, c=c):
                        srcg = view("ar1", 121344, 128, 2048, F32)[
                            rbase + 16 * q4:rbase + 16 * (q4 + 1),
                            c * 128:(c + 1) * 128]
                        dst = V["acrow"][0:1, :].rearrange(
                            "o (r t) -> o r t", r=16)
                        return E.dma_start(out=dst, in_=srcg)
                    p.do("sync", fgr, "sTr", 16)
                    p.wait("tensor", "sTr")
                pb = PV("acc0", 0, 512) if q % 2 == 0 else PV("acc0", 512,
                                                              1024)
                pbk = "pb0" if q % 2 == 0 else "pb1"
                ab = (sb["wbuf1"][:, 0:1024].bitcast(F32) if q % 2 == 0
                      else sb["wbuf1"][:, 1024:2048].bitcast(F32))
                p.wait("tensor", "sEv0", p2m[pbk])
                def fbc(E, q=q, pb=pb):
                    a0 = (q % 4) * 512
                    return E.matmul(out=pb, lhsT=sb["onesf"][:, :],
                                    rhs=V["acrow"][0:1, a0:a0 + 512],
                                    start=True, stop=True)
                p.do("tensor", fbc, "sP", 1)
                if q % 4 == 3:
                    p2m["acrow"] = p.val("sP")
                p.wait("vector", "sP")
                p.wait("vector", "sA", p2m.get(f"ab{q % 2}", 0))
                def fsub(E, q=q, pb=pb, ab=ab):
                    in1 = sb["aclT"][:, 4 * q:4 * q + 4].rearrange(
                        "p (r o) -> p r o", o=1).to_broadcast([128, 4, 128])
                    return E.tensor_tensor(
                        out=ab.rearrange("p (r t) -> p r t", r=4),
                        in0=pb.rearrange("p (r t) -> p r t", r=4),
                        in1=in1, op=ALU.subtract)
                p.do("vector", fsub, "sV", 1)
                p.wait("vector", "sV")
                def fcl(E, ab=ab):
                    return E.tensor_scalar_min(ab, ab, 80.0)
                p.do("vector", fcl, "sV", 1)
                p.wait("scalar", "sV")
                def fes(E, q=q, ab=ab):
                    return E.activation(
                        out=V["S_sb"][:, q * 512:(q + 1) * 512],
                        in_=ab, func=AF.Exp)
                p.do("scalar", fes, "sA", 1)
                p2m[f"ab{q % 2}"] = p.val("sA")
                def fee(E, q=q, pb=pb):
                    return E.activation(
                        out=V["Ce_sb"][:, q * 512:(q + 1) * 512],
                        in_=pb, func=AF.Exp)
                p.do("scalar", fee, "sEv0", 1)
                p2m[pbk] = p.val("sEv0")
                p.wait("gpsimd", "sA")
                def fsm(E, q=q):
                    in1 = sb["GT_sb"][:, :].rearrange(
                        "p (o t) -> p o t", o=1).to_broadcast([128, 4, 128])
                    return E.tensor_tensor(
                        out=V["S_sb"][:, q * 512:(q + 1) * 512].rearrange(
                            "p (r t) -> p r t", r=4),
                        in0=V["S_sb"][:, q * 512:(q + 1) * 512].rearrange(
                            "p (r t) -> p r t", r=4),
                        in1=in1, op=ALU.mult)
                p.do("gpsimd", fsm, "sG", 1)
                p.wait("gpsimd", "sEv0")
                def fcm(E, q=q, gl=gl):
                    in1 = xsgv[:, 33, gl * 128:(gl + 1) * 128].rearrange(
                        "p (o t) -> p o t", o=1).to_broadcast([128, 4, 128])
                    return E.tensor_tensor(
                        out=V["Ce_sb"][:, q * 512:(q + 1) * 512].rearrange(
                            "p (r t) -> p r t", r=4),
                        in0=V["Ce_sb"][:, q * 512:(q + 1) * 512].rearrange(
                            "p (r t) -> p r t", r=4),
                        in1=in1, op=ALU.mult)
                p.do("gpsimd", fcm, "sG", 1)
            p2m["acrow"] = p.val("sP")

            p.wait("vector", "sSc0", p2m["yg"])
            p.wait("tensor", "sG")
            p.wait("tensor", "sV")
            p.wait("tensor", "sA")
            for pp in range(32):
                py = (PV("acc0", 1024, 1536) if pp % 2 == 0
                      else PV("acc0", 1536, 2048))
                pyk = "py0" if pp % 2 == 0 else "py1"
                p.wait("tensor", "sEv1", p2m[pyk])
                p.wait("tensor", "sV", p2m["hbf"])
                def fy(E, pp=pp, py=py):
                    last = None
                    for rr in range(2):
                        r = 2 * pp + rr
                        po = py[64 * rr:64 * (rr + 1), 0:128]
                        last = E.matmul(
                            out=po,
                            lhsT=V["xs_T"][:, pp * 128 + 64 * rr:
                                           pp * 128 + 64 * (rr + 1)],
                            rhs=V["S_sb"][:, r * 128:(r + 1) * 128],
                            start=True, stop=False)
                        last = E.matmul(
                            out=po,
                            lhsT=V["h_bf"][:, pp * 128 + 64 * rr:
                                           pp * 128 + 64 * (rr + 1)],
                            rhs=V["Ce_sb"][:, r * 128:(r + 1) * 128],
                            start=False, stop=True)
                    return last
                p.do("tensor", fy, "sP", 1)
                p.wait("vector", "sP")
                def fye(E, pp=pp, py=py, gl=gl, c=c):
                    return E.scalar_tensor_tensor(
                        out=V["y_grp"][:, :].rearrange(
                            "p (f t) -> p f t", f=32)[:, pp, :],
                        in0=xsgv[:, pp, gl * 128:(gl + 1) * 128],
                        scalar=sb["Drep"][:, pp:pp + 1],
                        in1=py[:, 0:128],
                        op0=ALU.mult, op1=ALU.add)
                p.do("vector", fye, "sEv1", 1)
                p2m[pyk] = p.val("sEv1")
            p2m["sce"] = p.val("sP")

        def fgam(E, c=c):
            return E.dma_start(out=sb["grow"][:, :],
                               in_=sb["asumsT"][c:c + 1, :])
        p.do("sync", fgam, "sX0", 16)
        p.wait("tensor", "sX0")
        pbg = PV("acc0", 0, 512)
        p.wait("tensor", "sEv0", p2m["pb0"])
        def fbg(E):
            return E.matmul(out=pbg[:, 0:64], lhsT=sb["onesf"][:, :],
                            rhs=sb["grow"][0:1, :], start=True, stop=True)
        p.do("tensor", fbg, "sP", 1)
        p.wait("scalar", "sP")
        def feg(E):
            return E.activation(out=sb["Gexp"][:, :], in_=pbg[:, 0:64],
                                func=AF.Exp)
        p.do("scalar", feg, "sEv0", 1)
        p2m["pb0"] = p.val("sEv0")

        p.wait("gpsimd", "sEv0")
        p.wait("tensor", "sEv1")
        for w in range(8):
            pc = PV("acc1", 0, 512) if w % 2 == 0 else PV("acc1", 512, 1024)
            pck = "pc0" if w % 2 == 0 else "pc1"
            cvk = "cv0" if w % 2 == 0 else "cv1"
            p.wait("tensor", "sEv1", p2m[pck])
            p.wait("tensor", "sEv0", p2m[cvk])
            def fcb(E, w=w, pc=pc):
                return E.matmul(out=pc, lhsT=sb["B_T"][:, :],
                                rhs=V["xsw_T"][:, w * 512:(w + 1) * 512],
                                start=True, stop=True)
            p.do("tensor", fcb, "sP", 1)
            thv = V["th0"] if w % 2 == 0 else V["th1"]
            thk = "th0" if w % 2 == 0 else "th1"
            p.wait("gpsimd", "sEv1", p2m[thk])
            p.wait("gpsimd", "sEv1", p2m["hupd"][w])
            def fth(E, w=w, thv=thv):
                in1 = sb["Gexp"][:, 8 * w:8 * w + 8].rearrange(
                    "p (r o) -> p r o", o=1).to_broadcast([128, 8, 64])
                return E.tensor_tensor(
                    out=thv.rearrange("p (r q) -> p r q", r=8),
                    in0=V["h"][:, w * 512:(w + 1) * 512].rearrange(
                        "p (r q) -> p r q", r=8),
                    in1=in1, op=ALU.mult)
            p.do("gpsimd", fth, "sG", 1)
            p.wait("vector", "sG")
            p.wait("vector", "sP")
            def fhu(E, w=w, pc=pc, thv=thv):
                return E.tensor_add(V["h"][:, w * 512:(w + 1) * 512],
                                    thv, pc)
            p.do("vector", fhu, "sEv1", 1)
            p2m[pck] = p.val("sEv1")
            p2m["hupd"][w] = p.val("sEv1")
            p2m[thk] = p.val("sEv1")
        p2m["xsTc"] = p.val("sP")

        if full:
            p.wait("vector", "sEv1")
            def fhb(E):
                return E.tensor_copy(V["h_bf"][:, :], V["h"][:, :])
            p.do("vector", fhb, "sV", 1)
            p2m["hbf"] = p.val("sV")
            p.wait("sync", "sEv1")
            def fyf(E, c=c):
                src2 = V["y_grp"][:, :].rearrange("p (f t) -> p f t", f=32)
                dst = scr["y_scr"].rearrange(
                    "(f p) t -> p f t", p=128)[:, :,
                    c * 128:(c + 1) * 128]
                return E.dma_start(out=dst, in_=src2)
            p.do("sync", fyf, "sSc0", 16)
            p2m["yg"] = p.val("sSc0")

    def phase_p2_export(seg):
        V = p2v()
        ro = rows_of(seg)
        slot_key = "my_S_own" if seg == "own" else "my_S_red"
        as_key = "my_As_own" if seg == "own" else "my_As_red"
        p.wait("vector", "sEv1")
        p.wait("vector", "sV")
        def fat(E, ro=ro):
            return E.scalar_tensor_tensor(
                out=sb["small"][0:64, 8:9], in0=sb["offsi"][ro, 15:16],
                scalar=1.0, in1=sb["asums"][:, 15:16],
                op0=ALU.mult, op1=ALU.add)
        p.do("vector", fat, "sV", 1)
        p.wait("sync", "sV")
        p.wait("sync", "sEv1")
        def fws(E, rows, slot_key=slot_key, as_key=as_key):
            r = rows[slot_key]
            ra = rows[as_key]
            E.dma_start(
                out=scr["win_S"].rearrange("a b p n -> (a b p) n")[
                    r:r + 128, :],
                in_=V["h"][:, :]).then_inc(p.sems["sWin"], 16)
            return E.dma_start(
                out=scr["win_As"].rearrange("a b r -> (a b) r")[
                    ra:ra + 1, :],
                in_=sb["small"][0:64, 8:9])
        p.roled("sync", fws, "sWin", 16)
        p.inc("sWin", 16)
        if seg == "own":
            def fac(E):
                return E.dma_start(out=scr["acum_scr"][0:64, :],
                                   in_=view("ar1", 121344, 64, 2048, F32))
            p.do("sync", fac, "sWin", 16)
        p.wait("sync", "sWin")

    def write_flag(j):
        def ff(E, rows, j=j):
            r = rows[f"my_flag{j}"]
            return E.dma_start(
                out=scr["win_flag"].rearrange("a b c -> (a b) c")[r:r + 1, :],
                in_=sb["noncebuf"][:, :])
        p.roled("sync", ff, "sWin", 16)
        p.wait("sync", "sWin")

    def poll_flag(j):
        def fp(E, j=j):
            r0 = _WF(0, j)
            r1 = _WF(1, j)
            with E.register(f"pf{j}_{p.poll_ctr}a") as fv, \
                    E.register(f"pf{j}_{p.poll_ctr}b") as fw:
                E.reg_mov(fv, 1)
                with E.While(fv):
                    E.sem_clear(p.sems["sPoll"])
                    E.dma_start(
                        out=sb["flagrd"][:, 0:16],
                        in_=scr["win_flag"].rearrange(
                            "a b c -> (a b) c")[r0:r0 + 1, :]
                    ).then_inc(p.sems["sPoll"], 16)
                    E.dma_start(
                        out=sb["flagrd"][:, 16:32],
                        in_=scr["win_flag"].rearrange(
                            "a b c -> (a b) c")[r1:r1 + 1, :]
                    ).then_inc(p.sems["sPoll"], 16)
                    E.wait_ge(p.sems["sPoll"], 32)
                    E.reg_load(fv, sb["flagrd"][0:1, 0:1].bitcast(I32))
                    E.reg_sub(fv, fv, p.noncereg_val)
                    E.reg_load(fw, sb["flagrd"][0:1, 16:17].bitcast(I32))
                    E.reg_sub(fw, fw, p.noncereg_val)
                    E.reg_alu(fv, fv, fw, ALU.bitwise_or)
            return None
        if p.eng == "sync":
            fp(p.E)
            p.E.sem_inc(p.sems["sG"], 1)
        p.inc("sG", 1)
        p.poll_ctr += 1

    def exchange_combine(stage):
        V = p2v()
        dst = V["hst"] if stage == 1 else V["hin"]
        slots = ([(0, 0), (0, 1), (1, 0), (1, 1)] if stage == 1
                 else [(0, 0), (0, 1), (0, 2), (1, 0), (1, 1), (1, 2)])
        selA0 = 24 if stage == 1 else 36
        selB0 = 28 if stage == 1 else 42
        selG0 = 32 if stage == 1 else 48
        mgcol = 2 if stage == 1 else 5
        macol = 0 if stage == 1 else 3
        p.wait("sync", "sV")
        def fz(E, dst=dst):
            E.memset(V["SaS"][:, :], 0.0)
            E.memset(V["SbS"][:, :], 0.0)
            return E.memset(sb["grow"][:, :], 0.0)
        p.do("vector", fz, "sV", 1)
        for ki, (par, s) in enumerate(slots):
            wrow = _WS(par, s)
            arow = _WA(par, min(s, 1))
            wb = V["hwA"] if ki % 2 == 0 else V["hwB"]
            sxk = "sX0" if ki % 2 == 0 else "sX1"
            mk_key = f"exw{ki % 2}"
            p.wait("sync", "sV", p2m.get(mk_key, 0))
            def frd(E, wrow=wrow, wb=wb):
                return E.dma_start(
                    out=wb[:, :],
                    in_=scr["win_S"].rearrange("a b p n -> (a b p) n")[
                        wrow:wrow + 128, :])
            p.do("sync", frd, sxk, 16)
            def fga(E, arow=arow, ki=ki):
                return E.dma_start(
                    out=view("ar2", 32768, 1, 384, F32)[:, ki * 64:(ki + 1) * 64],
                    in_=scr["win_As"].rearrange("a b r -> (a b) r")[
                        arow:arow + 1, :])
            p.do("sync", fga, sxk, 16)
            p.wait("vector", sxk)
            p.wait("vector", "sV")
            def facc(E, ki=ki, wb=wb, selA0=selA0):
                return E.scalar_tensor_tensor(
                    out=V["SaS"][:, :], in0=wb[:, :],
                    scalar=sb["masks"][:, selA0 + ki:selA0 + ki + 1],
                    in1=V["SaS"][:, :], op0=ALU.mult, op1=ALU.add)
            p.do("vector", facc, "sV", 1)
            p.wait("vector", "sV")
            def facb(E, ki=ki, wb=wb, selB0=selB0):
                return E.scalar_tensor_tensor(
                    out=V["SbS"][:, :], in0=wb[:, :],
                    scalar=sb["masks"][:, selB0 + ki:selB0 + ki + 1],
                    in1=V["SbS"][:, :], op0=ALU.mult, op1=ALU.add)
            p.do("vector", facb, "sV", 1)
            p.wait("vector", "sV")
            def facg(E, ki=ki, selG0=selG0):
                return E.scalar_tensor_tensor(
                    out=sb["grow"][:, :],
                    in0=view("ar2", 32768, 1, 384, F32)[:, ki * 64:(ki + 1) * 64],
                    scalar=sb["masks"][0:1, selG0 + ki:selG0 + ki + 1],
                    in1=sb["grow"][:, :], op0=ALU.mult, op1=ALU.add)
            p.do("vector", facg, "sV", 1)
            p2m[f"exw{ki % 2}"] = p.val("sV")
        pbg = PV("acc0", 0, 512)
        p.wait("tensor", "sV")
        p.wait("tensor", "sEv0", p2m["pb0"])
        def fbg(E):
            return E.matmul(out=pbg[:, 0:64], lhsT=sb["onesf"][:, :],
                            rhs=sb["grow"][0:1, :], start=True, stop=True)
        p.do("tensor", fbg, "sP", 1)
        p.wait("scalar", "sP")
        def feg(E, mgcol=mgcol):
            return E.activation(out=V["G1e"][:, 0:64], in_=pbg[:, 0:64],
                                func=AF.Exp,
                                scale=sb["masks"][:, mgcol:mgcol + 1])
        p.do("scalar", feg, "sEv0", 1)
        p2m["pb0"] = p.val("sEv0")
        p.wait("scalar", "sEv0")
        def fmg(E, macol=macol):
            return E.activation(out=V["G1e"][:, 0:64], in_=V["G1e"][:, 0:64],
                                func=AF.Copy,
                                scale=sb["masks"][:, macol:macol + 1])
        p.do("scalar", fmg, "sA", 1)
        p.wait("vector", "sA")
        def fc1(E, dst=dst):
            in1 = V["G1e"][:, 0:64].rearrange(
                "p (r o) -> p r o", o=1).to_broadcast([128, 64, 64])
            return E.tensor_tensor(
                out=dst[:, :].rearrange("p (r q) -> p r q", r=64),
                in0=V["SaS"][:, :].rearrange("p (r q) -> p r q", r=64),
                in1=in1, op=ALU.mult)
        p.do("vector", fc1, "sV", 1)
        p.wait("vector", "sV")
        def fc2(E, dst=dst):
            return E.tensor_add(dst[:, :], dst[:, :], V["SbS"][:, :])
        p.do("vector", fc2, "sV", 1)
        p.wait("vector", "sV")
        if stage == 2:
            def fc3(E, dst=dst):
                return E.scalar_tensor_tensor(
                    out=dst[:, :], in0=V["hst"][:, :],
                    scalar=sb["masks"][:, 6:7], in1=dst[:, :],
                    op0=ALU.mult, op1=ALU.add)
            p.do("vector", fc3, "sV", 1)
            p.wait("vector", "sV")

    def phase_exchange():
        V = p2v()
        write_flag(1)
        poll_flag(1)
        p.wait("vector", "sG")
        p.wait("tensor", "sG")
        exchange_combine(1)
        p.wait("sync", "sV")
        def fws(E, rows):
            r = rows["my_stage1"]
            return E.dma_start(
                out=scr["win_S"].rearrange("a b p n -> (a b p) n")[
                    r:r + 128, :],
                in_=V["hst"][:, :])
        p.roled("sync", fws, "sWin", 16)
        p.wait("sync", "sWin")
        write_flag(2)
        poll_flag(2)
        p.wait("vector", "sG")
        p.wait("tensor", "sG")
        exchange_combine(2)
        hinb = view("ar2", 0, 128, 4096, BF16)
        def fhb(E, hinb=hinb):
            return E.tensor_copy(hinb, V["hin"][:, :])
        p.do("vector", fhb, "sV", 1)


    def p3v():
        return {
            "g_norm": view("ar1", 0, 128, NK * 2048, BF16),
            "hinb": view("ar2", 0, 128, 4096, BF16),
            "ytile": view("ar2", 8192, 128, 32 * 256, BF16),
            "ztile": view("ar2", 24576, 128, 32 * 256, BF16),
            "acrow3": sb["wbuf0"][:, :].bitcast(F32),
            "cdiag33": sb["evst0"][:, 0:512],
            "cchunk": sb["evst0"][:, 512:640],
            "acg": sb["evst1"][:, 0:256].bitcast(F32),
            "gsq": sb["evst1"][:, 1024:1536],
            "rstrow": view("ar2", 8192, 1, 2048, F32),
            "gtmp": sb["evst1"][:, 512:768].bitcast(F32),
        }

    def phase_p3a():
        V = p3v()
        gn3 = V["g_norm"][:, :].rearrange("p (f t) -> p f t", f=NK)
        def fcd(E):
            return E.dma_start(
                out=V["cdiag33"].rearrange("p (k c) -> p k c", k=KCONV),
                in_=ap["convdiag"][:, 33, :, :])
        p.do("gpsimd", fcd, "sW0", 16)
        def fnc(E):
            return E.dma_start(out=sb["nrmwc"][:, :], in_=ap["nrmwc"])
        p.do("sync", fnc, "sX0", 16)
        def fhd(E):
            return E.dma_start(out=sb["wrep"][0:64, :], in_=ap["hdrep"])
        p.do("gpsimd", fhd, "sW0", 16)
        eac = sb["wbuf1"][0:64, 0:256]
        acgv = sb["wbuf1"][:, :].bitcast(F32)[0:64, 256:512]
        slab = sb["wbuf0"][:, 0:260]
        cc = sb["wbuf0"][:, 512:768]
        for s in range(8):
            par = s % 2
            c0 = 2 * s
            ybuf = view("ar2", 8192 + par * 16384, 128, 32 * 256, BF16)
            yeng, ysem = ("sync", "sSc0") if par == 0 else ("scalar", "sSc1")
            p.wait(yeng, "sV", p2m.get(f"a3y{par}", 0))
            def fyl(E, c0=c0, ybuf=ybuf):
                return E.dma_start(
                    out=ybuf[:, :].rearrange("p (f t) -> p f t", f=32),
                    in_=scr["y_scr"].rearrange(
                        "(f p) t -> p f t",
                        p=128)[:, :, c0 * 128:(c0 + 2) * 128])
            p.do(yeng, fyl, ysem, 16)
            p2m[f"a3yl{par}"] = p.val(ysem)
            p.wait("sync", "sP", p2m.get("a3ac", 0))
            def fac(E, s=s):
                return E.dma_start(
                    out=acgv,
                    in_=scr["acum_scr"][0:64, s * 256:(s + 1) * 256])
            p.do("sync", fac, "sX0", 16)
            p.wait("vector", "sX0")
            for i in range(2):
                def fofs(E, i=i, c0=c0):
                    return E.tensor_scalar(
                        out=acgv[:, i * 128:(i + 1) * 128],
                        in0=acgv[:, i * 128:(i + 1) * 128],
                        scalar1=sb["offsi"][0:64, c0 + i:c0 + i + 1],
                        scalar2=None, op0=ALU.add)
                p.do("vector", fofs, "sV", 1)
            p.wait("scalar", "sV")
            p.wait("scalar", "sP", p2m.get("a3ac", 0))
            def fexp(E):
                return E.activation(out=eac, in_=acgv, func=AF.Exp)
            p.do("scalar", fexp, "sA", 1)
            p2m["a3ex"] = p.val("sA")
            p.wait("gpsimd", "sP", p2m.get("a3cv", 0))
            def fsl(E, s=s):
                if s == 0:
                    return E.dma_start(
                        out=slab[:, 3:259],
                        in_=scr["xbc_scr"][33 * 128:34 * 128, 0:256])
                lo = s * 256 - 3
                return E.dma_start(
                    out=slab[:, 0:259],
                    in_=scr["xbc_scr"][33 * 128:34 * 128, lo:lo + 259])
            p.do("gpsimd", fsl, "sW1", 16)
            if s == 0:
                p.wait("vector", "sW1")
                def fhz(E):
                    return E.memset(slab[:, 0:3], 0.0)
                p.do("vector", fhz, "sV", 1)
                p.wait("vector", "sV")
                for kslot in range(4):
                    def fhm(E, kslot=kslot):
                        return E.scalar_tensor_tensor(
                            out=slab[:, 0:3],
                            in0=sb["halos4"][:, kslot * 136 + 33 * 4:
                                             kslot * 136 + 33 * 4 + 3],
                            scalar=sb["masks"][:, 16 + kslot:17 + kslot],
                            in1=slab[:, 0:3],
                            op0=ALU.mult, op1=ALU.add)
                    p.do("vector", fhm, "sV", 1)
                    p.wait("vector", "sV")
            pcv = PV("acc1", 0, 256)
            p.wait("tensor", "sW1")
            p.wait("tensor", "sW0")
            p.wait("tensor", "sA", p2m.get("a3sl", 0))
            if s == 0:
                p.wait("tensor", "sV")
            def fcm(E, pcv=pcv):
                last = None
                for k in range(KCONV):
                    last = E.matmul(
                        out=pcv,
                        lhsT=V["cdiag33"].rearrange(
                            "p (k cc) -> p k cc", k=KCONV)[:, k, :],
                        rhs=slab[:, k:k + 256],
                        start=(k == 0), stop=(k == KCONV - 1))
                return last
            p.do("tensor", fcm, "sP", 1)
            p.wait("scalar", "sP")
            def fsilu(E, pcv=pcv):
                return E.activation(out=cc, in_=pcv, func=AF.Silu)
            p.do("scalar", fsilu, "sA", 1)
            p2m["a3sl"] = p.val("sA")
            p2m["a3slv"] = p.val("sA")
            for pp in range(32):
                ppar = pp % 2
                pe = PV("acc1", 1024 + 512 * ppar, 1024 + 512 * ppar + 256)
                py = PV("acc0", 512 * ppar, 512 * ppar + 256)
                ebuf = sb["evst0"][:, 1024 + 256 * ppar:1280 + 256 * ppar]
                p.wait("tensor", "sA", p2m.get("a3ex", 0))
                p.wait("tensor", "sA", p2m.get(f"a3e{ppar}", 0))
                def fmb(E, pp=pp, pe=pe):
                    return E.matmul(
                        out=pe,
                        lhsT=sb["wrep"][0:64, pp * 128:(pp + 1) * 128],
                        rhs=sb["wbuf1"][0:64, 0:256],
                        start=True, stop=True)
                p.do("tensor", fmb, "sP", 1)
                p2m["a3mb"] = p.val("sP")
                p.wait("scalar", "sP", p2m.get("a3mb", 0))
                p.wait("scalar", "sV", p2m.get(f"a3b{ppar}", 0))
                def fev(E, pe=pe, ebuf=ebuf):
                    return E.activation(out=ebuf, in_=pe, func=AF.Copy)
                p.do("scalar", fev, "sA", 1)
                p2m[f"a3e{ppar}"] = p.val("sA")
                p.wait("tensor", "sA", p2m.get("a3slv", 0))
                p.wait("tensor", "sV", p2m.get(f"a3p{ppar}", 0))
                def fmy(E, pp=pp, py=py):
                    return E.matmul(
                        out=py,
                        lhsT=V["hinb"][:, pp * 128:(pp + 1) * 128],
                        rhs=cc, start=True, stop=True)
                p.do("tensor", fmy, "sP", 1)
                p2m["a3my"] = p.val("sP")
                p.wait("vector", "sA", p2m.get(f"a3e{ppar}", 0))
                p.wait("vector", "sP", p2m.get("a3my", 0))
                def fd1(E, py=py, ebuf=ebuf):
                    return E.tensor_mul(py, py, ebuf)
                p.do("vector", fd1, "sV", 1)
                p2m[f"a3b{ppar}"] = p.val("sV")
                p.wait("vector", "sV")
                p.wait("vector", ysem, p2m.get(f"a3yl{par}", 0))
                def fd2(E, pp=pp, s=s, py=py, ybuf=ybuf):
                    yt = ybuf[:, :].rearrange(
                        "p (f t) -> p f t", f=32)[:, pp, :]
                    return E.tensor_add(
                        gn3[:, pp, s * 256:(s + 1) * 256], py, yt)
                p.do("vector", fd2, "sV", 1)
                p2m[f"a3p{ppar}"] = p.val("sV")
            p2m["a3ac"] = p.val("sP")
            p2m["a3cv"] = p.val("sP")
            p2m[f"a3y{par}"] = p.val("sV")
        barrier()
        for tt in range(4):
            prs = PV("acc1", 0, 512)
            p.wait("tensor", "sEv0", p2m.get("p3rs", 0))
            def frs(E, tt=tt):
                last = None
                for k in range(NK):
                    last = E.matmul(
                        out=prs[0:1, :],
                        lhsT=sb["onesb"][:, :],
                        rhs=V["g_norm"][:, :].rearrange(
                            "p (f t) -> p f t", f=NK)[:, k,
                            tt * 512:(tt + 1) * 512].rearrange(
                            "p t -> p (t)") if False else
                        V["gsq"][:, :],
                        start=(k == 0), stop=(k == NK - 1))
                return last
            def frs2(E, tt=tt, prs=prs):
                last = None
                for k in range(NK):
                    E.wait_ge(p.sems["sA"], 0)
                return last
            del frs, frs2
            for k in range(NK):
                zidx = tt * NK + k
                zpar = zidx % 2
                zeng, zsem = (("sync", "sSc0") if zpar == 0
                              else ("gpsimd", "sW1"))
                zk = sb["wbuf1"][:, 1024 + 512 * zpar:1536 + 512 * zpar]
                gsqp = sb["evst1"][:, 1024 + 512 * zpar:1536 + 512 * zpar]
                p.wait(zeng, "sV", p2m.get(f"z3{zpar}", 0))
                def fzl(E, tt=tt, k=k, zk=zk):
                    return E.dma_start(
                        out=zk,
                        in_=scr["zsil_scr"][k * 128:(k + 1) * 128,
                                            tt * 512:(tt + 1) * 512])
                p.do(zeng, fzl, zsem, 16)
                p.wait("vector", zsem)
                def fzm(E, tt=tt, k=k, zk=zk):
                    sl = V["g_norm"][:, :].rearrange(
                        "p (f t) -> p f t", f=NK)[:, k,
                        tt * 512:(tt + 1) * 512]
                    return E.tensor_mul(sl, sl, zk)
                p.do("vector", fzm, "sV", 1)
                p2m[f"z3{zpar}"] = p.val("sV")
                p.wait("scalar", "sV", p2m.get(f"z3{zpar}", 0))
                p.wait("scalar", "sP", p2m.get(f"p3sq{zpar}", 0))
                def fsq(E, tt=tt, k=k, gsqp=gsqp):
                    return E.activation(
                        out=gsqp,
                        in_=V["g_norm"][:, :].rearrange(
                            "p (f t) -> p f t", f=NK)[:, k,
                            tt * 512:(tt + 1) * 512],
                        func=AF.Square)
                p.do("scalar", fsq, "sA", 1)
                p.wait("tensor", "sA")
                def fmm(E, k=k, prs=prs, gsqp=gsqp):
                    return E.matmul(out=prs[0:1, :], lhsT=sb["onesb"][:, :],
                                    rhs=gsqp,
                                    start=(k == 0), stop=(k == NK - 1))
                p.do("tensor", fmm, "sP", 1)
                p2m[f"p3sq{zpar}"] = p.val("sP")
            p.wait("scalar", "sP")
            def fsr(E, tt=tt):
                return E.activation(out=p3v()["rstrow"][0:1, tt * 512:
                                                     (tt + 1) * 512],
                                    in_=prs[0:1, :], func=AF.Sqrt,
                                    scale=1.0 / DS, bias=EPS)
            p.do("scalar", fsr, "sEv0", 1)
            p2m["p3rs"] = p.val("sEv0")
        p.wait("vector", "sEv0")
        def frc(E):
            return E.reciprocal(p3v()["rstrow"][0:1, :], p3v()["rstrow"][0:1, :])
        p.do("vector", frc, "sV", 1)
        for tt in range(4):
            prb = PV("acc1", 512, 1024)
            p.wait("tensor", "sV")
            p.wait("tensor", "sEv1", p2m.get("p3rb", 0))
            def fbb(E, tt=tt, prb=prb):
                return E.matmul(out=prb, lhsT=sb["onesf"][:, :],
                                rhs=p3v()["rstrow"][0:1, tt * 512:
                                                 (tt + 1) * 512],
                                start=True, stop=True)
            p.do("tensor", fbb, "sP", 1)
            p.wait("vector", "sP")
            for k in range(NK):
                def fap(E, tt=tt, k=k, prb=prb):
                    sl = V["g_norm"][:, :].rearrange(
                        "p (f t) -> p f t", f=NK)[:, k,
                        tt * 512:(tt + 1) * 512]
                    return E.scalar_tensor_tensor(
                        out=sl, in0=sl, scalar=sb["nrmwc"][:, k:k + 1],
                        in1=prb, op0=ALU.mult, op1=ALU.mult)
                p.do("vector", fap, "sEv1", 1)
            p2m["p3rb"] = p.val("sEv1")

        barrier()
        for n in range(32):
            pr = n % 2
            wb = sb["wbuf0"] if pr == 0 else sb["wbuf1"]
            pb = ps["acc0"] if pr == 0 else ps["acc1"]
            eb = sb["evst0"] if pr == 0 else sb["evst1"]
            p.wait("gpsimd", "sP", p2m.get(f"p3w{pr}", 0))
            def fw(E, n=n, wb=wb):
                return E.dma_start(
                    out=wb[:, :].rearrange("p (k c) -> p k c", k=NK),
                    in_=ap["w_outproj"][n, :, :, :])
            p.do("gpsimd", fw, f"sW{pr}", 16)
            p.wait("tensor", f"sW{pr}")
            p.wait("tensor", "sEv0" if pr == 0 else "sEv1",
                   p2m.get(f"p3e{pr}", 0))
            def fm(E, wb=wb, pb=pb):
                last = None
                for k in range(NK):
                    lhsT = wb[:, :].rearrange("p (k c) -> p k c",
                                              k=NK)[:, k, :]
                    for t in range(4):
                        last = E.matmul(
                            out=pb[:, t * 512:(t + 1) * 512], lhsT=lhsT,
                            rhs=V["g_norm"][:, :].rearrange(
                                "p (f t) -> p f t", f=NK)[:, k,
                                t * 512:(t + 1) * 512],
                            start=(k == 0), stop=(k == NK - 1))
                return last
            p.do("tensor", fm, "sP", 1)
            p2m[f"p3w{pr}"] = p.val("sP")
            sevn = "sEv0" if pr == 0 else "sEv1"
            seng = "scalar" if pr == 0 else "vector"
            p.wait(seng, "sP")
            p.wait(seng, "sSc0" if pr == 0 else "sSc1",
                   p2m.get(f"p3o{pr}", 0))
            def fe(E, pb=pb, eb=eb, pr=pr):
                if pr == 0:
                    return E.activation(out=eb[:, :], in_=pb[:, :],
                                        func=AF.Copy)
                return E.tensor_copy(eb[:, :], pb[:, :])
            p.do(seng, fe, sevn, 1)
            p2m[f"p3e{pr}"] = p.val(sevn)
            p.wait("sync", sevn)
            def fo(E, n=n, eb=eb):
                return E.dma_start(
                    out=scr["attn_scr"][n * 128:(n + 1) * 128, :],
                    in_=eb[:, :])
            p.do("sync", fo, "sSc0" if pr == 0 else "sSc1", 16)
            p2m[f"p3o{pr}"] = p.val("sSc0" if pr == 0 else "sSc1")

    def p3bv():
        return {
            "x1f": view("ar1", 0, 128, NK * 2048, BF16),
            "attn_t": view("ar2", 0, 128, H, BF16),
            "xrow": view("ar2", 8192, 128, H, F32),
            "srow": view("ar2", 8192, 128, H, F32),
            "x1t": view("ar2", 24576, 128, H, BF16),
            "qsb2": view("ar2", 0, 16, 2048, BF16),
            "scs": view("ar2", 4096, 64, 2048, F32),
            "gatef": view("ar2", 12288, 64, 2048, BF16),
            "z2g": view("ar2", 16384, 128, 4 * 2048, BF16),
            "o2t": view("ar2", 32768, 128, 16 * 128, BF16),
            "wqs": view("ar2", 36864, 128, NK * 16, BF16),
            "expE": view("ar2", 37888, 64, 4 * 128, BF16),
            "sct": view("ar2", 38912, 128, 256, F32),
            "sct2": view("ar2", 39936, 128, 64, BF16),
            "keyss": view("ar2", 40064, 16, 64, BF16),
        }

    def phase_p3b_ln1():
        V = p3bv()
        x1f3 = V["x1f"][:, :].rearrange("p (f t) -> p f t", f=NK)
        for c in range(16):
            p.wait("sync", "sV", p2m.get("p3battn", 0))
            p.wait("gpsimd", "sV", p2m.get("p3battn", 0))
            def fat(E, c=c):
                return E.dma_start(
                    out=sb["evst0"][:, :].rearrange(
                        "p (f t) -> p f t", f=16),
                    in_=scr["attn_scr"].rearrange(
                        "(f p) t -> p f t",
                        p=128)[:, 0:16, c * 128:(c + 1) * 128])
            p.do("sync", fat, "sSc0", 16)
            def fat2(E, c=c):
                return E.dma_start(
                    out=sb["evst1"][:, :].rearrange(
                        "p (f t) -> p f t", f=16),
                    in_=scr["attn_scr"].rearrange(
                        "(f p) t -> p f t",
                        p=128)[:, 16:32, c * 128:(c + 1) * 128])
            p.do("gpsimd", fat2, "sW0", 16)
            def fxa(E, c=c):
                return E.dma_start(out=V["xrow"][:, 0:2048],
                                   in_=ap["xin"][c * 128:(c + 1) * 128,
                                                 0:2048])
            p.do("sync", fxa, "sX0", 16)
            def fxb(E, c=c):
                return E.dma_start(out=V["xrow"][:, 2048:4096],
                                   in_=ap["xin"][c * 128:(c + 1) * 128,
                                                 2048:4096])
            p.do("gpsimd", fxb, "sW1", 16)
            p.wait("scalar", "sSc0")
            def ftr(E):
                return E.dma_start(
                    out=V["attn_t"][:, 0:2048].rearrange(
                        "p (f q) -> p f q", f=16),
                    in_=sb["evst0"][:, :], transpose=True)
            p.do("scalar", ftr, "sTr", 16)
            p.wait("scalar", "sW0")
            def ftr2(E):
                return E.dma_start(
                    out=V["attn_t"][:, 2048:4096].rearrange(
                        "p (f q) -> p f q", f=16),
                    in_=sb["evst1"][:, :], transpose=True)
            p.do("scalar", ftr2, "sTr", 16)
            p.wait("vector", "sTr")
            p.wait("vector", "sX0")
            p.wait("vector", "sW1")
            def fs1(E):
                return E.scalar_tensor_tensor(
                    out=V["srow"][:, :], in0=V["attn_t"][:, :], scalar=1.0,
                    in1=V["xrow"][:, :], op0=ALU.mult, op1=ALU.add,
                    accum_out=sb["small"][:, 9:10])
            p.do("vector", fs1, "sV", 1)
            p.wait("vector", "sV")
            def fs2(E):
                return E.tensor_scalar(
                    out=V["srow"][:, :], in0=V["srow"][:, :],
                    scalar1=sb["small"][:, 9:10], scalar2=-1.0 / H,
                    op0=ALU.mult if False else ALU.bypass, op1=ALU.bypass)
            del fs2
            def fmn(E):
                return E.tensor_scalar_mul(sb["small"][:, 10:11],
                                           sb["small"][:, 9:10], -1.0 / H)
            p.do("vector", fmn, "sV", 1)
            p.wait("vector", "sV")
            def fs3(E):
                return E.tensor_scalar(
                    out=V["srow"][:, :], in0=V["srow"][:, :],
                    scalar1=sb["small"][:, 10:11], scalar2=None,
                    op0=ALU.add)
            p.do("vector", fs3, "sV", 1)
            p.wait("scalar", "sV")
            def fsq(E):
                return E.activation(out=V["attn_t"][:, :],
                                    in_=V["srow"][:, :], func=AF.Square,
                                    accum_out=sb["small"][:, 11:12])
            p.do("scalar", fsq, "sA", 1)
            p.wait("scalar", "sA")
            def fsd(E):
                return E.activation(out=sb["small"][:, 12:13],
                                    in_=sb["small"][:, 11:12],
                                    func=AF.Sqrt, scale=1.0 / H, bias=LNEPS)
            p.do("scalar", fsd, "sA", 1)
            p.wait("vector", "sA")
            def frv(E):
                return E.reciprocal(sb["small"][:, 12:13],
                                    sb["small"][:, 12:13])
            p.do("vector", frv, "sV", 1)
            p.wait("vector", "sV")
            p.wait("vector", "sTr", p2m.get("l1t", 0))
            p.wait("vector", "sW0", p2m.get("l1w", 0))
            def fx1(E):
                return E.tensor_scalar_mul(V["x1t"][:, :], V["srow"][:, :],
                                           sb["small"][:, 12:13])
            p.do("vector", fx1, "sV", 1)
            p2m["p3battn"] = p.val("sV")
            p.wait("sync", "sV")
            p.wait("gpsimd", "sV")
            def fxw(E, c=c):
                return E.dma_start(
                    out=scr["x1t_scr"][c * 128:(c + 1) * 128, :],
                    in_=V["x1t"][:, :])
            p.do("gpsimd", fxw, "sW0", 16)
            p2m["l1w"] = p.val("sW0")
            def fxf(E, c=c):
                outap = x1f3[:, :, c * 128:(c + 1) * 128]
                return E.dma_start(out=outap, in_=V["x1t"][:, :],
                                   transpose=True)
            p.do("sync", fxf, "sTr", 16)
            p2m["l1t"] = p.val("sTr")
        p.wait("tensor", "sTr")

    def phase_p3b_ffn():
        V = p3bv()
        x1f3 = V["x1f"][:, :].rearrange("p (f t) -> p f t", f=NK)
        def fwq(E):
            return E.dma_start(
                out=V["wqs"][:, :].rearrange("p (k c) -> p k c", k=NK),
                in_=ap["w_ffnQ"])
        p.do("gpsimd", fwq, "sW0", 16)
        def fky(E):
            return E.dma_start(out=V["keyss"][:, :], in_=ap["keysT"])
        p.do("gpsimd", fky, "sW0", 16)
        def fex(E):
            return E.dma_start(
                out=V["expE"][:, :].rearrange("p (j c) -> p j c", j=4),
                in_=ap["expandE"])
        p.do("gpsimd", fex, "sW0", 16)
        p.wait("tensor", "sW0")
        pq = ps["acc0"][0:16, :]
        def fq(E, pq=pq):
            last = None
            for k in range(NK):
                for t in range(4):
                    last = E.matmul(
                        out=pq[:, t * 512:(t + 1) * 512],
                        lhsT=V["wqs"][:, :].rearrange(
                            "p (k c) -> p k c", k=NK)[:, k, :],
                        rhs=x1f3[:, k, t * 512:(t + 1) * 512],
                        start=(k == 0), stop=(k == NK - 1))
            return last
        p.do("tensor", fq, "sP", 1)
        p.wait("scalar", "sP")
        def feq(E, pq=pq):
            return E.activation(out=V["qsb2"][0:16, :], in_=pq,
                                func=AF.Copy)
        p.do("scalar", feq, "sEv0", 1)
        p.wait("tensor", "sEv0")
        psc = ps["acc1"][0:64, :]
        def fsc(E, psc=psc):
            last = None
            for t in range(4):
                last = E.matmul(out=psc[:, t * 512:(t + 1) * 512],
                                lhsT=V["keyss"][0:16, :],
                                rhs=V["qsb2"][0:16, t * 512:(t + 1) * 512],
                                start=True, stop=True)
            return last
        p.do("tensor", fsc, "sP", 1)
        p.wait("vector", "sP")
        def fes(E, psc=psc):
            return E.tensor_copy(V["scs"][:, :], psc)
        p.do("vector", fes, "sEv1", 1)
        for c in range(16):
            ptr2 = ps["acc0"][0:128, 1024:1536]
            p.wait("tensor", "sEv1")
            p.wait("tensor", "sEv0", p2m.get("p3g", 0))
            def ft1(E, c=c, ptr2=ptr2):
                return E.transpose(out=ptr2[:, 0:64],
                                   in_=V["scs"][:, c * 128:(c + 1) * 128],
                                   identity=sb["identf"][0:64, 0:64])
            p.do("tensor", ft1, "sP", 1)
            p.wait("vector", "sP")
            def fge(E, ptr2=ptr2):
                return E.tensor_copy(V["sct"][:, 0:64], ptr2[:, 0:64])
            p.do("vector", fge, "sV", 1)
            p.wait("vector", "sV")
            def fm8(E):
                return E.max(out=V["sct"][:, 64:72], in_=V["sct"][:, 0:64])
            p.do("vector", fm8, "sV", 1)
            p.wait("vector", "sV")
            def fmr(E):
                return E.match_replace(out=V["sct"][:, 128:192],
                                       in_to_replace=V["sct"][:, 64:72],
                                       in_values=V["sct"][:, 0:64],
                                       imm_value=-1e30)
            p.do("vector", fmr, "sV", 1)
            p.wait("vector", "sV")
            def fm8b(E):
                return E.max(out=V["sct"][:, 72:80],
                             in_=V["sct"][:, 128:192])
            p.do("vector", fm8b, "sV", 1)
            p.wait("vector", "sV")
            def fnm(E):
                return E.tensor_scalar_mul(V["sct"][:, 80:81],
                                           V["sct"][:, 64:65], -1.0)
            p.do("vector", fnm, "sV", 1)
            p.wait("scalar", "sV")
            def fex2(E):
                return E.activation(out=V["sct"][:, 128:192],
                                    in_=V["sct"][:, 0:64], func=AF.Exp,
                                    bias=V["sct"][:, 80:81])
            p.do("scalar", fex2, "sA", 1)
            p.wait("vector", "sA")
            def fmk(E):
                return E.tensor_scalar(out=V["sct"][:, 192:256],
                                       in0=V["sct"][:, 0:64],
                                       scalar1=V["sct"][:, 79:80],
                                       scalar2=None, op0=ALU.is_ge)
            p.do("vector", fmk, "sV", 1)
            p.wait("vector", "sV")
            def fme(E):
                return E.tensor_mul(V["sct"][:, 128:192],
                                    V["sct"][:, 128:192],
                                    V["sct"][:, 192:256])
            p.do("vector", fme, "sV", 1)
            p.wait("vector", "sV")
            def fsm(E):
                return E.reduce_sum(V["sct"][:, 81:82],
                                    V["sct"][:, 128:192],
                                    axis=mybir.AxisListType.X)
            p.do("vector", fsm, "sV", 1)
            p.wait("vector", "sV")
            def frz(E):
                return E.reciprocal(V["sct"][:, 81:82], V["sct"][:, 81:82])
            p.do("vector", frz, "sV", 1)
            p.wait("vector", "sV")
            def fgt(E):
                return E.tensor_scalar_mul(
                    V["sct"][:, 128:192].bitcast(BF16)[:, 0:64]
                    if False else V["sct2"][:, 0:64],
                    V["sct"][:, 128:192], V["sct"][:, 81:82])
            p.do("vector", fgt, "sV", 1)
            p.wait("tensor", "sV")
            ptg = ps["acc0"][0:64, 1536:1600].bitcast(BF16)
            def ft2(E, ptg=ptg):
                return E.transpose(out=ptg,
                                   in_=V["sct2"][:, 0:64],
                                   identity=sb["identb"][:, :])
            p.do("tensor", ft2, "sP", 1)
            p.wait("vector", "sP")
            def fgf(E, c=c, ptg=ptg):
                return E.tensor_copy(V["gatef"][:, c * 128:(c + 1) * 128],
                                     ptg)
            p.do("vector", fgf, "sEv0", 1)
            p2m["p3g"] = p.val("sEv0")
        for nb in range(4):
            pz = ps["acc0"][:, 0:2048]
            pgx = ps["acc1"][:, 0:2048]
            p.wait("gpsimd", "sP", p2m.get("p3za", 0))
            def fwa(E, nb=nb):
                return E.dma_start(
                    out=sb["wbuf1"][:, :].rearrange("p (k c) -> p k c",
                                                    k=NK),
                    in_=ap["w_ffnA"][nb, :, :, :])
            p.do("gpsimd", fwa, "sW1", 16)
            p.wait("tensor", "sW1")
            p.wait("tensor", "sV", p2m.get("p3zv", 0))
            def fz(E, pz=pz):
                last = None
                for k in range(NK):
                    for t in range(4):
                        last = E.matmul(
                            out=pz[:, t * 512:(t + 1) * 512],
                            lhsT=sb["wbuf1"][:, :].rearrange(
                                "p (k c) -> p k c", k=NK)[:, k, :],
                            rhs=x1f3[:, k, t * 512:(t + 1) * 512],
                            start=(k == 0), stop=(k == NK - 1))
                return last
            p.do("tensor", fz, "sP", 1)
            p2m["p3za"] = p.val("sP")
            def fgx(E, nb=nb, pgx=pgx):
                last = None
                for t in range(4):
                    last = E.matmul(
                        out=pgx[:, t * 512:(t + 1) * 512],
                        lhsT=V["expE"][:, :].rearrange(
                            "p (j c) -> p j c", j=4)[0:64, nb, :],
                        rhs=V["gatef"][:, t * 512:(t + 1) * 512],
                        start=True, stop=True)
                return last
            p.do("tensor", fgx, "sP", 1)
            p.wait("vector", "sP")
            def feg(E, pgx=pgx):
                return E.tensor_copy(sb["evst0"][:, :], pgx)
            p.do("vector", feg, "sV", 1)
            p.wait("vector", "sV")
            def fzg(E, nb=nb, pz=pz):
                return E.tensor_mul(
                    V["z2g"][:, nb * 2048:(nb + 1) * 2048],
                    sb["evst0"][:, :].to_broadcast([128, 2048])
                    if False else sb["evst0"][:, :], pz)
            def fzg2(E, nb=nb, pz=pz):
                return E.tensor_mul(
                    V["z2g"][:, nb * 2048:(nb + 1) * 2048],
                    sb["evst0"][:, :], pz)
            del fzg
            p.do("vector", fzg2, "sV", 1)
            p2m["p3zv"] = p.val("sV")
        for nf in range(32):
            pr = nf % 2
            po = ps["acc0"][:, 0:2048] if pr == 0 else ps["acc1"][:, 0:2048]
            eb = sb["evst0"] if pr == 0 else sb["evst1"]
            p.wait("gpsimd", "sP", p2m.get("p3wb", 0))
            if nf == 0:
                p.wait("gpsimd", "sV")
            wslot = sb["wbuf0"] if pr == 0 else sb["wbuf1"]
            def fwb(E, nf=nf, wslot=wslot):
                return E.dma_start(
                    out=wslot[:, 0:512].rearrange("p (k c) -> p k c", k=4),
                    in_=ap["w_ffnB"][nf, :, :, :])
            p.do("gpsimd", fwb, f"sW{pr}", 16)
            p.wait("tensor", f"sW{pr}")
            p.wait("tensor", "sV" if nf < 2 else "sEv1",
                   p2m.get(f"p3bo{pr}", 0))
            def fb(E, po=po, wslot=wslot):
                last = None
                for k in range(4):
                    for t in range(4):
                        last = E.matmul(
                            out=po[:, t * 512:(t + 1) * 512],
                            lhsT=wslot[:, 0:512].rearrange(
                                "p (k c) -> p k c", k=4)[:, k, :],
                            rhs=V["z2g"][:, k * 2048 + t * 512:
                                          k * 2048 + (t + 1) * 512],
                            start=(k == 0), stop=(k == 3))
                return last
            p.do("tensor", fb, "sP", 1)
            p2m["p3wb"] = p.val("sP")
            seng = "scalar" if pr == 0 else "vector"
            sevn = "sEv0" if pr == 0 else "sEv1"
            p.wait(seng, "sP")
            p.wait(seng, "sTr", p2m.get(f"p3ot{pr}", 0))
            def fe(E, po=po, eb=eb, pr=pr):
                if pr == 0:
                    return E.activation(out=eb[:, :], in_=po,
                                        func=AF.Copy, scale=FFN_SCALE)
                return E.tensor_scalar_mul(eb[:, :], po, FFN_SCALE)
            p.do(seng, fe, sevn, 1)
            p2m[f"p3bo{pr}"] = p.val(sevn)
            p.wait("sync", sevn)
            p.wait("sync", "sW0", p2m.get("p3fo0", 0))
            p.wait("sync", "sSc1", p2m.get("p3fo1", 0))
            def ftr(E, nf=nf, eb=eb):
                return E.dma_start(
                    out=V["o2t"][:, :].rearrange("p (cc f) -> p cc f",
                                                  cc=16),
                    in_=eb[:, :], transpose=True)
            p.do("sync", ftr, "sTr", 16)
            p2m[f"p3ot{pr}"] = p.val("sTr")
            oeng = "gpsimd" if pr == 0 else "scalar"
            p.wait(oeng, "sTr")
            def fo(E, nf=nf):
                return E.dma_start(
                    out=scr["out2t_scr"].rearrange(
                        "(cc p) f -> p cc f", p=128)[:, :,
                        nf * 128:(nf + 1) * 128],
                    in_=V["o2t"][:, :].rearrange("p (cc f) -> p cc f",
                                                  cc=16))
            p.do(oeng, fo, "sSc1" if pr else "sW0", 16)
            p2m[f"p3fo{pr}"] = p.val("sSc1" if pr else "sW0")

    def phase_p3b_final():
        V = p3bv()
        for c in range(16):
            par = c % 2
            at = V["attn_t"] if par == 0 else view("ar1", 0, 128, H, BF16)
            sr = V["srow"] if par == 0 else view("ar1", 8192, 128, H, F32)
            xt = V["x1t"] if par == 0 else view("ar1", 24576, 128, H, BF16)
            off = 9 if par == 0 else 13
            for eng in ("sync", "gpsimd", "scalar", "vector"):
                p.wait(eng, "sV", p2m.get(f"f3c{par}", 0))
            def fx1a(E, c=c, xt=xt):
                return E.dma_start(
                    out=xt[:, :],
                    in_=scr["x1t_scr"][c * 128:(c + 1) * 128, :])
            p.do("gpsimd", fx1a, "sW0", 16)
            def fo2a(E, c=c, at=at):
                return E.dma_start(
                    out=at[:, :],
                    in_=scr["out2t_scr"][c * 128:(c + 1) * 128, :])
            p.do("scalar", fo2a, "sX1", 16)
            p.wait("vector", "sW0")
            p.wait("vector", "sX1")
            p.wait("vector", "sOut", p2m.get(f"f3w{par}", 0))
            p.wait("vector", "sW1", p2m.get(f"f3wW{par}", 0))
            def fs1(E, at=at, sr=sr, xt=xt, off=off):
                return E.scalar_tensor_tensor(
                    out=sr[:, :], in0=at[:, :], scalar=1.0,
                    in1=xt[:, :], op0=ALU.mult, op1=ALU.add,
                    accum_out=sb["small"][:, off:off + 1])
            p.do("vector", fs1, "sV", 1)
            p.wait("vector", "sV")
            def fmn(E, off=off):
                return E.tensor_scalar_mul(sb["small"][:, off + 1:off + 2],
                                           sb["small"][:, off:off + 1],
                                           -1.0 / H)
            p.do("vector", fmn, "sV", 1)
            p.wait("vector", "sV")
            def fs3(E, sr=sr, off=off):
                return E.tensor_scalar(
                    out=sr[:, :], in0=sr[:, :],
                    scalar1=sb["small"][:, off + 1:off + 2], scalar2=None,
                    op0=ALU.add)
            p.do("vector", fs3, "sV", 1)
            p.wait("scalar", "sV")
            def fsq(E, at=at, sr=sr, off=off):
                return E.activation(out=at[:, :],
                                    in_=sr[:, :], func=AF.Square,
                                    accum_out=sb["small"][:, off + 2:off + 3])
            p.do("scalar", fsq, "sA", 1)
            p.wait("scalar", "sA")
            def fsd(E, off=off):
                return E.activation(out=sb["small"][:, off + 3:off + 4],
                                    in_=sb["small"][:, off + 2:off + 3],
                                    func=AF.Sqrt, scale=1.0 / H, bias=LNEPS)
            p.do("scalar", fsd, "sA", 1)
            p.wait("vector", "sA")
            def frv(E, off=off):
                return E.reciprocal(sb["small"][:, off + 3:off + 4],
                                    sb["small"][:, off + 3:off + 4])
            p.do("vector", frv, "sV", 1)
            p.wait("vector", "sV")
            def fyo2(E, sr=sr, off=off):
                return E.tensor_scalar_mul(sr[:, :], sr[:, :],
                                           sb["small"][:, off + 3:off + 4])
            p.do("vector", fyo2, "sV", 1)
            p2m[f"f3c{par}"] = p.val("sV")
            p.wait("sync", "sV")
            p.wait("gpsimd", "sV")
            p.wait("scalar", "sV")
            def fwa(E, c=c, sr=sr):
                return E.dma_start(
                    out=ap["yout"][c * 128:(c + 1) * 128, 0:2048],
                    in_=sr[:, 0:2048])
            p.do("sync", fwa, "sOut", 16)
            def fwb(E, c=c, sr=sr):
                return E.dma_start(
                    out=ap["yout"][c * 128:(c + 1) * 128, 2048:3072],
                    in_=sr[:, 2048:3072])
            p.do("gpsimd", fwb, "sW1", 16)
            def fwc(E, c=c, sr=sr):
                return E.dma_start(
                    out=ap["yout"][c * 128:(c + 1) * 128, 3072:4096],
                    in_=sr[:, 3072:4096])
            p.do("scalar", fwc, "sOut", 16)
            p2m[f"f3w{par}"] = p.val("sOut")
            p2m[f"f3wW{par}"] = p.val("sW1")
        p.wait("sync", "sOut")
        p.wait("sync", "sW1")

    def phase_debug():
        srcmap = {
            "zsil_scr": scr["zsil_scr"],
            "xbc_scr": scr["xbc_scr"],
            "y_scr": scr["y_scr"],
            "attn_scr": scr["attn_scr"],
            "x1f_scr": scr["x1f_scr"],
        }
        p.wait("sync", "sSc0")
        p.wait("sync", "sSc1")
        p.wait("sync", "sV")
        p.wait("sync", "sA")
        for nm in cfg["debug"]:
            if nm == "hS_dbg":
                def fd(E):
                    return E.dma_start(
                        out=ap["dbg_hS_dbg"],
                        in_=scr["win_S"].rearrange("a b p n -> (a b p) n"))
                p.do("sync", fd, "sOut", 16)
                continue
            if nm == "hin_dbg":
                def fd(E):
                    return E.dma_start(out=ap["dbg_hin_dbg"],
                                       in_=view("ar1", 49152, 128, 4096, F32))
                p.do("sync", fd, "sOut", 16)
                continue
            if nm == "dtp_dbg":
                def fd(E):
                    return E.dma_start(out=ap["dbg_dtp_dbg"],
                                       in_=scr["dtp_scr"][0:64, :])
            elif nm == "dtpr_dbg":
                def fd(E):
                    return E.dma_start(out=ap["dbg_dtpr_dbg"],
                                       in_=scr["dtp_scr"][64:128, :])
            else:
                def fd(E, nm=nm):
                    return E.dma_start(out=ap["dbg_" + nm], in_=srcmap[nm])
            p.do("sync", fd, "sOut", 16)

    def walk(eng_name, E):
        p.eng = eng_name
        p.E = E
        for k in list(p.cnt):
            p.cnt[k] = 0
        for k in list(inproj_marks):
            inproj_marks[k] = 0
        for k in list(p2m):
            if isinstance(p2m[k], list):
                p2m[k] = [0] * 8
            else:
                p2m[k] = 0
        p.poll_ctr = 0
        phase_init()
        barrier()
        ph = cfg["phases"]
        if "p1n_own" in ph:
            phase_p1_norm("own")
        if "p1i_own" in ph:
            phase_p1_inproj("own")
        if "p1n_red" in ph:
            barrier()
            phase_p1_norm("red")
        if "p1i_red" in ph:
            phase_p1_inproj("red")
            phase_halo_export()
        if "p2" in ph:
            barrier()
            write_flag(0)
            poll_flag(0)
            barrier()
            for seg in ("red", "own"):
                full = seg == "own"
                phase_p2_prep(seg)
                for g in range(4):
                    phase_p2_conv_group(seg, g)
                    for c in range(4 * g, 4 * g + 4):
                        phase_p2_chunk(seg, c, full)
                phase_p2_export(seg)
            phase_exchange()
        if "p3a" in ph:
            barrier()
            phase_p3a()
        if "p3b" in ph:
            barrier()
            phase_p3b_ln1()
            barrier()
            phase_p3b_ffn()
            barrier()
            phase_p3b_final()
        barrier()
        phase_debug()
        p.wait("sync", "sOut")

    with nc.Block() as block:
        @block.sync
        def _(E):
            walk("sync", E)

        @block.gpsimd
        def _(E):
            walk("gpsimd", E)

        @block.vector
        def _(E):
            walk("vector", E)

        @block.scalar
        def _(E):
            walk("scalar", E)

        @block.tensor
        def _(E):
            walk("tensor", E)


def _prep_base(inputs):
    w_in = _block_w(np.asarray(inputs["L0_attn_in_proj_w"], np.float32),
                    NK, NPROJ_T)
    w_out = _block_w(np.asarray(inputs["L0_attn_out_proj_w"], np.float32),
                     NK, 32)
    A_w = np.asarray(inputs["L0_ffn_A_w"], np.float32)
    B_w = np.asarray(inputs["L0_ffn_B_w"], np.float32)
    Wq = np.asarray(inputs["L0_ffn_Wq_w"], np.float32)
    keys = np.asarray(inputs["L0_ffn_keys"], np.float32)
    w_A = _block_w(np.ascontiguousarray(A_w.T), NK, 4)
    w_B = _block_w(np.ascontiguousarray(B_w.T), 4, 32)
    w_Q = np.ascontiguousarray(
        Wq.T.reshape(NK, 128, 16).transpose(1, 0, 2))
    mup = np.asarray(inputs["L0_attn_mup_vector"], np.float32)
    mup_pad = np.zeros(NPROJ_T * 128, np.float32)
    mup_pad[:PROJ] = mup
    mupc = np.ascontiguousarray(mup_pad.reshape(NPROJ_T, 128).T)
    ln_w = np.asarray(inputs["L0_attn_ln_w"], np.float32)
    nw = np.asarray(inputs["L0_attn_norm_w"], np.float32)
    A_log = np.asarray(inputs["L0_attn_A_log"], np.float32)
    return {
        "w_inproj": w_in,
        "w_outproj": w_out,
        "w_ffnA": w_A,
        "w_ffnB": w_B,
        "w_ffnQ": w_Q,
        "keysT": np.ascontiguousarray(keys.T),
        "convdiag": _conv_diag(np.asarray(inputs["L0_attn_conv_w"],
                                          np.float32)),
        "lnw_rep": np.ascontiguousarray(np.repeat(ln_w[None, :], 128, 0)),
        "nrmw_rep": np.ascontiguousarray(np.repeat(nw[None, :], 128, 0)),
        "nrmwc": np.ascontiguousarray(nw.reshape(NK, 128).T),
        "mupc": mupc,
        "dt_bias": np.asarray(inputs["L0_attn_dt_bias"],
                              np.float32)[:, None],
        "A_log": np.ascontiguousarray(A_log[:, None]),
        "Dvec": np.asarray(inputs["L0_attn_D"], np.float32)[:, None],
        "expandE": _expand_mat(),
        "ident": np.eye(128, dtype=np.float32),
        "hdrep": (np.arange(64)[:, None] ==
                  (2 * (np.arange(4096)[None, :] // 128) +
                   (np.arange(4096)[None, :] % 128) // 64))
        .astype(np.float32),
        "tri": np.triu(np.ones((128, 128), np.float32)),
        "Drep": np.ascontiguousarray(
            np.asarray(inputs["L0_attn_D"], np.float32)[
                (2 * np.arange(32)[None, :] +
                 (np.arange(128)[:, None] >= 64))]),
    }


def _per_core_maps(inputs, base):
    x = np.asarray(inputs["L0_attn_hidden_states"], dtype=np.float32)
    nonce = (np.random.rand(1, 16).astype(np.float32) + 1.0)
    in_maps = []
    for core in range(8):
        b = core // 4
        q = core % 4
        own = x[b, q * 2048:(q + 1) * 2048, :]
        if q == 2:
            red = x[b, 2048:4096, :]
        elif q == 3:
            red = x[b, 0:2048, :]
        else:
            red = own
        par = core % 2
        opp = 1 - par
        cfg_f = np.zeros((1, 64), np.float32)
        if q == 1:
            cfg_f[0, 16 + opp * 2 + 0] = 1
            cfg_f[0, 28 + opp * 2 + 0] = 1
            cfg_f[0, 6] = 1
        elif q == 2:
            cfg_f[0, 16 + par * 2 + 1] = 1
            cfg_f[0, 20 + opp * 2 + 1] = 1
            cfg_f[0, 24 + opp * 2 + 1] = 1
            cfg_f[0, 28 + par * 2 + 1] = 1
            cfg_f[0, 32 + par * 2 + 1] = 1
            cfg_f[0, 0] = 1
            cfg_f[0, 2] = 1
            cfg_f[0, 6] = 1
        elif q == 3:
            cfg_f[0, 16 + opp * 2 + 0] = 1
            cfg_f[0, 36 + opp * 3 + 2] = 1
            cfg_f[0, 42 + opp * 3 + 0] = 1
            cfg_f[0, 48 + opp * 3 + 0] = 1
            cfg_f[0, 3] = 1
            cfg_f[0, 5] = 1
        cfg_i = np.zeros((1, 64), np.int32)
        m = {k: v for k, v in base.items() if not k.endswith("_")}
        m["xin"] = np.ascontiguousarray(np.concatenate([own, red], axis=0))
        m["cfg_f"] = cfg_f
        m["cfg_i"] = cfg_i
        m["nonce"] = nonce
        in_maps.append(m)
    return in_maps


_PROG_CACHE = {}


def _get_prog():
    key = (tuple(CFG["phases"]), tuple(CFG["debug"]))
    if key not in _PROG_CACHE:
        _PROG_CACHE[key] = build_program(CFG)
    return _PROG_CACHE[key]


def run_cores(inputs):
    nc = _get_prog()
    base = _prep_base(inputs)
    in_maps = _per_core_maps(inputs, base)
    return run_bass_kernel_spmd(nc, in_maps, core_ids=list(range(8)))


def kernel(**inputs):
    inputs.pop("L0_attn_residual", None)
    res = run_cores(inputs)
    out = np.zeros((B, L, H), dtype=np.float32)
    for core in range(8):
        b = core // 4
        q = core % 4
        out[b, q * 2048:(q + 1) * 2048, :] = res.results[core]["yout"]
    return out


# version 05
# speedup vs baseline: 1.4412x; 1.0645x over previous
import numpy as np
import ml_dtypes
import concourse.bass as bass
import concourse.mybir as mybir
from concourse.bass_utils import run_bass_kernel_spmd

F32 = mybir.dt.float32
BF16 = mybir.dt.bfloat16
I32 = mybir.dt.int32
AF = mybir.ActivationFunctionType
ALU = mybir.AluOpType

H = 4096
DS = 4096
S = 128
NH = 64
HD = 64
KCONV = 4
CONV_DIM = DS + 2 * S
PROJ = 2 * DS + 2 * S + NH
NPROJ_T = 67
NK = 32
EPS = 1e-5
LNEPS = 1e-5

B = 2
L = 8192
TOWN = 2048
TRED = 2048
NCHUNK = TOWN // 128

NZT = 32
NXT = 34
FFN_RANK = 8
FFN_SCALE = 2.0

def _WS(pp, s):
    return (pp * 3 + s) * 128


def _WA(pp, s):
    return pp * 3 + s


def _WH(pp, s):
    return (pp * 2 + s) * 128


def _WF(pp, j):
    return pp * 4 + j


def _role_rows(q):
    par = q % 2
    opp = 1 - par
    ci = {
        "my_S_own": _WS(par, 0), "my_S_red": _WS(par, 1),
        "my_stage1": _WS(par, 2), "my_As_own": _WA(par, 0),
        "my_As_red": _WA(par, 1), "my_halo_own": _WH(par, 0),
        "my_halo_red": _WH(par, 1),
        "my_flag0": _WF(par, 0), "my_flag1": _WF(par, 1),
        "my_flag2": _WF(par, 2), "pa_flag0": _WF(opp, 0),
        "pa_flag1": _WF(opp, 1), "pa_flag2": _WF(opp, 2),
        "halo_src_own": _WH(par, 0), "halo_src_red": _WH(par, 1),
        "Sa1": _WS(opp, 0), "Sb1": _WS(opp, 0), "G1": _WA(par, 0),
        "Sa2": _WS(opp, 0), "Sb2": _WS(opp, 0), "G2": _WA(par, 0),
    }
    if q == 1:
        ci["halo_src_own"] = _WH(opp, 0)
    elif q == 2:
        ci["halo_src_own"] = _WH(par, 1)
        ci["halo_src_red"] = _WH(opp, 1)
        ci["Sa1"] = _WS(opp, 1)
        ci["G1"] = _WA(par, 1)
        ci["Sb1"] = _WS(par, 1)
    elif q == 3:
        ci["halo_src_own"] = _WH(opp, 0)
        ci["Sa2"] = _WS(opp, 2)
        ci["G2"] = _WA(opp, 0)
        ci["Sb2"] = _WS(opp, 0)
    return ci

CFG = {
    "phases": ("p1n_own", "p1i_own", "p1n_red", "p1i_red", "p2",
               "p3a", "p3b"),
    "debug": (),
}

_DBG_SHAPES = {
    "zsil_scr": ([DS, TOWN], BF16),
    "xbc_scr": ([CONV_DIM, TOWN + TRED], BF16),
    "dtp_dbg": ([NH, TOWN], F32),
    "dtpr_dbg": ([NH, TRED], F32),
    "y_scr": ([DS, TOWN], BF16),
    "attn_scr": ([H, TOWN], BF16),
    "x1f_scr": ([H, TOWN], BF16),
    "hS_dbg": ([6 * 128, NH * HD], F32),
    "hin_dbg": ([128, NH * HD], F32),
}


def _block_w(w, nk, nn):
    K, N = w.shape
    assert K == nk * 128
    wp = np.zeros((K, nn * 128), dtype=np.float32)
    wp[:, :N] = w
    wb = wp.reshape(nk, 128, nn, 128).transpose(2, 1, 0, 3)
    return np.ascontiguousarray(wb)


def _conv_diag(conv_w):
    out = np.zeros((NXT, KCONV, 128, 128), dtype=np.float32)
    idx = np.arange(128)
    for fb in range(NXT):
        for k in range(KCONV):
            out[fb, k, idx, idx] = conv_w[fb * 128:(fb + 1) * 128, k]
    return np.ascontiguousarray(out.transpose(2, 0, 1, 3))


def _expand_mat():
    E = np.zeros((64, 4, 128), dtype=np.float32)
    for j in range(4):
        for m in range(128):
            E[(j * 128 + m) // FFN_RANK, j, m] = 1.0
    return E


class P:
    def __init__(self, nc):
        self.nc = nc
        self.cnt = {}
        self.sems = {}
        self.eng = None
        self.E = None
        self.dyn = {}
        self.poll_ctr = 0
        self.pbody = 0
        self.noncereg_val = None
        self._pending = []

    def inc(self, name, by):
        self.cnt[name] = self.cnt.get(name, 0) + by
        return self.cnt[name]

    def val(self, name):
        return self.cnt.get(name, 0)

    def do(self, eng, fn, sem=None, by=None):
        inst = None
        if self.eng == eng:
            inst = fn(self.E)
            if self._pending:
                for r in self._pending:
                    self.E.free_register(r)
                self._pending = []
        if sem is not None:
            self.inc(sem, by)
            if inst is not None:
                inst.then_inc(self.sems[sem], by)
        return inst

    def roled(self, eng, build_fn, sem=None, by=None):
        if self.eng == eng:
            for role in self.E.Switch(self.role_rv, 4):
                inst = build_fn(self.E, _role_rows(role))
                if sem is not None and inst is not None:
                    inst.then_inc(self.sems[sem], by)
        if sem is not None:
            self.inc(sem, by)

    def wait(self, eng, sem, val=None):
        if val is None:
            val = self.val(sem)
        if val <= 0:
            return
        if self.eng == eng:
            self.E.wait_ge(self.sems[sem], val)


def build_program(cfg):
    nc = bass.Bass("TRN2", target_bir_lowering=False, debug=False)
    NTOK = TOWN + TRED

    ap = {}

    def ein(name, shape, dt=F32):
        ap[name] = nc.dram_tensor(name, shape, dt, kind="ExternalInput").ap()

    def eout(name, shape, dt=F32):
        ap[name] = nc.dram_tensor(name, shape, dt, kind="ExternalOutput").ap()

    ein("xin", [NTOK, H])
    ein("w_inproj", [NPROJ_T, 128, NK, 128])
    ein("w_outproj", [32, 128, NK, 128])
    ein("w_ffnA", [4, 128, NK, 128])
    ein("w_ffnB", [32, 128, 4, 128])
    ein("w_ffnQ", [128, NK, 16])
    ein("keysT", [16, 64])
    ein("convdiag", [128, NXT, KCONV, 128])
    ein("lnw_rep", [128, H])
    ein("nrmw_rep", [128, H])
    ein("nrmwc", [128, NK])
    ein("mupc", [128, NPROJ_T])
    ein("dt_bias", [NH, 1])
    ein("A_log", [NH, 1])
    ein("Dvec", [NH, 1])
    ein("expandE", [64, 4, 128])
    ein("ident", [128, 128])
    ein("tri", [128, 128])
    ein("Drep", [128, 32])
    ein("cfg_f", [1, 64])
    ein("cfg_i", [1, 64], I32)
    ein("nonce", [1, 16])
    ein("hdrep", [64, 4096])

    eout("yout", [TOWN, H])
    for nm in cfg["debug"]:
        shp, dt = _DBG_SHAPES[nm]
        eout("dbg_" + nm, shp, dt)

    scr = {}

    def sloc(name, shape, dt=BF16):
        scr[name] = nc.dram_tensor(name, shape, dt).ap()

    sloc("zsil_scr", [DS, TOWN])
    sloc("dtp_scr", [128, 2048], F32)
    sloc("acum_scr", [128, 2048], F32)
    sloc("xbc_scr", [CONV_DIM, NTOK])
    sloc("y_scr", [DS, TOWN])
    sloc("attn_scr", [H, TOWN])
    sloc("x1f_scr", [H, TOWN])
    sloc("x1t_scr", [TOWN, H])
    sloc("out2t_scr", [TOWN, H])

    def sshr(name, shape, dt=F32):
        scr[name] = nc.dram_tensor(name, shape, dt, addr_space="Shared").ap()

    sshr("win_S", [2, 3, S, NH * HD])
    sshr("win_As", [2, 3, NH])
    scr["win_halo"] = nc.dram_tensor("win_halo", [2, 2, 128, NXT * 4], BF16, addr_space="Shared").ap()
    sshr("win_flag", [2, 4, 16])

    _emit(nc, ap, scr, cfg)
    return nc


def _emit(nc, ap, scr, cfg):
    import contextlib
    est = contextlib.ExitStack()

    p = P(nc)
    for s in ["sX0", "sX1", "sW0", "sW1", "sTr", "sSc0", "sSc1", "sScR",
              "sWin", "sV", "sA", "sP", "sG", "sPoll", "sEv0", "sEv1",
              "sOut"]:
        p.sems[s] = est.enter_context(nc.semaphore(s))

    sb = {}

    def sbuf(name, shape, dt):
        sb[name] = est.enter_context(nc.sbuf_tensor("sb_" + name, shape, dt))

    ps = {}

    def psum(name, shape, dt=F32):
        ps[name] = est.enter_context(nc.psum_tensor(name, shape, dt))

    sbuf("ar1", [128, 131072], mybir.dt.int8)
    sbuf("ar2", [128, 40960], mybir.dt.int8)
    sbuf("wbuf0", [128, NK * 128], BF16)
    sbuf("wbuf1", [128, NK * 128], BF16)
    sbuf("evst0", [128, 2048], BF16)
    sbuf("evst1", [128, 2048], BF16)
    sbuf("wrep", [128, H], BF16)
    sbuf("mupc", [128, NPROJ_T], F32)
    sbuf("consts", [128, 8], F32)
    sbuf("dtbias", [NH, 1], F32)
    sbuf("negA", [NH, 1], F32)
    sbuf("dvec", [NH, 1], F32)
    sbuf("small", [128, 256], F32)
    sbuf("identf", [128, 128], F32)
    sbuf("identb", [128, 128], BF16)
    sbuf("trib", [128, 128], BF16)
    sbuf("Drep", [128, 32], F32)
    sbuf("onesf", [1, 128], F32)
    sbuf("masks", [128, 64], F32)
    sbuf("noncebuf", [1, 16], F32)
    sbuf("cfgis", [1, 64], I32)
    sbuf("cfgfs", [1, 64], F32)
    sbuf("offsi", [128, 16], F32)
    sbuf("aclT", [128, 64], F32)
    sbuf("wT", [128, 64], F32)
    sbuf("B_T", [128, 128], BF16)
    sbuf("GT_sb", [128, 128], BF16)
    sbuf("Gexp", [128, 64], F32)
    sbuf("grow", [1, 64], F32)
    sbuf("halost", [128, 272], BF16)
    sbuf("halos4", [128, 544], BF16)
    sbuf("asums", [64, 16], F32)
    sbuf("nrmwc", [128, NK], F32)
    sbuf("onesb", [128, 1], BF16)
    sbuf("asumsT", [16, 64], F32)
    sbuf("flagrd", [1, 32], F32)

    _esz = {F32: 4, BF16: 2, I32: 4}

    def view(arena, off, parts, free, dt):
        return sb[arena][0:parts, off:off + free * _esz[dt]].bitcast(dt)

    psum("acc0", [128, 2048])
    psum("acc1", [128, 2048])

    p.sb_cfgis = sb["cfgis"]

    CONSTS = {}
    for i, v in enumerate((1.0, -1.0, float(EPS), float(LNEPS), 1.0 / H,
                           80.0, FFN_SCALE, 0.0)):
        CONSTS[v] = i
        nc.const_aps.register(F32, v, sb["consts"][:, i:i + 1])

    ALL_ENGS = ("sync", "gpsimd", "vector", "scalar", "tensor")

    def barrier():
        if cfg.get("snap") is not None:
            cfg["snap"].setdefault(p.eng, []).append(dict(p.cnt))
        for e in ALL_ENGS:
            for s in ("sV", "sA", "sP", "sEv0", "sEv1", "sSc0", "sSc1",
                      "sX0", "sX1", "sTr", "sW0", "sW1", "sG", "sWin",
                      "sOut"):
                p.wait(e, s)

    def phase_init():
        for v, i in CONSTS.items():
            def f(E, v=v, i=i):
                return E.memset(sb["consts"][:, i:i + 1], v)
            p.do("vector", f, "sV", 1)

        for nm_src, nm_dst in (("mupc", "mupc"), ("dt_bias", "dtbias"),
                               ("A_log", "negA"), ("Dvec", "dvec")):
            def f2(E, nm_src=nm_src, nm_dst=nm_dst):
                return E.dma_start(out=sb[nm_dst][:, :], in_=ap[nm_src])
            p.do("sync", f2, "sX0", 16)

        def f2b(E):
            return E.dma_start(out=sb["wrep"][:, :], in_=ap["lnw_rep"])
        p.do("gpsimd", f2b, "sW0", 16)
        for nm_src, nm_dst in (("ident", "identf"), ("Drep", "Drep"),
                               ("cfg_f", "cfgfs"), ("cfg_i", "cfgis"),
                               ("nonce", "noncebuf")):
            def f2c(E, nm_src=nm_src, nm_dst=nm_dst):
                return E.dma_start(out=sb[nm_dst][:, :], in_=ap[nm_src])
            p.do("sync", f2c, "sX0", 16)
        def f2cb(E):
            return E.dma_start(out=sb["identb"][:, :], in_=ap["ident"])
        p.do("gpsimd", f2cb, "sW1", 16)
        def f2cc(E):
            return E.dma_start(out=sb["trib"][:, :], in_=ap["tri"])
        p.do("gpsimd", f2cc, "sW1", 16)
        p.wait("vector", "sX0")
        p.wait("vector", "sW1")
        def f2d(E):
            E.memset(sb["halost"][:, :], 0.0)
            E.memset(sb["onesb"][:, :], 1.0)
            return E.memset(sb["onesf"][:, :], 1.0)
        p.do("vector", f2d, "sV", 1)
        p.wait("tensor", "sV")
        p.wait("tensor", "sX0")
        def f2e(E):
            return E.matmul(out=ps["acc0"][:, 0:64], lhsT=sb["onesf"][:, :],
                            rhs=sb["cfgfs"][0:1, 0:64], start=True, stop=True)
        p.do("tensor", f2e, "sP", 1)
        p.wait("scalar", "sP")
        def f2f(E):
            return E.activation(out=sb["masks"][:, :],
                                in_=ps["acc0"][:, 0:64], func=AF.Copy)
        p.do("scalar", f2f, "sEv0", 1)
        if p.eng == "sync":
            p.E.wait_ge(p.sems["sX0"], p.val("sX0"))
            nreg = p.E.alloc_register("nonceval")
            p.E.reg_load(nreg, sb["noncebuf"][0:1, 0:1].bitcast(I32))
            p.noncereg_val = p.E.snap(nreg)
            rreg = p.E.alloc_register("roleidx")
            pid = p.E.partition_id()
            p.E.reg_alu(rreg, pid, 4, ALU.mod)
            p.role_rv = p.E.snap(rreg)
        p.wait("scalar", "sV")
        p.wait("scalar", "sX0")
        def fneg(E):
            return E.activation(out=sb["negA"][:, :], in_=sb["negA"][:, :],
                                func=AF.Exp)
        p.do("scalar", fneg, "sA", 1)
        p.wait("vector", "sA")
        def fneg2(E):
            return E.tensor_scalar_mul(sb["negA"][:, :], sb["negA"][:, :],
                                       -1.0)
        p.do("vector", fneg2, "sV", 1)

    def phase_p1_norm(seg):
        T = TOWN if seg == "own" else TRED
        base = 0 if seg == "own" else TOWN
        nch = T // 128
        xnorm_f = view("ar1", 0, 128, NK * 2048, BF16)
        xc = [view("ar2", 0, 128, H, F32), view("ar2", 16384, 128, H, F32)]
        xnts = [view("ar2", 32768, 128, H, BF16), sb["wbuf1"][:, :]]
        marks = {"bx0": 0, "bx1": 0, "rs0": 0, "rs1": 0, "tr0": 0, "tr1": 0}
        for i in range(nch):
            pr = i % 2
            bx = xc[pr]
            xnt = xnts[pr]
            rstd = sb["small"][:, pr:pr + 1]
            ssq = sb["small"][:, 2 + pr:3 + pr]
            p.wait("sync", "sV", marks[f"bx{pr}"])

            def fl(E, i=i, bx=bx):
                return E.dma_start(
                    out=bx,
                    in_=ap["xin"][base + i * 128: base + (i + 1) * 128, :])
            p.do("sync", fl, f"sX{pr}", 16)
            p.wait("scalar", f"sX{pr}")
            p.wait("scalar", "sV", marks[f"rs{pr}"])
            p.wait("scalar", "sTr", marks[f"tr{pr}"])

            def fs(E, bx=bx, ssq=ssq, xnt=xnt):
                return E.activation(out=xnt, in_=bx,
                                    func=AF.Square, accum_out=ssq)
            p.do("scalar", fs, "sA", 1)
            p.wait("scalar", "sA")

            def fsq(E, ssq=ssq, rstd=rstd):
                return E.activation(out=rstd, in_=ssq,
                                    func=AF.Sqrt, scale=1.0 / H, bias=EPS)
            p.do("scalar", fsq, "sA", 1)
            p.wait("vector", "sA")

            def fr(E, rstd=rstd):
                return E.reciprocal(rstd, rstd)
            p.do("vector", fr, "sV", 1)
            p.wait("vector", "sV")
            p.wait("vector", "sTr", marks[f"tr{pr}"])

            def fn(E, bx=bx, rstd=rstd, xnt=xnt):
                return E.scalar_tensor_tensor(
                    out=xnt, in0=bx, scalar=rstd,
                    in1=sb["wrep"][:, :], op0=ALU.mult, op1=ALU.mult)
            p.do("vector", fn, "sV", 1)
            marks[f"bx{pr}"] = p.val("sV")
            marks[f"rs{pr}"] = p.val("sV")
            p.wait("sync", "sV")

            def ft(E, i=i, xnt=xnt):
                outap = xnorm_f.rearrange(
                    "p (k t) -> p k t", k=NK)[:, :, i * 128:(i + 1) * 128]
                return E.dma_start(out=outap, in_=xnt, transpose=True)
            p.do("sync", ft, "sTr", 16)
            marks[f"tr{pr}"] = p.val("sTr")
        p.wait("tensor", "sTr")
        inproj_marks["trguard"] = p.val("sTr")

    inproj_marks = {"w0": 0, "w1": 0, "p0": 0, "p1": 0, "e0": 0, "e1": 0,
                    "halo_w": 0}

    def phase_halo_export():
        p.wait("sync", "sV")
        def fh(E, rows):
            ro = rows["my_halo_own"]
            rr = rows["my_halo_red"]
            E.dma_start(
                out=scr["win_halo"].rearrange("a b p c -> (a b p) c")[
                    ro:ro + 128, :],
                in_=sb["halost"][:, 0:136]).then_inc(p.sems["sWin"], 16)
            return E.dma_start(
                out=scr["win_halo"].rearrange("a b p c -> (a b p) c")[
                    rr:rr + 128, :],
                in_=sb["halost"][:, 136:272])
        p.roled("sync", fh, "sWin", 16)
        p.inc("sWin", 16)
        inproj_marks["halo_w"] = p.val("sWin")
        p.wait("sync", "sWin")

    def phase_p1_inproj(seg):
        T = TOWN if seg == "own" else TRED
        base = 0 if seg == "own" else TOWN
        ntiles = list(range(NPROJ_T)) if seg == "own" \
            else list(range(NZT, NPROJ_T))
        ntt = T // 512
        mk = inproj_marks
        xnorm_f = view("ar1", 0, 128, NK * 2048, BF16)
        dtt1 = view("ar2", 0, 64, 2048, F32)
        dtt2 = view("ar2", 16384, 64, 2048, F32)
        dtrow = slice(0, 64) if seg == "own" else slice(64, 128)
        for j, n in enumerate(ntiles):
            pr = j % 2
            wb = sb["wbuf0"] if pr == 0 else sb["wbuf1"]
            pb = ps["acc0"] if pr == 0 else ps["acc1"]
            eb = sb["evst0"] if pr == 0 else sb["evst1"]
            p.wait("gpsimd", "sP", mk[f"w{pr}"])
            if j == 1:
                p.wait("gpsimd", "sTr", mk.get("trguard", 0))

            def fw(E, n=n, wb=wb):
                return E.dma_start(
                    out=wb[:, :].rearrange("p (k c) -> p k c", k=NK),
                    in_=ap["w_inproj"][n, :, :, :])
            p.do("gpsimd", fw, f"sW{pr}", 16)

            p.wait("tensor", f"sW{pr}")
            p.wait("tensor", f"sEv{pr}", mk[f"p{pr}"])

            def fm(E, wb=wb, pb=pb, ntt=ntt):
                last = None
                for k in range(NK):
                    lhsT = wb[:, :].rearrange(
                        "p (k c) -> p k c", k=NK)[:, k, :]
                    for t in range(ntt):
                        last = E.matmul(
                            out=pb[:, t * 512:(t + 1) * 512],
                            lhsT=lhsT,
                            rhs=xnorm_f.rearrange(
                                "p (k t) -> p k t",
                                k=NK)[:, k, t * 512:(t + 1) * 512],
                            start=(k == 0), stop=(k == NK - 1))
                return last
            p.do("tensor", fm, "sP", 1)
            mk[f"w{pr}"] = p.val("sP")

            if n < NZT:
                p.wait("scalar", "sP")
                p.wait("scalar", f"sSc{pr}", mk[f"e{pr}"])

                def fe(E, n=n, pb=pb, eb=eb, T=T):
                    return E.activation(
                        out=eb[:, 0:T], in_=pb[:, 0:T],
                        func=AF.Silu, scale=sb["mupc"][:, n:n + 1])
                p.do("scalar", fe, f"sEv{pr}", 1)
                mk[f"p{pr}"] = p.val(f"sEv{pr}")
                p.wait("sync", f"sEv{pr}")

                def fo(E, n=n, eb=eb, T=T, base=base):
                    return E.dma_start(
                        out=scr["zsil_scr"][n * 128:(n + 1) * 128,
                                            base:base + T],
                        in_=eb[:, 0:T])
                p.do("sync", fo, f"sSc{pr}", 16)
                mk[f"e{pr}"] = p.val(f"sSc{pr}")
            elif n < NZT + NXT:
                p.wait("vector", "sP")
                p.wait("vector", f"sSc{pr}", mk[f"e{pr}"])

                def fe(E, n=n, pb=pb, eb=eb, T=T):
                    return E.tensor_scalar_mul(
                        eb[:, 0:T], pb[:, 0:T], sb["mupc"][:, n:n + 1])
                p.do("vector", fe, f"sEv{pr}", 1)
                mk[f"p{pr}"] = p.val(f"sEv{pr}")
                p.wait("vector", f"sEv{pr}")
                p.wait("vector", "sWin", mk.get("halo_w", 0))
                hoff = 0 if seg == "own" else 136
                def fhc(E, n=n, eb=eb, T=T, hoff=hoff):
                    fb = n - NZT
                    return E.tensor_copy(
                        sb["halost"][:, hoff + fb * 4:hoff + fb * 4 + 3],
                        eb[:, T - 3:T])
                p.do("vector", fhc, "sV", 1)
                p.wait("sync", f"sEv{pr}")

                def fo(E, n=n, eb=eb, T=T, base=base):
                    fb = n - NZT
                    return E.dma_start(
                        out=scr["xbc_scr"][fb * 128:(fb + 1) * 128,
                                           base:base + T],
                        in_=eb[:, 0:T])
                p.do("sync", fo, f"sSc{pr}", 16)
                mk[f"e{pr}"] = p.val(f"sSc{pr}")
            else:
                p.wait("vector", "sP")

                def fe(E, pb=pb, T=T, dtt1=dtt1):
                    return E.tensor_scalar(
                        out=dtt1[:, 0:T], in0=pb[0:NH, 0:T],
                        scalar1=sb["mupc"][0:NH, NPROJ_T - 1:NPROJ_T],
                        scalar2=sb["dtbias"][:, :],
                        op0=ALU.mult, op1=ALU.add)
                p.do("vector", fe, f"sEv{pr}", 1)
                mk[f"p{pr}"] = p.val(f"sEv{pr}")
                p.wait("scalar", f"sEv{pr}")

                def fa1(E, T=T, dtt1=dtt1, dtt2=dtt2):
                    return E.activation(out=dtt2[:, 0:T],
                                        in_=dtt1[:, 0:T], func=AF.Abs)
                p.do("scalar", fa1, "sA", 1)
                p.wait("scalar", "sA")

                def fa2(E, T=T, dtt2=dtt2):
                    return E.activation(out=dtt2[:, 0:T], in_=dtt2[:, 0:T],
                                        func=AF.Exp, scale=-1.0)
                p.do("scalar", fa2, "sA", 1)
                p.wait("scalar", "sA")

                def fa3(E, T=T, dtt2=dtt2):
                    return E.activation(out=dtt2[:, 0:T], in_=dtt2[:, 0:T],
                                        func=AF.Ln, bias=1.0)
                p.do("scalar", fa3, "sA", 1)
                p.wait("vector", "sA")

                def fv1(E, T=T, dtt1=dtt1):
                    return E.tensor_relu(dtt1[:, 0:T], dtt1[:, 0:T])
                p.do("vector", fv1, "sV", 1)
                p.wait("vector", "sV")

                def fv2(E, T=T, dtt1=dtt1, dtt2=dtt2):
                    return E.tensor_add(dtt1[:, 0:T], dtt1[:, 0:T],
                                        dtt2[:, 0:T])
                p.do("vector", fv2, "sV", 1)
                p.wait("sync", "sV")

                def fdt(E, T=T, dtt1=dtt1, dtrow=dtrow):
                    return E.dma_start(out=scr["dtp_scr"][dtrow, 0:T],
                                       in_=dtt1[:, 0:T])
                p.do("sync", fdt, "sG", 16)


    def PV(t, a, b):
        return ps[t][:, a:b]

    def PVb(t, a, b):
        return ps[t][:, a:b].bitcast(BF16)

    def p2v():
        return {
            "xbcg": view("ar1", 0, 128, 34 * 516, BF16),
            "xsg": view("ar1", 35328, 128, 34 * 512, BF16),
            "S_sb": view("ar1", 70144, 128, 8192, BF16),
            "Ce_sb": view("ar1", 86528, 128, 8192, BF16),
            "cdg": view("ar1", 70144, 128, 34 * 4 * 128, BF16),
            "y_grp": view("ar1", 104960, 128, 32 * 128, BF16),
            "h_bf": view("ar1", 113152, 128, 4096, BF16),
            "acumP": view("ar1", 121344, 128, 2048, F32),
            "h": view("ar2", 0, 128, 4096, F32),
            "dtv": view("ar2", 16384, 64, 2048, F32),
            "xs_T": view("ar2", 16384, 128, 4096, BF16),
            "amlw": view("ar2", 24576, 128, 2048, F32),
            "acrow": view("ar2", 32768, 1, 2048, F32),
            "xsw_T": sb["wbuf0"][:, :].bitcast(BF16),
            "argb": sb["wbuf1"][:, 0:1024].bitcast(F32),
            "th0": sb["evst0"][:, 0:1024].bitcast(F32),
            "th1": sb["evst0"][:, 1024:2048].bitcast(F32),
            "hwA": view("ar1", 0, 128, 4096, F32),
            "hwB": view("ar1", 16384, 128, 4096, F32),
            "hst": view("ar1", 32768, 128, 4096, F32),
            "hin": view("ar1", 49152, 128, 4096, F32),
            "G1e": view("ar1", 65536, 128, 64, F32),
            "SaS": view("ar1", 66048, 128, 4096, F32),
            "SbS": view("ar1", 82432, 128, 4096, F32),
        }

    p2m = {"pb0": 0, "pb1": 0, "py0": 0, "py1": 0, "pc0": 0, "pc1": 0,
           "cv0": 0, "cv1": 0, "ptr": 0, "yg": 0, "acrow": 0, "sce": 0,
           "xbcg": 0, "hbf": 0, "hupd": [0] * 8, "th0": 0, "th1": 0,
           "xsTc": 0, "hread": [0] * 8, "exw0": 0, "exw1": 0,
           "p3acg": 0, "p3xb": 0, "p3cc": 0, "p3yz": 0, "p3ac": 0,
           "p3b0": 0, "p3b1": 0, "p3ce": 0, "p3y0": 0, "p3y1": 0,
           "p3rs": 0, "p3sq": 0, "p3rb": 0, "p3w0": 0, "p3w1": 0,
           "p3e0": 0, "p3e1": 0, "p3o0": 0, "p3o1": 0, "p3battn": 0, "p3yzE": 0, "p3yzV": 0,
           "p3g": 0, "p3za": 0, "p3zv": 0, "p3wb": 0, "p3bo0": 0,
           "p3bo1": 0, "p3ot0": 0, "p3ot1": 0, "p3f": 0}

    def rows_of(seg):
        return slice(0, 64)

    def phase_p2_prep(seg):
        V = p2v()
        ro = rows_of(seg)
        dtrow = slice(0, 64) if seg == "own" else slice(64, 128)
        def fl(E, dtrow=dtrow):
            return E.dma_start(out=V["dtv"][:, :],
                               in_=scr["dtp_scr"][dtrow, :])
        p.do("sync", fl, "sX0", 16)
        p.wait("vector", "sX0")
        def fadot(E):
            return E.tensor_scalar_mul(V["amlw"][0:64, :], V["dtv"][:, :],
                                       sb["negA"][:, :])
        p.do("vector", fadot, "sV", 1)
        p.wait("vector", "sV")
        for c in range(16):
            def fc(E, c=c):
                sl = slice(c * 128, (c + 1) * 128)
                return E.tensor_tensor_scan(
                    out=view("ar1", 121344, 128, 2048, F32)[ro, sl], data0=V["amlw"][0:64, sl],
                    data1=V["amlw"][0:64, sl], initial=0.0,
                    op0=ALU.add, op1=ALU.bypass)
            p.do("vector", fc, "sV", 1)
            p.wait("vector", "sV")
        def fg(E):
            return E.tensor_copy(
                sb["asums"][:, :],
                view("ar1", 121344, 128, 2048, F32)[ro, :].rearrange("p (c t) -> p c t",
                                             c=16)[:, :, 127])
        p.do("vector", fg, "sV", 1)
        p.wait("vector", "sV")
        def fz(E):
            return E.memset(sb["offsi"][ro, 0:1], 0.0)
        p.do("vector", fz, "sV", 1)
        p.wait("vector", "sV")
        def fsc(E):
            return E.tensor_tensor_scan(
                out=sb["offsi"][ro, 1:16],
                data0=sb["asums"][:, 0:15], data1=sb["asums"][:, 0:15],
                initial=0.0, op0=ALU.add, op1=ALU.bypass)
        p.do("vector", fsc, "sV", 1)
        ptr = PV("acc1", 1024, 1536)
        p.wait("tensor", "sV")
        p.wait("tensor", "sEv1", p2m["ptr"])
        def ftA(E):
            return E.transpose(out=ptr[0:16, 0:64], in_=sb["asums"][:, :],
                               identity=sb["identf"][0:64, 0:64])
        p.do("tensor", ftA, "sP", 1)
        p.wait("vector", "sP")
        def feA(E):
            return E.tensor_copy(sb["asumsT"][:, :], ptr[0:16, 0:64])
        p.do("vector", feA, "sEv1", 1)
        p2m["ptr"] = p.val("sEv1")
        p.wait("scalar", "sV")
        p.wait("scalar", "sX0")
        p.wait("scalar", "sEv1")
        def fln(E):
            return E.activation(out=V["amlw"][0:64, :], in_=V["dtv"][:, :],
                                func=AF.Ln)
        p.do("scalar", fln, "sA", 1)
        p.wait("vector", "sA")
        def faml(E):
            return E.tensor_sub(V["amlw"][0:64, :],
                                view("ar1", 121344, 128, 2048, F32)[ro, :],
                                V["amlw"][0:64, :])
        p.do("vector", faml, "sV", 1)
        p.wait("vector", "sV")
        for c in range(16):
            sl = slice(c * 128, (c + 1) * 128)
            def fw1(E, sl=sl, c=c):
                return E.tensor_scalar(
                    out=V["amlw"][64:128, sl], in0=V["amlw"][0:64, sl],
                    scalar1=sb["asums"][:, c:c + 1], scalar2=None,
                    op0=ALU.subtract)
            p.do("vector", fw1, "sV", 1)
        p.wait("scalar", "sV")
        def fw2(E):
            return E.activation(out=V["amlw"][64:128, :],
                                in_=V["amlw"][64:128, :], func=AF.Exp,
                                scale=-1.0)
        p.do("scalar", fw2, "sA", 1)
        p.wait("vector", "sA")
        p.wait("vector", "sWin")
        def fh(E):
            E.memset(p2v()["h"][:, :], 0.0)
            return E.memset(p2v()["h_bf"][:, :], 0.0)
        p.do("vector", fh, "sV", 1)

    def phase_p2_conv_group(seg, g):
        V = p2v()
        base = 0 if seg == "own" else TOWN
        xsgv3 = V["xsg"][:, :].rearrange("p (f t) -> p f t", f=NXT)
        xbcg3 = V["xbcg"][:, :].rearrange("p (f t) -> p f t", f=NXT)
        p.wait("gpsimd", "sP", p2m["sce"])
        p.wait("gpsimd", "sSc0", p2m["yg"])
        def fcd(E):
            return E.dma_start(
                out=V["cdg"][:, :].rearrange("p (f k c) -> p f k c",
                                             f=NXT, k=KCONV),
                in_=ap["convdiag"])
        p.do("gpsimd", fcd, "sW0", 16)
        p.wait("sync", "sP", p2m["xbcg"])
        if g == 0:
            def fx(E, base=base):
                return E.dma_start(
                    out=xbcg3[:, :, 3:515],
                    in_=scr["xbc_scr"].rearrange(
                        "(f p) t -> p f t", p=128)[:, :, base:base + 512])
            p.do("sync", fx, "sX1", 16)
            if seg == "red" and g == 0:
                for kslot in range(4):
                    def fhl(E, kslot=kslot):
                        return E.dma_start(
                            out=sb["halos4"][:, kslot * 136:
                                             (kslot + 1) * 136],
                            in_=scr["win_halo"].rearrange(
                                "a b p c -> (a b p) c")[
                                kslot * 128:(kslot + 1) * 128, :])
                    p.do("sync", fhl, "sX0", 16)
            p.wait("vector", "sX0")
            p.wait("vector", "sX1")
            mbase = 16 if seg == "own" else 20
            def fhz(E):
                return E.memset(xbcg3[:, :, 0:3], 0.0)
            p.do("vector", fhz, "sV", 1)
            p.wait("vector", "sV")
            for kslot in range(4):
                def fhm(E, mbase=mbase, kslot=kslot):
                    return E.scalar_tensor_tensor(
                        out=xbcg3[:, :, 0:3],
                        in0=sb["halos4"][:, kslot * 136:(kslot + 1) * 136]
                        .rearrange("p (f t) -> p f t", f=NXT)[:, :, 0:3],
                        scalar=sb["masks"][:, mbase + kslot:
                                           mbase + kslot + 1],
                        in1=xbcg3[:, :, 0:3], op0=ALU.mult, op1=ALU.add)
                p.do("vector", fhm, "sV", 1)
                p.wait("vector", "sV")
            p.wait("tensor", "sV")
        else:
            def fx(E, base=base, g=g):
                return E.dma_start(
                    out=xbcg3[:, :, 0:515],
                    in_=scr["xbc_scr"].rearrange(
                        "(f p) t -> p f t", p=128)[:, :,
                        base + g * 512 - 3:base + (g + 1) * 512])
            p.do("sync", fx, "sX1", 16)
        p.wait("tensor", "sX1")
        p.wait("tensor", "sW0")
        p.wait("scalar", "sV")
        p.wait("scalar", "sEv1")
        for fb in range(NXT):
            pcv = PV("acc1", 0, 512) if fb % 2 == 0 else PV("acc1", 512, 1024)
            prk = "cv0" if fb % 2 == 0 else "cv1"
            p.wait("tensor", "sEv0", p2m[prk])
            def fm(E, fb=fb, pcv=pcv):
                last = None
                for k in range(KCONV):
                    last = E.matmul(
                        out=pcv,
                        lhsT=V["cdg"][:, :].rearrange(
                            "p (f k c) -> p f k c", f=NXT,
                            k=KCONV)[:, fb, k, :],
                        rhs=xbcg3[:, fb, k:k + 512],
                        start=(k == 0), stop=(k == KCONV - 1))
                return last
            p.do("tensor", fm, "sP", 1)
            p.wait("scalar", "sP")
            def fs(E, fb=fb, pcv=pcv):
                return E.activation(out=xsgv3[:, fb, :], in_=pcv,
                                    func=AF.Silu)
            p.do("scalar", fs, "sEv0", 1)
            p2m[prk] = p.val("sEv0")
        p2m["xbcg"] = p.val("sP")
        p.wait("tensor", "sEv0")

    def phase_p2_chunk(seg, c, full):
        V = p2v()
        ro = rows_of(seg)
        rbase = 0
        gl = c % 4
        xsgv = V["xsg"][:, :].rearrange("p (f t) -> p f t", f=NXT)
        ptr = PV("acc1", 1024, 1536)
        ptrb = PVb("acc1", 1536, 2048)

        def pe_transpose_to(dst_sb, src_ap, bf, width, ibase=0):
            p.wait("tensor", "sEv1", p2m["ptr"])
            pdst = ptrb if bf else ptr
            ident = sb["identb"] if bf else sb["identf"]
            iw = src_ap.shape[0]
            def ft(E, pdst=pdst, src_ap=src_ap, ident=ident, iw=iw,
                   width=width, ibase=ibase):
                return E.transpose(
                    out=pdst[:, 0:width], in_=src_ap,
                    identity=ident[ibase:ibase + iw, ibase:ibase + iw])
            p.do("tensor", ft, "sP", 1)
            p.wait("vector", "sP")
            def fe(E, dst_sb=dst_sb, pdst=pdst, width=width):
                return E.tensor_copy(dst_sb, pdst[:, 0:width])
            p.do("vector", fe, "sEv1", 1)
            p2m["ptr"] = p.val("sEv1")

        pe_transpose_to(sb["wT"][:, :],
                        V["amlw"][64:128, c * 128:(c + 1) * 128], False, 64,
                        ibase=64)
        if full:
            pe_transpose_to(sb["aclT"][:, :],
                            V["amlw"][0:64, c * 128:(c + 1) * 128], False, 64)
        pe_transpose_to(sb["B_T"][:, :], xsgv[:, 32, gl * 128:(gl + 1) * 128],
                        True, 128)
        if full:
            p.wait("tensor", "sEv1", p2m["ptr"])
            def fgt(E, gl=gl):
                return E.matmul(out=ptr[:, 0:128],
                                lhsT=xsgv[:, 32, gl * 128:(gl + 1) * 128],
                                rhs=xsgv[:, 33, gl * 128:(gl + 1) * 128],
                                start=True, stop=True)
            p.do("tensor", fgt, "sP", 1)
            p.wait("vector", "sP")
            def fegt(E):
                return E.tensor_mul(sb["GT_sb"][:, :], ptr[:, 0:128],
                                    sb["trib"][:, :])
            p.do("vector", fegt, "sEv1", 1)
            p2m["ptr"] = p.val("sEv1")

        p.wait("vector", "sP", p2m["xsTc"])
        for fb in range(32):
            p.wait("tensor", "sEv1", p2m["ptr"])
            def ftx(E, fb=fb, gl=gl):
                return E.transpose(out=ptrb[:, 0:128],
                                   in_=xsgv[:, fb, gl * 128:(gl + 1) * 128],
                                   identity=sb["identb"][:, :])
            p.do("tensor", ftx, "sP", 1)
            p.wait("vector", "sP")
            if full:
                def fxr(E, fb=fb):
                    return E.tensor_copy(
                        V["xs_T"][:, fb * 128:(fb + 1) * 128], ptrb[:, 0:128])
                p.do("vector", fxr, "sV", 1)
            def fxw(E, fb=fb):
                in1 = sb["wT"][:, 2 * fb:2 * fb + 2].rearrange(
                    "p (r o) -> p r o", o=1).to_broadcast([128, 2, 64])
                return E.tensor_tensor(
                    out=V["xsw_T"][:, fb * 128:(fb + 1) * 128].rearrange(
                        "p (r q) -> p r q", r=2),
                    in0=ptrb[:, 0:128].rearrange("p (r q) -> p r q", r=2),
                    in1=in1, op=ALU.mult)
            p.do("vector", fxw, "sEv1", 1)
            p2m["ptr"] = p.val("sEv1")

        if full:
            p.wait("vector", "sEv1")
            p.wait("gpsimd", "sEv1")
            for q in range(16):
                if q % 4 == 0:
                    q4 = q // 4
                    p.wait("sync", "sP", p2m["acrow"])
                    p.wait("sync", "sV")
                    def fgr(E, q4=q4, c=c):
                        srcg = view("ar1", 121344, 128, 2048, F32)[
                            rbase + 16 * q4:rbase + 16 * (q4 + 1),
                            c * 128:(c + 1) * 128]
                        dst = V["acrow"][0:1, :].rearrange(
                            "o (r t) -> o r t", r=16)
                        return E.dma_start(out=dst, in_=srcg)
                    p.do("sync", fgr, "sTr", 16)
                    p.wait("tensor", "sTr")
                pb = PV("acc0", 0, 512) if q % 2 == 0 else PV("acc0", 512,
                                                              1024)
                pbk = "pb0" if q % 2 == 0 else "pb1"
                ab = (sb["wbuf1"][:, 0:1024].bitcast(F32) if q % 2 == 0
                      else sb["wbuf1"][:, 1024:2048].bitcast(F32))
                p.wait("tensor", "sEv0", p2m[pbk])
                def fbc(E, q=q, pb=pb):
                    a0 = (q % 4) * 512
                    return E.matmul(out=pb, lhsT=sb["onesf"][:, :],
                                    rhs=V["acrow"][0:1, a0:a0 + 512],
                                    start=True, stop=True)
                p.do("tensor", fbc, "sP", 1)
                if q % 4 == 3:
                    p2m["acrow"] = p.val("sP")
                p.wait("vector", "sP")
                p.wait("vector", "sA", p2m.get(f"ab{q % 2}", 0))
                def fsub(E, q=q, pb=pb, ab=ab):
                    in1 = sb["aclT"][:, 4 * q:4 * q + 4].rearrange(
                        "p (r o) -> p r o", o=1).to_broadcast([128, 4, 128])
                    return E.tensor_tensor(
                        out=ab.rearrange("p (r t) -> p r t", r=4),
                        in0=pb.rearrange("p (r t) -> p r t", r=4),
                        in1=in1, op=ALU.subtract)
                p.do("vector", fsub, "sV", 1)
                p.wait("vector", "sV")
                def fcl(E, ab=ab):
                    return E.tensor_scalar_min(ab, ab, 80.0)
                p.do("vector", fcl, "sV", 1)
                p.wait("scalar", "sV")
                def fes(E, q=q, ab=ab):
                    return E.activation(
                        out=V["S_sb"][:, q * 512:(q + 1) * 512],
                        in_=ab, func=AF.Exp)
                p.do("scalar", fes, "sA", 1)
                p2m[f"ab{q % 2}"] = p.val("sA")
                def fee(E, q=q, pb=pb):
                    return E.activation(
                        out=V["Ce_sb"][:, q * 512:(q + 1) * 512],
                        in_=pb, func=AF.Exp)
                p.do("scalar", fee, "sEv0", 1)
                p2m[pbk] = p.val("sEv0")
                p.wait("gpsimd", "sA")
                def fsm(E, q=q):
                    in1 = sb["GT_sb"][:, :].rearrange(
                        "p (o t) -> p o t", o=1).to_broadcast([128, 4, 128])
                    return E.tensor_tensor(
                        out=V["S_sb"][:, q * 512:(q + 1) * 512].rearrange(
                            "p (r t) -> p r t", r=4),
                        in0=V["S_sb"][:, q * 512:(q + 1) * 512].rearrange(
                            "p (r t) -> p r t", r=4),
                        in1=in1, op=ALU.mult)
                p.do("gpsimd", fsm, "sG", 1)
                p.wait("gpsimd", "sEv0")
                def fcm(E, q=q, gl=gl):
                    in1 = xsgv[:, 33, gl * 128:(gl + 1) * 128].rearrange(
                        "p (o t) -> p o t", o=1).to_broadcast([128, 4, 128])
                    return E.tensor_tensor(
                        out=V["Ce_sb"][:, q * 512:(q + 1) * 512].rearrange(
                            "p (r t) -> p r t", r=4),
                        in0=V["Ce_sb"][:, q * 512:(q + 1) * 512].rearrange(
                            "p (r t) -> p r t", r=4),
                        in1=in1, op=ALU.mult)
                p.do("gpsimd", fcm, "sG", 1)
            p2m["acrow"] = p.val("sP")

            p.wait("vector", "sSc0", p2m["yg"])
            p.wait("tensor", "sG")
            p.wait("tensor", "sV")
            p.wait("tensor", "sA")
            for pp in range(32):
                py = (PV("acc0", 1024, 1536) if pp % 2 == 0
                      else PV("acc0", 1536, 2048))
                pyk = "py0" if pp % 2 == 0 else "py1"
                p.wait("tensor", "sEv1", p2m[pyk])
                p.wait("tensor", "sV", p2m["hbf"])
                def fy(E, pp=pp, py=py):
                    last = None
                    for rr in range(2):
                        r = 2 * pp + rr
                        po = py[64 * rr:64 * (rr + 1), 0:128]
                        last = E.matmul(
                            out=po,
                            lhsT=V["xs_T"][:, pp * 128 + 64 * rr:
                                           pp * 128 + 64 * (rr + 1)],
                            rhs=V["S_sb"][:, r * 128:(r + 1) * 128],
                            start=True, stop=False)
                        last = E.matmul(
                            out=po,
                            lhsT=V["h_bf"][:, pp * 128 + 64 * rr:
                                           pp * 128 + 64 * (rr + 1)],
                            rhs=V["Ce_sb"][:, r * 128:(r + 1) * 128],
                            start=False, stop=True)
                    return last
                p.do("tensor", fy, "sP", 1)
                p.wait("vector", "sP")
                def fye(E, pp=pp, py=py, gl=gl, c=c):
                    return E.scalar_tensor_tensor(
                        out=V["y_grp"][:, :].rearrange(
                            "p (f t) -> p f t", f=32)[:, pp, :],
                        in0=xsgv[:, pp, gl * 128:(gl + 1) * 128],
                        scalar=sb["Drep"][:, pp:pp + 1],
                        in1=py[:, 0:128],
                        op0=ALU.mult, op1=ALU.add)
                p.do("vector", fye, "sEv1", 1)
                p2m[pyk] = p.val("sEv1")
            p2m["sce"] = p.val("sP")

        def fgam(E, c=c):
            return E.dma_start(out=sb["grow"][:, :],
                               in_=sb["asumsT"][c:c + 1, :])
        p.do("sync", fgam, "sX0", 16)
        p.wait("tensor", "sX0")
        pbg = PV("acc0", 0, 512)
        p.wait("tensor", "sEv0", p2m["pb0"])
        def fbg(E):
            return E.matmul(out=pbg[:, 0:64], lhsT=sb["onesf"][:, :],
                            rhs=sb["grow"][0:1, :], start=True, stop=True)
        p.do("tensor", fbg, "sP", 1)
        p.wait("scalar", "sP")
        def feg(E):
            return E.activation(out=sb["Gexp"][:, :], in_=pbg[:, 0:64],
                                func=AF.Exp)
        p.do("scalar", feg, "sEv0", 1)
        p2m["pb0"] = p.val("sEv0")

        p.wait("gpsimd", "sEv0")
        p.wait("tensor", "sEv1")
        for w in range(8):
            pc = PV("acc1", 0, 512) if w % 2 == 0 else PV("acc1", 512, 1024)
            pck = "pc0" if w % 2 == 0 else "pc1"
            cvk = "cv0" if w % 2 == 0 else "cv1"
            p.wait("tensor", "sEv1", p2m[pck])
            p.wait("tensor", "sEv0", p2m[cvk])
            def fcb(E, w=w, pc=pc):
                return E.matmul(out=pc, lhsT=sb["B_T"][:, :],
                                rhs=V["xsw_T"][:, w * 512:(w + 1) * 512],
                                start=True, stop=True)
            p.do("tensor", fcb, "sP", 1)
            thv = V["th0"] if w % 2 == 0 else V["th1"]
            thk = "th0" if w % 2 == 0 else "th1"
            p.wait("gpsimd", "sEv1", p2m[thk])
            p.wait("gpsimd", "sEv1", p2m["hupd"][w])
            def fth(E, w=w, thv=thv):
                in1 = sb["Gexp"][:, 8 * w:8 * w + 8].rearrange(
                    "p (r o) -> p r o", o=1).to_broadcast([128, 8, 64])
                return E.tensor_tensor(
                    out=thv.rearrange("p (r q) -> p r q", r=8),
                    in0=V["h"][:, w * 512:(w + 1) * 512].rearrange(
                        "p (r q) -> p r q", r=8),
                    in1=in1, op=ALU.mult)
            p.do("gpsimd", fth, "sG", 1)
            p.wait("vector", "sG")
            p.wait("vector", "sP")
            def fhu(E, w=w, pc=pc, thv=thv):
                return E.tensor_add(V["h"][:, w * 512:(w + 1) * 512],
                                    thv, pc)
            p.do("vector", fhu, "sEv1", 1)
            p2m[pck] = p.val("sEv1")
            p2m["hupd"][w] = p.val("sEv1")
            p2m[thk] = p.val("sEv1")
        p2m["xsTc"] = p.val("sP")

        if full:
            p.wait("vector", "sEv1")
            def fhb(E):
                return E.tensor_copy(V["h_bf"][:, :], V["h"][:, :])
            p.do("vector", fhb, "sV", 1)
            p2m["hbf"] = p.val("sV")
            p.wait("sync", "sEv1")
            def fyf(E, c=c):
                src2 = V["y_grp"][:, :].rearrange("p (f t) -> p f t", f=32)
                dst = scr["y_scr"].rearrange(
                    "(f p) t -> p f t", p=128)[:, :,
                    c * 128:(c + 1) * 128]
                return E.dma_start(out=dst, in_=src2)
            p.do("sync", fyf, "sSc0", 16)
            p2m["yg"] = p.val("sSc0")

    def phase_p2_export(seg):
        V = p2v()
        ro = rows_of(seg)
        slot_key = "my_S_own" if seg == "own" else "my_S_red"
        as_key = "my_As_own" if seg == "own" else "my_As_red"
        p.wait("vector", "sEv1")
        p.wait("vector", "sV")
        def fat(E, ro=ro):
            return E.scalar_tensor_tensor(
                out=sb["small"][0:64, 8:9], in0=sb["offsi"][ro, 15:16],
                scalar=1.0, in1=sb["asums"][:, 15:16],
                op0=ALU.mult, op1=ALU.add)
        p.do("vector", fat, "sV", 1)
        p.wait("sync", "sV")
        p.wait("sync", "sEv1")
        def fws(E, rows, slot_key=slot_key, as_key=as_key):
            r = rows[slot_key]
            ra = rows[as_key]
            E.dma_start(
                out=scr["win_S"].rearrange("a b p n -> (a b p) n")[
                    r:r + 128, :],
                in_=V["h"][:, :]).then_inc(p.sems["sWin"], 16)
            return E.dma_start(
                out=scr["win_As"].rearrange("a b r -> (a b) r")[
                    ra:ra + 1, :],
                in_=sb["small"][0:64, 8:9])
        p.roled("sync", fws, "sWin", 16)
        p.inc("sWin", 16)
        if seg == "own":
            def fac(E):
                return E.dma_start(out=scr["acum_scr"][0:64, :],
                                   in_=view("ar1", 121344, 64, 2048, F32))
            p.do("sync", fac, "sWin", 16)
        p.wait("sync", "sWin")

    def write_flag(j):
        def ff(E, rows, j=j):
            r = rows[f"my_flag{j}"]
            return E.dma_start(
                out=scr["win_flag"].rearrange("a b c -> (a b) c")[r:r + 1, :],
                in_=sb["noncebuf"][:, :])
        p.roled("sync", ff, "sWin", 16)
        p.wait("sync", "sWin")

    def poll_flag(j):
        def fp(E, j=j):
            r0 = _WF(0, j)
            r1 = _WF(1, j)
            with E.register(f"pf{j}_{p.poll_ctr}a") as fv, \
                    E.register(f"pf{j}_{p.poll_ctr}b") as fw:
                E.reg_mov(fv, 1)
                with E.While(fv):
                    E.sem_clear(p.sems["sPoll"])
                    E.dma_start(
                        out=sb["flagrd"][:, 0:16],
                        in_=scr["win_flag"].rearrange(
                            "a b c -> (a b) c")[r0:r0 + 1, :]
                    ).then_inc(p.sems["sPoll"], 16)
                    E.dma_start(
                        out=sb["flagrd"][:, 16:32],
                        in_=scr["win_flag"].rearrange(
                            "a b c -> (a b) c")[r1:r1 + 1, :]
                    ).then_inc(p.sems["sPoll"], 16)
                    E.wait_ge(p.sems["sPoll"], 32)
                    E.reg_load(fv, sb["flagrd"][0:1, 0:1].bitcast(I32))
                    E.reg_sub(fv, fv, p.noncereg_val)
                    E.reg_load(fw, sb["flagrd"][0:1, 16:17].bitcast(I32))
                    E.reg_sub(fw, fw, p.noncereg_val)
                    E.reg_alu(fv, fv, fw, ALU.bitwise_or)
            return None
        if p.eng == "sync":
            fp(p.E)
            p.E.sem_inc(p.sems["sG"], 1)
        p.inc("sG", 1)
        p.poll_ctr += 1

    def exchange_combine(stage):
        V = p2v()
        dst = V["hst"] if stage == 1 else V["hin"]
        slots = ([(0, 0), (0, 1), (1, 0), (1, 1)] if stage == 1
                 else [(0, 0), (0, 1), (0, 2), (1, 0), (1, 1), (1, 2)])
        selA0 = 24 if stage == 1 else 36
        selB0 = 28 if stage == 1 else 42
        selG0 = 32 if stage == 1 else 48
        mgcol = 2 if stage == 1 else 5
        macol = 0 if stage == 1 else 3
        p.wait("sync", "sV")
        def fz(E, dst=dst):
            E.memset(V["SaS"][:, :], 0.0)
            E.memset(V["SbS"][:, :], 0.0)
            return E.memset(sb["grow"][:, :], 0.0)
        p.do("vector", fz, "sV", 1)
        for ki, (par, s) in enumerate(slots):
            wrow = _WS(par, s)
            arow = _WA(par, min(s, 1))
            wb = V["hwA"] if ki % 2 == 0 else V["hwB"]
            sxk = "sX0" if ki % 2 == 0 else "sX1"
            mk_key = f"exw{ki % 2}"
            p.wait("sync", "sV", p2m.get(mk_key, 0))
            def frd(E, wrow=wrow, wb=wb):
                return E.dma_start(
                    out=wb[:, :],
                    in_=scr["win_S"].rearrange("a b p n -> (a b p) n")[
                        wrow:wrow + 128, :])
            p.do("sync", frd, sxk, 16)
            def fga(E, arow=arow, ki=ki):
                return E.dma_start(
                    out=view("ar2", 32768, 1, 384, F32)[:, ki * 64:(ki + 1) * 64],
                    in_=scr["win_As"].rearrange("a b r -> (a b) r")[
                        arow:arow + 1, :])
            p.do("sync", fga, sxk, 16)
            p.wait("vector", sxk)
            p.wait("vector", "sV")
            def facc(E, ki=ki, wb=wb, selA0=selA0):
                return E.scalar_tensor_tensor(
                    out=V["SaS"][:, :], in0=wb[:, :],
                    scalar=sb["masks"][:, selA0 + ki:selA0 + ki + 1],
                    in1=V["SaS"][:, :], op0=ALU.mult, op1=ALU.add)
            p.do("vector", facc, "sV", 1)
            p.wait("vector", "sV")
            def facb(E, ki=ki, wb=wb, selB0=selB0):
                return E.scalar_tensor_tensor(
                    out=V["SbS"][:, :], in0=wb[:, :],
                    scalar=sb["masks"][:, selB0 + ki:selB0 + ki + 1],
                    in1=V["SbS"][:, :], op0=ALU.mult, op1=ALU.add)
            p.do("vector", facb, "sV", 1)
            p.wait("vector", "sV")
            def facg(E, ki=ki, selG0=selG0):
                return E.scalar_tensor_tensor(
                    out=sb["grow"][:, :],
                    in0=view("ar2", 32768, 1, 384, F32)[:, ki * 64:(ki + 1) * 64],
                    scalar=sb["masks"][0:1, selG0 + ki:selG0 + ki + 1],
                    in1=sb["grow"][:, :], op0=ALU.mult, op1=ALU.add)
            p.do("vector", facg, "sV", 1)
            p2m[f"exw{ki % 2}"] = p.val("sV")
        pbg = PV("acc0", 0, 512)
        p.wait("tensor", "sV")
        p.wait("tensor", "sEv0", p2m["pb0"])
        def fbg(E):
            return E.matmul(out=pbg[:, 0:64], lhsT=sb["onesf"][:, :],
                            rhs=sb["grow"][0:1, :], start=True, stop=True)
        p.do("tensor", fbg, "sP", 1)
        p.wait("scalar", "sP")
        def feg(E, mgcol=mgcol):
            return E.activation(out=V["G1e"][:, 0:64], in_=pbg[:, 0:64],
                                func=AF.Exp,
                                scale=sb["masks"][:, mgcol:mgcol + 1])
        p.do("scalar", feg, "sEv0", 1)
        p2m["pb0"] = p.val("sEv0")
        p.wait("scalar", "sEv0")
        def fmg(E, macol=macol):
            return E.activation(out=V["G1e"][:, 0:64], in_=V["G1e"][:, 0:64],
                                func=AF.Copy,
                                scale=sb["masks"][:, macol:macol + 1])
        p.do("scalar", fmg, "sA", 1)
        p.wait("vector", "sA")
        def fc1(E, dst=dst):
            in1 = V["G1e"][:, 0:64].rearrange(
                "p (r o) -> p r o", o=1).to_broadcast([128, 64, 64])
            return E.tensor_tensor(
                out=dst[:, :].rearrange("p (r q) -> p r q", r=64),
                in0=V["SaS"][:, :].rearrange("p (r q) -> p r q", r=64),
                in1=in1, op=ALU.mult)
        p.do("vector", fc1, "sV", 1)
        p.wait("vector", "sV")
        def fc2(E, dst=dst):
            return E.tensor_add(dst[:, :], dst[:, :], V["SbS"][:, :])
        p.do("vector", fc2, "sV", 1)
        p.wait("vector", "sV")
        if stage == 2:
            def fc3(E, dst=dst):
                return E.scalar_tensor_tensor(
                    out=dst[:, :], in0=V["hst"][:, :],
                    scalar=sb["masks"][:, 6:7], in1=dst[:, :],
                    op0=ALU.mult, op1=ALU.add)
            p.do("vector", fc3, "sV", 1)
            p.wait("vector", "sV")

    def phase_exchange():
        V = p2v()
        write_flag(1)
        poll_flag(1)
        p.wait("vector", "sG")
        p.wait("tensor", "sG")
        exchange_combine(1)
        p.wait("sync", "sV")
        def fws(E, rows):
            r = rows["my_stage1"]
            return E.dma_start(
                out=scr["win_S"].rearrange("a b p n -> (a b p) n")[
                    r:r + 128, :],
                in_=V["hst"][:, :])
        p.roled("sync", fws, "sWin", 16)
        p.wait("sync", "sWin")
        write_flag(2)
        poll_flag(2)
        p.wait("vector", "sG")
        p.wait("tensor", "sG")
        exchange_combine(2)
        hinb = view("ar2", 0, 128, 4096, BF16)
        def fhb(E, hinb=hinb):
            return E.tensor_copy(hinb, V["hin"][:, :])
        p.do("vector", fhb, "sV", 1)


    def p3v():
        return {
            "g_norm": view("ar1", 0, 128, NK * 2048, BF16),
            "hinb": view("ar2", 0, 128, 4096, BF16),
            "ytile": view("ar2", 8192, 128, 32 * 256, BF16),
            "ztile": view("ar2", 24576, 128, 32 * 256, BF16),
            "acrow3": sb["wbuf0"][:, :].bitcast(F32),
            "cdiag33": sb["evst0"][:, 0:512],
            "cchunk": sb["evst0"][:, 512:640],
            "acg": sb["evst1"][:, 0:256].bitcast(F32),
            "gsq": sb["evst1"][:, 1024:1536],
            "rstrow": view("ar2", 8192, 1, 2048, F32),
            "gtmp": sb["evst1"][:, 512:768].bitcast(F32),
        }

    def phase_p3a():
        V = p3v()
        gn3 = V["g_norm"][:, :].rearrange("p (f t) -> p f t", f=NK)
        def fcd(E):
            return E.dma_start(
                out=V["cdiag33"].rearrange("p (k c) -> p k c", k=KCONV),
                in_=ap["convdiag"][:, 33, :, :])
        p.do("gpsimd", fcd, "sW0", 16)
        def fnc(E):
            return E.dma_start(out=sb["nrmwc"][:, :], in_=ap["nrmwc"])
        p.do("sync", fnc, "sX0", 16)
        def fhd(E):
            return E.dma_start(out=sb["wrep"][0:64, :], in_=ap["hdrep"])
        p.do("gpsimd", fhd, "sW0", 16)
        eac = sb["wbuf1"][0:64, 0:256]
        acgv = sb["wbuf1"][:, :].bitcast(F32)[0:64, 256:512]
        slab = sb["wbuf0"][:, 0:260]
        cc = sb["wbuf0"][:, 512:768]
        for s in range(8):
            par = s % 2
            c0 = 2 * s
            ybuf = view("ar2", 8192 + par * 16384, 128, 32 * 256, BF16)
            yeng, ysem = ("sync", "sSc0") if par == 0 else ("scalar", "sSc1")
            p.wait(yeng, "sV", p2m.get(f"a3y{par}", 0))
            def fyl(E, c0=c0, ybuf=ybuf):
                return E.dma_start(
                    out=ybuf[:, :].rearrange("p (f t) -> p f t", f=32),
                    in_=scr["y_scr"].rearrange(
                        "(f p) t -> p f t",
                        p=128)[:, :, c0 * 128:(c0 + 2) * 128])
            p.do(yeng, fyl, ysem, 16)
            p2m[f"a3yl{par}"] = p.val(ysem)
            p.wait("sync", "sP", p2m.get("a3ac", 0))
            def fac(E, s=s):
                return E.dma_start(
                    out=acgv,
                    in_=scr["acum_scr"][0:64, s * 256:(s + 1) * 256])
            p.do("sync", fac, "sX0", 16)
            p.wait("vector", "sX0")
            for i in range(2):
                def fofs(E, i=i, c0=c0):
                    return E.tensor_scalar(
                        out=acgv[:, i * 128:(i + 1) * 128],
                        in0=acgv[:, i * 128:(i + 1) * 128],
                        scalar1=sb["offsi"][0:64, c0 + i:c0 + i + 1],
                        scalar2=None, op0=ALU.add)
                p.do("vector", fofs, "sV", 1)
            p.wait("scalar", "sV")
            p.wait("scalar", "sP", p2m.get("a3ac", 0))
            def fexp(E):
                return E.activation(out=eac, in_=acgv, func=AF.Exp)
            p.do("scalar", fexp, "sA", 1)
            p2m["a3ex"] = p.val("sA")
            p.wait("gpsimd", "sP", p2m.get("a3cv", 0))
            def fsl(E, s=s):
                if s == 0:
                    return E.dma_start(
                        out=slab[:, 3:259],
                        in_=scr["xbc_scr"][33 * 128:34 * 128, 0:256])
                lo = s * 256 - 3
                return E.dma_start(
                    out=slab[:, 0:259],
                    in_=scr["xbc_scr"][33 * 128:34 * 128, lo:lo + 259])
            p.do("gpsimd", fsl, "sW1", 16)
            if s == 0:
                p.wait("vector", "sW1")
                def fhz(E):
                    return E.memset(slab[:, 0:3], 0.0)
                p.do("vector", fhz, "sV", 1)
                p.wait("vector", "sV")
                for kslot in range(4):
                    def fhm(E, kslot=kslot):
                        return E.scalar_tensor_tensor(
                            out=slab[:, 0:3],
                            in0=sb["halos4"][:, kslot * 136 + 33 * 4:
                                             kslot * 136 + 33 * 4 + 3],
                            scalar=sb["masks"][:, 16 + kslot:17 + kslot],
                            in1=slab[:, 0:3],
                            op0=ALU.mult, op1=ALU.add)
                    p.do("vector", fhm, "sV", 1)
                    p.wait("vector", "sV")
            pcv = PV("acc1", 0, 256)
            p.wait("tensor", "sW1")
            p.wait("tensor", "sW0")
            p.wait("tensor", "sA", p2m.get("a3sl", 0))
            if s == 0:
                p.wait("tensor", "sV")
            def fcm(E, pcv=pcv):
                last = None
                for k in range(KCONV):
                    last = E.matmul(
                        out=pcv,
                        lhsT=V["cdiag33"].rearrange(
                            "p (k cc) -> p k cc", k=KCONV)[:, k, :],
                        rhs=slab[:, k:k + 256],
                        start=(k == 0), stop=(k == KCONV - 1))
                return last
            p.do("tensor", fcm, "sP", 1)
            p.wait("scalar", "sP")
            def fsilu(E, pcv=pcv):
                return E.activation(out=cc, in_=pcv, func=AF.Silu)
            p.do("scalar", fsilu, "sA", 1)
            p2m["a3sl"] = p.val("sA")
            p2m["a3slv"] = p.val("sA")
            for pp in range(32):
                ppar = pp % 2
                pe = PV("acc1", 1024 + 512 * ppar, 1024 + 512 * ppar + 256)
                py = PV("acc0", 512 * ppar, 512 * ppar + 256)
                ebuf = sb["evst0"][:, 1024 + 256 * ppar:1280 + 256 * ppar]
                p.wait("tensor", "sA", p2m.get("a3ex", 0))
                p.wait("tensor", "sA", p2m.get(f"a3e{ppar}", 0))
                def fmb(E, pp=pp, pe=pe):
                    return E.matmul(
                        out=pe,
                        lhsT=sb["wrep"][0:64, pp * 128:(pp + 1) * 128],
                        rhs=sb["wbuf1"][0:64, 0:256],
                        start=True, stop=True)
                p.do("tensor", fmb, "sP", 1)
                p2m["a3mb"] = p.val("sP")
                p.wait("scalar", "sP", p2m.get("a3mb", 0))
                p.wait("scalar", "sV", p2m.get(f"a3b{ppar}", 0))
                def fev(E, pe=pe, ebuf=ebuf):
                    return E.activation(out=ebuf, in_=pe, func=AF.Copy)
                p.do("scalar", fev, "sA", 1)
                p2m[f"a3e{ppar}"] = p.val("sA")
                p.wait("tensor", "sA", p2m.get("a3slv", 0))
                p.wait("tensor", "sV", p2m.get(f"a3p{ppar}", 0))
                def fmy(E, pp=pp, py=py):
                    return E.matmul(
                        out=py,
                        lhsT=V["hinb"][:, pp * 128:(pp + 1) * 128],
                        rhs=cc, start=True, stop=True)
                p.do("tensor", fmy, "sP", 1)
                p2m["a3my"] = p.val("sP")
                p.wait("vector", "sA", p2m.get(f"a3e{ppar}", 0))
                p.wait("vector", "sP", p2m.get("a3my", 0))
                def fd1(E, py=py, ebuf=ebuf):
                    return E.tensor_mul(py, py, ebuf)
                p.do("vector", fd1, "sV", 1)
                p2m[f"a3b{ppar}"] = p.val("sV")
                p.wait("vector", "sV")
                p.wait("vector", ysem, p2m.get(f"a3yl{par}", 0))
                def fd2(E, pp=pp, s=s, py=py, ybuf=ybuf):
                    yt = ybuf[:, :].rearrange(
                        "p (f t) -> p f t", f=32)[:, pp, :]
                    return E.tensor_add(
                        gn3[:, pp, s * 256:(s + 1) * 256], py, yt)
                p.do("vector", fd2, "sV", 1)
                p2m[f"a3p{ppar}"] = p.val("sV")
            p2m["a3ac"] = p.val("sP")
            p2m["a3cv"] = p.val("sP")
            p2m[f"a3y{par}"] = p.val("sV")
        barrier()
        for tt in range(4):
            prs = PV("acc1", 0, 512)
            p.wait("tensor", "sEv0", p2m.get("p3rs", 0))
            def frs(E, tt=tt):
                last = None
                for k in range(NK):
                    last = E.matmul(
                        out=prs[0:1, :],
                        lhsT=sb["onesb"][:, :],
                        rhs=V["g_norm"][:, :].rearrange(
                            "p (f t) -> p f t", f=NK)[:, k,
                            tt * 512:(tt + 1) * 512].rearrange(
                            "p t -> p (t)") if False else
                        V["gsq"][:, :],
                        start=(k == 0), stop=(k == NK - 1))
                return last
            def frs2(E, tt=tt, prs=prs):
                last = None
                for k in range(NK):
                    E.wait_ge(p.sems["sA"], 0)
                return last
            del frs, frs2
            for k in range(NK):
                zidx = tt * NK + k
                zpar = zidx % 2
                zeng, zsem = (("sync", "sSc0") if zpar == 0
                              else ("gpsimd", "sW1"))
                zk = sb["wbuf1"][:, 1024 + 512 * zpar:1536 + 512 * zpar]
                gsqp = sb["evst1"][:, 1024 + 512 * zpar:1536 + 512 * zpar]
                p.wait(zeng, "sV", p2m.get(f"z3{zpar}", 0))
                def fzl(E, tt=tt, k=k, zk=zk):
                    return E.dma_start(
                        out=zk,
                        in_=scr["zsil_scr"][k * 128:(k + 1) * 128,
                                            tt * 512:(tt + 1) * 512])
                p.do(zeng, fzl, zsem, 16)
                p.wait("vector", zsem)
                def fzm(E, tt=tt, k=k, zk=zk):
                    sl = V["g_norm"][:, :].rearrange(
                        "p (f t) -> p f t", f=NK)[:, k,
                        tt * 512:(tt + 1) * 512]
                    return E.tensor_mul(sl, sl, zk)
                p.do("vector", fzm, "sV", 1)
                p2m[f"z3{zpar}"] = p.val("sV")
                p.wait("scalar", "sV", p2m.get(f"z3{zpar}", 0))
                p.wait("scalar", "sP", p2m.get(f"p3sq{zpar}", 0))
                def fsq(E, tt=tt, k=k, gsqp=gsqp):
                    return E.activation(
                        out=gsqp,
                        in_=V["g_norm"][:, :].rearrange(
                            "p (f t) -> p f t", f=NK)[:, k,
                            tt * 512:(tt + 1) * 512],
                        func=AF.Square)
                p.do("scalar", fsq, "sA", 1)
                p.wait("tensor", "sA")
                def fmm(E, k=k, prs=prs, gsqp=gsqp):
                    return E.matmul(out=prs[0:1, :], lhsT=sb["onesb"][:, :],
                                    rhs=gsqp,
                                    start=(k == 0), stop=(k == NK - 1))
                p.do("tensor", fmm, "sP", 1)
                p2m[f"p3sq{zpar}"] = p.val("sP")
            p.wait("scalar", "sP")
            def fsr(E, tt=tt):
                return E.activation(out=p3v()["rstrow"][0:1, tt * 512:
                                                     (tt + 1) * 512],
                                    in_=prs[0:1, :], func=AF.Sqrt,
                                    scale=1.0 / DS, bias=EPS)
            p.do("scalar", fsr, "sEv0", 1)
            p2m["p3rs"] = p.val("sEv0")
        p.wait("vector", "sEv0")
        def frc(E):
            return E.reciprocal(p3v()["rstrow"][0:1, :], p3v()["rstrow"][0:1, :])
        p.do("vector", frc, "sV", 1)
        for tt in range(4):
            prb = PV("acc1", 512, 1024)
            p.wait("tensor", "sV")
            p.wait("tensor", "sEv1", p2m.get("p3rb", 0))
            def fbb(E, tt=tt, prb=prb):
                return E.matmul(out=prb, lhsT=sb["onesf"][:, :],
                                rhs=p3v()["rstrow"][0:1, tt * 512:
                                                 (tt + 1) * 512],
                                start=True, stop=True)
            p.do("tensor", fbb, "sP", 1)
            p.wait("vector", "sP")
            for k in range(NK):
                def fap(E, tt=tt, k=k, prb=prb):
                    sl = V["g_norm"][:, :].rearrange(
                        "p (f t) -> p f t", f=NK)[:, k,
                        tt * 512:(tt + 1) * 512]
                    return E.scalar_tensor_tensor(
                        out=sl, in0=sl, scalar=sb["nrmwc"][:, k:k + 1],
                        in1=prb, op0=ALU.mult, op1=ALU.mult)
                p.do("vector", fap, "sEv1", 1)
            p2m["p3rb"] = p.val("sEv1")

        barrier()
        for n in range(32):
            pr = n % 2
            wb = sb["wbuf0"] if pr == 0 else sb["wbuf1"]
            pb = ps["acc0"] if pr == 0 else ps["acc1"]
            eb = sb["evst0"] if pr == 0 else sb["evst1"]
            p.wait("gpsimd", "sP", p2m.get(f"p3w{pr}", 0))
            def fw(E, n=n, wb=wb):
                return E.dma_start(
                    out=wb[:, :].rearrange("p (k c) -> p k c", k=NK),
                    in_=ap["w_outproj"][n, :, :, :])
            p.do("gpsimd", fw, f"sW{pr}", 16)
            p.wait("tensor", f"sW{pr}")
            p.wait("tensor", "sEv0" if pr == 0 else "sEv1",
                   p2m.get(f"p3e{pr}", 0))
            def fm(E, wb=wb, pb=pb):
                last = None
                for k in range(NK):
                    lhsT = wb[:, :].rearrange("p (k c) -> p k c",
                                              k=NK)[:, k, :]
                    for t in range(4):
                        last = E.matmul(
                            out=pb[:, t * 512:(t + 1) * 512], lhsT=lhsT,
                            rhs=V["g_norm"][:, :].rearrange(
                                "p (f t) -> p f t", f=NK)[:, k,
                                t * 512:(t + 1) * 512],
                            start=(k == 0), stop=(k == NK - 1))
                return last
            p.do("tensor", fm, "sP", 1)
            p2m[f"p3w{pr}"] = p.val("sP")
            sevn = "sEv0" if pr == 0 else "sEv1"
            seng = "scalar" if pr == 0 else "vector"
            p.wait(seng, "sP")
            p.wait(seng, "sSc0" if pr == 0 else "sSc1",
                   p2m.get(f"p3o{pr}", 0))
            def fe(E, pb=pb, eb=eb, pr=pr):
                if pr == 0:
                    return E.activation(out=eb[:, :], in_=pb[:, :],
                                        func=AF.Copy)
                return E.tensor_copy(eb[:, :], pb[:, :])
            p.do(seng, fe, sevn, 1)
            p2m[f"p3e{pr}"] = p.val(sevn)
            p.wait("sync", sevn)
            def fo(E, n=n, eb=eb):
                return E.dma_start(
                    out=scr["attn_scr"][n * 128:(n + 1) * 128, :],
                    in_=eb[:, :])
            p.do("sync", fo, "sSc0" if pr == 0 else "sSc1", 16)
            p2m[f"p3o{pr}"] = p.val("sSc0" if pr == 0 else "sSc1")

    def p3bv():
        return {
            "x1f": view("ar1", 0, 128, NK * 2048, BF16),
            "attn_t": view("ar2", 0, 128, H, BF16),
            "xrow": view("ar2", 8192, 128, H, F32),
            "srow": view("ar2", 8192, 128, H, F32),
            "x1t": view("ar2", 24576, 128, H, BF16),
            "qsb2": view("ar2", 0, 16, 2048, BF16),
            "scs": view("ar2", 4096, 64, 2048, F32),
            "gatef": view("ar2", 12288, 64, 2048, BF16),
            "z2g": view("ar2", 16384, 128, 4 * 2048, BF16),
            "o2t": view("ar2", 32768, 128, 16 * 128, BF16),
            "wqs": view("ar2", 36864, 128, NK * 16, BF16),
            "expE": view("ar2", 37888, 64, 4 * 128, BF16),
            "sct": view("ar2", 38912, 128, 256, F32),
            "sct2": view("ar2", 39936, 128, 64, BF16),
            "keyss": view("ar2", 40064, 16, 64, BF16),
        }

    def phase_p3b_ln1():
        V = p3bv()
        x1f3 = V["x1f"][:, :].rearrange("p (f t) -> p f t", f=NK)
        for c in range(16):
            p.wait("sync", "sV", p2m.get("p3battn", 0))
            p.wait("gpsimd", "sV", p2m.get("p3battn", 0))
            def fat(E, c=c):
                return E.dma_start(
                    out=sb["evst0"][:, :].rearrange(
                        "p (f t) -> p f t", f=16),
                    in_=scr["attn_scr"].rearrange(
                        "(f p) t -> p f t",
                        p=128)[:, 0:16, c * 128:(c + 1) * 128])
            p.do("sync", fat, "sSc0", 16)
            def fat2(E, c=c):
                return E.dma_start(
                    out=sb["evst1"][:, :].rearrange(
                        "p (f t) -> p f t", f=16),
                    in_=scr["attn_scr"].rearrange(
                        "(f p) t -> p f t",
                        p=128)[:, 16:32, c * 128:(c + 1) * 128])
            p.do("gpsimd", fat2, "sW0", 16)
            def fxa(E, c=c):
                return E.dma_start(out=V["xrow"][:, 0:2048],
                                   in_=ap["xin"][c * 128:(c + 1) * 128,
                                                 0:2048])
            p.do("sync", fxa, "sX0", 16)
            def fxb(E, c=c):
                return E.dma_start(out=V["xrow"][:, 2048:4096],
                                   in_=ap["xin"][c * 128:(c + 1) * 128,
                                                 2048:4096])
            p.do("gpsimd", fxb, "sW1", 16)
            p.wait("scalar", "sSc0")
            def ftr(E):
                return E.dma_start(
                    out=V["attn_t"][:, 0:2048].rearrange(
                        "p (f q) -> p f q", f=16),
                    in_=sb["evst0"][:, :], transpose=True)
            p.do("scalar", ftr, "sTr", 16)
            p.wait("scalar", "sW0")
            def ftr2(E):
                return E.dma_start(
                    out=V["attn_t"][:, 2048:4096].rearrange(
                        "p (f q) -> p f q", f=16),
                    in_=sb["evst1"][:, :], transpose=True)
            p.do("scalar", ftr2, "sTr", 16)
            p.wait("vector", "sTr")
            p.wait("vector", "sX0")
            p.wait("vector", "sW1")
            def fs1(E):
                return E.scalar_tensor_tensor(
                    out=V["srow"][:, :], in0=V["attn_t"][:, :], scalar=1.0,
                    in1=V["xrow"][:, :], op0=ALU.mult, op1=ALU.add,
                    accum_out=sb["small"][:, 9:10])
            p.do("vector", fs1, "sV", 1)
            p.wait("vector", "sV")
            def fs2(E):
                return E.tensor_scalar(
                    out=V["srow"][:, :], in0=V["srow"][:, :],
                    scalar1=sb["small"][:, 9:10], scalar2=-1.0 / H,
                    op0=ALU.mult if False else ALU.bypass, op1=ALU.bypass)
            del fs2
            def fmn(E):
                return E.tensor_scalar_mul(sb["small"][:, 10:11],
                                           sb["small"][:, 9:10], -1.0 / H)
            p.do("vector", fmn, "sV", 1)
            p.wait("vector", "sV")
            def fs3(E):
                return E.tensor_scalar(
                    out=V["srow"][:, :], in0=V["srow"][:, :],
                    scalar1=sb["small"][:, 10:11], scalar2=None,
                    op0=ALU.add)
            p.do("vector", fs3, "sV", 1)
            p.wait("scalar", "sV")
            def fsq(E):
                return E.activation(out=V["attn_t"][:, :],
                                    in_=V["srow"][:, :], func=AF.Square,
                                    accum_out=sb["small"][:, 11:12])
            p.do("scalar", fsq, "sA", 1)
            p.wait("scalar", "sA")
            def fsd(E):
                return E.activation(out=sb["small"][:, 12:13],
                                    in_=sb["small"][:, 11:12],
                                    func=AF.Sqrt, scale=1.0 / H, bias=LNEPS)
            p.do("scalar", fsd, "sA", 1)
            p.wait("vector", "sA")
            def frv(E):
                return E.reciprocal(sb["small"][:, 12:13],
                                    sb["small"][:, 12:13])
            p.do("vector", frv, "sV", 1)
            p.wait("vector", "sV")
            p.wait("vector", "sTr", p2m.get("l1t", 0))
            p.wait("vector", "sW0", p2m.get("l1w", 0))
            def fx1(E):
                return E.tensor_scalar_mul(V["x1t"][:, :], V["srow"][:, :],
                                           sb["small"][:, 12:13])
            p.do("vector", fx1, "sV", 1)
            p2m["p3battn"] = p.val("sV")
            p.wait("sync", "sV")
            p.wait("gpsimd", "sV")
            def fxw(E, c=c):
                return E.dma_start(
                    out=scr["x1t_scr"][c * 128:(c + 1) * 128, :],
                    in_=V["x1t"][:, :])
            p.do("gpsimd", fxw, "sW0", 16)
            p2m["l1w"] = p.val("sW0")
            def fxf(E, c=c):
                outap = x1f3[:, :, c * 128:(c + 1) * 128]
                return E.dma_start(out=outap, in_=V["x1t"][:, :],
                                   transpose=True)
            p.do("sync", fxf, "sTr", 16)
            p2m["l1t"] = p.val("sTr")
        p.wait("tensor", "sTr")

    def phase_p3b_ffn():
        V = p3bv()
        x1f3 = V["x1f"][:, :].rearrange("p (f t) -> p f t", f=NK)
        def fwq(E):
            return E.dma_start(
                out=V["wqs"][:, :].rearrange("p (k c) -> p k c", k=NK),
                in_=ap["w_ffnQ"])
        p.do("gpsimd", fwq, "sW0", 16)
        def fky(E):
            return E.dma_start(out=V["keyss"][:, :], in_=ap["keysT"])
        p.do("gpsimd", fky, "sW0", 16)
        def fex(E):
            return E.dma_start(
                out=V["expE"][:, :].rearrange("p (j c) -> p j c", j=4),
                in_=ap["expandE"])
        p.do("gpsimd", fex, "sW0", 16)
        p.wait("tensor", "sW0")
        pq = ps["acc0"][0:16, :]
        def fq(E, pq=pq):
            last = None
            for k in range(NK):
                for t in range(4):
                    last = E.matmul(
                        out=pq[:, t * 512:(t + 1) * 512],
                        lhsT=V["wqs"][:, :].rearrange(
                            "p (k c) -> p k c", k=NK)[:, k, :],
                        rhs=x1f3[:, k, t * 512:(t + 1) * 512],
                        start=(k == 0), stop=(k == NK - 1))
            return last
        p.do("tensor", fq, "sP", 1)
        p.wait("scalar", "sP")
        def feq(E, pq=pq):
            return E.activation(out=V["qsb2"][0:16, :], in_=pq,
                                func=AF.Copy)
        p.do("scalar", feq, "sEv0", 1)
        p.wait("tensor", "sEv0")
        psc = ps["acc1"][0:64, :]
        def fsc(E, psc=psc):
            last = None
            for t in range(4):
                last = E.matmul(out=psc[:, t * 512:(t + 1) * 512],
                                lhsT=V["keyss"][0:16, :],
                                rhs=V["qsb2"][0:16, t * 512:(t + 1) * 512],
                                start=True, stop=True)
            return last
        p.do("tensor", fsc, "sP", 1)
        p.wait("vector", "sP")
        def fes(E, psc=psc):
            return E.tensor_copy(V["scs"][:, :], psc)
        p.do("vector", fes, "sEv1", 1)
        for c in range(16):
            ptr2 = ps["acc0"][0:128, 1024:1536]
            p.wait("tensor", "sEv1")
            p.wait("tensor", "sEv0", p2m.get("p3g", 0))
            def ft1(E, c=c, ptr2=ptr2):
                return E.transpose(out=ptr2[:, 0:64],
                                   in_=V["scs"][:, c * 128:(c + 1) * 128],
                                   identity=sb["identf"][0:64, 0:64])
            p.do("tensor", ft1, "sP", 1)
            p.wait("vector", "sP")
            def fge(E, ptr2=ptr2):
                return E.tensor_copy(V["sct"][:, 0:64], ptr2[:, 0:64])
            p.do("vector", fge, "sV", 1)
            p.wait("vector", "sV")
            def fm8(E):
                return E.max(out=V["sct"][:, 64:72], in_=V["sct"][:, 0:64])
            p.do("vector", fm8, "sV", 1)
            p.wait("vector", "sV")
            def fmr(E):
                return E.match_replace(out=V["sct"][:, 128:192],
                                       in_to_replace=V["sct"][:, 64:72],
                                       in_values=V["sct"][:, 0:64],
                                       imm_value=-1e30)
            p.do("vector", fmr, "sV", 1)
            p.wait("vector", "sV")
            def fm8b(E):
                return E.max(out=V["sct"][:, 72:80],
                             in_=V["sct"][:, 128:192])
            p.do("vector", fm8b, "sV", 1)
            p.wait("vector", "sV")
            def fnm(E):
                return E.tensor_scalar_mul(V["sct"][:, 80:81],
                                           V["sct"][:, 64:65], -1.0)
            p.do("vector", fnm, "sV", 1)
            p.wait("scalar", "sV")
            def fex2(E):
                return E.activation(out=V["sct"][:, 128:192],
                                    in_=V["sct"][:, 0:64], func=AF.Exp,
                                    bias=V["sct"][:, 80:81])
            p.do("scalar", fex2, "sA", 1)
            p.wait("vector", "sA")
            def fmk(E):
                return E.tensor_scalar(out=V["sct"][:, 192:256],
                                       in0=V["sct"][:, 0:64],
                                       scalar1=V["sct"][:, 79:80],
                                       scalar2=None, op0=ALU.is_ge)
            p.do("vector", fmk, "sV", 1)
            p.wait("vector", "sV")
            def fme(E):
                return E.tensor_mul(V["sct"][:, 128:192],
                                    V["sct"][:, 128:192],
                                    V["sct"][:, 192:256])
            p.do("vector", fme, "sV", 1)
            p.wait("vector", "sV")
            def fsm(E):
                return E.reduce_sum(V["sct"][:, 81:82],
                                    V["sct"][:, 128:192],
                                    axis=mybir.AxisListType.X)
            p.do("vector", fsm, "sV", 1)
            p.wait("vector", "sV")
            def frz(E):
                return E.reciprocal(V["sct"][:, 81:82], V["sct"][:, 81:82])
            p.do("vector", frz, "sV", 1)
            p.wait("vector", "sV")
            def fgt(E):
                return E.tensor_scalar_mul(
                    V["sct"][:, 128:192].bitcast(BF16)[:, 0:64]
                    if False else V["sct2"][:, 0:64],
                    V["sct"][:, 128:192], V["sct"][:, 81:82])
            p.do("vector", fgt, "sV", 1)
            p.wait("tensor", "sV")
            ptg = ps["acc0"][0:64, 1536:1600].bitcast(BF16)
            def ft2(E, ptg=ptg):
                return E.transpose(out=ptg,
                                   in_=V["sct2"][:, 0:64],
                                   identity=sb["identb"][:, :])
            p.do("tensor", ft2, "sP", 1)
            p.wait("vector", "sP")
            def fgf(E, c=c, ptg=ptg):
                return E.tensor_copy(V["gatef"][:, c * 128:(c + 1) * 128],
                                     ptg)
            p.do("vector", fgf, "sEv0", 1)
            p2m["p3g"] = p.val("sEv0")
        for nb in range(4):
            pz = ps["acc0"][:, 0:2048]
            pgx = ps["acc1"][:, 0:2048]
            p.wait("gpsimd", "sP", p2m.get("p3za", 0))
            def fwa(E, nb=nb):
                return E.dma_start(
                    out=sb["wbuf1"][:, :].rearrange("p (k c) -> p k c",
                                                    k=NK),
                    in_=ap["w_ffnA"][nb, :, :, :])
            p.do("gpsimd", fwa, "sW1", 16)
            p.wait("tensor", "sW1")
            p.wait("tensor", "sV", p2m.get("p3zv", 0))
            def fz(E, pz=pz):
                last = None
                for k in range(NK):
                    for t in range(4):
                        last = E.matmul(
                            out=pz[:, t * 512:(t + 1) * 512],
                            lhsT=sb["wbuf1"][:, :].rearrange(
                                "p (k c) -> p k c", k=NK)[:, k, :],
                            rhs=x1f3[:, k, t * 512:(t + 1) * 512],
                            start=(k == 0), stop=(k == NK - 1))
                return last
            p.do("tensor", fz, "sP", 1)
            p2m["p3za"] = p.val("sP")
            def fgx(E, nb=nb, pgx=pgx):
                last = None
                for t in range(4):
                    last = E.matmul(
                        out=pgx[:, t * 512:(t + 1) * 512],
                        lhsT=V["expE"][:, :].rearrange(
                            "p (j c) -> p j c", j=4)[0:64, nb, :],
                        rhs=V["gatef"][:, t * 512:(t + 1) * 512],
                        start=True, stop=True)
                return last
            p.do("tensor", fgx, "sP", 1)
            p.wait("vector", "sP")
            def feg(E, pgx=pgx):
                return E.tensor_copy(sb["evst0"][:, :], pgx)
            p.do("vector", feg, "sV", 1)
            p.wait("vector", "sV")
            def fzg(E, nb=nb, pz=pz):
                return E.tensor_mul(
                    V["z2g"][:, nb * 2048:(nb + 1) * 2048],
                    sb["evst0"][:, :].to_broadcast([128, 2048])
                    if False else sb["evst0"][:, :], pz)
            def fzg2(E, nb=nb, pz=pz):
                return E.tensor_mul(
                    V["z2g"][:, nb * 2048:(nb + 1) * 2048],
                    sb["evst0"][:, :], pz)
            del fzg
            p.do("vector", fzg2, "sV", 1)
            p2m["p3zv"] = p.val("sV")
        for nf in range(32):
            pr = nf % 2
            po = ps["acc0"][:, 0:2048] if pr == 0 else ps["acc1"][:, 0:2048]
            eb = sb["evst0"] if pr == 0 else sb["evst1"]
            p.wait("gpsimd", "sP", p2m.get("p3wb", 0))
            if nf == 0:
                p.wait("gpsimd", "sV")
            wslot = sb["wbuf0"] if pr == 0 else sb["wbuf1"]
            def fwb(E, nf=nf, wslot=wslot):
                return E.dma_start(
                    out=wslot[:, 0:512].rearrange("p (k c) -> p k c", k=4),
                    in_=ap["w_ffnB"][nf, :, :, :])
            p.do("gpsimd", fwb, f"sW{pr}", 16)
            p.wait("tensor", f"sW{pr}")
            p.wait("tensor", "sV" if nf < 2 else "sEv1",
                   p2m.get(f"p3bo{pr}", 0))
            def fb(E, po=po, wslot=wslot):
                last = None
                for k in range(4):
                    for t in range(4):
                        last = E.matmul(
                            out=po[:, t * 512:(t + 1) * 512],
                            lhsT=wslot[:, 0:512].rearrange(
                                "p (k c) -> p k c", k=4)[:, k, :],
                            rhs=V["z2g"][:, k * 2048 + t * 512:
                                          k * 2048 + (t + 1) * 512],
                            start=(k == 0), stop=(k == 3))
                return last
            p.do("tensor", fb, "sP", 1)
            p2m["p3wb"] = p.val("sP")
            seng = "scalar" if pr == 0 else "vector"
            sevn = "sEv0" if pr == 0 else "sEv1"
            p.wait(seng, "sP")
            p.wait(seng, "sTr", p2m.get(f"p3ot{pr}", 0))
            def fe(E, po=po, eb=eb, pr=pr):
                if pr == 0:
                    return E.activation(out=eb[:, :], in_=po,
                                        func=AF.Copy, scale=FFN_SCALE)
                return E.tensor_scalar_mul(eb[:, :], po, FFN_SCALE)
            p.do(seng, fe, sevn, 1)
            p2m[f"p3bo{pr}"] = p.val(sevn)
            p.wait("sync", sevn)
            p.wait("sync", "sW0", p2m.get("p3fo0", 0))
            p.wait("sync", "sSc1", p2m.get("p3fo1", 0))
            def ftr(E, nf=nf, eb=eb):
                return E.dma_start(
                    out=V["o2t"][:, :].rearrange("p (cc f) -> p cc f",
                                                  cc=16),
                    in_=eb[:, :], transpose=True)
            p.do("sync", ftr, "sTr", 16)
            p2m[f"p3ot{pr}"] = p.val("sTr")
            oeng = "gpsimd" if pr == 0 else "scalar"
            p.wait(oeng, "sTr")
            def fo(E, nf=nf):
                return E.dma_start(
                    out=scr["out2t_scr"].rearrange(
                        "(cc p) f -> p cc f", p=128)[:, :,
                        nf * 128:(nf + 1) * 128],
                    in_=V["o2t"][:, :].rearrange("p (cc f) -> p cc f",
                                                  cc=16))
            p.do(oeng, fo, "sSc1" if pr else "sW0", 16)
            p2m[f"p3fo{pr}"] = p.val("sSc1" if pr else "sW0")

    def phase_p3b_final():
        V = p3bv()
        for c in range(16):
            par = c % 2
            at = V["attn_t"] if par == 0 else view("ar1", 0, 128, H, BF16)
            sr = V["srow"] if par == 0 else view("ar1", 8192, 128, H, F32)
            xt = V["x1t"] if par == 0 else view("ar1", 24576, 128, H, BF16)
            off = 9 if par == 0 else 13
            for eng in ("sync", "gpsimd", "scalar", "vector"):
                p.wait(eng, "sV", p2m.get(f"f3c{par}", 0))
            def fx1a(E, c=c, xt=xt):
                return E.dma_start(
                    out=xt[:, :],
                    in_=scr["x1t_scr"][c * 128:(c + 1) * 128, :])
            p.do("gpsimd", fx1a, "sW0", 16)
            def fo2a(E, c=c, at=at):
                return E.dma_start(
                    out=at[:, :],
                    in_=scr["out2t_scr"][c * 128:(c + 1) * 128, :])
            p.do("scalar", fo2a, "sX1", 16)
            p.wait("vector", "sW0")
            p.wait("vector", "sX1")
            p.wait("vector", "sOut", p2m.get(f"f3w{par}", 0))
            p.wait("vector", "sW1", p2m.get(f"f3wW{par}", 0))
            def fs1(E, at=at, sr=sr, xt=xt, off=off):
                return E.scalar_tensor_tensor(
                    out=sr[:, :], in0=at[:, :], scalar=1.0,
                    in1=xt[:, :], op0=ALU.mult, op1=ALU.add,
                    accum_out=sb["small"][:, off:off + 1])
            p.do("vector", fs1, "sV", 1)
            p.wait("vector", "sV")
            def fmn(E, off=off):
                return E.tensor_scalar_mul(sb["small"][:, off + 1:off + 2],
                                           sb["small"][:, off:off + 1],
                                           -1.0 / H)
            p.do("vector", fmn, "sV", 1)
            p.wait("vector", "sV")
            def fs3(E, sr=sr, off=off):
                return E.tensor_scalar(
                    out=sr[:, :], in0=sr[:, :],
                    scalar1=sb["small"][:, off + 1:off + 2], scalar2=None,
                    op0=ALU.add)
            p.do("vector", fs3, "sV", 1)
            p.wait("scalar", "sV")
            def fsq(E, at=at, sr=sr, off=off):
                return E.activation(out=at[:, :],
                                    in_=sr[:, :], func=AF.Square,
                                    accum_out=sb["small"][:, off + 2:off + 3])
            p.do("scalar", fsq, "sA", 1)
            p.wait("scalar", "sA")
            def fsd(E, off=off):
                return E.activation(out=sb["small"][:, off + 3:off + 4],
                                    in_=sb["small"][:, off + 2:off + 3],
                                    func=AF.Sqrt, scale=1.0 / H, bias=LNEPS)
            p.do("scalar", fsd, "sA", 1)
            p.wait("vector", "sA")
            def frv(E, off=off):
                return E.reciprocal(sb["small"][:, off + 3:off + 4],
                                    sb["small"][:, off + 3:off + 4])
            p.do("vector", frv, "sV", 1)
            p.wait("vector", "sV")
            def fyo2(E, sr=sr, off=off):
                return E.tensor_scalar_mul(sr[:, :], sr[:, :],
                                           sb["small"][:, off + 3:off + 4])
            p.do("vector", fyo2, "sV", 1)
            p2m[f"f3c{par}"] = p.val("sV")
            p.wait("sync", "sV")
            p.wait("gpsimd", "sV")
            p.wait("scalar", "sV")
            def fwa(E, c=c, sr=sr):
                return E.dma_start(
                    out=ap["yout"][c * 128:(c + 1) * 128, 0:2048],
                    in_=sr[:, 0:2048])
            p.do("sync", fwa, "sOut", 16)
            def fwb(E, c=c, sr=sr):
                return E.dma_start(
                    out=ap["yout"][c * 128:(c + 1) * 128, 2048:3072],
                    in_=sr[:, 2048:3072])
            p.do("gpsimd", fwb, "sW1", 16)
            def fwc(E, c=c, sr=sr):
                return E.dma_start(
                    out=ap["yout"][c * 128:(c + 1) * 128, 3072:4096],
                    in_=sr[:, 3072:4096])
            p.do("scalar", fwc, "sOut", 16)
            p2m[f"f3w{par}"] = p.val("sOut")
            p2m[f"f3wW{par}"] = p.val("sW1")
        p.wait("sync", "sOut")
        p.wait("sync", "sW1")

    def phase_debug():
        srcmap = {
            "zsil_scr": scr["zsil_scr"],
            "xbc_scr": scr["xbc_scr"],
            "y_scr": scr["y_scr"],
            "attn_scr": scr["attn_scr"],
            "x1f_scr": scr["x1f_scr"],
        }
        p.wait("sync", "sSc0")
        p.wait("sync", "sSc1")
        p.wait("sync", "sV")
        p.wait("sync", "sA")
        for nm in cfg["debug"]:
            if nm == "hS_dbg":
                def fd(E):
                    return E.dma_start(
                        out=ap["dbg_hS_dbg"],
                        in_=scr["win_S"].rearrange("a b p n -> (a b p) n"))
                p.do("sync", fd, "sOut", 16)
                continue
            if nm == "hin_dbg":
                def fd(E):
                    return E.dma_start(out=ap["dbg_hin_dbg"],
                                       in_=view("ar1", 49152, 128, 4096, F32))
                p.do("sync", fd, "sOut", 16)
                continue
            if nm == "dtp_dbg":
                def fd(E):
                    return E.dma_start(out=ap["dbg_dtp_dbg"],
                                       in_=scr["dtp_scr"][0:64, :])
            elif nm == "dtpr_dbg":
                def fd(E):
                    return E.dma_start(out=ap["dbg_dtpr_dbg"],
                                       in_=scr["dtp_scr"][64:128, :])
            else:
                def fd(E, nm=nm):
                    return E.dma_start(out=ap["dbg_" + nm], in_=srcmap[nm])
            p.do("sync", fd, "sOut", 16)

    def walk(eng_name, E):
        p.eng = eng_name
        p.E = E
        for k in list(p.cnt):
            p.cnt[k] = 0
        for k in list(inproj_marks):
            inproj_marks[k] = 0
        for k in list(p2m):
            if isinstance(p2m[k], list):
                p2m[k] = [0] * 8
            else:
                p2m[k] = 0
        p.poll_ctr = 0
        phase_init()
        barrier()
        ph = cfg["phases"]
        if "p1n_own" in ph:
            phase_p1_norm("own")
        if "p1i_own" in ph:
            phase_p1_inproj("own")
        if "p1n_red" in ph:
            barrier()
            phase_p1_norm("red")
        if "p1i_red" in ph:
            phase_p1_inproj("red")
            phase_halo_export()
        if "p2" in ph:
            barrier()
            write_flag(0)
            poll_flag(0)
            barrier()
            for seg in ("red", "own"):
                full = seg == "own"
                phase_p2_prep(seg)
                for g in range(4):
                    phase_p2_conv_group(seg, g)
                    for c in range(4 * g, 4 * g + 4):
                        phase_p2_chunk(seg, c, full)
                phase_p2_export(seg)
            phase_exchange()
        if "p3a" in ph:
            barrier()
            phase_p3a()
        if "p3b" in ph:
            barrier()
            phase_p3b_ln1()
            barrier()
            phase_p3b_ffn()
            barrier()
            phase_p3b_final()
        barrier()
        phase_debug()
        p.wait("sync", "sOut")

    with nc.Block() as block:
        @block.sync
        def _(E):
            walk("sync", E)

        @block.gpsimd
        def _(E):
            walk("gpsimd", E)

        @block.vector
        def _(E):
            walk("vector", E)

        @block.scalar
        def _(E):
            walk("scalar", E)

        @block.tensor
        def _(E):
            walk("tensor", E)


def _prep_base(inputs):
    w_in = _block_w(np.asarray(inputs["L0_attn_in_proj_w"], np.float32),
                    NK, NPROJ_T)
    w_out = _block_w(np.asarray(inputs["L0_attn_out_proj_w"], np.float32),
                     NK, 32)
    A_w = np.asarray(inputs["L0_ffn_A_w"], np.float32)
    B_w = np.asarray(inputs["L0_ffn_B_w"], np.float32)
    Wq = np.asarray(inputs["L0_ffn_Wq_w"], np.float32)
    keys = np.asarray(inputs["L0_ffn_keys"], np.float32)
    w_A = _block_w(np.ascontiguousarray(A_w.T), NK, 4)
    w_B = _block_w(np.ascontiguousarray(B_w.T), 4, 32)
    w_Q = np.ascontiguousarray(
        Wq.T.reshape(NK, 128, 16).transpose(1, 0, 2))
    mup = np.asarray(inputs["L0_attn_mup_vector"], np.float32)
    mup_pad = np.zeros(NPROJ_T * 128, np.float32)
    mup_pad[:PROJ] = mup
    mupc = np.ascontiguousarray(mup_pad.reshape(NPROJ_T, 128).T)
    ln_w = np.asarray(inputs["L0_attn_ln_w"], np.float32)
    nw = np.asarray(inputs["L0_attn_norm_w"], np.float32)
    A_log = np.asarray(inputs["L0_attn_A_log"], np.float32)
    return {
        "w_inproj": w_in,
        "w_outproj": w_out,
        "w_ffnA": w_A,
        "w_ffnB": w_B,
        "w_ffnQ": w_Q,
        "keysT": np.ascontiguousarray(keys.T),
        "convdiag": _conv_diag(np.asarray(inputs["L0_attn_conv_w"],
                                          np.float32)),
        "lnw_rep": np.ascontiguousarray(np.repeat(ln_w[None, :], 128, 0)),
        "nrmw_rep": np.ascontiguousarray(np.repeat(nw[None, :], 128, 0)),
        "nrmwc": np.ascontiguousarray(nw.reshape(NK, 128).T),
        "mupc": mupc,
        "dt_bias": np.asarray(inputs["L0_attn_dt_bias"],
                              np.float32)[:, None],
        "A_log": np.ascontiguousarray(A_log[:, None]),
        "Dvec": np.asarray(inputs["L0_attn_D"], np.float32)[:, None],
        "expandE": _expand_mat(),
        "ident": np.eye(128, dtype=np.float32),
        "hdrep": (np.arange(64)[:, None] ==
                  (2 * (np.arange(4096)[None, :] // 128) +
                   (np.arange(4096)[None, :] % 128) // 64))
        .astype(np.float32),
        "tri": np.triu(np.ones((128, 128), np.float32)),
        "Drep": np.ascontiguousarray(
            np.asarray(inputs["L0_attn_D"], np.float32)[
                (2 * np.arange(32)[None, :] +
                 (np.arange(128)[:, None] >= 64))]),
    }


def _per_core_maps(inputs, base):
    x = np.asarray(inputs["L0_attn_hidden_states"], dtype=np.float32)
    nonce = (np.random.rand(1, 16).astype(np.float32) + 1.0)
    in_maps = []
    for core in range(8):
        b = core // 4
        q = core % 4
        own = x[b, q * 2048:(q + 1) * 2048, :]
        if q == 2:
            red = x[b, 2048:4096, :]
        elif q == 3:
            red = x[b, 0:2048, :]
        else:
            red = own
        par = core % 2
        opp = 1 - par
        cfg_f = np.zeros((1, 64), np.float32)
        if q == 1:
            cfg_f[0, 16 + opp * 2 + 0] = 1
            cfg_f[0, 28 + opp * 2 + 0] = 1
            cfg_f[0, 6] = 1
        elif q == 2:
            cfg_f[0, 16 + par * 2 + 1] = 1
            cfg_f[0, 20 + opp * 2 + 1] = 1
            cfg_f[0, 24 + opp * 2 + 1] = 1
            cfg_f[0, 28 + par * 2 + 1] = 1
            cfg_f[0, 32 + par * 2 + 1] = 1
            cfg_f[0, 0] = 1
            cfg_f[0, 2] = 1
            cfg_f[0, 6] = 1
        elif q == 3:
            cfg_f[0, 16 + opp * 2 + 0] = 1
            cfg_f[0, 36 + opp * 3 + 2] = 1
            cfg_f[0, 42 + opp * 3 + 0] = 1
            cfg_f[0, 48 + opp * 3 + 0] = 1
            cfg_f[0, 3] = 1
            cfg_f[0, 5] = 1
        cfg_i = np.zeros((1, 64), np.int32)
        m = {k: v for k, v in base.items() if not k.endswith("_")}
        m["xin"] = np.ascontiguousarray(np.concatenate([own, red], axis=0))
        m["cfg_f"] = cfg_f
        m["cfg_i"] = cfg_i
        m["nonce"] = nonce
        in_maps.append(m)
    return in_maps


_PROG_CACHE = {}


def _get_prog():
    key = (tuple(CFG["phases"]), tuple(CFG["debug"]))
    if key not in _PROG_CACHE:
        _PROG_CACHE[key] = build_program(CFG)
    return _PROG_CACHE[key]


def run_cores(inputs):
    nc = _get_prog()
    base = _prep_base(inputs)
    in_maps = _per_core_maps(inputs, base)
    return run_bass_kernel_spmd(nc, in_maps, core_ids=list(range(8)))


def kernel(**inputs):
    inputs.pop("L0_attn_residual", None)
    res = run_cores(inputs)
    out = np.zeros((B, L, H), dtype=np.float32)
    for core in range(8):
        b = core // 4
        q = core % 4
        out[b, q * 2048:(q + 1) * 2048, :] = res.results[core]["yout"]
    return out
